# version 03
# baseline (speedup 1.0000x reference)
import numpy as np
import contextlib
import concourse.bass as bass
import concourse.mybir as mybir
from concourse.alu_op_type import AluOpType

F32 = mybir.dt.float32
BF16 = mybir.dt.bfloat16
AF = mybir.ActivationFunctionType
AX = mybir.AxisListType
OP = AluOpType

D = 1024
NCH = 8
LN_EPS = 1e-5


def chunks(total, step):
    return [(s, min(step, total - s)) for s in range(0, total, step)]


class Eng:
    def __init__(self, sem, step=1):
        self.sem = sem
        self.step = step
        self.cnt = 0

    def inc(self, inst):
        inst.then_inc(self.sem, self.step)
        self.cnt += 1
        return self.cnt

    def wait(self, h, cnt=None):
        c = self.cnt if cnt is None else cnt
        if c > 0:
            h.wait_ge(self.sem, c * self.step)


class DmaGroup:
    def __init__(self, eng):
        self.eng = eng
        self.target = eng.cnt

    def issue(self, inst):
        self.eng.inc(inst)
        self.target = self.eng.cnt

    def wait(self, h):
        if self.target > 0:
            h.wait_ge(self.eng.sem, self.target * 16)


class Arena:
    def __init__(self, tensor, nbytes_per_part):
        self.t = tensor
        self.nbytes = nbytes_per_part
        self.off = 0
        self.marks = []

    def push(self):
        self.marks.append(self.off)

    def pop(self):
        self.off = self.marks.pop()

    def alloc(self, cols, dtype):
        esz = 4 if dtype == F32 else 2
        self.off = (self.off + 63) // 64 * 64
        start = self.off
        self.off += cols * esz
        assert self.off <= self.nbytes, f"arena overflow: {self.off} > {self.nbytes}"
        base = self.t[:].bitcast(dtype) if dtype != BF16 else self.t[:]
        return base[:, start // esz: start // esz + cols]


class Ctx:
    def __init__(self, nc, es, T_LOC):
        self.nc = nc
        self.es = es
        self.T = T_LOC
        self.NT = T_LOC // 128

        sem = lambda n: es.enter_context(nc.semaphore(n))
        self.pe = Eng(sem("s_pe"))
        self.dve = Eng(sem("s_dve"))
        self.act = Eng(sem("s_act"))
        self.pool = Eng(sem("s_pool"))
        self.q_cc = Eng(sem("q_cc"), step=16)
        self._dsems = [Eng(sem(f"dq{i}"), step=16) for i in range(24)]
        self._dsi = 0
        self._hsems = [Eng(sem(f"hq{i}"), step=16) for i in range(16)]
        self._hsi = 0

        sb = lambda n, shape, dt: es.enter_context(nc.sbuf_tensor(n, shape, dt))
        self.X = sb("X", [128, self.NT * D], F32)
        ar_cols = (206 * 1024 - self.NT * D * 4 - 8 * 1024) // 2
        self.AR = sb("AR", [128, ar_cols], BF16)
        self.arena = Arena(self.AR, ar_cols * 2)
        self.COL = sb("COLS", [128, 768], F32)
        self.colmap = {}
        self.coloff = 0
        self.constvals = [1.0, LN_EPS, 1e-9, 1e-12, 1e-8, 0.5, 1e-24]
        self.constmap = {}
        self.ones_row = sb("ones_row", [1, 128], BF16)
        self.ones_rowf = sb("ones_rowf", [1, 128], F32)
        self.ident = sb("ident", [128, 128], BF16)
        self.identf = sb("identf", [128, 128], F32)
        self.CB = sb("CB", [128, 2 * D], BF16)
        self.ps = [es.enter_context(nc.psum_tensor(f"ps{i}", [128, 512], F32)) for i in range(7)]
        self.pt = es.enter_context(nc.psum_tensor("pt", [128, 1024], BF16))
        self._bank = 0

    def dgroup(self):
        e = self._dsems[self._dsi % len(self._dsems)]
        self._dsi += 1
        return DmaGroup(e)

    def hgroup(self):
        e = self._hsems[self._hsi % len(self._hsems)]
        self._hsi += 1
        return DmaGroup(e)

    def cols(self, name, n=None):
        if name not in self.colmap:
            assert n is not None
            self.colmap[name] = (self.coloff, n)
            self.coloff += n
            assert self.coloff <= 768
        o, n2 = self.colmap[name]
        return self.COL[:, o:o + n2]

    def bank(self, i=None):
        if i is None:
            b = self.ps[self._bank]
            self._bank = (self._bank + 1) % 7
            return b
        return self.ps[i]

    def constcol(self, val):
        assert val in self.constmap, f"const {val} not pre-registered"
        return self.cols(self.constmap[val], 1)

    def fence_all(self, blk):
        engs = [self.pe, self.dve, self.act, self.pool]
        qs = [self.q_cc] + self._dsems + self._hsems
        for name in ["tensor", "vector", "scalar", "gpsimd", "sync"]:
            def f(h):
                for e in engs + qs:
                    e.wait(h)
            getattr(blk, name)(f)

    def init_consts(self, blk):
        pool, dve = self.pool, self.dve
        for v_ in self.constvals:
            self.constmap[v_] = f"c_{v_}"
            self.cols(self.constmap[v_], 1)

        @blk.vector
        def _(v):
            v.memset(self.ones_row[:], 1.0)
            v.memset(self.ones_rowf[:], 1.0)
            v.memset(self.COL[:], 0.0)
            dve.inc(v.engine_nop())
            v.wait_ge(dve.sem, dve.cnt)
            for val in self.constvals:
                v.memset(self.cols(self.constmap[val]), val)
            dve.inc(v.engine_nop())

        @blk.gpsimd
        def _(g):
            g.memset(self.ident[:], 0.0)
            self.pool.inc(g.engine_nop())
            g.wait_ge(self.pool.sem, self.pool.cnt)
            g.affine_select(out=self.ident[:], in_=self.ident[:],
                            compare_op=OP.not_equal, fill=1.0,
                            base=0, pattern=[[-1, 128]], channel_multiplier=1)
            g.memset(self.identf[:], 0.0)
            self.pool.inc(g.engine_nop())
            g.wait_ge(self.pool.sem, self.pool.cnt)
            g.affine_select(out=self.identf[:], in_=self.identf[:],
                            compare_op=OP.not_equal, fill=1.0,
                            base=0, pattern=[[-1, 128]], channel_multiplier=1)
            self.pool.inc(g.engine_nop())



def load_weight(ctx, blk, dram_ap, dst, wait=None, group=None):
    K, N = dram_ap.shape
    kch = chunks(K, 128)
    grp = group or ctx.dgroup()

    @blk.gpsimd
    def _(g):
        if wait is not None:
            wait(g)
        for i, (k0, ksz) in enumerate(kch):
            grp.issue(g.dma_start(out=dst[0:ksz, i * N:(i + 1) * N],
                                  in_=dram_ap[k0:k0 + ksz, :]))
    return grp


def tm_to_fm(ctx, blk, src_f32, dst_fm, tiles=None, Tdst=None, scale_cols=None):
    tiles = list(range(ctx.NT)) if tiles is None else tiles
    Tdst = Tdst or ctx.T
    cast_cnt = {}
    tr_cnt = {}
    evac_cnt = {}
    dve_entry = ctx.dve.cnt
    for i, t in enumerate(tiles):
        @blk.vector
        def _(v, i=i, t=t):
            if i == 0:
                v.wait_ge(ctx.dve.sem, dve_entry)
            if i >= 2:
                ctx.pe.wait(v, tr_cnt[i - 2])
            src = src_f32[:, t * D:(t + 1) * D]
            if scale_cols is None:
                ctx.dve.inc(v.tensor_copy(out=ctx.CB[:, (i % 2) * D:(i % 2 + 1) * D], in_=src))
            else:
                ctx.dve.inc(v.tensor_scalar(out=ctx.CB[:, (i % 2) * D:(i % 2 + 1) * D],
                                            in0=src, scalar1=scale_cols(t), scalar2=None,
                                            op0=OP.mult))
            cast_cnt[i] = ctx.dve.cnt

        @blk.tensor
        def _(t_, i=i):
            ctx.dve.wait(t_, cast_cnt[i])
            if i >= 1:
                ctx.dve.wait(t_, evac_cnt[i - 1])
            for c in range(NCH):
                mm = t_.transpose(ctx.pt[:, c * 128:(c + 1) * 128],
                                  ctx.CB[:, (i % 2) * D + c * 128:(i % 2) * D + (c + 1) * 128],
                                  ctx.ident[:])
                ctx.pe.inc(mm)
            tr_cnt[i] = ctx.pe.cnt

        @blk.vector
        def _(v, i=i):
            ctx.pe.wait(v, tr_cnt[i])
            src3 = ctx.pt[:, 0:1024].rearrange("p (c j) -> p c j", c=NCH)
            dst3 = dst_fm.rearrange("p (c t) -> p c t", t=Tdst)[:, :, i * 128:(i + 1) * 128]
            ctx.dve.inc(v.tensor_copy(out=dst3, in_=src3))
            evac_cnt[i] = ctx.dve.cnt


def bias_mm(ctx, t_, psum_ap, brow_ap, start=False, stop=False):
    n = psum_ap.shape[-1]
    return t_.matmul(psum_ap, ctx.ones_row[:, 0:psum_ap.shape[0]], brow_ap[0:1, 0:n],
                     start=start, stop=stop)


def resid_update(ctx, blk, get_y, accum_name="lnsum", scale_y=None):
    sums = ctx.cols(accum_name, ctx.NT)

    @blk.vector
    def _(v):
        for t in range(ctx.NT):
            y = get_y(t)
            ctx.dve.inc(v.scalar_tensor_tensor(
                out=ctx.X[:, t * D:(t + 1) * D], in0=y,
                scalar=1.0 if scale_y is None else scale_y,
                in1=ctx.X[:, t * D:(t + 1) * D],
                op0=OP.mult, op1=OP.add,
                accum_out=sums[:, t:t + 1]))


def post_ln(ctx, blk, scr=None):
    sums = ctx.cols("lnsum")
    sumsq = ctx.cols("lnsq", ctx.NT)
    mu = ctx.cols("lnmu", ctx.NT)
    rstd = ctx.cols("lnrstd", ctx.NT)
    ctx.arena.push()
    scr = ctx.arena.alloc(D, BF16) if scr is None else scr
    dve0 = ctx.dve.cnt

    @blk.scalar
    def _(s):
        ctx.dve.wait(s)
        for t in range(ctx.NT):
            if t > 0:
                s.wait_ge(ctx.act.sem, ctx.act.cnt)
            ctx.act.inc(s.activation(out=scr[:, 0:D], in_=ctx.X[:, t * D:(t + 1) * D],
                                     func=AF.Square, accum_out=sumsq[:, t:t + 1]))

    @blk.vector
    def _(v):
        ctx.act.wait(v)
        ctx.dve.inc(v.tensor_scalar(out=mu, in0=sums, scalar1=1.0 / D, scalar2=None, op0=OP.mult))
        v.wait_ge(ctx.dve.sem, ctx.dve.cnt)
        ctx.dve.inc(v.tensor_tensor(out=rstd, in0=mu, in1=mu, op=OP.mult))
        v.wait_ge(ctx.dve.sem, ctx.dve.cnt)
        ctx.dve.inc(v.scalar_tensor_tensor(out=sumsq, in0=sumsq, scalar=1.0 / D,
                                           in1=rstd, op0=OP.mult, op1=OP.subtract))

    @blk.scalar
    def _(s):
        ctx.dve.wait(s)
        ctx.act.inc(s.activation(out=sumsq, in_=sumsq, func=AF.Ln, bias=ctx.constcol(LN_EPS)))
        s.wait_ge(ctx.act.sem, ctx.act.cnt)
        ctx.act.inc(s.activation(out=rstd, in_=sumsq, func=AF.Exp, scale=-0.5))

    @blk.vector
    def _(v):
        ctx.act.wait(v)
        for t in range(ctx.NT):
            ctx.dve.inc(v.tensor_scalar(out=ctx.X[:, t * D:(t + 1) * D],
                                        in0=ctx.X[:, t * D:(t + 1) * D],
                                        scalar1=mu[:, t:t + 1], scalar2=rstd[:, t:t + 1],
                                        op0=OP.subtract, op1=OP.mult))
    ctx.arena.pop()



def emit_L0a(ctx, blk, W):
    T, NT = ctx.T, ctx.NT
    ar = ctx.arena
    ar.push()
    xfm = ar.alloc(NCH * T, BF16)
    w1s = ar.alloc(8 * 1024, BF16)
    w23s = ar.alloc(8 * 32, BF16)
    rows = ar.alloc(2048, BF16)
    dscr = [ar.alloc(1024, F32) for _ in range(2)]
    escr = [ar.alloc(1024, F32) for _ in range(2)]
    uscr = [ar.alloc(1024, F32) for _ in range(2)]
    bc = ctx.cols("l0bc", 96)
    cb = ctx.cols("l0cb", 2)
    sums = ctx.cols("lnsum", NT)

    wg = ctx.dgroup()
    load_weight(ctx, blk, W["L0_W1T"], w1s, group=wg)
    load_weight(ctx, blk, W["L0_W23T"], w23s, group=wg)

    @blk.gpsimd
    def _(g):
        wg.issue(g.dma_start(out=rows[0:1, 0:1024], in_=W["L0_b1"][0:1, :]))
        wg.issue(g.dma_start(out=rows[0:1, 1024:1024 + 32], in_=W["L0_b23"][0:1, :]))

    tm_to_fm(ctx, blk, ctx.X[:], xfm)
    fm_dve = ctx.dve.cnt

    consumed = {}
    for t in range(NT):
        par = t % 2
        pb = [ctx.bank(), ctx.bank(), ctx.bank()]

        @blk.tensor
        def _(t_, t=t, pb=pb):
            if t == 0:
                ctx.dve.wait(t_, fm_dve)
                wg.wait(t_)
            if t >= 2:
                ctx.act.wait(t_, consumed[t - 2][0])
                ctx.dve.wait(t_, consumed[t - 2][1])
            for nb in range(2):
                for kc in range(NCH):
                    t_.matmul(pb[nb][:, 0:512],
                              xfm[:, kc * T + t * 128: kc * T + (t + 1) * 128],
                              w1s[:, kc * 1024 + nb * 512: kc * 1024 + (nb + 1) * 512],
                              start=(kc == 0), stop=False)
                ctx.pe.inc(bias_mm(ctx, t_, pb[nb][:, 0:512],
                                   rows[0:1, nb * 512:(nb + 1) * 512], stop=True))
            for kc in range(NCH):
                t_.matmul(pb[2][:, 0:32],
                          xfm[:, kc * T + t * 128: kc * T + (t + 1) * 128],
                          w23s[:, kc * 32:(kc + 1) * 32],
                          start=(kc == 0), stop=False)
            ctx.pe.inc(bias_mm(ctx, t_, pb[2][:, 0:32], rows[0:1, 1024:1024 + 32], stop=True))
        mm_done = ctx.pe.cnt

        @blk.scalar
        def _(s, t=t, pb=pb, par=par):
            ctx.pe.wait(s, mm_done)
            if t >= 2:
                ctx.act.wait(s, consumed[t - 2][0])
                ctx.dve.wait(s, consumed[t - 2][1])
            for nb in range(2):
                ctx.act.inc(s.activation(out=escr[par][:, nb * 512:(nb + 1) * 512],
                                         in_=pb[nb][:, 0:512], func=AF.Exp))
            s.wait_ge(ctx.act.sem, ctx.act.cnt)
            ctx.act.inc(s.activation(out=dscr[par][:, 0:1024], in_=escr[par][:, 0:1024],
                                     func=AF.Ln, bias=ctx.constcol(1.0)))
        act_done = ctx.act.cnt

        @blk.vector
        def _(v, t=t, pb=pb, par=par):
            ctx.pe.wait(v, mm_done)
            if t >= 2:
                ctx.dve.wait(v, consumed[t - 2][1])
            ctx.dve.inc(v.tensor_copy(out=bc[:, par * 48:par * 48 + 32], in_=pb[2][:, 0:32]))
            v.wait_ge(ctx.dve.sem, ctx.dve.cnt)
            ctx.dve.inc(v.tensor_tensor(out=bc[:, par * 48 + 32:par * 48 + 48],
                                        in0=bc[:, par * 48:par * 48 + 16],
                                        in1=bc[:, par * 48 + 16:par * 48 + 32],
                                        op=OP.mult))
            v.wait_ge(ctx.dve.sem, ctx.dve.cnt)
            ctx.dve.inc(v.tensor_scalar(out=bc[:, par * 48 + 32:par * 48 + 48],
                                        in0=bc[:, par * 48 + 32:par * 48 + 48],
                                        scalar1=1.0, scalar2=None, op0=OP.mult,
                                        op1=OP.add, accum_out=cb[:, par:par + 1]))
            ctx.act.wait(v, act_done)
            v.wait_ge(ctx.dve.sem, ctx.dve.cnt)
            ctx.dve.inc(v.tensor_scalar(out=uscr[par][:, 0:1024], in0=dscr[par][:, 0:1024],
                                        scalar1=cb[:, par:par + 1], scalar2=1.0,
                                        op0=OP.mult, op1=OP.add))
            v.wait_ge(ctx.dve.sem, ctx.dve.cnt)
            ctx.dve.inc(v.scalar_tensor_tensor(out=ctx.X[:, t * D:(t + 1) * D],
                                               in0=uscr[par][:, 0:1024], scalar=1.0,
                                               in1=ctx.X[:, t * D:(t + 1) * D],
                                               op0=OP.mult, op1=OP.mult,
                                               accum_out=sums[:, t:t + 1]))
        consumed[t] = (act_done, ctx.dve.cnt)

    post_ln(ctx, blk)
    ar.pop()



def emit_L0f(ctx, blk, W):
    T, NT = ctx.T, ctx.NT
    NE, H = 16, 256
    ar = ctx.arena
    ar.push()
    xfm = ar.alloc(NCH * T, BF16)
    selS = ar.alloc(8 * NE, BF16)
    keysS = [ar.alloc(8 * H, BF16) for _ in range(2)]
    valsS = [ar.alloc(2 * 1024, BF16) for _ in range(2)]
    sfm = ar.alloc(2 * 512, BF16)
    wts = ar.alloc(NT * NE, F32)
    probs = ctx.cols("l0f_probs", 16)
    top8 = ctx.cols("l0f_top8", 8)
    wcol = ctx.cols("l0f_w", 4)
    sums = ctx.cols("lnsum", NT)

    selg = ctx.dgroup()
    load_weight(ctx, blk, W["L0_selT"], selS, group=selg)

    rsg = ctx.hgroup()

    @blk.sync
    def _(s):
        rsg.issue(s.dma_start(out=wcol[:, 2:3],
                              in_=W["L0_rs"][0:1, 0:1].partition_broadcast(128).squeeze(1)))

    tm_to_fm(ctx, blk, ctx.X[:], xfm)
    fm_dve = ctx.dve.cnt

    for t in range(NT):
        b = ctx.bank()

        @blk.tensor
        def _(t_, t=t, b=b):
            if t == 0:
                ctx.dve.wait(t_, fm_dve)
                selg.wait(t_)
            if t >= 4:
                ctx.act.wait(t_)
            for kc in range(NCH):
                mm = t_.matmul(b[:, 0:NE], xfm[:, kc * T + t * 128: kc * T + (t + 1) * 128],
                               selS[:, kc * NE:(kc + 1) * NE], start=(kc == 0), stop=(kc == NCH - 1))
            ctx.pe.inc(mm)
        mmc = ctx.pe.cnt

        @blk.scalar
        def _(s, t=t, b=b):
            ctx.pe.wait(s, mmc)
            if t > 0:
                ctx.dve.wait(s)
            ctx.act.inc(s.activation(out=probs, in_=b[:, 0:NE], func=AF.Sigmoid))

        @blk.vector
        def _(v, t=t):
            ctx.act.wait(v)
            if t == 0:
                rsg.wait(v)
            ctx.dve.inc(v.max(out=top8, in_=probs))
            v.wait_ge(ctx.dve.sem, ctx.dve.cnt)
            ctx.dve.inc(v.tensor_scalar(out=wts[:, t * NE:(t + 1) * NE], in0=probs,
                                        scalar1=top8[:, 3:4], scalar2=None, op0=OP.is_ge))
            v.wait_ge(ctx.dve.sem, ctx.dve.cnt)
            ctx.dve.inc(v.tensor_tensor(out=wts[:, t * NE:(t + 1) * NE],
                                        in0=wts[:, t * NE:(t + 1) * NE], in1=probs, op=OP.mult,))
            v.wait_ge(ctx.dve.sem, ctx.dve.cnt)
            ctx.dve.inc(v.tensor_scalar(out=probs, in0=wts[:, t * NE:(t + 1) * NE],
                                        scalar1=1.0, scalar2=None, op0=OP.mult,
                                        op1=OP.add, accum_out=wcol[:, 0:1]))
            v.wait_ge(ctx.dve.sem, ctx.dve.cnt)
            ctx.dve.inc(v.tensor_scalar(out=wcol[:, 0:1], in0=wcol[:, 0:1],
                                        scalar1=ctx.constcol(1e-9), scalar2=None, op0=OP.max))
            v.wait_ge(ctx.dve.sem, ctx.dve.cnt)
            ctx.dve.inc(v.reciprocal(out=wcol[:, 1:2], in_=wcol[:, 0:1]))
            v.wait_ge(ctx.dve.sem, ctx.dve.cnt)
            ctx.dve.inc(v.tensor_scalar(out=wcol[:, 1:2], in0=wcol[:, 1:2],
                                        scalar1=wcol[:, 2:3], scalar2=None, op0=OP.mult))
            v.wait_ge(ctx.dve.sem, ctx.dve.cnt)
            ctx.dve.inc(v.tensor_scalar(out=wts[:, t * NE:(t + 1) * NE],
                                        in0=wts[:, t * NE:(t + 1) * NE],
                                        scalar1=wcol[:, 1:2], scalar2=None, op0=OP.mult))
    routing_done = ctx.dve.cnt

    lns = ctx.cols("l0f_lns", 2)
    expert_q = {}

    def load_expert(e, par):
        grp = ctx.dgroup()

        @blk.gpsimd
        def _(g, e=e, par=par):
            if e >= 2:
                ctx.pe.wait(g)
            for kc in range(NCH):
                grp.issue(g.dma_start(out=keysS[par][0:128, kc * H:(kc + 1) * H],
                                      in_=W["L0_keys"][e, kc * 128:(kc + 1) * 128, :]))
            for hc in range(2):
                grp.issue(g.dma_start(out=valsS[par][0:128, hc * 1024:(hc + 1) * 1024],
                                      in_=W["L0_vals"][e, hc * 128:(hc + 1) * 128, :]))
        expert_q[e] = grp

    load_expert(0, 0)
    for e in range(NE):
        par = e % 2
        if e + 1 < NE:
            load_expert(e + 1, (e + 1) % 2)
        for tg in range(NT // 4):
            sb_banks = [ctx.bank(), ctx.bank()]
            vb = [ctx.bank(), ctx.bank()]

            @blk.tensor
            def _(t_, e=e, par=par, tg=tg, sb_banks=sb_banks):
                expert_q[e].wait(t_)
                if e == 0 and tg == 0:
                    ctx.dve.wait(t_, routing_done)
                ctx.act.wait(t_)
                ctx.dve.wait(t_)
                for hc in range(2):
                    for kc in range(NCH):
                        mm = t_.matmul(sb_banks[hc][:, 0:512],
                                       keysS[par][:, kc * H + hc * 128: kc * H + (hc + 1) * 128],
                                       xfm[:, kc * T + tg * 512: kc * T + (tg + 1) * 512],
                                       start=(kc == 0), stop=(kc == NCH - 1))
                    ctx.pe.inc(mm)
            sc_mm = ctx.pe.cnt

            @blk.scalar
            def _(s, sb_banks=sb_banks, sc_mm=sc_mm):
                ctx.pe.wait(s, sc_mm)
                for hc in range(2):
                    if hc:
                        s.wait_ge(ctx.act.sem, ctx.act.cnt)
                    ctx.act.inc(s.activation(out=sfm[:, hc * 512:(hc + 1) * 512],
                                             in_=sb_banks[hc][:, 0:512], func=AF.Relu))
            relu_done = ctx.act.cnt

            for ti in range(4):
                mm_done = {}

                @blk.tensor
                def _(t_, e=e, par=par, tg=tg, ti=ti, relu_done=relu_done, mm_done=mm_done):
                    if ti == 0:
                        ctx.act.wait(t_, relu_done)
                    ctx.dve.wait(t_)
                    for nb in range(2):
                        for hc in range(2):
                            mm = t_.matmul(vb[nb][:, 0:512],
                                           sfm[:, hc * 512 + ti * 128: hc * 512 + (ti + 1) * 128],
                                           valsS[par][:, hc * 1024 + nb * 512: hc * 1024 + nb * 512 + 512],
                                           start=(hc == 0), stop=(hc == 1))
                        ctx.pe.inc(mm)
                        mm_done[nb] = ctx.pe.cnt

                @blk.vector
                def _(v, e=e, tg=tg, ti=ti, mm_done=mm_done):
                    t = tg * 4 + ti
                    last = (e == NE - 1)
                    for nb in range(2):
                        ctx.pe.wait(v, mm_done[nb])
                        ctx.dve.inc(v.scalar_tensor_tensor(
                            out=ctx.X[:, t * D + nb * 512: t * D + (nb + 1) * 512],
                            in0=vb[nb][:, 0:512],
                            scalar=wts[:, t * NE + e: t * NE + e + 1],
                            in1=ctx.X[:, t * D + nb * 512: t * D + (nb + 1) * 512],
                            op0=OP.mult, op1=OP.add,
                            accum_out=lns[:, nb:nb + 1] if last else None))
                    if last:
                        v.wait_ge(ctx.dve.sem, ctx.dve.cnt)
                        ctx.dve.inc(v.tensor_scalar(
                            out=sums[:, t:t + 1], in0=lns[:, 0:1],
                            scalar1=lns[:, 1:2], scalar2=None, op0=OP.add))

    post_ln(ctx, blk)
    ar.pop()



def emit_L1f(ctx, blk, W):
    T, NT = ctx.T, ctx.NT
    NE, DF = 8, 4096
    NG = T // 1024
    ar = ctx.arena
    ar.push()
    xfm = ar.alloc(NCH * 1024, BF16)
    hbuf = ar.alloc(32 * 1024, BF16)
    w2buf = ar.alloc(32 * 512, BF16)
    w1buf = [ar.alloc(8 * 512, BF16) for _ in range(2)]
    wgS = ar.alloc(8 * NE, BF16)
    rows = ar.alloc(1056, BF16)
    b1c = ar.alloc(64, F32)
    wts = ar.alloc(NT * NE, F32)
    sums = ctx.cols("lnsum", NT)
    pcol = ctx.cols("l1f_p", 12)
    lns16 = ctx.cols("l1f_lns", 16)
    probs = ctx.cols("l1f_probs", 8)
    top8 = ctx.cols("l1f_top8", 8)

    ctx.w1q_hist = []
    ctx.w1q_grps = []
    wg0 = ctx.dgroup()
    load_weight(ctx, blk, W["L1f_Wg"], wgS, group=wg0)

    @blk.gpsimd
    def _(g):
        wg0.issue(g.dma_start(out=rows[0:1, 1024:1032], in_=W["L1f_bg"][0:1, :]))

    tm_to_fm(ctx, blk, ctx.X[:], xfm, tiles=list(range(8)), Tdst=1024)
    fm_dve = ctx.dve.cnt

    def routing(t, xfm_ap, toff):
        b = ctx.bank()

        @blk.tensor
        def _(t_, t=t, b=b):
            if t == 0:
                ctx.dve.wait(t_, fm_dve)
                wg0.wait(t_)
            ctx.act.wait(t_)
            ctx.dve.wait(t_)
            for kc in range(NCH):
                t_.matmul(b[:, 0:NE], xfm_ap[:, kc * 1024 + toff * 128:kc * 1024 + (toff + 1) * 128],
                          wgS[:, kc * NE:(kc + 1) * NE], start=(kc == 0), stop=False)
            ctx.pe.inc(bias_mm(ctx, t_, b[:, 0:NE], rows[0:1, 1024:1032], stop=True))
        mmc = ctx.pe.cnt

        @blk.vector
        def _(v, b=b):
            ctx.pe.wait(v, mmc)
            ctx.dve.inc(v.reduce_max(out=pcol[:, 0:1], in_=b[:, 0:NE], axis=AX.X, negate=True))

        @blk.scalar
        def _(s, b=b):
            ctx.dve.wait(s)
            ctx.act.inc(s.activation(out=probs, in_=b[:, 0:NE], func=AF.Exp,
                                     bias=pcol[:, 0:1], accum_out=pcol[:, 1:2]))

        @blk.vector
        def _(v, t=t):
            ctx.act.wait(v)
            ctx.dve.inc(v.reciprocal(out=pcol[:, 2:3], in_=pcol[:, 1:2]))
            v.wait_ge(ctx.dve.sem, ctx.dve.cnt)
            ctx.dve.inc(v.tensor_scalar(out=probs, in0=probs, scalar1=pcol[:, 2:3],
                                        scalar2=None, op0=OP.mult))
            v.wait_ge(ctx.dve.sem, ctx.dve.cnt)
            ctx.dve.inc(v.max(out=top8, in_=probs))
            v.wait_ge(ctx.dve.sem, ctx.dve.cnt)
            ctx.dve.inc(v.tensor_scalar(out=wts[:, t * NE:(t + 1) * NE], in0=probs,
                                        scalar1=top8[:, 1:2], scalar2=None, op0=OP.is_ge))
            v.wait_ge(ctx.dve.sem, ctx.dve.cnt)
            ctx.dve.inc(v.tensor_tensor(out=wts[:, t * NE:(t + 1) * NE],
                                        in0=wts[:, t * NE:(t + 1) * NE], in1=probs, op=OP.mult))
            v.wait_ge(ctx.dve.sem, ctx.dve.cnt)
            ctx.dve.inc(v.tensor_scalar(out=probs, in0=wts[:, t * NE:(t + 1) * NE],
                                        scalar1=1.0, scalar2=None, op0=OP.mult,
                                        op1=OP.add, accum_out=pcol[:, 4:5]))
            v.wait_ge(ctx.dve.sem, ctx.dve.cnt)
            ctx.dve.inc(v.tensor_scalar(out=pcol[:, 5:6], in0=pcol[:, 4:5],
                                        scalar1=0.0, scalar2=None, op0=OP.is_equal))
            v.wait_ge(ctx.dve.sem, ctx.dve.cnt)
            ctx.dve.inc(v.tensor_tensor(out=pcol[:, 4:5], in0=pcol[:, 4:5],
                                        in1=pcol[:, 5:6], op=OP.add))
            v.wait_ge(ctx.dve.sem, ctx.dve.cnt)
            ctx.dve.inc(v.reciprocal(out=pcol[:, 2:3], in_=pcol[:, 4:5]))
            v.wait_ge(ctx.dve.sem, ctx.dve.cnt)
            ctx.dve.inc(v.tensor_scalar(out=wts[:, t * NE:(t + 1) * NE],
                                        in0=wts[:, t * NE:(t + 1) * NE],
                                        scalar1=pcol[:, 2:3], scalar2=None, op0=OP.mult))

    for g in range(NG):
        if g > 0:
            tm_to_fm(ctx, blk, ctx.X[:, g * 8 * D:(g + 1) * 8 * D], xfm, tiles=list(range(8)), Tdst=1024)
        for ti in range(8):
            routing(g * 8 + ti, xfm, ti)
        routing_done = ctx.dve.cnt

        for e in range(NE):
            bg1 = ctx.hgroup()

            bg2 = ctx.dgroup()

            @blk.scalar
            def _(s, e=e):
                ctx.act.wait(s)
                bg1.issue(s.dma_start(out=b1c[:, 0:32], in_=W["L1f_b1c"][e]))

            @blk.gpsimd
            def _(g, e=e):
                ctx.pe.wait(g)
                bg2.issue(g.dma_start(out=rows[0:1, 0:1024], in_=W["L1f_b2"][e:e + 1, :]))

            for q in range(8):
                wq = ctx.dgroup()

                @blk.gpsimd
                def _(gg, e=e, q=q):
                    qh = ctx.w1q_hist
                    if len(qh) >= 2:
                        ctx.pe.wait(gg, qh[-2])
                    qg_ = ctx.w1q_grps
                    if len(qg_) >= 2:
                        qg_[-2].wait(gg)
                    for fc in range(NCH):
                        wq.issue(gg.dma_start(
                            out=w1buf[q % 2][0:128, fc * 512:(fc + 1) * 512],
                            in_=W["L1f_W1"][e, fc * 128:(fc + 1) * 128, q * 512:(q + 1) * 512]))

                for hc in range(4):
                    for ts_ in range(2):
                        b = ctx.bank()

                        @blk.tensor
                        def _(t_, e=e, g=g, q=q, hc=hc, ts_=ts_, b=b):
                            wq.wait(t_)
                            if e == 0 and q == 0 and hc == 0 and ts_ == 0:
                                ctx.dve.wait(t_, routing_done)
                            ctx.act.wait(t_)
                            ctx.dve.wait(t_)
                            for fc in range(NCH):
                                mm = t_.matmul(b[:, 0:512],
                                               w1buf[q % 2][:, fc * 512 + hc * 128: fc * 512 + (hc + 1) * 128],
                                               xfm[:, fc * 1024 + ts_ * 512: fc * 1024 + (ts_ + 1) * 512],
                                               start=(fc == 0), stop=(fc == NCH - 1))
                            ctx.pe.inc(mm)
                        mmc = ctx.pe.cnt

                        @blk.scalar
                        def _(s, q=q, hc=hc, ts_=ts_, b=b, mmc=mmc):
                            ctx.pe.wait(s, mmc)
                            if q == 0 and hc == 0 and ts_ == 0:
                                bg1.wait(s)
                            hglob = q * 4 + hc
                            ctx.act.inc(s.activation(
                                out=hbuf[:, hglob * 1024 + ts_ * 512: hglob * 1024 + (ts_ + 1) * 512],
                                in_=b[:, 0:512], func=AF.Gelu, bias=b1c[:, hglob:hglob + 1]))
                ctx.w1q_hist.append(ctx.pe.cnt)
                ctx.w1q_grps.append(wq)
            h_done = ctx.act.cnt

            for vh in range(2):
                w2g = ctx.dgroup()

                @blk.gpsimd
                def _(gg, e=e, vh=vh):
                    ctx.pe.wait(gg)
                    w2r = W["L1f_W2"][e].rearrange("(hc p) v -> p hc v", p=128)
                    wv = w2r[:, :, vh * 512:(vh + 1) * 512]
                    w2g.issue(gg.dma_start(
                        out=w2buf[0:128, :].rearrange("p (hc v) -> p hc v", hc=32), in_=wv))

                for tset in ([0, 1, 2, 3, 4, 5], [6, 7]):
                    banks = {ti: ctx.bank() for ti in tset}
                    evac_dve = {}

                    @blk.tensor
                    def _(t_, e=e, g=g, vh=vh, tset=tset, banks=banks):
                        w2g.wait(t_)
                        bg2.wait(t_)
                        ctx.act.wait(t_, h_done)
                        ctx.dve.wait(t_)
                        for ti in tset:
                            bias_mm(ctx, t_, banks[ti][:, 0:512],
                                    rows[0:1, vh * 512:(vh + 1) * 512],
                                    start=True)
                        for hc in range(32):
                            for ti in tset:
                                mm = t_.matmul(banks[ti][:, 0:512],
                                               hbuf[:, hc * 1024 + ti * 128: hc * 1024 + (ti + 1) * 128],
                                               w2buf[:, hc * 512:(hc + 1) * 512],
                                               start=False, stop=(hc == 31))
                                if hc == 31:
                                    ctx.pe.inc(mm)
                                    evac_dve[ti] = ctx.pe.cnt

                    @blk.vector
                    def _(v, e=e, g=g, vh=vh, tset=tset, banks=banks, evac_dve=evac_dve):
                        for ti in tset:
                            ctx.pe.wait(v, evac_dve[ti])
                            t = g * 8 + ti
                            last = (e == NE - 1)
                            ctx.dve.inc(v.scalar_tensor_tensor(
                                out=ctx.X[:, t * D + vh * 512: t * D + (vh + 1) * 512],
                                in0=banks[ti][:, 0:512],
                                scalar=wts[:, t * NE + e: t * NE + e + 1],
                                in1=ctx.X[:, t * D + vh * 512: t * D + (vh + 1) * 512],
                                op0=OP.mult, op1=OP.add,
                                accum_out=lns16[:, vh * 8 + ti:vh * 8 + ti + 1] if last else None))
                            if last and vh == 1:
                                v.wait_ge(ctx.dve.sem, ctx.dve.cnt)
                                ctx.dve.inc(v.tensor_scalar(
                                    out=sums[:, t:t + 1], in0=lns16[:, ti:ti + 1],
                                    scalar1=lns16[:, 8 + ti:9 + ti], scalar2=None, op0=OP.add))

    post_ln(ctx, blk)
    ar.pop()



def emit_L3f(ctx, blk, W):
    T, NT = ctx.T, ctx.NT
    NE = 8
    NG = T // 1024
    ar = ctx.arena
    ar.push()
    xfm = ar.alloc(NCH * 1024, BF16)
    hbuf = ar.alloc(12 * 1024, BF16)
    wdbuf = ar.alloc(12 * 512, BF16)
    wgubuf = [ar.alloc(8 * 512, BF16) for _ in range(2)]
    gscr = [ar.alloc(1024, F32) for _ in range(2)]
    gateS = ar.alloc(8 * NE, BF16)
    wts = ar.alloc(NT * NE, F32)
    lcol = ctx.cols("l3f_l", 8)
    top8 = ctx.cols("l3f_top8", 8)
    pcol = ctx.cols("l3f_p", 8)
    lns16 = ctx.cols("l3f_lns", 16)
    sums = ctx.cols("lnsum", NT)

    wg0 = ctx.dgroup()
    load_weight(ctx, blk, W["L3f_gT"], gateS, group=wg0)
    tm_to_fm(ctx, blk, ctx.X[:], xfm, tiles=list(range(8)), Tdst=1024)
    fm_dve = ctx.dve.cnt

    def routing(t, xfm_ap, toff):
        b = ctx.bank()

        @blk.tensor
        def _(t_, t=t, b=b, toff=toff):
            if t == 0:
                ctx.dve.wait(t_, fm_dve)
                wg0.wait(t_)
            ctx.act.wait(t_)
            ctx.dve.wait(t_)
            for kc in range(NCH):
                mm = t_.matmul(b[:, 0:NE],
                               xfm_ap[:, kc * 1024 + toff * 128:kc * 1024 + (toff + 1) * 128],
                               gateS[:, kc * NE:(kc + 1) * NE], start=(kc == 0), stop=(kc == NCH - 1))
            ctx.pe.inc(mm)
        mmc = ctx.pe.cnt

        @blk.vector
        def _(v, b=b):
            ctx.pe.wait(v, mmc)
            ctx.dve.inc(v.tensor_copy(out=lcol, in_=b[:, 0:NE]))
            v.wait_ge(ctx.dve.sem, ctx.dve.cnt)
            ctx.dve.inc(v.reduce_max(out=pcol[:, 0:1], in_=lcol, axis=AX.X, negate=True))
            v.wait_ge(ctx.dve.sem, ctx.dve.cnt)
            ctx.dve.inc(v.max(out=top8, in_=lcol))

        @blk.scalar
        def _(s):
            ctx.dve.wait(s)
            ctx.act.inc(s.activation(out=pcol[:, 1:2], in_=top8[:, 1:2], func=AF.Exp,
                                     bias=pcol[:, 0:1]))
            s.wait_ge(ctx.act.sem, ctx.act.cnt)
            ctx.act.inc(s.activation(out=probs_ap, in_=lcol, func=AF.Exp, bias=pcol[:, 0:1]))

        @blk.vector
        def _(v, t=t):
            ctx.act.wait(v)
            ctx.dve.inc(v.tensor_scalar(out=wts[:, t * NE:(t + 1) * NE], in0=lcol,
                                        scalar1=top8[:, 1:2], scalar2=None, op0=OP.is_ge))
            v.wait_ge(ctx.dve.sem, ctx.dve.cnt)
            ctx.dve.inc(v.tensor_tensor(out=wts[:, t * NE:(t + 1) * NE],
                                        in0=wts[:, t * NE:(t + 1) * NE], in1=probs_ap, op=OP.mult))
            v.wait_ge(ctx.dve.sem, ctx.dve.cnt)
            ctx.dve.inc(v.tensor_scalar(out=probs_ap, in0=wts[:, t * NE:(t + 1) * NE],
                                        scalar1=1.0, scalar2=None, op0=OP.mult,
                                        op1=OP.add, accum_out=pcol[:, 2:3]))
            v.wait_ge(ctx.dve.sem, ctx.dve.cnt)
            ctx.dve.inc(v.reciprocal(out=pcol[:, 3:4], in_=pcol[:, 2:3]))
            v.wait_ge(ctx.dve.sem, ctx.dve.cnt)
            ctx.dve.inc(v.tensor_scalar(out=wts[:, t * NE:(t + 1) * NE],
                                        in0=wts[:, t * NE:(t + 1) * NE],
                                        scalar1=pcol[:, 3:4], scalar2=None, op0=OP.mult))

    probs_ap = ctx.cols("l3f_probs", 8)
    ctx.w1q_hist2 = []
    ctx.w1q_grps2 = []
    for g in range(NG):
        if g > 0:
            tm_to_fm(ctx, blk, ctx.X[:, g * 8 * D:(g + 1) * 8 * D], xfm, tiles=list(range(8)), Tdst=1024)
        for ti in range(8):
            routing(g * 8 + ti, xfm, ti)
        routing_done = ctx.dve.cnt

        for e in range(NE + 1):
            for q in range(6):
                wq = ctx.dgroup()

                @blk.gpsimd
                def _(gg, e=e, q=q):
                    qh = ctx.w1q_hist2
                    if len(qh) >= 2:
                        ctx.pe.wait(gg, qh[-2])
                    qg_ = ctx.w1q_grps2
                    if len(qg_) >= 2:
                        qg_[-2].wait(gg)
                    for fc in range(NCH):
                        wq.issue(gg.dma_start(
                            out=wgubuf[q % 2][0:128, fc * 512:(fc + 1) * 512],
                            in_=W["L3f_Wgu"][e, fc * 128:(fc + 1) * 128, q * 512:(q + 1) * 512]))

                for pj in range(2):
                    j = q * 2 + pj
                    for ts_ in range(2):
                        bg_ = ctx.bank()
                        bu_ = ctx.bank()

                        @blk.tensor
                        def _(t_, e=e, g=g, q=q, pj=pj, ts_=ts_, bg_=bg_, bu_=bu_):
                            wq.wait(t_)
                            if e == 0 and q == 0 and pj == 0 and ts_ == 0:
                                ctx.dve.wait(t_, routing_done)
                            ctx.act.wait(t_)
                            ctx.dve.wait(t_)
                            for which, bb in ((0, bg_), (1, bu_)):
                                off = (pj * 2 + which) * 128
                                for fc in range(NCH):
                                    mm = t_.matmul(bb[:, 0:512],
                                                   wgubuf[q % 2][:, fc * 512 + off: fc * 512 + off + 128],
                                                   xfm[:, fc * 1024 + ts_ * 512: fc * 1024 + (ts_ + 1) * 512],
                                                   start=(fc == 0), stop=(fc == NCH - 1))
                                ctx.pe.inc(mm)
                        mmc = ctx.pe.cnt

                        @blk.scalar
                        def _(s, ts_=ts_, bg_=bg_, mmc=mmc, par=ts_):
                            ctx.pe.wait(s, mmc)
                            ctx.dve.wait(s)
                            ctx.act.inc(s.activation(out=gscr[par][:, 0:512], in_=bg_[:, 0:512],
                                                     func=AF.Silu))
                        silu_done = ctx.act.cnt

                        @blk.vector
                        def _(v, j=j, ts_=ts_, bu_=bu_, silu_done=silu_done, par=ts_):
                            ctx.act.wait(v, silu_done)
                            ctx.dve.inc(v.tensor_tensor(
                                out=hbuf[:, j * 1024 + ts_ * 512: j * 1024 + (ts_ + 1) * 512],
                                in0=gscr[par][:, 0:512], in1=bu_[:, 0:512], op=OP.mult))
                ctx.w1q_hist2.append(ctx.pe.cnt)
                ctx.w1q_grps2.append(wq)
            h_done = ctx.dve.cnt

            for vh in range(2):
                w2g = ctx.dgroup()

                @blk.gpsimd
                def _(gg, e=e, vh=vh):
                    ctx.pe.wait(gg)
                    for hc in range(12):
                        w2g.issue(gg.dma_start(
                            out=wdbuf[0:128, hc * 512:(hc + 1) * 512],
                            in_=W["L3f_Wd"][e, hc * 128:(hc + 1) * 128, vh * 512:(vh + 1) * 512]))

                for tset in ([0, 1, 2, 3, 4, 5], [6, 7]):
                    banks = {ti: ctx.bank() for ti in tset}
                    evac_pe = {}

                    @blk.tensor
                    def _(t_, e=e, vh=vh, tset=tset, banks=banks, evac_pe=evac_pe):
                        w2g.wait(t_)
                        ctx.dve.wait(t_)
                        for hc in range(12):
                            for ti in tset:
                                mm = t_.matmul(banks[ti][:, 0:512],
                                               hbuf[:, hc * 1024 + ti * 128: hc * 1024 + (ti + 1) * 128],
                                               wdbuf[:, hc * 512:(hc + 1) * 512],
                                               start=(hc == 0), stop=(hc == 11))
                                if hc == 11:
                                    ctx.pe.inc(mm)
                                    evac_pe[ti] = ctx.pe.cnt

                    @blk.vector
                    def _(v, e=e, g=g, vh=vh, tset=tset, banks=banks, evac_pe=evac_pe):
                        for ti in tset:
                            ctx.pe.wait(v, evac_pe[ti])
                            t = g * 8 + ti
                            last = (e == NE)
                            ctx.dve.inc(v.scalar_tensor_tensor(
                                out=ctx.X[:, t * D + vh * 512: t * D + (vh + 1) * 512],
                                in0=banks[ti][:, 0:512],
                                scalar=1.0 if e == NE else wts[:, t * NE + e: t * NE + e + 1],
                                in1=ctx.X[:, t * D + vh * 512: t * D + (vh + 1) * 512],
                                op0=OP.mult, op1=OP.add,
                                accum_out=lns16[:, vh * 8 + ti:vh * 8 + ti + 1] if last else None))
                            if last and vh == 1:
                                v.wait_ge(ctx.dve.sem, ctx.dve.cnt)
                                ctx.dve.inc(v.tensor_scalar(
                                    out=sums[:, t:t + 1], in0=lns16[:, ti:ti + 1],
                                    scalar1=lns16[:, 8 + ti:9 + ti], scalar2=None, op0=OP.add))

    post_ln(ctx, blk)
    ar.pop()



def emit_L2a_pre(ctx, blk, W, dram, gsize=4):
    T, NT = ctx.T, ctx.NT
    TK = gsize * T
    KBS = min(512, T)
    SCS = KBS // 128
    NKB = TK // KBS
    ar = ctx.arena
    ar.push()
    norm = ctx.cols("l2a_norm", NT)
    rscale = ctx.cols("l2a_rsc", NT)
    sums = ctx.cols("lnsum", NT)
    qfm = ar.alloc(NCH * T, BF16)
    scr = ar.alloc(2048, BF16)
    sqscr = ar.alloc(1024, F32)

    @blk.scalar
    def _(s):
        ctx.dve.wait(s)
        for t in range(NT):
            if t > 0:
                s.wait_ge(ctx.act.sem, ctx.act.cnt)
            ctx.act.inc(s.activation(out=sqscr[:, 0:1024], in_=ctx.X[:, t * D:(t + 1) * D],
                                     func=AF.Square, accum_out=norm[:, t:t + 1]))
        s.wait_ge(ctx.act.sem, ctx.act.cnt)
        ctx.act.inc(s.activation(out=norm, in_=norm, func=AF.Sqrt))

    @blk.vector
    def _(v):
        ctx.act.wait(v)
        ctx.dve.inc(v.tensor_scalar(out=rscale, in0=norm, scalar1=ctx.constcol(1e-9),
                                    scalar2=None, op0=OP.add))
        v.wait_ge(ctx.dve.sem, ctx.dve.cnt)
        ctx.dve.inc(v.reciprocal(out=rscale, in_=rscale))

    ar.push()
    sfm = ar.alloc(NCH * T, BF16)
    wbuf = ar.alloc(8 * 1024, BF16)
    bqc = ctx.cols("l2a_bqc", 8)

    tm_to_fm(ctx, blk, ctx.X[:], sfm, scale_cols=lambda t: rscale[:, t:t + 1])
    sfm_done = ctx.dve.cnt

    wgq = ctx.dgroup()
    load_weight(ctx, blk, W["L2a_WqT"], wbuf, group=wgq)
    bq_g = ctx.hgroup()

    @blk.sync
    def _(s):
        bq_g.issue(s.dma_start(out=bqc, in_=W["L2a_bqc"]))

    TG = min(512, T)
    for oc in range(NCH):
        for tg in range(T // TG):
            b = ctx.bank()

            @blk.tensor
            def _(t_, oc=oc, tg=tg, b=b):
                if oc == 0 and tg == 0:
                    wgq.wait(t_)
                    ctx.dve.wait(t_, sfm_done)
                ctx.dve.wait(t_)
                for kc in range(NCH):
                    mm = t_.matmul(b[:, 0:TG],
                                   wbuf[:, kc * 1024 + oc * 128: kc * 1024 + (oc + 1) * 128],
                                   sfm[:, kc * T + tg * TG: kc * T + (tg + 1) * TG],
                                   start=(kc == 0), stop=(kc == NCH - 1))
                ctx.pe.inc(mm)
            mmc = ctx.pe.cnt

            @blk.vector
            def _(v, oc=oc, tg=tg, b=b, mmc=mmc):
                ctx.pe.wait(v, mmc)
                if oc == 0 and tg == 0:
                    bq_g.wait(v)
                ctx.dve.inc(v.tensor_scalar(out=qfm[:, oc * T + tg * TG: oc * T + (tg + 1) * TG],
                                            in0=b[:, 0:TG], scalar1=bqc[:, oc:oc + 1],
                                            scalar2=None, op0=OP.add))
    q_done = ctx.dve.cnt

    wgv = ctx.dgroup()
    load_weight(ctx, blk, W["L2a_WvT"], wbuf,
                wait=lambda h: (ctx.pe.wait(h), wgq.wait(h)), group=wgv)
    bv_g = ctx.dgroup()

    @blk.gpsimd
    def _(g):
        ctx.act.wait(g)
        bv_g.issue(g.dma_start(out=scr[0:1, 0:1024], in_=W["L2a_bv"][0:1, :]))

    vst = ctx.hgroup()
    for t in range(NT):
        vb = [ctx.bank(), ctx.bank()]

        @blk.tensor
        def _(t_, t=t, vb=vb):
            if t == 0:
                wgv.wait(t_)
                bv_g.wait(t_)
            ctx.dve.wait(t_)
            ctx.act.wait(t_)
            for nb in range(2):
                bias_mm(ctx, t_, vb[nb][:, 0:512], scr[0:1, nb * 512:(nb + 1) * 512], start=True)
                for kc in range(NCH):
                    mm = t_.matmul(vb[nb][:, 0:512],
                                   sfm[:, kc * T + t * 128: kc * T + (t + 1) * 128],
                                   wbuf[:, kc * 1024 + nb * 512: kc * 1024 + (nb + 1) * 512],
                                   start=False, stop=(kc == NCH - 1))
                ctx.pe.inc(mm)
        mmc = ctx.pe.cnt

        @blk.scalar
        def _(s, t=t, vb=vb, mmc=mmc):
            ctx.pe.wait(s, mmc)
            vst.wait(s)
            for nb in range(2):
                if nb:
                    s.wait_ge(ctx.act.sem, ctx.act.cnt)
                ctx.act.inc(s.activation(out=scr[:, 1024 + nb * 512:1024 + (nb + 1) * 512],
                                         in_=vb[nb][:, 0:512], func=AF.Copy, bias=0.0))
        ev = ctx.act.cnt

        @blk.sync
        def _(s, t=t, ev=ev):
            ctx.act.wait(s, ev)
            vst.issue(s.dma_start(out=dram["l2a_v"][t * 128:(t + 1) * 128, :],
                                  in_=scr[:, 1024:2048]))

    wga = ctx.dgroup()
    load_weight(ctx, blk, W["L2a_A"], wbuf,
                wait=lambda h: (ctx.pe.wait(h), wgv.wait(h)), group=wga)
    tqst = ctx.hgroup()
    tqfm = sfm
    for oc in range(NCH):
        for tg in range(T // TG):
            b = ctx.bank()

            @blk.tensor
            def _(t_, oc=oc, tg=tg, b=b):
                if oc == 0 and tg == 0:
                    wga.wait(t_)
                    ctx.dve.wait(t_, q_done)
                ctx.dve.wait(t_)
                ctx.act.wait(t_)
                for kc in range(NCH):
                    mm = t_.matmul(b[:, 0:TG],
                                   wbuf[:, kc * 1024 + oc * 128: kc * 1024 + (oc + 1) * 128],
                                   qfm[:, kc * T + tg * TG: kc * T + (tg + 1) * TG],
                                   start=(kc == 0), stop=(kc == NCH - 1))
                ctx.pe.inc(mm)
            mmc = ctx.pe.cnt

            @blk.scalar
            def _(s, oc=oc, tg=tg, b=b, mmc=mmc):
                ctx.pe.wait(s, mmc)
                s.wait_ge(ctx.act.sem, ctx.act.cnt)
                ctx.act.inc(s.activation(out=tqfm[:, oc * T + tg * TG: oc * T + (tg + 1) * TG],
                                         in_=b[:, 0:TG], func=AF.Copy, bias=0.0))
    tq_act = ctx.act.cnt

    @blk.sync
    def _(s):
        ctx.act.wait(s, tq_act)
        tqst.issue(s.dma_start(
            out=dram["l2a_tq"][:, :].rearrange("(c p) t -> p c t", p=128),
            in_=tqfm[:].rearrange("p (c t) -> p c t", c=NCH)))

    qfst = ctx.hgroup()

    @blk.sync
    def _(s):
        ctx.dve.wait(s, q_done)
        ctx.act.wait(s)
        qfst.issue(s.dma_start(
            out=dram["l2a_qf"][:, :].rearrange("(c p) t -> p c t", p=128),
            in_=qfm[:].rearrange("p (c t) -> p c t", c=NCH)))
        qfst.issue(s.dma_start(out=dram["l2a_nrm"][:, :], in_=norm))

    @blk.sync
    def _(s):
        vst.wait(s)
        tqst.wait(s)
        qfst.wait(s)
    ar.pop()
    ar.pop()


def emit_L2a_flash(ctx, blk, W, dram, gsize=4):
    T, NT = ctx.T, ctx.NT
    TK = gsize * T
    KBS = min(512, T)
    SCS = KBS // 128
    NKB = TK // KBS
    ar = ctx.arena
    ar.push()
    norm = ctx.cols("l2a_norm", NT)
    sums = ctx.cols("lnsum", NT)
    qfm = ar.alloc(NCH * T, BF16)
    qfg = ctx.hgroup()

    @blk.sync
    def _(s):
        ctx.dve.wait(s)
        qfg.issue(s.dma_start(out=qfm[:].rearrange("p (c t) -> p c t", c=NCH),
                              in_=W["l2a_qf"][:, :].rearrange("(c p) t -> p c t", p=128)))
        qfg.issue(s.dma_start(out=norm, in_=W["l2a_nrm"][:, :]))

    @blk.vector
    def _(v):
        qfg.wait(v)
        ctx.dve.inc(v.engine_nop())

    acc = [ar.alloc(1024, F32) for _ in range(2)]
    tqkb = [ar.alloc(NCH * KBS, BF16) for _ in range(2)]
    vkb = [ar.alloc(SCS * 1024, BF16) for _ in range(2)]
    ptb = ar.alloc(SCS * 128, BF16)
    pb_ = ar.alloc(KBS, BF16)
    fcol = ctx.cols("l2a_f", 12)

    def load_kb(kb, wait_pe):
        kbg = ctx.dgroup()

        @blk.gpsimd
        def _(g, kb=kb, wait_pe=wait_pe):
            if wait_pe is not None:
                ctx.pe.wait(g, wait_pe)
            r = (kb * KBS) // T
            c0 = (kb * KBS) % T
            kbg.issue(g.dma_start(
                out=tqkb[kb % 2][0:128, :].rearrange("p (c t) -> p c t", c=NCH),
                in_=W["l2a_tqa"][:, :].rearrange("(rc p) t -> p rc t", p=128)[:, r * NCH:(r + 1) * NCH, c0:c0 + KBS]))
            kbg.issue(g.dma_start(
                out=vkb[kb % 2][0:128, :].rearrange("p (kc v) -> p kc v", kc=SCS),
                in_=W["l2a_va"][kb * KBS:(kb + 1) * KBS, :].rearrange("(kc p) v -> p kc v", p=128)))
        return kbg

    for qp in range(NT // 2):
        pair = [qp * 2, qp * 2 + 1]
        if qp == 0:
            kb_grp = {0: load_kb(0, None)}
            kb_pe = {}

        @blk.vector
        def _(v, qp=qp):
            ctx.dve.wait(v)
            for par in range(2):
                v.memset(fcol[:, par * 2:par * 2 + 1], 3.0e38)
                v.memset(fcol[:, 4 + par:5 + par], 0.0)
                ctx.dve.inc(v.memset(acc[par][:, 0:1024], 0.0))

        for kb in range(NKB):
            if kb + 1 < NKB:
                kb_grp[kb + 1] = load_kb(kb + 1, kb_pe.get(kb - 1, ctx.pe.cnt))
            for par in range(2):
                t = pair[par]
                sb = ctx.bank()
                pv = [ctx.bank(), ctx.bank()]

                @blk.tensor
                def _(t_, t=t, kb=kb, sb=sb):
                    kb_grp[kb].wait(t_)
                    ctx.dve.wait(t_)
                    ctx.act.wait(t_)
                    for kc in range(NCH):
                        mm = t_.matmul(sb[:, 0:KBS],
                                       qfm[:, kc * T + t * 128: kc * T + (t + 1) * 128],
                                       tqkb[kb % 2][:, kc * KBS:(kc + 1) * KBS],
                                       start=(kc == 0), stop=(kc == NCH - 1))
                    ctx.pe.inc(mm)
                mmc = ctx.pe.cnt

                @blk.vector
                def _(v, kb=kb, sb=sb, par=par, mmc=mmc):
                    ctx.pe.wait(v, mmc)
                    ctx.act.wait(v)
                    ctx.dve.inc(v.reduce_max(out=fcol[:, 8:9], in_=sb[:, 0:KBS], axis=AX.X, negate=True))
                    v.wait_ge(ctx.dve.sem, ctx.dve.cnt)
                    ctx.dve.inc(v.tensor_scalar(out=fcol[:, 9:10], in0=fcol[:, 8:9],
                                                scalar1=fcol[:, par * 2:par * 2 + 1], scalar2=None,
                                                op0=OP.min))
                    v.wait_ge(ctx.dve.sem, ctx.dve.cnt)
                    ctx.dve.inc(v.tensor_scalar(out=fcol[:, 10:11], in0=fcol[:, 9:10],
                                                scalar1=fcol[:, par * 2:par * 2 + 1], scalar2=None,
                                                op0=OP.subtract))
                    v.wait_ge(ctx.dve.sem, ctx.dve.cnt)
                    ctx.dve.inc(v.tensor_copy(out=fcol[:, par * 2:par * 2 + 1], in_=fcol[:, 9:10]))
                    v.wait_ge(ctx.dve.sem, ctx.dve.cnt)
                    ctx.dve.inc(v.tensor_scalar(out=fcol[:, 11:12], in0=fcol[:, 9:10],
                                                scalar1=1.0 / 32.0, scalar2=None, op0=OP.mult))
                sm_dve = ctx.dve.cnt

                @blk.scalar
                def _(s, sb=sb, sm_dve=sm_dve):
                    ctx.dve.wait(s, sm_dve)
                    ctx.act.inc(s.activation(out=fcol[:, 10:11], in_=fcol[:, 10:11],
                                             func=AF.Exp, scale=1.0 / 32.0))
                    s.wait_ge(ctx.act.sem, ctx.act.cnt)
                    ctx.act.inc(s.activation(out=pb_[:, 0:KBS], in_=sb[:, 0:KBS],
                                             func=AF.Exp, scale=1.0 / 32.0,
                                             bias=fcol[:, 11:12], accum_out=fcol[:, 8:9]))
                exp_done = ctx.act.cnt

                @blk.vector
                def _(v, par=par, exp_done=exp_done):
                    ctx.act.wait(v, exp_done)
                    ctx.dve.inc(v.scalar_tensor_tensor(out=fcol[:, 4 + par:5 + par],
                                                       in0=fcol[:, 4 + par:5 + par],
                                                       scalar=fcol[:, 10:11],
                                                       in1=fcol[:, 8:9], op0=OP.mult, op1=OP.add))

                @blk.tensor
                def _(t_, exp_done=exp_done):
                    ctx.act.wait(t_, exp_done)
                    ctx.dve.wait(t_)
                    for sc in range(SCS):
                        tr = t_.transpose(ctx.pt[:, sc * 128:(sc + 1) * 128],
                                          pb_[:, sc * 128:(sc + 1) * 128], ctx.ident[:])
                    ctx.pe.inc(tr)
                trc = ctx.pe.cnt

                @blk.vector
                def _(v, trc=trc):
                    ctx.pe.wait(v, trc)
                    ctx.dve.inc(v.tensor_copy(out=ptb[:, 0:SCS * 128], in_=ctx.pt[:, 0:SCS * 128]))
                ptc = ctx.dve.cnt

                @blk.tensor
                def _(t_, kb=kb, pv=pv, ptc=ptc):
                    ctx.dve.wait(t_, ptc)
                    for nb in range(2):
                        for sc in range(SCS):
                            mm = t_.matmul(pv[nb][:, 0:512],
                                           ptb[:, sc * 128:(sc + 1) * 128],
                                           vkb[kb % 2][:, sc * 1024 + nb * 512: sc * 1024 + (nb + 1) * 512],
                                           start=(sc == 0), stop=(sc == SCS - 1))
                        ctx.pe.inc(mm)
                pvc = ctx.pe.cnt

                @blk.vector
                def _(v, par=par, pv=pv, pvc=pvc):
                    ctx.pe.wait(v, pvc)
                    for nb in range(2):
                        ctx.dve.inc(v.scalar_tensor_tensor(
                            out=acc[par][:, nb * 512:(nb + 1) * 512],
                            in0=acc[par][:, nb * 512:(nb + 1) * 512],
                            scalar=fcol[:, 10:11], in1=pv[nb][:, 0:512],
                            op0=OP.mult, op1=OP.add))
            kb_pe[kb] = ctx.pe.cnt

        for par in range(2):
            t = pair[par]

            @blk.vector
            def _(v, t=t, par=par):
                v.wait_ge(ctx.dve.sem, ctx.dve.cnt)
                ctx.dve.inc(v.reciprocal(out=fcol[:, 8:9], in_=fcol[:, 4 + par:5 + par]))
                v.wait_ge(ctx.dve.sem, ctx.dve.cnt)
                ctx.dve.inc(v.tensor_scalar(out=fcol[:, 8:9], in0=fcol[:, 8:9],
                                            scalar1=norm[:, t:t + 1], scalar2=None, op0=OP.mult))
                v.wait_ge(ctx.dve.sem, ctx.dve.cnt)
                ctx.dve.inc(v.scalar_tensor_tensor(
                    out=ctx.X[:, t * D:(t + 1) * D], in0=acc[par][:, 0:1024],
                    scalar=fcol[:, 8:9], in1=ctx.X[:, t * D:(t + 1) * D],
                    op0=OP.mult, op1=OP.add, accum_out=sums[:, t:t + 1]))
        if qp + 1 < NT // 2:
            kb_grp = {0: load_kb(0, kb_pe[NKB - 2] if NKB >= 2 else None)}
            kb_pe = {}

    post_ln(ctx, blk)
    ar.pop()



def emit_L3a_pre(ctx, blk, W, dram, gsize=4):
    T, NT = ctx.T, ctx.NT
    NH = 16
    TK = gsize * T
    TG = min(512, T)
    QGS = TG
    NQG = T // QGS
    NTI = QGS // 128
    KQS = min(2048, T)
    NKQ = TK // KQS
    NKCQ = KQS // 128
    ar = ctx.arena
    ar.push()
    sums = ctx.cols("lnsum", NT)
    qfm = ar.alloc(NCH * T, BF16)

    ar.push()
    xfm = ar.alloc(NCH * T, BF16)
    kst = ar.alloc(T, BF16)
    wqk = [ar.alloc(8 * 128, BF16) for _ in range(2)]
    stair_sb = ar.alloc(T, BF16)
    bqc = ctx.cols("l3a_bqc", 8)
    bkc = ctx.cols("l3a_bkc", 8)

    tm_to_fm(ctx, blk, ctx.X[:], xfm)
    fm_dve = ctx.dve.cnt

    bg = ctx.hgroup()
    sg = ctx.dgroup()

    @blk.sync
    def _(s):
        bg.issue(s.dma_start(out=bqc, in_=W["L3a_bqc"]))
        bg.issue(s.dma_start(out=bkc, in_=W["L3a_bkc"]))

    @blk.gpsimd
    def _(g):
        sg.issue(g.dma_start(out=stair_sb[0:64, :], in_=W["L3a_kstair"]))

    kstore = ctx.hgroup()

    @blk.gpsimd
    def _(g):
        sg.wait(g)
        for h in range(NH):
            sg.issue(g.dma_start(out=dram["l3a_k"][128 * h + 64:128 * (h + 1), :],
                                 in_=stair_sb[0:64, 0:T]))

    wqk_grps = []
    for oc in range(16):
        wg_ = ctx.dgroup()

        @blk.gpsimd
        def _(g, oc=oc):
            ctx.pe.wait(g)
            if len(wqk_grps) >= 2:
                wqk_grps[-2].wait(g)
            wg_.issue(g.dma_start(out=wqk[oc % 2][0:128, 0:8 * 128].rearrange("p (f c) -> p f c", f=NCH),
                                  in_=W["L3a_Wqkv"][:, oc * 128:(oc + 1) * 128]
                                  .rearrange("(f p) c -> p f c", p=128)))
        wqk_grps.append(wg_)

        for tg in range(T // TG):
            b = ctx.bank()

            @blk.tensor
            def _(t_, oc=oc, tg=tg, b=b):
                wg_.wait(t_)
                if oc == 0 and tg == 0:
                    ctx.dve.wait(t_, fm_dve)
                ctx.dve.wait(t_)
                ctx.act.wait(t_)
                for kc in range(NCH):
                    mm = t_.matmul(b[:, 0:TG],
                                   wqk[oc % 2][:, kc * 128:(kc + 1) * 128],
                                   xfm[:, kc * T + tg * TG: kc * T + (tg + 1) * TG],
                                   start=(kc == 0), stop=(kc == NCH - 1))
                ctx.pe.inc(mm)
            mmc = ctx.pe.cnt

            @blk.vector
            def _(v, oc=oc, tg=tg, b=b, mmc=mmc):
                ctx.pe.wait(v, mmc)
                if oc == 0 and tg == 0:
                    bg.wait(v)
                if oc < 8:
                    ctx.dve.inc(v.tensor_scalar(out=qfm[:, oc * T + tg * TG: oc * T + (tg + 1) * TG],
                                                in0=b[:, 0:TG], scalar1=bqc[:, oc:oc + 1],
                                                scalar2=0.125, op0=OP.add, op1=OP.mult))
                else:
                    if tg == 0:
                        kstore.wait(v)
                    ctx.dve.inc(v.tensor_scalar(out=kst[:, tg * TG:(tg + 1) * TG],
                                                in0=b[:, 0:TG], scalar1=bkc[:, oc - 8:oc - 7],
                                                scalar2=None, op0=OP.add))
        if oc >= 8:
            kd = ctx.dve.cnt

            @blk.sync
            def _(s, oc=oc, kd=kd):
                ctx.dve.wait(s, kd)
                c = oc - 8
                kstore.issue(s.dma_start(out=dram["l3a_k"][128 * (2 * c):128 * (2 * c) + 64, :],
                                         in_=kst[0:64, 0:T]))
                kstore.issue(s.dma_start(out=dram["l3a_k"][128 * (2 * c + 1):128 * (2 * c + 1) + 64, :],
                                         in_=kst[64:128, 0:T]))

    wvbuf = ar.alloc(8 * 1024, BF16)
    vrow = ar.alloc(1024, BF16)
    vtile = ar.alloc(1056, BF16)
    wgv = ctx.dgroup()
    load_weight(ctx, blk, W["L3a_Wqkv"][:, 2048:3072], wvbuf,
                wait=lambda h: ctx.pe.wait(h), group=wgv)
    bvg = ctx.dgroup()

    @blk.gpsimd
    def _(g):
        bvg.issue(g.dma_start(out=vrow[0:1, 0:1024], in_=W["L3a_bv"][0:1, :]))

    vstore = ctx.hgroup()
    for t in range(NT):
        vb = [ctx.bank(), ctx.bank()]

        @blk.tensor
        def _(t_, t=t, vb=vb):
            if t == 0:
                wgv.wait(t_)
                bvg.wait(t_)
            ctx.dve.wait(t_)
            ctx.act.wait(t_)
            for nb in range(2):
                bias_mm(ctx, t_, vb[nb][:, 0:512], vrow[0:1, nb * 512:(nb + 1) * 512], start=True)
                for kc in range(NCH):
                    mm = t_.matmul(vb[nb][:, 0:512],
                                   xfm[:, kc * T + t * 128: kc * T + (t + 1) * 128],
                                   wvbuf[:, kc * 1024 + nb * 512: kc * 1024 + (nb + 1) * 512],
                                   start=False, stop=(kc == NCH - 1))
                ctx.pe.inc(mm)
        mmc = ctx.pe.cnt

        @blk.vector
        def _(v, t=t, vb=vb, mmc=mmc):
            ctx.pe.wait(v, mmc)
            vstore.wait(v)
            ctx.dve.inc(v.memset(vtile[:, 0:1056], 1.0))
            v.wait_ge(ctx.dve.sem, ctx.dve.cnt)
            dst = vtile[:, 0:1040].rearrange("p (h c) -> p h c", c=65)
            for nb in range(2):
                ctx.dve.inc(v.tensor_copy(
                    out=dst[:, nb * 8:(nb + 1) * 8, 0:64],
                    in_=vb[nb][:, 0:512].rearrange("p (h c) -> p h c", c=64)))
        vd = ctx.dve.cnt

        @blk.sync
        def _(s, t=t, vd=vd):
            ctx.dve.wait(s, vd)
            vstore.issue(s.dma_start(out=dram["l3a_v"][t * 128:(t + 1) * 128, :],
                                     in_=vtile[:, 0:1040]))

    qfst = ctx.hgroup()
    qfd = ctx.dve.cnt

    @blk.sync
    def _(s, qfd=qfd):
        ctx.dve.wait(s, qfd)
        qfst.issue(s.dma_start(
            out=dram["l3a_qf"][:, :].rearrange("(c p) t -> p c t", p=128),
            in_=qfm[:].rearrange("p (c t) -> p c t", c=NCH)))

    @blk.sync
    def _(s):
        kstore.wait(s)
        vstore.wait(s)
        qfst.wait(s)
        sg.wait(s)
    ar.pop()
    ar.pop()


def emit_L3a_flash(ctx, blk, W, dram, gsize=4):
    T, NT = ctx.T, ctx.NT
    NH = 16
    TK = gsize * T
    TG = min(512, T)
    QGS = TG
    NQG = T // QGS
    NTI = QGS // 128
    KQS = min(2048, T)
    NKQ = TK // KQS
    NKCQ = KQS // 128
    ar = ctx.arena
    ar.push()
    sums = ctx.cols("lnsum", NT)
    qfm = ar.alloc(NCH * T, BF16)
    qfg = ctx.hgroup()

    @blk.sync
    def _(s):
        ctx.dve.wait(s)
        qfg.issue(s.dma_start(out=qfm[:].rearrange("p (c t) -> p c t", c=NCH),
                              in_=W["l3a_qf"][:, :].rearrange("(c p) t -> p c t", p=128)))

    @blk.vector
    def _(v):
        qfg.wait(v)
        ctx.dve.inc(v.engine_nop())

    ctxfm = ar.alloc(NCH * T, BF16)
    ar.push()
    kdiag = ar.alloc(16 * QGS, BF16)
    vdiag = ar.alloc(NTI * 1040, BF16)
    qsb = ar.alloc(T, BF16)
    QB = ar.alloc(4 * QGS, BF16)
    kq_t = [ar.alloc(KQS, BF16) for _ in range(4)]
    vkb = [ar.alloc(1040, BF16) for _ in range(2)]
    pb = [ar.alloc(QGS, BF16) for _ in range(2)]
    tri = ar.alloc(128, F32)
    rlb = ar.alloc(QGS, BF16)
    rlf = ar.alloc(QGS, F32)
    rl32 = ar.alloc(QGS, F32)
    qsg = ctx.dgroup()

    @blk.gpsimd
    def _(g):
        ctx.dve.wait(g)
        ctx.pe.wait(g)
        qsg.issue(g.dma_start(out=qsb[0:64, 0:T], in_=W["L3a_qstair"]))
        qsg.issue(g.dma_start(out=tri[0:128, 0:128], in_=W["L3a_tri"]))

    first_flash = [True]
    ctx.vkb_grps = []
    for qg in range(NQG):
        dgq = ctx.hgroup()

        @blk.sync
        def _(s, qg=qg):
            ctx.pe.wait(s)
            ctx.dve.wait(s)
            dgq.issue(s.dma_start(
                out=kdiag[0:128, :].rearrange("p (h t) -> p h t", h=16),
                in_=W["l3a_k"][:, qg * QGS:(qg + 1) * QGS].rearrange("(h p) t -> p h t", p=128)))
            dgq.issue(s.dma_start(
                out=vdiag[0:128, :].rearrange("p (c v) -> p c v", c=NTI),
                in_=W["l3a_v"][qg * QGS:(qg + 1) * QGS, :].rearrange("(c p) v -> p c v", p=128)))

        for hg in range(4):
            heads = [hg * 4 + i for i in range(4)]
            cbanks = {h: ctx.ps[i] for i, h in enumerate(heads)}
            rot = [4, 5, 6]
            ri = [0]

            def lbank():
                b = ctx.ps[rot[ri[0] % 3]]
                ri[0] += 1
                return b

            @blk.vector
            def _(v, qg=qg, heads=heads):
                ctx.pe.wait(v)
                if first_flash[0]:
                    qsg.wait(v)
                    first_flash[0] = False
                for i, h in enumerate(heads):
                    ctx.dve.inc(v.tensor_copy(
                        out=QB[0:64, i * QGS:(i + 1) * QGS],
                        in_=qfm[64 * (h % 2):64 * (h % 2) + 64,
                                (h // 2) * T + qg * QGS:(h // 2) * T + (qg + 1) * QGS]))
                    ctx.dve.inc(v.tensor_copy(
                        out=QB[64:128, i * QGS:(i + 1) * QGS],
                        in_=qsb[0:64, qg * QGS:(qg + 1) * QGS]))
            qb_done = ctx.dve.cnt

            for kq in range(NKQ):
                kqg = ctx.dgroup()

                @blk.gpsimd
                def _(g, kq=kq, heads=heads):
                    ctx.pe.wait(g)
                    for i, h in enumerate(heads):
                        kqg.issue(g.dma_start(
                            out=kq_t[i][0:128, 0:KQS],
                            in_=W["l3a_ka"][:, :].rearrange("(r hh p) t -> p r hh t", p=128, hh=16)
                            [:, (kq * KQS) // T, h, (kq * KQS) % T:(kq * KQS) % T + KQS]))

                for kc in range(NKCQ):
                    kbidx = kq * NKCQ + kc
                    vg = ctx.dgroup()

                    @blk.gpsimd
    
                    def _(g, kbidx=kbidx, kc=kc):
                        ctx.pe.wait(g)
                        if len(ctx.vkb_grps) >= 2:
                            ctx.vkb_grps[-2].wait(g)
                        vg.issue(g.dma_start(
                            out=vkb[kc % 2][0:128, 0:1040],
                            in_=W["l3a_va"][kbidx * 128:(kbidx + 1) * 128, :]))
                    ctx.vkb_grps.append(vg)

                    for i, h in enumerate(heads):
                        sb = lbank()

                        @blk.tensor
                        def _(t_, i=i, kq=kq, kc=kc, sb=sb):
                            if kq == 0 and kc == 0 and i == 0:
                                ctx.dve.wait(t_, qb_done)
                            kqg.wait(t_)
                            vg.wait(t_)
                            ctx.act.wait(t_)
                            ctx.dve.wait(t_)
                            mm = t_.matmul(sb[:, 0:QGS],
                                           kq_t[i][:, kc * 128:(kc + 1) * 128],
                                           QB[:, i * QGS:(i + 1) * QGS],
                                           start=True, stop=True)
                            ctx.pe.inc(mm)
                        smm = ctx.pe.cnt

                        @blk.scalar
                        def _(s, i=i, kc=kc, sb=sb, smm=smm):
                            ctx.pe.wait(s, smm)
                            s.wait_ge(ctx.act.sem, ctx.act.cnt)
                            ctx.act.inc(s.activation(out=pb[(kc * 4 + i) % 2][:, 0:QGS],
                                                     in_=sb[:, 0:QGS], func=AF.Exp))
                        expd = ctx.act.cnt

                        @blk.tensor
                        def _(t_, h=h, i=i, kq=kq, kc=kc, expd=expd):
                            ctx.act.wait(t_, expd)
                            mm = t_.matmul(cbanks[h][0:65, 0:QGS],
                                           vkb[kc % 2][:, 65 * h:65 * h + 65],
                                           pb[(kc * 4 + i) % 2][:, 0:QGS],
                                           start=(kq == 0 and kc == 0), stop=False,
                                           skip_group_check=True)
                            ctx.pe.inc(mm)

            for i, h in enumerate(heads):
                for ti in range(NTI):
                    sb = lbank()

                    @blk.tensor
                    def _(t_, i=i, h=h, ti=ti, sb=sb):
                        dgq.wait(t_)
                        ctx.act.wait(t_)
                        ctx.dve.wait(t_)
                        mm = t_.matmul(sb[:, 0:128],
                                       kdiag[0:64, h * QGS + ti * 128: h * QGS + (ti + 1) * 128],
                                       QB[0:64, i * QGS + ti * 128: i * QGS + (ti + 1) * 128],
                                       start=True, stop=True)
                        ctx.pe.inc(mm)
                    smm = ctx.pe.cnt

                    @blk.vector
                    def _(v, sb=sb, smm=smm):
                        ctx.pe.wait(v, smm)
                        ctx.dve.inc(v.tensor_tensor(out=sb[:, 0:128], in0=tri[:, 0:128],
                                                    in1=sb[:, 0:128], op=OP.add))
                    trid = ctx.dve.cnt

                    @blk.scalar
                    def _(s, ti=ti, sb=sb, trid=trid):
                        ctx.dve.wait(s, trid)
                        s.wait_ge(ctx.act.sem, ctx.act.cnt)
                        ctx.act.inc(s.activation(out=pb[ti % 2][:, 0:128], in_=sb[:, 0:128],
                                                 func=AF.Exp))
                    expd = ctx.act.cnt

                    @blk.tensor
                    def _(t_, h=h, ti=ti, expd=expd):
                        ctx.act.wait(t_, expd)
                        mm = t_.matmul(cbanks[h][0:65, ti * 128:(ti + 1) * 128],
                                       vdiag[:, ti * 1040 + 65 * h: ti * 1040 + 65 * h + 65],
                                       pb[ti % 2][:, 0:128],
                                       start=False, stop=(ti == NTI - 1), skip_group_check=True)
                        ctx.pe.inc(mm)
            pv_done = ctx.pe.cnt

            for i, h in enumerate(heads):
                @blk.vector
                def _(v, h=h, pv_done=pv_done):
                    ctx.pe.wait(v, pv_done)
                    ctx.dve.wait(v)
                    ctx.dve.inc(v.reciprocal(out=rl32[0:1, 0:QGS], in_=cbanks[h][64:65, 0:QGS]))
                    v.wait_ge(ctx.dve.sem, ctx.dve.cnt)
                    ctx.dve.inc(v.tensor_copy(out=rlb[0:1, 0:QGS], in_=rl32[0:1, 0:QGS]))
                rc = ctx.dve.cnt

                b2 = lbank()

                @blk.tensor
                def _(t_, b2=b2, rc=rc):
                    ctx.dve.wait(t_, rc)
                    ctx.pe.inc(t_.matmul(b2[0:64, 0:QGS], ctx.ones_row[:, 0:64],
                                         rlb[0:1, 0:QGS], start=True, stop=True))
                bcc = ctx.pe.cnt

                @blk.vector
                def _(v, qg=qg, h=h, b2=b2, bcc=bcc):
                    ctx.pe.wait(v, bcc)
                    ctx.dve.inc(v.tensor_copy(out=rlf[0:64, 0:QGS], in_=b2[0:64, 0:QGS]))
                    v.wait_ge(ctx.dve.sem, ctx.dve.cnt)
                    ctx.dve.inc(v.tensor_tensor(
                        out=ctxfm[64 * (h % 2):64 * (h % 2) + 64,
                                  (h // 2) * T + qg * QGS:(h // 2) * T + (qg + 1) * QGS],
                        in0=cbanks[h][0:64, 0:QGS],
                        in1=rlf[0:64, 0:QGS],
                        op=OP.mult))

    ctx.dbg_ctxfm = ctxfm
    ar.pop()
    wpb = ar.alloc(8 * 1024, BF16)
    prow = ar.alloc(1024, BF16)
    lns = ctx.cols("l3a_lns", 2)
    wgp = ctx.dgroup()
    load_weight(ctx, blk, W["L3a_WpT"], wpb, wait=lambda g: ctx.pe.wait(g), group=wgp)

    @blk.gpsimd
    def _(g):
        wgp.issue(g.dma_start(out=prow[0:1, 0:1024], in_=W["L3a_bp"][0:1, :]))

    for t in range(NT):
        ob = [ctx.bank(), ctx.bank()]

        @blk.tensor
        def _(t_, t=t, ob=ob):
            if t == 0:
                wgp.wait(t_)
            ctx.dve.wait(t_)
            ctx.act.wait(t_)
            for nb in range(2):
                bias_mm(ctx, t_, ob[nb][:, 0:512], prow[0:1, nb * 512:(nb + 1) * 512], start=True)
                for kc in range(NCH):
                    mm = t_.matmul(ob[nb][:, 0:512],
                                   ctxfm[:, kc * T + t * 128: kc * T + (t + 1) * 128],
                                   wpb[:, kc * 1024 + nb * 512: kc * 1024 + (nb + 1) * 512],
                                   start=False, stop=(kc == NCH - 1))
                ctx.pe.inc(mm)
        mmc = ctx.pe.cnt

        @blk.vector
        def _(v, t=t, ob=ob, mmc=mmc):
            ctx.pe.wait(v, mmc)
            for nb in range(2):
                ctx.dve.inc(v.scalar_tensor_tensor(
                    out=ctx.X[:, t * D + nb * 512: t * D + (nb + 1) * 512],
                    in0=ob[nb][:, 0:512], scalar=1.0,
                    in1=ctx.X[:, t * D + nb * 512: t * D + (nb + 1) * 512],
                    op0=OP.mult, op1=OP.add,
                    accum_out=lns[:, nb:nb + 1]))
            v.wait_ge(ctx.dve.sem, ctx.dve.cnt)
            ctx.dve.inc(v.tensor_scalar(out=sums[:, t:t + 1], in0=lns[:, 0:1],
                                        scalar1=lns[:, 1:2], scalar2=None, op0=OP.add))

    post_ln(ctx, blk)
    ar.pop()



def halo_from_row(ctx, blk, W, dram, rowkey, validkey, scratchkey, grow, brow, hpc, gb_wait):
    hrow = ctx.cols("hfr_s", 4)
    ar = ctx.arena
    rbuf = ar.alloc(1024, F32)
    rb16 = ar.alloc(1024, BF16)
    vcol = ctx.cols("hfr_v", 1)
    g1 = ctx.hgroup()

    @blk.sync
    def _(s):
        ctx.dve.wait(s)
        ctx.act.wait(s)
        g1.issue(s.dma_start(out=rbuf[0:1, 0:1024], in_=W[rowkey][0:1, :]))
        g1.issue(s.dma_start(out=vcol, in_=W[validkey]))

    @blk.scalar
    def _(s):
        g1.wait(s)
        s.wait_ge(ctx.act.sem, ctx.act.cnt)
        ctx.act.inc(s.activation(out=rb16[0:1, 0:1024], in_=rbuf[0:1, 0:1024],
                                 func=AF.Copy, bias=0.0, accum_out=hrow[0:1, 0:1]))
        s.wait_ge(ctx.act.sem, ctx.act.cnt)
        ctx.act.inc(s.activation(out=rb16[0:1, 0:1024], in_=rbuf[0:1, 0:1024],
                                 func=AF.Square, accum_out=hrow[0:1, 1:2]))

    @blk.vector
    def _(v):
        ctx.act.wait(v)
        v.wait_ge(ctx.dve.sem, ctx.dve.cnt)
        ctx.dve.inc(v.tensor_scalar(out=hrow[0:1, 0:1], in0=hrow[0:1, 0:1],
                                    scalar1=1.0 / D, scalar2=None, op0=OP.mult))
        v.wait_ge(ctx.dve.sem, ctx.dve.cnt)
        ctx.dve.inc(v.tensor_tensor(out=hrow[0:1, 2:3], in0=hrow[0:1, 0:1],
                                    in1=hrow[0:1, 0:1], op=OP.mult))
        v.wait_ge(ctx.dve.sem, ctx.dve.cnt)
        ctx.dve.inc(v.scalar_tensor_tensor(out=hrow[0:1, 1:2], in0=hrow[0:1, 1:2],
                                           scalar=1.0 / D, in1=hrow[0:1, 2:3],
                                           op0=OP.mult, op1=OP.subtract))

    @blk.scalar
    def _(s):
        ctx.dve.wait(s)
        s.wait_ge(ctx.act.sem, ctx.act.cnt)
        ctx.act.inc(s.activation(out=hrow[0:1, 1:2], in_=hrow[0:1, 1:2], func=AF.Ln,
                                 bias=ctx.constcol(LN_EPS)[0:1, 0:1]))
        s.wait_ge(ctx.act.sem, ctx.act.cnt)
        ctx.act.inc(s.activation(out=hrow[0:1, 3:4], in_=hrow[0:1, 1:2], func=AF.Exp,
                                 scale=-0.5))

    @blk.vector
    def _(v):
        ctx.act.wait(v)
        if gb_wait is not None:
            gb_wait(v)
        v.wait_ge(ctx.dve.sem, ctx.dve.cnt)
        ctx.dve.inc(v.tensor_scalar(out=rbuf[0:1, 0:1024], in0=rbuf[0:1, 0:1024],
                                    scalar1=hrow[0:1, 0:1], scalar2=hrow[0:1, 3:4],
                                    op0=OP.subtract, op1=OP.mult))
        v.wait_ge(ctx.dve.sem, ctx.dve.cnt)
        if grow is not None:
            ctx.dve.inc(v.tensor_tensor(out=rbuf[0:1, 0:1024], in0=rbuf[0:1, 0:1024],
                                        in1=grow[0:1, 0:1024], op=OP.mult))
            v.wait_ge(ctx.dve.sem, ctx.dve.cnt)
            ctx.dve.inc(v.tensor_tensor(out=rbuf[0:1, 0:1024], in0=rbuf[0:1, 0:1024],
                                        in1=brow[0:1, 0:1024], op=OP.add))
            v.wait_ge(ctx.dve.sem, ctx.dve.cnt)
        ctx.dve.inc(v.tensor_copy(out=rb16[0:1, 0:1024], in_=rbuf[0:1, 0:1024]))
    rowd = ctx.dve.cnt

    stg = ctx.hgroup()

    @blk.sync
    def _(s):
        ctx.dve.wait(s, rowd)
        stg.issue(s.dma_start(out=dram[scratchkey][0:1, :], in_=rb16[0:1, 0:1024]))

    ldg = ctx.hgroup()

    @blk.sync
    def _(s):
        stg.wait(s)
        ctx.dve.wait(s)
        ldg.issue(s.dma_start(out=hpc[:, 0:8],
                              in_=dram[scratchkey][0:1, :].rearrange("o (c p) -> p (o c)", p=128),
                              allow_slow_non_contiguous=True))

    @blk.vector
    def _(v):
        ldg.wait(v)
        v.wait_ge(ctx.dve.sem, ctx.dve.cnt)
        ctx.dve.inc(v.tensor_scalar(out=hpc[:, 0:8], in0=hpc[:, 0:8],
                                    scalar1=vcol[:, 0:1], scalar2=None, op0=OP.mult))
    return ctx.dve.cnt


def ln_affine(ctx, blk, gb_psum_rows, wait_grp=None):
    for row, dst in gb_psum_rows:
        b = [ctx.bank(), ctx.bank()]

        @blk.tensor
        def _(t_, row=row, b=b):
            if wait_grp is not None:
                wait_grp.wait(t_)
            ctx.dve.wait(t_)
            ctx.act.wait(t_)
            for nb in range(2):
                ctx.pe.inc(t_.matmul(b[nb][:, 0:512], ctx.ones_row[:, 0:128],
                                     row[0:1, nb * 512:(nb + 1) * 512], start=True, stop=True))
        mmc = ctx.pe.cnt

        @blk.vector
        def _(v, dst=dst, b=b, mmc=mmc):
            ctx.pe.wait(v, mmc)
            for nb in range(2):
                ctx.dve.inc(v.tensor_copy(out=dst[:, nb * 512:(nb + 1) * 512], in_=b[nb][:, 0:512]))


def emit_L1a_stage(ctx, blk, W, dram, stage, gsize=4):
    T, NT = ctx.T, ctx.NT
    ar = ctx.arena
    ar.push()
    sums = ctx.cols("lnsum", NT)
    mu = ctx.cols("l1a_mu", NT)
    rstd = ctx.cols("l1a_rstd", NT)
    ssq = ctx.cols("l1a_ssq", NT)
    ssum = ctx.cols("l1a_ssum", NT)
    carr = ctx.cols("l1a_carr", 48)
    gbuf = ar.alloc(1024, BF16)
    bbuf = ar.alloc(1024, BF16)
    gb_bc = [ar.alloc(1024, BF16), ar.alloc(1024, BF16)]
    TG = min(512, T)
    nt4 = TG // 128

    def build_sh(mix_idx, ready_markers, hsrc, hpcols, hdone):
        for c in range(NCH):
            @blk.vector
            def _(v, c=c):
                if ready_markers:
                    for m in ready_markers:
                        ctx.pe.wait(v, m)
                    ready_markers.clear()
                v.wait_ge(ctx.dve.sem, ctx.dve.cnt)
                if c == 0:
                    ctx.dve.wait(v, hdone)
                    ctx.act.wait(v)
                ctx.dve.inc(v.tensor_tensor(out=sh[:, c * T + 1: c * T + T],
                                            in0=hsrc[:, c * T + 1: c * T + T],
                                            in1=hsrc[:, c * T: c * T + T - 1], op=OP.subtract))
                v.wait_ge(ctx.dve.sem, ctx.dve.cnt)
                ctx.dve.inc(v.scalar_tensor_tensor(out=sh[:, c * T + 1: c * T + T],
                                                   in0=sh[:, c * T + 1: c * T + T],
                                                   scalar=mixc[:, mix_idx * 8 + c: mix_idx * 8 + c + 1],
                                                   in1=hsrc[:, c * T: c * T + T - 1],
                                                   op0=OP.mult, op1=OP.add))
                v.wait_ge(ctx.dve.sem, ctx.dve.cnt)
                ctx.dve.inc(v.tensor_tensor(out=sh[:, c * T: c * T + 1],
                                            in0=hsrc[:, c * T: c * T + 1],
                                            in1=hpcols[:, c:c + 1], op=OP.subtract))
                v.wait_ge(ctx.dve.sem, ctx.dve.cnt)
                ctx.dve.inc(v.scalar_tensor_tensor(out=sh[:, c * T: c * T + 1],
                                                   in0=sh[:, c * T: c * T + 1],
                                                   scalar=mixc[:, mix_idx * 8 + c: mix_idx * 8 + c + 1],
                                                   in1=hpcols[:, c:c + 1],
                                                   op0=OP.mult, op1=OP.add))
        return ctx.dve.cnt

    if stage == 'A':
        ar.push()
        hfm = ar.alloc(NCH * T, BF16)
        htile = [ar.alloc(1024, F32) for _ in range(2)]
        mixc = ctx.cols("l1a_mix", 8 * 5)

        rowsg = ctx.dgroup()

        @blk.gpsimd
        def _(g):
            ctx.pe.wait(g)
            ctx.dve.wait(g)
            rowsg.issue(g.dma_start(out=gbuf[0:1, 0:1024], in_=W["L1a_ln1g"][0:1, :]))
            rowsg.issue(g.dma_start(out=bbuf[0:1, 0:1024], in_=W["L1a_ln1b"][0:1, :]))

        mixg = ctx.hgroup()

        @blk.sync
        def _(s):
            for i, nm in enumerate(["L1a_mixk_c", "L1a_mixv_c", "L1a_mixr_c",
                                    "L1a_cmixk_c", "L1a_cmixr_c"]):
                mixg.issue(s.dma_start(out=mixc[:, i * 8:(i + 1) * 8], in_=W[nm]))
            mixg.issue(s.dma_start(out=carr[:, 8:16], in_=W["L1a_u_c"]))

        @blk.scalar
        def _(s):
            mixg.wait(s)
            ctx.dve.wait(s)
            ctx.act.inc(s.activation(out=carr[:, 8:16], in_=carr[:, 8:16], func=AF.Exp))
        eu_done = ctx.act.cnt

        @blk.gpsimd
        def _(g):
            rowsg.wait(g)
            ctx.pool.inc(g.engine_nop())
        ln_affine(ctx, blk, [(gbuf[0:1, 0:1024], gb_bc[0]), (bbuf[0:1, 0:1024], gb_bc[1])],
                  wait_grp=rowsg)
        gb_done = ctx.dve.cnt

        def ln_pass(src_tile_fn, gbc, bbc, out_fn, sums_to=None):
            for t in range(NT):
                @blk.scalar
                def _(s, t=t):
                    ctx.dve.wait(s)
                    if t > 0:
                        s.wait_ge(ctx.act.sem, ctx.act.cnt)
                    src = src_tile_fn(t)
                    ctx.act.inc(s.activation(out=htile[1][:, 0:1024], in_=src, func=AF.Copy,
                                             bias=0.0, accum_out=ssum[:, t:t + 1]))
                    s.wait_ge(ctx.act.sem, ctx.act.cnt)
                    ctx.act.inc(s.activation(out=htile[1][:, 0:1024], in_=src, func=AF.Square,
                                             accum_out=ssq[:, t:t + 1]))

            @blk.vector
            def _(v):
                ctx.act.wait(v)
                ctx.dve.inc(v.tensor_scalar(out=mu, in0=ssum, scalar1=1.0 / D, scalar2=None, op0=OP.mult))
                v.wait_ge(ctx.dve.sem, ctx.dve.cnt)
                ctx.dve.inc(v.tensor_tensor(out=rstd, in0=mu, in1=mu, op=OP.mult))
                v.wait_ge(ctx.dve.sem, ctx.dve.cnt)
                ctx.dve.inc(v.scalar_tensor_tensor(out=ssq, in0=ssq, scalar=1.0 / D,
                                                   in1=rstd, op0=OP.mult, op1=OP.subtract))

            @blk.scalar
            def _(s):
                ctx.dve.wait(s)
                ctx.act.inc(s.activation(out=ssq, in_=ssq, func=AF.Ln, bias=ctx.constcol(LN_EPS)))
                s.wait_ge(ctx.act.sem, ctx.act.cnt)
                ctx.act.inc(s.activation(out=rstd, in_=ssq, func=AF.Exp, scale=-0.5))

            for t in range(NT):
                @blk.vector
                def _(v, t=t):
                    ctx.act.wait(v)
                    if t == 0:
                        ctx.dve.wait(v, gb_done)
                    par = t % 2
                    v.wait_ge(ctx.dve.sem, ctx.dve.cnt)
                    src = src_tile_fn(t)
                    ctx.dve.inc(v.tensor_scalar(out=htile[par][:, 0:1024], in0=src,
                                                scalar1=mu[:, t:t + 1], scalar2=rstd[:, t:t + 1],
                                                op0=OP.subtract, op1=OP.mult))
                    v.wait_ge(ctx.dve.sem, ctx.dve.cnt)
                    ctx.dve.inc(v.tensor_tensor(out=htile[par][:, 0:1024], in0=htile[par][:, 0:1024],
                                                in1=gbc, op=OP.mult))
                    v.wait_ge(ctx.dve.sem, ctx.dve.cnt)
                    ctx.dve.inc(v.tensor_tensor(out=htile[par][:, 0:1024], in0=htile[par][:, 0:1024],
                                                in1=bbc, op=OP.add))
                    out_fn(v, t, htile[par][:, 0:1024])

        tr_state = {"pe": None}

        def to_fm(v, t, src, dst):
            pass

        hfm_done_markers = {}

        def out_h(v, t, src):
            par = t % 2
            v.wait_ge(ctx.dve.sem, ctx.dve.cnt)
            if t >= 2:
                ctx.pe.wait(v, hfm_done_markers[t - 2][0])
            ctx.dve.inc(v.tensor_copy(out=ctx.CB[:, par * D:(par + 1) * D], in_=src))
            cast_cnt = ctx.dve.cnt

            @blk.tensor
            def _(t_, t=t, par=par, cast_cnt=cast_cnt):
                ctx.dve.wait(t_, cast_cnt)
                ctx.dve.wait(t_)
                for c in range(NCH):
                    mm = t_.transpose(ctx.pt[:, c * 128:(c + 1) * 128],
                                      ctx.CB[:, par * D + c * 128:(par + 1 - 1) * D + par * 0 + c * 128 + 128],
                                      ctx.ident[:])
                    ctx.pe.inc(mm)
            hfm_done_markers[t] = (ctx.pe.cnt, None)

            @blk.vector
            def _(v2, t=t):
                ctx.pe.wait(v2, hfm_done_markers[t][0])
                src3 = ctx.pt[:, 0:1024].rearrange("p (c j) -> p c j", c=NCH)
                dst3 = hfm.rearrange("p (c t) -> p c t", t=T)[:, :, t * 128:(t + 1) * 128]
                ctx.dve.inc(v2.tensor_copy(out=dst3, in_=src3))

        ln_pass(lambda t: ctx.X[:, t * D:(t + 1) * D], gb_bc[0], gb_bc[1], out_h)
        h_done = ctx.dve.cnt

        hprevc = ar.alloc(8, BF16)
        halo_done = halo_from_row(ctx, blk, W, dram, "L1a_xprev", "L1a_hvalid", "l1a_hrs",
                                  gbuf[0:1, 0:1024], bbuf[0:1, 0:1024], hprevc,
                                  gb_wait=lambda v: ctx.dve.wait(v, gb_done))
        sh = ar.alloc(NCH * T, BF16)
        wpc = [ar.alloc(8 * 128, BF16) for _ in range(2)]
        kvevac = ar.alloc(T, BF16)
        srtile = ar.alloc(4 * 1024, BF16)

        def proj_fm_spill(wname, dname, sh_done, extra_waits=()):
            st = ctx.hgroup()
            pgr = []
            for oc in range(NCH):
                wg_ = ctx.dgroup()

                @blk.gpsimd
                def _(g, oc=oc):
                    ctx.pe.wait(g)
                    if len(ctx.wpc_grps) >= 2:
                        ctx.wpc_grps[-2].wait(g)
                    wg_.issue(g.dma_start(out=wpc[oc % 2][0:128, 0:8 * 128].rearrange("p (f c) -> p f c", f=NCH),
                                          in_=W[wname][:, oc * 128:(oc + 1) * 128]
                                          .rearrange("(f p) c -> p f c", p=128)))
                ctx.wpc_grps.append(wg_)

                for tg in range(T // TG):
                    b = ctx.bank()

                    @blk.tensor
                    def _(t_, oc=oc, tg=tg, b=b):
                        wg_.wait(t_)
                        if oc == 0 and tg == 0:
                            ctx.dve.wait(t_, sh_done)
                        ctx.dve.wait(t_)
                        ctx.act.wait(t_)
                        for kc in range(NCH):
                            mm = t_.matmul(b[:, 0:TG],
                                           wpc[oc % 2][:, kc * 128:(kc + 1) * 128],
                                           sh[:, kc * T + tg * TG: kc * T + (tg + 1) * TG],
                                           start=(kc == 0), stop=(kc == NCH - 1))
                        ctx.pe.inc(mm)
                    mmc = ctx.pe.cnt

                    @blk.scalar
                    def _(s, oc=oc, tg=tg, b=b, mmc=mmc):
                        ctx.pe.wait(s, mmc)
                        if tg == 0:
                            st.wait(s)
                            for g2 in extra_waits:
                                g2.wait(s)
                        s.wait_ge(ctx.act.sem, ctx.act.cnt)
                        ctx.act.inc(s.activation(out=kvevac[:, tg * TG:(tg + 1) * TG],
                                                 in_=b[:, 0:TG], func=AF.Copy, bias=0.0))
                evd = ctx.act.cnt

                @blk.sync
                def _(s, oc=oc, evd=evd):
                    ctx.act.wait(s, evd)
                    st.issue(s.dma_start(out=dram[dname][oc * 128:(oc + 1) * 128, :],
                                         in_=kvevac[:, 0:T]))
            return st

        ctx.wpc_grps = []
        shk_done = build_sh(0, [], hfm, hprevc, halo_done)
        kst_grp = proj_fm_spill("L1a_WkT", "l1a_k", shk_done)
        markers = [ctx.pe.cnt]
        shv_done = build_sh(1, markers, hfm, hprevc, halo_done)
        vst_grp = proj_fm_spill("L1a_WvT", "l1a_v", shv_done, extra_waits=(kst_grp,))
        markers = [ctx.pe.cnt]
        shr_done = build_sh(2, markers, hfm, hprevc, halo_done)

        rst_grp = ctx.hgroup()
        wrhalf = ar.alloc(8 * 512, BF16)
        srtile = ar.alloc(2 * 512, BF16)
        nt4 = TG // 128
        ctx.wrh_grps = []
        for nb in range(2):
            wrg = ctx.dgroup()

            @blk.gpsimd
            def _(g, nb=nb):
                ctx.pe.wait(g)
                if len(ctx.wrh_grps) >= 1:
                    ctx.wrh_grps[-1].wait(g)
                for kc in range(NCH):
                    wrg.issue(g.dma_start(out=wrhalf[0:128, kc * 512:(kc + 1) * 512],
                                          in_=W["L1a_WrT"][kc * 128:(kc + 1) * 128,
                                                           nb * 512:(nb + 1) * 512]))
            ctx.wrh_grps.append(wrg)
            for t in range(NT):
                b_ = ctx.bank()

                @blk.tensor
                def _(t_, nb=nb, t=t, b_=b_):
                    wrg.wait(t_)
                    if nb == 0 and t == 0:
                        ctx.dve.wait(t_, shr_done)
                    ctx.act.wait(t_)
                    ctx.dve.wait(t_)
                    for kc in range(NCH):
                        mm = t_.matmul(b_[:, 0:512],
                                       sh[:, kc * T + t * 128: kc * T + (t + 1) * 128],
                                       wrhalf[:, kc * 512:(kc + 1) * 512],
                                       start=(kc == 0), stop=(kc == NCH - 1))
                    ctx.pe.inc(mm)
                mmc = ctx.pe.cnt

                @blk.scalar
                def _(s, nb=nb, t=t, b_=b_, mmc=mmc):
                    ctx.pe.wait(s, mmc)
                    rst_grp.wait(s)
                    s.wait_ge(ctx.act.sem, ctx.act.cnt)
                    ctx.act.inc(s.activation(out=srtile[:, (t % 2) * 512:(t % 2 + 1) * 512],
                                             in_=b_[:, 0:512], func=AF.Sigmoid))
                evd = ctx.act.cnt

                @blk.sync
                def _(s, nb=nb, t=t, evd=evd):
                    ctx.act.wait(s, evd)
                    rst_grp.issue(s.dma_start(
                        out=dram["l1a_r"][t * 128:(t + 1) * 128, nb * 512:(nb + 1) * 512],
                        in_=srtile[:, (t % 2) * 512:(t % 2 + 1) * 512]))
        r_spilled = ctx.pe.cnt

        markers = [ctx.pe.cnt]
        ar.pop()
        delta = ar.alloc(NCH * T, BF16)
        ar.push()
        ctx.kv_grps = []
        kj = [ar.alloc(T, BF16) for _ in range(2)]
        vj = [ar.alloc(T, BF16) for _ in range(2)]
        pe_ = [ar.alloc(T, F32) for _ in range(2)]
        psc = ar.alloc(T, F32)
        send = ar.alloc(16, F32)

        def load_kv(j, which, grp=None):
            g2 = grp or ctx.dgroup()

            @blk.gpsimd
            def _(g, j=j, which=which):
                ctx.act.wait(g)
                ctx.dve.wait(g)
                if len(ctx.kv_grps) >= 2:
                    ctx.kv_grps[-2].wait(g)
                if which in ("k", "kv"):
                    g2.issue(g.dma_start(out=kj[j % 2][0:128, 0:T],
                                         in_=dram["l1a_k"][j * 128:(j + 1) * 128, :]))
                if which in ("v", "kv"):
                    g2.issue(g.dma_start(out=vj[j % 2][0:128, 0:T],
                                         in_=dram["l1a_v"][j * 128:(j + 1) * 128, :]))
            ctx.kv_grps.append(g2)
            return g2

        for j in range(NCH):
            grp = load_kv(j, "k")

            @blk.scalar
            def _(s, j=j):
                grp.wait(s)
                kst_grp.wait(s)
                s.wait_ge(ctx.act.sem, ctx.act.cnt)
                ctx.dve.wait(s)
                ctx.act.inc(s.activation(out=pe_[j % 2][:, 0:T], in_=kj[j % 2][:, 0:T], func=AF.Exp))
            ed = ctx.act.cnt

            @blk.vector
            def _(v, j=j, ed=ed):
                ctx.act.wait(v, ed)
                ctx.dve.inc(v.tensor_tensor_scan(out=psc[:, 0:T], data0=pe_[j % 2][:, 0:T],
                                                 data1=pe_[j % 2][:, 0:T], initial=0.0,
                                                 op0=OP.add, op1=OP.bypass))
                v.wait_ge(ctx.dve.sem, ctx.dve.cnt)
                ctx.dve.inc(v.tensor_copy(out=send[:, j:j + 1], in_=psc[:, T - 1:T]))
        send_done = ctx.dve.cnt


        sndst = ctx.hgroup()

        @blk.sync
        def _(s):
            ctx.dve.wait(s, send_done)
            sndst.issue(s.dma_start(out=dram["l1a_s"][:, :], in_=send[:, 0:8]))

        @blk.sync
        def _(s):
            sndst.wait(s)
            kst_grp.wait(s)
            vst_grp.wait(s)
            rst_grp.wait(s)

        ar.pop()
        return

    if stage in ('D1', 'D2'):
        ctx.kv_grps = []
        kj = [ar.alloc(T, BF16) for _ in range(2)]
        vj = [ar.alloc(T, BF16) for _ in range(2)]
        pe_ = [ar.alloc(T, F32) for _ in range(2)]
        psc = ar.alloc(T, F32)
        send = ar.alloc(16, F32)
        pd = ar.alloc(T, F32)
        pm_ = ar.alloc(T, F32)
        pr2 = ar.alloc(T, F32)
        pbw = ar.alloc(T, F32)
        pRR = ar.alloc(T, F32)
        wrcols = ar.alloc(32, F32)

        def load_kv(j, which, grp=None):
            g2 = grp or ctx.dgroup()

            @blk.gpsimd
            def _(g, j=j, which=which):
                ctx.act.wait(g)
                ctx.dve.wait(g)
                if len(ctx.kv_grps) >= 2:
                    ctx.kv_grps[-2].wait(g)
                if which in ("k", "kv"):
                    g2.issue(g.dma_start(out=kj[j % 2][0:128, 0:T],
                                         in_=W["l1a_k"][j * 128:(j + 1) * 128, :]))
                if which in ("v", "kv"):
                    g2.issue(g.dma_start(out=vj[j % 2][0:128, 0:T],
                                         in_=W["l1a_v"][j * 128:(j + 1) * 128, :]))
            ctx.kv_grps.append(g2)
            return g2

    if stage == 'D1':
        sga = ar.alloc(4 * 8, F32)
        pmb = ctx.cols("l1a_pm", 4)
        sgg = ctx.hgroup()
        pmg = ctx.hgroup()

        @blk.sync
        def _(s):
            ctx.dve.wait(s)
            sgg.issue(s.dma_start(out=sga[0:128, 0:32].rearrange("p (r c) -> p r c", r=4),
                                  in_=W["l1a_sa"][:, :].rearrange("(r p) c -> p r c", p=128)))
            pmg.issue(s.dma_start(out=pmb, in_=W["L1a_pm"]))

        @blk.vector
        def _(v):
            sgg.wait(v)
            pmg.wait(v)
            v.wait_ge(ctx.dve.sem, ctx.dve.cnt)
            v.memset(carr[:, 0:8], 0.0)
            ctx.dve.inc(v.engine_nop())
            for r in range(gsize):
                v.wait_ge(ctx.dve.sem, ctx.dve.cnt)
                ctx.dve.inc(v.scalar_tensor_tensor(out=carr[:, 0:8], in0=sga[:, r * 8:(r + 1) * 8],
                                                   scalar=pmb[:, r:r + 1], in1=carr[:, 0:8],
                                                   op0=OP.mult, op1=OP.add))
        din_done = ctx.dve.cnt

        spl = ctx.hgroup()

        for j in range(NCH):
            grp = load_kv(j, "kv")

            @blk.scalar
            def _(s, j=j):
                grp.wait(s)
                pass
                s.wait_ge(ctx.act.sem, ctx.act.cnt)
                ctx.dve.wait(s)
                ctx.act.inc(s.activation(out=pe_[j % 2][:, 0:T], in_=kj[j % 2][:, 0:T], func=AF.Exp))
            ed = ctx.act.cnt

            @blk.vector
            def _(v, j=j, ed=ed):
                ctx.act.wait(v, ed)
                if j == 0:
                    ctx.dve.wait(v, din_done)
                else:
                    spl.wait(v)
                v.wait_ge(ctx.dve.sem, ctx.dve.cnt)
                e = pe_[j % 2][:, 0:T]
                ctx.dve.inc(v.tensor_tensor_scan(out=psc[:, 0:T], data0=e, data1=e,
                                                 initial=carr[:, j:j + 1], op0=OP.add, op1=OP.bypass))
                v.wait_ge(ctx.dve.sem, ctx.dve.cnt)
                ctx.dve.inc(v.tensor_tensor(out=pd[:, 0:T], in0=psc[:, 0:T], in1=e, op=OP.subtract))
                v.wait_ge(ctx.dve.sem, ctx.dve.cnt)
                ctx.dve.inc(v.tensor_tensor(out=pm_[:, 0:T], in0=pd[:, 0:T], in1=e, op=OP.max))
                v.wait_ge(ctx.dve.sem, ctx.dve.cnt)
                ctx.dve.inc(v.reciprocal(out=pr2[:, 0:T], in_=pm_[:, 0:T]))
                v.wait_ge(ctx.dve.sem, ctx.dve.cnt)
                ctx.dve.inc(v.tensor_tensor(out=pm_[:, 0:T], in0=psc[:, 0:T], in1=pr2[:, 0:T], op=OP.mult))
                v.wait_ge(ctx.dve.sem, ctx.dve.cnt)
                ctx.dve.inc(v.tensor_tensor(out=psc[:, 0:T], in0=e, in1=vj[j % 2][:, 0:T], op=OP.mult))
                v.wait_ge(ctx.dve.sem, ctx.dve.cnt)
                ctx.dve.inc(v.tensor_tensor(out=psc[:, 0:T], in0=psc[:, 0:T], in1=pm_[:, 0:T], op=OP.mult))
                v.wait_ge(ctx.dve.sem, ctx.dve.cnt)
                ctx.dve.inc(v.tensor_tensor_scan(out=pbw[:, 0:T], data0=pm_[:, 0:T], data1=psc[:, 0:T],
                                                 initial=0.0, op0=OP.mult, op1=OP.add))
                v.wait_ge(ctx.dve.sem, ctx.dve.cnt)
                ctx.dve.inc(v.tensor_tensor_scan(out=pRR[:, 0:T], data0=pm_[:, 0:T], data1=pm_[:, 0:T],
                                                 initial=1.0, op0=OP.mult, op1=OP.bypass))
                v.wait_ge(ctx.dve.sem, ctx.dve.cnt)
                ctx.dve.inc(v.tensor_copy(out=wrcols[:, j:j + 1], in_=pbw[:, T - 1:T]))
                v.wait_ge(ctx.dve.sem, ctx.dve.cnt)
                ctx.dve.inc(v.tensor_copy(out=wrcols[:, 8 + j:9 + j], in_=pRR[:, T - 1:T]))
            pld = ctx.dve.cnt

            @blk.sync
            def _(s, j=j, pld=pld):
                ctx.dve.wait(s, pld)
                spl.issue(s.dma_start(out=dram["l1a_dp"][j * 128:(j + 1) * 128, :], in_=pd[:, 0:T]))
                spl.issue(s.dma_start(out=dram["l1a_wl"][j * 128:(j + 1) * 128, :], in_=pbw[:, 0:T]))
                spl.issue(s.dma_start(out=dram["l1a_rr"][j * 128:(j + 1) * 128, :], in_=pRR[:, 0:T]))

        wrst = ctx.hgroup()

        @blk.sync
        def _(s):
            ctx.dve.wait(s)
            wrst.issue(s.dma_start(out=dram["l1a_wr"][:, :], in_=wrcols[:, 0:16]))

        @blk.sync
        def _(s):
            wrst.wait(s)
            spl.wait(s)

        ar.pop()
        return

    if stage == 'D2':
        wra = ar.alloc(4 * 16, F32)
        wrg2 = ctx.hgroup()
        pmb = ctx.cols("l1a_pm", 4)
        pmg = ctx.hgroup()

        @blk.sync
        def _(s):
            ctx.dve.wait(s)
            wrg2.issue(s.dma_start(out=wra[0:128, 0:64].rearrange("p (r c) -> p r c", r=4),
                                   in_=W["l1a_wra"][:, :].rearrange("(r p) c -> p r c", p=128)))
            pmg.issue(s.dma_start(out=pmb, in_=W["L1a_pm"]))
            pmg.issue(s.dma_start(out=carr[:, 8:16], in_=W["L1a_u_c"]))

        @blk.scalar
        def _(s):
            pmg.wait(s)
            s.wait_ge(ctx.act.sem, ctx.act.cnt)
            ctx.act.inc(s.activation(out=carr[:, 8:16], in_=carr[:, 8:16], func=AF.Exp))
        eu_done = ctx.act.cnt

        @blk.vector
        def _(v):
            wrg2.wait(v)
            pmg.wait(v)
            v.wait_ge(ctx.dve.sem, ctx.dve.cnt)
            v.memset(carr[:, 16:24], 0.0)
            ctx.dve.inc(v.tensor_scalar(out=carr[:, 24:28], in0=pmb, scalar1=-1.0, scalar2=1.0,
                                        op0=OP.mult, op1=OP.add))
            for r in range(gsize - 1):
                v.wait_ge(ctx.dve.sem, ctx.dve.cnt)
                ctx.dve.inc(v.tensor_scalar(out=carr[:, 32:40], in0=wra[:, r * 16 + 8:r * 16 + 16],
                                            scalar1=pmb[:, r:r + 1], scalar2=carr[:, 24 + r:25 + r],
                                            op0=OP.mult, op1=OP.add))
                v.wait_ge(ctx.dve.sem, ctx.dve.cnt)
                ctx.dve.inc(v.tensor_tensor(out=carr[:, 16:24], in0=carr[:, 16:24],
                                            in1=carr[:, 32:40], op=OP.mult))
                v.wait_ge(ctx.dve.sem, ctx.dve.cnt)
                ctx.dve.inc(v.tensor_scalar(out=carr[:, 40:48], in0=wra[:, r * 16:r * 16 + 8],
                                            scalar1=pmb[:, r:r + 1], scalar2=None, op0=OP.mult))
                v.wait_ge(ctx.dve.sem, ctx.dve.cnt)
                ctx.dve.inc(v.tensor_tensor(out=carr[:, 16:24], in0=carr[:, 16:24],
                                            in1=carr[:, 40:48], op=OP.add))
        win_done = ctx.dve.cnt

        delta = ar.alloc(NCH * T, BF16)

        @blk.vector
        def _(v):
            ctx.act.wait(v, eu_done)
            v.wait_ge(ctx.dve.sem, ctx.dve.cnt)
            ctx.dve.inc(v.memset(delta[:, 0:NCH * T], 0.0))
        pwl, prr_, pdp_ = pbw, pRR, pr2
        sigp = [ar.alloc(128, BF16) for _ in range(2)]

        for j in range(NCH):
            grp = load_kv(j, "kv")
            grp2 = ctx.hgroup()

            @blk.sync
            def _(s, j=j):
                ctx.dve.wait(s)
                pass
                grp2.issue(s.dma_start(out=pwl[:, 0:T], in_=W["l1a_wl"][j * 128:(j + 1) * 128, :]))
                grp2.issue(s.dma_start(out=prr_[:, 0:T], in_=W["l1a_rr"][j * 128:(j + 1) * 128, :]))
                grp2.issue(s.dma_start(out=pdp_[:, 0:T], in_=W["l1a_dp"][j * 128:(j + 1) * 128, :]))

            @blk.scalar
            def _(s, j=j):
                grp.wait(s)
                s.wait_ge(ctx.act.sem, ctx.act.cnt)
                ctx.dve.wait(s)
                ctx.act.inc(s.activation(out=pe_[j % 2][:, 0:T], in_=kj[j % 2][:, 0:T], func=AF.Exp))
            ed = ctx.act.cnt

            @blk.vector
            def _(v, j=j, ed=ed):
                ctx.act.wait(v, ed)
                grp2.wait(v)
                ctx.pe.wait(v)
                if j == 0:
                    ctx.dve.wait(v, win_done)
                v.wait_ge(ctx.dve.sem, ctx.dve.cnt)
                e = pe_[j % 2][:, 0:T]
                ctx.dve.inc(v.tensor_scalar(out=psc[:, 0:T], in0=e, scalar1=carr[:, 8 + j:9 + j],
                                            scalar2=None, op0=OP.mult))
                v.wait_ge(ctx.dve.sem, ctx.dve.cnt)
                ctx.dve.inc(v.tensor_tensor(out=pd[:, 0:T], in0=pdp_[:, 0:T], in1=psc[:, 0:T], op=OP.max))
                v.wait_ge(ctx.dve.sem, ctx.dve.cnt)
                ctx.dve.inc(v.scalar_tensor_tensor(out=pd[:, 0:T], in0=pd[:, 0:T], scalar=1e-8,
                                                   in1=pdp_[:, 0:T], op0=OP.mult, op1=OP.add))
                v.wait_ge(ctx.dve.sem, ctx.dve.cnt)
                ctx.dve.inc(v.tensor_tensor(out=pd[:, 0:T], in0=pd[:, 0:T], in1=psc[:, 0:T], op=OP.add))
                v.wait_ge(ctx.dve.sem, ctx.dve.cnt)
                ctx.dve.inc(v.reciprocal(out=pd[:, 0:T], in_=pd[:, 0:T]))
                v.wait_ge(ctx.dve.sem, ctx.dve.cnt)
                ctx.dve.inc(v.scalar_tensor_tensor(out=pm_[:, 0:T], in0=prr_[:, 0:T],
                                                   scalar=carr[:, 16 + j:17 + j],
                                                   in1=pwl[:, 0:T], op0=OP.mult, op1=OP.add))
                v.wait_ge(ctx.dve.sem, ctx.dve.cnt)
                ctx.dve.inc(v.tensor_tensor(out=psc[:, 0:T], in0=psc[:, 0:T], in1=vj[j % 2][:, 0:T], op=OP.mult))
                v.wait_ge(ctx.dve.sem, ctx.dve.cnt)
                ctx.dve.inc(v.tensor_tensor(out=psc[:, 1:T], in0=psc[:, 1:T], in1=pm_[:, 0:T - 1], op=OP.add))
                v.wait_ge(ctx.dve.sem, ctx.dve.cnt)
                ctx.dve.inc(v.tensor_scalar(out=psc[:, 0:1], in0=psc[:, 0:1],
                                            scalar1=carr[:, 16 + j:17 + j], scalar2=None, op0=OP.add))
                v.wait_ge(ctx.dve.sem, ctx.dve.cnt)
                ctx.dve.inc(v.tensor_tensor(out=psc[:, 0:T], in0=psc[:, 0:T], in1=pd[:, 0:T], op=OP.mult))
            wkv_done = ctx.dve.cnt

            for t in range(NT):
                sgp = ctx.hgroup()

                @blk.scalar
                def _(s, j=j, t=t):
                    ctx.dve.wait(s)
                    sgp.issue(s.dma_start(out=sigp[t % 2][0:128, 0:128],
                                          in_=W["l1a_r"][t * 128:(t + 1) * 128, j * 128:(j + 1) * 128]))
                b = ctx.bank()

                @blk.tensor
                def _(t_, j=j, t=t, b=b, wkv_done=wkv_done):
                    ctx.dve.wait(t_, wkv_done)
                    ctx.dve.wait(t_)
                    ctx.pe.inc(t_.transpose(b[:, 0:128], psc[:, t * 128:(t + 1) * 128], ctx.identf[:]))
                trd = ctx.pe.cnt

                @blk.vector
                def _(v, j=j, t=t, b=b, trd=trd):
                    ctx.pe.wait(v, trd)
                    sgp.wait(v)
                    ctx.dve.inc(v.tensor_tensor(out=delta[:, t * D + j * 128:(t) * D + (j + 1) * 128],
                                                in0=sigp[t % 2][:, 0:128], in1=b[:, 0:128], op=OP.mult))


        for t in range(NT):
            @blk.vector
            def _(v, t=t):
                v.wait_ge(ctx.dve.sem, ctx.dve.cnt)
                ctx.pe.wait(v)
                ctx.dve.inc(v.tensor_tensor(out=ctx.X[:, t * D:(t + 1) * D],
                                            in0=ctx.X[:, t * D:(t + 1) * D],
                                            in1=delta[:, t * D:(t + 1) * D], op=OP.add))

        ar.pop()
        return

    if stage == 'E':
        mixc = ctx.cols("l1a_mix", 8 * 5)
        mixg = ctx.hgroup()

        @blk.sync
        def _(s):
            ctx.dve.wait(s)
            for i, nm in enumerate(["L1a_mixk_c", "L1a_mixv_c", "L1a_mixr_c",
                                    "L1a_cmixk_c", "L1a_cmixr_c"]):
                mixg.issue(s.dma_start(out=mixc[:, i * 8:(i + 1) * 8], in_=W[nm]))

        @blk.vector
        def _(v):
            mixg.wait(v)
            ctx.dve.inc(v.engine_nop())

        delta = ar.alloc(NCH * T, BF16)

        @blk.vector
        def _(v):
            v.wait_ge(ctx.dve.sem, ctx.dve.cnt)
            ctx.dve.inc(v.memset(delta[:, 0:NCH * T], 0.0))
        delta1_done = ctx.dve.cnt
        rowsg = ctx.dgroup()

        @blk.gpsimd
        def _(g):
            ctx.pe.wait(g)
            ctx.dve.wait(g)
            rowsg.issue(g.dma_start(out=gbuf[0:1, 0:1024], in_=W["L1a_ln2g"][0:1, :]))
            rowsg.issue(g.dma_start(out=bbuf[0:1, 0:1024], in_=W["L1a_ln2b"][0:1, :]))
        ln_affine(ctx, blk, [(gbuf[0:1, 0:1024], gb_bc[0]), (bbuf[0:1, 0:1024], gb_bc[1])],
                  wait_grp=rowsg)
        gb_done = ctx.dve.cnt

        sh = ar.alloc(NCH * T, BF16)
        ar.push()
        h2fm = ar.alloc(NCH * T, BF16)
        htile = [ar.alloc(1024, F32), ar.alloc(1024, F32)]
        x2t = [ar.alloc(1024, F32)]
        hprevc2 = ar.alloc(8, BF16)

        def src_x2(t):
            return x2t[t % 2][:, 0:1024]

        for t in range(NT):
            @blk.vector
            def _(v, t=t):
                if t == 0:
                    ctx.dve.wait(v, delta1_done)
                    ctx.act.wait(v)
                    ctx.pe.wait(v)
                ctx.act.wait(v)
                v.wait_ge(ctx.dve.sem, ctx.dve.cnt)
                ctx.dve.inc(v.tensor_tensor(out=x2t[0][:, 0:1024], in0=ctx.X[:, t * D:(t + 1) * D],
                                            in1=delta[:, t * D:(t + 1) * D], op=OP.add))

            @blk.scalar
            def _(s, t=t):
                ctx.dve.wait(s)
                s.wait_ge(ctx.act.sem, ctx.act.cnt)
                ctx.act.inc(s.activation(out=htile[1][:, 0:1024], in_=x2t[0][:, 0:1024],
                                         func=AF.Copy, bias=0.0, accum_out=ssum[:, t:t + 1]))
                s.wait_ge(ctx.act.sem, ctx.act.cnt)
                ctx.act.inc(s.activation(out=htile[1][:, 0:1024], in_=x2t[0][:, 0:1024],
                                         func=AF.Square, accum_out=ssq[:, t:t + 1]))

        @blk.vector
        def _(v):
            ctx.act.wait(v)
            v.wait_ge(ctx.dve.sem, ctx.dve.cnt)
            ctx.dve.inc(v.tensor_scalar(out=mu, in0=ssum, scalar1=1.0 / D, scalar2=None, op0=OP.mult))
            v.wait_ge(ctx.dve.sem, ctx.dve.cnt)
            ctx.dve.inc(v.tensor_tensor(out=rstd, in0=mu, in1=mu, op=OP.mult))
            v.wait_ge(ctx.dve.sem, ctx.dve.cnt)
            ctx.dve.inc(v.scalar_tensor_tensor(out=ssq, in0=ssq, scalar=1.0 / D,
                                               in1=rstd, op0=OP.mult, op1=OP.subtract))

        @blk.scalar
        def _(s):
            ctx.dve.wait(s)
            s.wait_ge(ctx.act.sem, ctx.act.cnt)
            ctx.act.inc(s.activation(out=ssq, in_=ssq, func=AF.Ln, bias=ctx.constcol(LN_EPS)))
            s.wait_ge(ctx.act.sem, ctx.act.cnt)
            ctx.act.inc(s.activation(out=rstd, in_=ssq, func=AF.Exp, scale=-0.5))
        stats2 = ctx.act.cnt

        h2_markers = {}
        for t in range(NT):
            @blk.vector
            def _(v, t=t):
                ctx.act.wait(v, stats2)
                if t == 0:
                    ctx.dve.wait(v, gb_done)
                par = t % 2
                v.wait_ge(ctx.dve.sem, ctx.dve.cnt)
                ctx.dve.inc(v.tensor_tensor(out=x2t[0][:, 0:1024], in0=ctx.X[:, t * D:(t + 1) * D],
                                            in1=delta[:, t * D:(t + 1) * D], op=OP.add))
                v.wait_ge(ctx.dve.sem, ctx.dve.cnt)
                ctx.dve.inc(v.tensor_scalar(out=htile[par][:, 0:1024], in0=x2t[0][:, 0:1024],
                                            scalar1=mu[:, t:t + 1], scalar2=rstd[:, t:t + 1],
                                            op0=OP.subtract, op1=OP.mult))
                v.wait_ge(ctx.dve.sem, ctx.dve.cnt)
                ctx.dve.inc(v.tensor_tensor(out=htile[par][:, 0:1024], in0=htile[par][:, 0:1024],
                                            in1=gb_bc[0], op=OP.mult))
                v.wait_ge(ctx.dve.sem, ctx.dve.cnt)
                ctx.dve.inc(v.tensor_tensor(out=htile[par][:, 0:1024], in0=htile[par][:, 0:1024],
                                            in1=gb_bc[1], op=OP.add))
                if t >= 2:
                    ctx.pe.wait(v, h2_markers[t - 2])
                v.wait_ge(ctx.dve.sem, ctx.dve.cnt)
                ctx.dve.inc(v.tensor_copy(out=ctx.CB[:, par * D:(par + 1) * D], in_=htile[par][:, 0:1024]))
            cast_c = ctx.dve.cnt

            @blk.tensor
            def _(t_, t=t, cast_c=cast_c):
                ctx.dve.wait(t_, cast_c)
                ctx.dve.wait(t_)
                par = t % 2
                for c in range(NCH):
                    mm = t_.transpose(ctx.pt[:, c * 128:(c + 1) * 128],
                                      ctx.CB[:, par * D + c * 128: par * D + c * 128 + 128],
                                      ctx.ident[:])
                    ctx.pe.inc(mm)
            h2_markers[t] = ctx.pe.cnt

            @blk.vector
            def _(v, t=t):
                ctx.pe.wait(v, h2_markers[t])
                src3 = ctx.pt[:, 0:1024].rearrange("p (c j) -> p c j", c=NCH)
                dst3 = h2fm.rearrange("p (c t) -> p c t", t=T)[:, :, t * 128:(t + 1) * 128]
                ctx.dve.inc(v.tensor_copy(out=dst3, in_=src3))
        h2_done = ctx.dve.cnt

        halo2_done = halo_from_row(ctx, blk, W, dram, "L1a_x2prev", "L1a_hvalid", "l1a_hrs",
                                   gbuf[0:1, 0:1024], bbuf[0:1, 0:1024], hprevc2,
                                   gb_wait=lambda v: ctx.dve.wait(v, gb_done))
        shrc_done = build_sh(4, [], h2fm, hprevc2, halo2_done)
        rcst_grp = ctx.hgroup()
        wrchalf = ar.alloc(8 * 512, BF16)
        srtile2 = ar.alloc(2 * 512, BF16)
        ctx.wrch_grps = []
        for nb in range(2):
            wrcg = ctx.dgroup()

            @blk.gpsimd
            def _(g, nb=nb):
                ctx.pe.wait(g)
                if len(ctx.wrch_grps) >= 1:
                    ctx.wrch_grps[-1].wait(g)
                for kc in range(NCH):
                    wrcg.issue(g.dma_start(out=wrchalf[0:128, kc * 512:(kc + 1) * 512],
                                           in_=W["L1a_WrcT"][kc * 128:(kc + 1) * 128,
                                                             nb * 512:(nb + 1) * 512]))
            ctx.wrch_grps.append(wrcg)
            for t in range(NT):
                b_ = ctx.bank()

                @blk.tensor
                def _(t_, nb=nb, t=t, b_=b_):
                    wrcg.wait(t_)
                    if nb == 0 and t == 0:
                        ctx.dve.wait(t_, shrc_done)
                    ctx.act.wait(t_)
                    ctx.dve.wait(t_)
                    for kc in range(NCH):
                        mm = t_.matmul(b_[:, 0:512],
                                       sh[:, kc * T + t * 128: kc * T + (t + 1) * 128],
                                       wrchalf[:, kc * 512:(kc + 1) * 512],
                                       start=(kc == 0), stop=(kc == NCH - 1))
                    ctx.pe.inc(mm)
                mmc = ctx.pe.cnt

                @blk.scalar
                def _(s, nb=nb, t=t, b_=b_, mmc=mmc):
                    ctx.pe.wait(s, mmc)
                    rcst_grp.wait(s)
                    s.wait_ge(ctx.act.sem, ctx.act.cnt)
                    ctx.act.inc(s.activation(out=srtile2[:, (t % 2) * 512:(t % 2 + 1) * 512],
                                             in_=b_[:, 0:512], func=AF.Sigmoid))
                evd = ctx.act.cnt

                @blk.sync
                def _(s, nb=nb, t=t, evd=evd):
                    ctx.act.wait(s, evd)
                    rcst_grp.issue(s.dma_start(
                        out=dram["l1a_rc"][t * 128:(t + 1) * 128, nb * 512:(nb + 1) * 512],
                        in_=srtile2[:, (t % 2) * 512:(t % 2 + 1) * 512]))

        markers = [ctx.pe.cnt]
        shkc_done = build_sh(3, markers, h2fm, hprevc2, halo2_done)
        ar.pop()

        ctx.wkc_grps = []
        ctx.wvc_grps = []
        kcsq = ar.alloc(32 * TG, BF16)
        wkcb = [ar.alloc(8 * 128, BF16) for _ in range(2)]
        wvcb = [ar.alloc(512, BF16) for _ in range(2)]
        rcld = [ar.alloc(512, BF16) for _ in range(2)]
        relsc = ar.alloc(TG, F32)
        lns2 = ctx.cols("l1a_lns2", 2)

        for tg in range(T // TG):
            for oc in range(32):
                wkg = ctx.dgroup()

                @blk.gpsimd
                def _(g, tg=tg, oc=oc):
                    ctx.pe.wait(g)
                    ctx.dve.wait(g)
                    if len(ctx.wkc_grps) >= 2:
                        ctx.wkc_grps[-2].wait(g)
                    wkg.issue(g.dma_start(out=wkcb[oc % 2][0:128, 0:8 * 128].rearrange("p (f c) -> p f c", f=NCH),
                                          in_=W["L1a_WkcT"][:, oc * 128:(oc + 1) * 128]
                                          .rearrange("(f p) c -> p f c", p=128)))
                ctx.wkc_grps.append(wkg)
                b = ctx.bank()

                @blk.tensor
                def _(t_, tg=tg, oc=oc, b=b):
                    wkg.wait(t_)
                    if tg == 0 and oc == 0:
                        ctx.dve.wait(t_, shkc_done)
                    ctx.act.wait(t_)
                    ctx.dve.wait(t_)
                    for kc in range(NCH):
                        mm = t_.matmul(b[:, 0:TG],
                                       wkcb[oc % 2][:, kc * 128:(kc + 1) * 128],
                                       sh[:, kc * T + tg * TG: kc * T + (tg + 1) * TG],
                                       start=(kc == 0), stop=(kc == NCH - 1))
                    ctx.pe.inc(mm)
                mmc = ctx.pe.cnt

                @blk.scalar
                def _(s, oc=oc, b=b, mmc=mmc):
                    ctx.pe.wait(s, mmc)
                    ctx.dve.wait(s)
                    s.wait_ge(ctx.act.sem, ctx.act.cnt)
                    ctx.act.inc(s.activation(out=relsc[:, 0:TG], in_=b[:, 0:TG], func=AF.Relu))
                    s.wait_ge(ctx.act.sem, ctx.act.cnt)
                    ctx.act.inc(s.activation(out=kcsq[:, oc * TG:(oc + 1) * TG], in_=relsc[:, 0:TG],
                                             func=AF.Square))
            kc_done = ctx.act.cnt

            for nb in range(2):
                banks = [ctx.bank() for _ in range(nt4)]
                for kc in range(32):
                    wvg = ctx.dgroup()

                    @blk.gpsimd
                    def _(g, nb=nb, kc=kc):
                        ctx.pe.wait(g)
                        if len(ctx.wvc_grps) >= 2:
                            ctx.wvc_grps[-2].wait(g)
                        wvg.issue(g.dma_start(out=wvcb[kc % 2][0:128, 0:512],
                                              in_=W["L1a_WvcT"][kc * 128:(kc + 1) * 128,
                                                                nb * 512:(nb + 1) * 512]))
                    ctx.wvc_grps.append(wvg)

                    @blk.tensor
                    def _(t_, tg=tg, nb=nb, kc=kc, banks=banks):
                        wvg.wait(t_)
                        ctx.act.wait(t_, kc_done)
                        if kc == 0:
                            ctx.dve.wait(t_)
                        for ti in range(nt4):
                            mm = t_.matmul(banks[ti][:, 0:512],
                                           kcsq[:, kc * TG + ti * 128: kc * TG + (ti + 1) * 128],
                                           wvcb[kc % 2][:, 0:512],
                                           start=(kc == 0), stop=(kc == 31))
                        ctx.pe.inc(mm)
                vv_done = ctx.pe.cnt

                for ti in range(nt4):
                    rcg = ctx.hgroup()

                    @blk.scalar
                    def _(s, tg=tg, nb=nb, ti=ti):
                        ctx.dve.wait(s)
                        rcst_grp.wait(s)
                        t = tg * nt4 + ti
                        rcg.issue(s.dma_start(out=rcld[ti % 2][0:128, 0:512],
                                              in_=dram["l1a_rc"][t * 128:(t + 1) * 128,
                                                                 nb * 512:(nb + 1) * 512]))

                    @blk.vector
                    def _(v, tg=tg, nb=nb, ti=ti, banks=banks, vv_done=vv_done):
                        ctx.pe.wait(v, vv_done)
                        rcg.wait(v)
                        t = tg * nt4 + ti
                        v.wait_ge(ctx.dve.sem, ctx.dve.cnt)
                        ctx.dve.inc(v.tensor_tensor(out=rcld[ti % 2][:, 0:512], in0=rcld[ti % 2][:, 0:512],
                                                    in1=banks[ti][:, 0:512], op=OP.mult))
                        v.wait_ge(ctx.dve.sem, ctx.dve.cnt)
                        ctx.dve.inc(v.tensor_tensor(out=delta[:, t * D + nb * 512: t * D + (nb + 1) * 512],
                                                    in0=delta[:, t * D + nb * 512: t * D + (nb + 1) * 512],
                                                    in1=rcld[ti % 2][:, 0:512], op=OP.add))

        x1t = ar.alloc(1024, F32)
        for t in range(NT):
            gg = ctx.hgroup()

            @blk.sync
            def _(s, t=t, gg=gg):
                ctx.dve.wait(s)
                gg.issue(s.dma_start(out=x1t[:, 0:1024],
                                     in_=W["l1a_x1"][t * 128:(t + 1) * 128, :]))

            @blk.vector
            def _(v, t=t, gg=gg):
                gg.wait(v)
                v.wait_ge(ctx.dve.sem, ctx.dve.cnt)
                ctx.dve.inc(v.tensor_tensor(out=ctx.X[:, t * D:(t + 1) * D],
                                            in0=ctx.X[:, t * D:(t + 1) * D],
                                            in1=delta[:, t * D:(t + 1) * D], op=OP.add))
                v.wait_ge(ctx.dve.sem, ctx.dve.cnt)
                ctx.dve.inc(v.scalar_tensor_tensor(out=ctx.X[:, t * D:(t + 1) * D],
                                                   in0=x1t[:, 0:1024], scalar=1.0,
                                                   in1=ctx.X[:, t * D:(t + 1) * D],
                                                   op0=OP.mult, op1=OP.add,
                                                   accum_out=sums[:, t:t + 1]))
        post_ln(ctx, blk)
        ar.pop()

        return


def emit_L2f_qkv(ctx, blk, W, dram, l):
    T, NT = ctx.T, ctx.NT
    NH = 8
    ar = ctx.arena
    ar.push()
    rsn = ctx.cols("l2f_rsn", 8)
    qh = ar.alloc(4 * T, BF16)
    kh = ar.alloc(4 * T, BF16)
    vtm = ar.alloc(NT * 512, BF16)
    hsh = ar.alloc(NCH * T, BF16)
    wpart = ar.alloc(8 * 512, BF16)
    tbl = [ar.alloc(4 * 64, F32) for _ in range(2)]
    qksb = ar.alloc(4 * 64, F32)
    t1 = ar.alloc(64, F32)
    t2 = ar.alloc(64, F32)
    ts_ = ar.alloc(64, F32)
    qe = ar.alloc(512, F32)
    qsq = ar.alloc(512, F32)
    bqc_ = ar.alloc(512, BF16)
    qrow = ar.alloc(256, BF16)
    hpc = ar.alloc(8, F32)

    hpg = ctx.hgroup()

    @blk.sync
    def _(s):
        ctx.dve.wait(s)
        hpg.issue(s.dma_start(out=hpc[:, 0:8], in_=W["L2f_xprevc"][:, :]))

    @blk.vector
    def _(v):
        hpg.wait(v)
        ctx.dve.inc(v.engine_nop())
    hd = ctx.dve.cnt

    tr_marks = {}
    for t in range(NT):
        @blk.vector
        def _(v, t=t):
            v.wait_ge(ctx.dve.sem, ctx.dve.cnt)
            if t >= 2:
                ctx.pe.wait(v, tr_marks[t - 2])
            ctx.dve.inc(v.tensor_copy(out=ctx.CB[:, (t % 2) * D:(t % 2 + 1) * D],
                                      in_=ctx.X[:, t * D:(t + 1) * D]))
        cc_ = ctx.dve.cnt

        @blk.tensor
        def _(t_, t=t, cc_=cc_):
            ctx.dve.wait(t_, cc_)
            ctx.dve.wait(t_)
            for c in range(NCH):
                mm = t_.transpose(ctx.pt[:, c * 128:(c + 1) * 128],
                                  ctx.CB[:, (t % 2) * D + c * 128:(t % 2) * D + (c + 1) * 128],
                                  ctx.ident[:])
                ctx.pe.inc(mm)
        tr_marks[t] = ctx.pe.cnt

        @blk.vector
        def _(v, t=t):
            ctx.pe.wait(v, tr_marks[t])
            if t == 0:
                ctx.dve.wait(v, hd)
            src3 = ctx.pt[:, 0:1024].rearrange("p (c j) -> p c j", c=NCH)
            dst3 = hsh.rearrange("p (c t) -> p c t", t=T)
            ctx.dve.inc(v.tensor_copy(out=dst3[:, 0:4, t * 128:(t + 1) * 128],
                                      in_=src3[:, 0:4, :]))
            v.wait_ge(ctx.dve.sem, ctx.dve.cnt)
            ncols = 128 if t < NT - 1 else 127
            ctx.dve.inc(v.tensor_copy(out=dst3[:, 4:8, t * 128 + 1: t * 128 + 1 + ncols],
                                      in_=src3[:, 4:8, 0:ncols]))
            if t == 0:
                v.wait_ge(ctx.dve.sem, ctx.dve.cnt)
                ctx.dve.inc(v.tensor_copy(out=dst3[:, 4:8, 0:1],
                                          in_=hpc[:, 4:8].unsqueeze(-1)))
    hsh_done = ctx.dve.cnt


    qg_ = ctx.dgroup()

    @blk.gpsimd
    def _(g, l=l):
        ctx.pe.wait(g)
        ctx.dve.wait(g)
        for i, nm in enumerate(["L2f_qs", "L2f_qsr", "L2f_ks", "L2f_ksr"]):
            qg_.issue(g.dma_start(out=qrow[0:1, i * 64:(i + 1) * 64], in_=W[nm][l:l + 1, :]))
    bb_ = [ctx.bank(), ctx.bank(), ctx.bank(), ctx.bank()]

    @blk.tensor
    def _(t_):
        qg_.wait(t_)
        ctx.dve.wait(t_)
        ctx.act.wait(t_)
        for i in range(4):
            ctx.pe.inc(t_.matmul(bb_[i][:, 0:64], ctx.ones_row[:, 0:128],
                                 qrow[0:1, i * 64:(i + 1) * 64], start=True, stop=True))
    qsb_mm = ctx.pe.cnt

    @blk.vector
    def _(v):
        ctx.pe.wait(v, qsb_mm)
        v.wait_ge(ctx.dve.sem, ctx.dve.cnt)
        for i in range(4):
            sc = 8.0 if i < 2 else 1.0
            ctx.dve.inc(v.tensor_scalar(out=qksb[:, i * 64:(i + 1) * 64], in0=bb_[i][:, 0:64],
                                        scalar1=sc, scalar2=None, op0=OP.mult))
    qsb_done = ctx.dve.cnt

    ctx.wpart_grps = []
    for which in range(2):
        wg_ = ctx.dgroup()

        @blk.gpsimd
        def _(g, l=l, which=which):
            ctx.pe.wait(g)
            if len(ctx.wpart_grps) >= 1:
                ctx.wpart_grps[-1].wait(g)
            wg_.issue(g.dma_start(
                out=wpart[0:128, 0:8 * 512].rearrange("p (f c) -> p f c", f=NCH),
                in_=W["L2f_wqkv"][l][:, which * 512:(which + 1) * 512]
                .rearrange("(f p) c -> p f c", p=128)))

        ctx.wpart_grps.append(wg_)
        for t in range(NT):
            tg_ = ctx.hgroup()

            @blk.scalar
            def _(s, l=l, t=t):
                ctx.dve.wait(s)
                tg_.issue(s.dma_start(out=tbl[t % 2][0:128, 0:256].rearrange("p (i c) -> p i c", i=4),
                                      in_=W["L2f_tbl"][t * 128:(t + 1) * 128, :, :]))
            b = ctx.bank()

            @blk.tensor
            def _(t_, t=t, b=b, which=which):
                wg_.wait(t_)
                if t == 0 and which == 0:
                    ctx.dve.wait(t_, hsh_done)
                ctx.dve.wait(t_)
                ctx.act.wait(t_)
                for kc in range(NCH):
                    mm = t_.matmul(b[:, 0:512],
                                   hsh[:, kc * T + t * 128: kc * T + (t + 1) * 128],
                                   wpart[:, kc * 512:(kc + 1) * 512],
                                   start=(kc == 0), stop=(kc == NCH - 1))
                ctx.pe.inc(mm)
            mmc = ctx.pe.cnt

            @blk.scalar
            def _(s, t=t, b=b, mmc=mmc):
                ctx.pe.wait(s, mmc)
                ctx.dve.wait(s)
                s.wait_ge(ctx.act.sem, ctx.act.cnt)
                ctx.act.inc(s.activation(out=qsq[:, 0:512], in_=b[:, 0:512], func=AF.Square))
            sqd = ctx.act.cnt

            @blk.vector
            def _(v, t=t, b=b, sqd=sqd, which=which):
                ctx.act.wait(v, sqd)
                tg_.wait(v)
                if t == 0:
                    ctx.dve.wait(v, qsb_done)
                v.wait_ge(ctx.dve.sem, ctx.dve.cnt)
                ctx.dve.inc(v.tensor_reduce(out=rsn, in_=qsq[:, 0:512].rearrange("p (h d) -> p h d", h=8),
                                            axis=AX.X, op=OP.add))
                v.wait_ge(ctx.dve.sem, ctx.dve.cnt)
                ctx.dve.inc(v.tensor_scalar(out=rsn, in0=rsn, scalar1=ctx.constcol(1e-24),
                                            scalar2=None, op0=OP.max))
            sqd2 = ctx.dve.cnt

            @blk.scalar
            def _(s, sqd2=sqd2):
                ctx.dve.wait(s, sqd2)
                s.wait_ge(ctx.act.sem, ctx.act.cnt)
                ctx.act.inc(s.activation(out=rsn, in_=rsn, func=AF.Ln))
                s.wait_ge(ctx.act.sem, ctx.act.cnt)
                ctx.act.inc(s.activation(out=rsn, in_=rsn, func=AF.Exp, scale=-0.5))
            rsd = ctx.act.cnt

            @blk.vector
            def _(v, t=t, b=b, rsd=rsd, which=which):
                ctx.act.wait(v, rsd)
                v.wait_ge(ctx.dve.sem, ctx.dve.cnt)
                q3 = b[:, 0:512].rearrange("p (h d) -> p h d", h=8)
                rb = rsn.unsqueeze(-1).broadcast_to([128, 8, 64])
                ctx.dve.inc(v.tensor_tensor(out=q3, in0=q3, in1=rb, op=OP.mult))
                v.wait_ge(ctx.dve.sem, ctx.dve.cnt)
                o = 0 if which == 0 else 2
                sci = 2 if which == 0 else 3
                ctx.dve.inc(v.tensor_tensor(out=ts_[:, 0:64], in0=tbl[t % 2][:, 0:64],
                                            in1=tbl[t % 2][:, sci * 64:(sci + 1) * 64], op=OP.mult))
                v.wait_ge(ctx.dve.sem, ctx.dve.cnt)
                ctx.dve.inc(v.tensor_tensor(out=t1[:, 0:64], in0=ts_[:, 0:64],
                                            in1=qksb[:, o * 64:(o + 1) * 64], op=OP.mult))
                v.wait_ge(ctx.dve.sem, ctx.dve.cnt)
                ctx.dve.inc(v.tensor_tensor(out=ts_[:, 0:64], in0=tbl[t % 2][:, 64:128],
                                            in1=tbl[t % 2][:, sci * 64:(sci + 1) * 64], op=OP.mult))
                v.wait_ge(ctx.dve.sem, ctx.dve.cnt)
                ctx.dve.inc(v.tensor_tensor(out=t2[:, 0:64], in0=ts_[:, 0:64],
                                            in1=qksb[:, (o + 1) * 64:(o + 2) * 64], op=OP.mult))
                v.wait_ge(ctx.dve.sem, ctx.dve.cnt)
                qn3 = q3
                qe3 = qe[:, 0:512].rearrange("p (h d) -> p h d", h=8)
                t1b = t1[:, 0:64].unsqueeze(1).broadcast_to([128, 8, 64])
                ctx.dve.inc(v.tensor_tensor(out=qe3, in0=qn3, in1=t1b, op=OP.mult))
                v.wait_ge(ctx.dve.sem, ctx.dve.cnt)
                t2a = t2[:, 0:32].unsqueeze(1).broadcast_to([128, 8, 32])
                ctx.dve.inc(v.tensor_tensor(out=qsq[:, 0:512].rearrange("p (h d) -> p h d", h=8)[:, :, 0:32],
                                            in0=qn3[:, :, 32:64], in1=t2a, op=OP.mult))
                v.wait_ge(ctx.dve.sem, ctx.dve.cnt)
                ctx.dve.inc(v.tensor_tensor(out=qe3[:, :, 0:32], in0=qe3[:, :, 0:32],
                                            in1=qsq[:, 0:512].rearrange("p (h d) -> p h d", h=8)[:, :, 0:32],
                                            op=OP.subtract))
                v.wait_ge(ctx.dve.sem, ctx.dve.cnt)
                t2b = t2[:, 32:64].unsqueeze(1).broadcast_to([128, 8, 32])
                ctx.dve.inc(v.tensor_tensor(out=qsq[:, 0:512].rearrange("p (h d) -> p h d", h=8)[:, :, 32:64],
                                            in0=qn3[:, :, 0:32], in1=t2b, op=OP.mult))
                v.wait_ge(ctx.dve.sem, ctx.dve.cnt)
                ctx.dve.inc(v.tensor_tensor(out=qe3[:, :, 32:64], in0=qe3[:, :, 32:64],
                                            in1=qsq[:, 0:512].rearrange("p (h d) -> p h d", h=8)[:, :, 32:64],
                                            op=OP.add))
                v.wait_ge(ctx.dve.sem, ctx.dve.cnt)
                ctx.dve.inc(v.tensor_copy(out=bqc_[:, 0:512], in_=qe[:, 0:512]))
            casted = ctx.dve.cnt

            @blk.tensor
            def _(t_, t=t, casted=casted):
                ctx.dve.wait(t_, casted)
                ctx.dve.wait(t_)
                for c in range(4):
                    mm = t_.transpose(ctx.pt[:, c * 128:(c + 1) * 128],
                                      bqc_[:, c * 128:(c + 1) * 128], ctx.ident[:])
                    ctx.pe.inc(mm)
            trd = ctx.pe.cnt

            @blk.vector
            def _(v, t=t, trd=trd, which=which):
                ctx.pe.wait(v, trd)
                dst = (qh if which == 0 else kh).rearrange("p (c t) -> p c t", t=T)
                src3 = ctx.pt[:, 0:512].rearrange("p (c j) -> p c j", c=4)
                ctx.dve.inc(v.tensor_copy(out=dst[:, :, t * 128:(t + 1) * 128], in_=src3))
    qk_done = ctx.dve.cnt

    wg_ = ctx.dgroup()

    @blk.gpsimd
    def _(g, l=l):
        ctx.pe.wait(g)
        if len(ctx.wpart_grps) >= 1:
            ctx.wpart_grps[-1].wait(g)
        wg_.issue(g.dma_start(
            out=wpart[0:128, 0:8 * 512].rearrange("p (f c) -> p f c", f=NCH),
            in_=W["L2f_wqkv"][l][:, 1024:1536].rearrange("(f p) c -> p f c", p=128)))
    ctx.wpart_grps.append(wg_)
    for t in range(NT):
        b = ctx.bank()

        @blk.tensor
        def _(t_, t=t, b=b):
            wg_.wait(t_)
            ctx.dve.wait(t_)
            ctx.act.wait(t_)
            for kc in range(NCH):
                mm = t_.matmul(b[:, 0:512],
                               hsh[:, kc * T + t * 128: kc * T + (t + 1) * 128],
                               wpart[:, kc * 512:(kc + 1) * 512],
                               start=(kc == 0), stop=(kc == NCH - 1))
            ctx.pe.inc(mm)
        mmc = ctx.pe.cnt

        @blk.scalar
        def _(s, t=t, b=b, mmc=mmc):
            ctx.pe.wait(s, mmc)
            s.wait_ge(ctx.act.sem, ctx.act.cnt)
            ctx.act.inc(s.activation(out=vtm[:, t * 512:(t + 1) * 512], in_=b[:, 0:512],
                                     func=AF.Copy, bias=0.0))
    v_done = ctx.act.cnt


    outg = ctx.hgroup()

    @blk.sync
    def _(s):
        ctx.dve.wait(s, qk_done)
        ctx.act.wait(s, v_done)
        outg.issue(s.dma_start(out=dram["l2f_qh"][:, :].rearrange("(c p) t -> p c t", p=128),
                               in_=qh.rearrange("p (c t) -> p c t", t=T)))
        outg.issue(s.dma_start(out=dram["l2f_khx"][:, :].rearrange("(c p) t -> p c t", p=128),
                               in_=kh.rearrange("p (c t) -> p c t", t=T)))
        outg.issue(s.dma_start(out=dram["l2f_vt"][:, :].rearrange("(w p) c -> p w c", p=128),
                               in_=vtm[:, 0:NT * 512].rearrange("p (w c) -> p w c", c=512)))

    @blk.sync
    def _(s):
        outg.wait(s)
    ar.pop()


def emit_L2f_attn(ctx, blk, W, dram, l):
    T, NT = ctx.T, ctx.NT
    NH = 8
    ar = ctx.arena
    ar.push()
    nmc = ctx.cols("l2f_nm", 12)
    mleft = ar.alloc(128, F32)
    mdiag = ar.alloc(128, F32)
    qh = ar.alloc(4 * T, BF16)
    kh = ar.alloc(4 * T, BF16)
    vtm = ar.alloc(NT * 512, BF16)
    ctm = ar.alloc(NT * 512, BF16)
    ar.push()
    khp = ar.alloc(4 * 512, BF16)
    vhp = ar.alloc(4 * 512, BF16)
    cg = ctx.hgroup()

    @blk.sync
    def _(s):
        ctx.dve.wait(s)
        cg.issue(s.dma_start(out=mleft[0:128, 0:128], in_=W["L2f_mleft"][:, :]))
        cg.issue(s.dma_start(out=mdiag[0:128, 0:128], in_=W["L2f_mdiag"][:, :]))
        cg.issue(s.dma_start(out=qh.rearrange("p (c t) -> p c t", t=T),
                             in_=W["l2f_qh"][:, :].rearrange("(c p) t -> p c t", p=128)))
        cg.issue(s.dma_start(out=kh.rearrange("p (c t) -> p c t", t=T),
                             in_=W["l2f_khx"][:, :].rearrange("(c p) t -> p c t", p=128)))
        cg.issue(s.dma_start(out=vtm[:, 0:NT * 512].rearrange("p (w c) -> p w c", c=512),
                             in_=W["l2f_vt"][:, :].rearrange("(w p) c -> p w c", p=128)))
        cg.issue(s.dma_start(out=khp[0:128, 0:4 * 512].rearrange("p (c t) -> p c t", c=4),
                             in_=W["L2f_khp"][:, :].rearrange("(c p) t -> p c t", p=128)))
        cg.issue(s.dma_start(out=vhp[0:128, 0:4 * 512].rearrange("p (w c) -> p w c", w=4),
                             in_=W["L2f_vhp"][:, :].rearrange("(w p) c -> p w c", p=128)))

    @blk.vector
    def _(v):
        cg.wait(v)
        ctx.dve.inc(v.engine_nop())
    kvh_done = ctx.dve.cnt
    v_done = ctx.act.cnt

    pA = ar.alloc(512, BF16)
    pB = ar.alloc(128, BF16)
    ptb2 = ar.alloc(640, BF16)
    bmq = ar.alloc(512, F32)
    rot6 = [0]
    def lbank():
        b = ctx.ps[rot6[0] % 6]
        rot6[0] += 1
        return b
    for t in range(NT):
        bmg = ctx.hgroup()
        if t < 4:
            @blk.scalar
            def _(s, l=l, t=t):
                ctx.dve.wait(s)
                bmg.issue(s.dma_start(out=bmq[0:128, 0:512],
                                      in_=W["L2f_bmask"][t]))
        cb = ctx.ps[6]

        for h in range(NH):
            sA = lbank()
            sB = lbank()

            @blk.tensor
            def _(t_, t=t, h=h, sA=sA, sB=sB):
                if t == 0 and h == 0:
                    ctx.dve.wait(t_, kvh_done)
                    ctx.act.wait(t_, v_done)
                ctx.dve.wait(t_)
                ctx.act.wait(t_)
                lq = qh[64 * (h % 2):64 * (h % 2) + 64,
                        (h // 2) * T + t * 128:(h // 2) * T + (t + 1) * 128]
                for ki in range(5):
                    kb = t - 4 + ki
                    if kb < 0:
                        ksrc = khp[64 * (h % 2):64 * (h % 2) + 64,
                                   (h // 2) * 512 + (kb + 4) * 128:(h // 2) * 512 + (kb + 5) * 128]
                    else:
                        ksrc = kh[64 * (h % 2):64 * (h % 2) + 64,
                                  (h // 2) * T + kb * 128:(h // 2) * T + (kb + 1) * 128]

                    dstp = sA[:, ki * 128:(ki + 1) * 128] if ki < 4 else sB[:, 0:128]
                    mm = t_.matmul(dstp, lq, ksrc, start=True, stop=True)
                ctx.pe.inc(mm)
            smm = ctx.pe.cnt

            @blk.vector
            def _(v, t=t, h=h, sA=sA, sB=sB, smm=smm):
                ctx.pe.wait(v, smm)
                ctx.act.wait(v)
                if t < 4:
                    bmg.wait(v)
                v.wait_ge(ctx.dve.sem, ctx.dve.cnt)
                ctx.dve.inc(v.tensor_tensor(out=sA[:, 0:128], in0=sA[:, 0:128],
                                            in1=mleft[:, 0:128], op=OP.add))
                v.wait_ge(ctx.dve.sem, ctx.dve.cnt)
                ctx.dve.inc(v.tensor_tensor(out=sB[:, 0:128], in0=sB[:, 0:128],
                                            in1=mdiag[:, 0:128], op=OP.add))
                v.wait_ge(ctx.dve.sem, ctx.dve.cnt)
                if t < 4:
                    ctx.dve.inc(v.tensor_tensor(out=sA[:, 0:512], in0=sA[:, 0:512],
                                                in1=bmq[:, 0:512], op=OP.add))
                    v.wait_ge(ctx.dve.sem, ctx.dve.cnt)
                ctx.dve.inc(v.reduce_max(out=nmc[:, 0:1], in_=sA[:, 0:512], axis=AX.X, negate=True))
                v.wait_ge(ctx.dve.sem, ctx.dve.cnt)
                ctx.dve.inc(v.reduce_max(out=nmc[:, 1:2], in_=sB[:, 0:128], axis=AX.X, negate=True))
                v.wait_ge(ctx.dve.sem, ctx.dve.cnt)
                ctx.dve.inc(v.tensor_scalar(out=nmc[:, 0:1], in0=nmc[:, 0:1],
                                            scalar1=nmc[:, 1:2], scalar2=None, op0=OP.min))
            nmd = ctx.dve.cnt

            @blk.scalar
            def _(s, sA=sA, sB=sB, nmd=nmd):
                ctx.dve.wait(s, nmd)
                s.wait_ge(ctx.act.sem, ctx.act.cnt)
                ctx.act.inc(s.activation(out=pA[:, 0:512], in_=sA[:, 0:512], func=AF.Exp,
                                         bias=nmc[:, 0:1], accum_out=nmc[:, 2:3]))
                s.wait_ge(ctx.act.sem, ctx.act.cnt)
                ctx.act.inc(s.activation(out=pB[:, 0:128], in_=sB[:, 0:128], func=AF.Exp,
                                         bias=nmc[:, 0:1], accum_out=nmc[:, 3:4]))
            expd = ctx.act.cnt

            @blk.vector
            def _(v, expd=expd):
                ctx.act.wait(v, expd)
                v.wait_ge(ctx.dve.sem, ctx.dve.cnt)
                ctx.dve.inc(v.tensor_scalar(out=nmc[:, 2:3], in0=nmc[:, 2:3],
                                            scalar1=nmc[:, 3:4], scalar2=None, op0=OP.add))
                v.wait_ge(ctx.dve.sem, ctx.dve.cnt)
                ctx.dve.inc(v.reciprocal(out=nmc[:, 4:5], in_=nmc[:, 2:3]))
                v.wait_ge(ctx.dve.sem, ctx.dve.cnt)
                ctx.dve.inc(v.tensor_scalar(out=pA[:, 0:512], in0=pA[:, 0:512],
                                            scalar1=nmc[:, 4:5], scalar2=None, op0=OP.mult))
                v.wait_ge(ctx.dve.sem, ctx.dve.cnt)
                ctx.dve.inc(v.tensor_scalar(out=pB[:, 0:128], in0=pB[:, 0:128],
                                            scalar1=nmc[:, 4:5], scalar2=None, op0=OP.mult))
            pdone = ctx.dve.cnt

            @blk.tensor
            def _(t_, pdone=pdone):
                ctx.dve.wait(t_, pdone)
                ctx.dve.wait(t_)
                for ki in range(4):
                    mm = t_.transpose(ctx.pt[:, ki * 128:(ki + 1) * 128],
                                      pA[:, ki * 128:(ki + 1) * 128], ctx.ident[:])
                    ctx.pe.inc(mm)
            trA = ctx.pe.cnt

            @blk.vector
            def _(v, trA=trA):
                ctx.pe.wait(v, trA)
                ctx.dve.inc(v.tensor_copy(out=ptb2[:, 0:512], in_=ctx.pt[:, 0:512]))
            evA = ctx.dve.cnt

            @blk.tensor
            def _(t_, evA=evA):
                ctx.dve.wait(t_, evA)
                ctx.pe.inc(t_.transpose(ctx.pt[:, 0:128], pB[:, 0:128], ctx.ident[:]))
            trB = ctx.pe.cnt

            @blk.vector
            def _(v, trB=trB):
                ctx.pe.wait(v, trB)
                ctx.dve.inc(v.tensor_copy(out=ptb2[:, 512:640], in_=ctx.pt[:, 0:128]))
            evB = ctx.dve.cnt

            @blk.tensor
            def _(t_, t=t, h=h, cb=cb, evB=evB):
                ctx.dve.wait(t_, evB)
                if h == 0:
                    ctx.act.wait(t_)
                for ki in range(5):
                    kb = t - 4 + ki
                    if kb < 0:
                        vsrc = vhp[:, (kb + 4) * 512 + 64 * h:(kb + 4) * 512 + 64 * h + 64]
                    else:
                        vsrc = vtm[:, kb * 512 + 64 * h: kb * 512 + 64 * h + 64]
                    mm = t_.matmul(cb[:, 64 * h:64 * h + 64],
                                   ptb2[:, ki * 128:(ki + 1) * 128], vsrc,
                                   start=(h == 0 and ki == 0), stop=(h == NH - 1 and ki == 4),
                                   skip_group_check=True)
                ctx.pe.inc(mm)
        pvd = ctx.pe.cnt

        @blk.scalar
        def _(s, t=t, cb=cb, pvd=pvd):
            ctx.pe.wait(s, pvd)
            s.wait_ge(ctx.act.sem, ctx.act.cnt)
            ctx.act.inc(s.activation(out=ctm[:, t * 512:(t + 1) * 512], in_=cb[:, 0:512],
                                     func=AF.Copy, bias=0.0))
    attn_done = ctx.act.cnt

    ar.pop()
    cfm = ar.alloc(4 * T, BF16)
    for t in range(NT):
        @blk.tensor
        def _(t_, t=t):
            ctx.act.wait(t_, attn_done)
            ctx.dve.wait(t_)
            for c in range(4):
                mm = t_.transpose(ctx.pt[:, c * 128:(c + 1) * 128],
                                  ctm[:, t * 512 + c * 128: t * 512 + (c + 1) * 128],
                                  ctx.ident[:])
                ctx.pe.inc(mm)
        trd = ctx.pe.cnt

        @blk.vector
        def _(v, t=t, trd=trd):
            ctx.pe.wait(v, trd)
            dst = cfm.rearrange("p (c t) -> p c t", t=T)
            ctx.dve.inc(v.tensor_copy(out=dst[:, :, t * 128:(t + 1) * 128],
                                      in_=ctx.pt[:, 0:512].rearrange("p (c j) -> p c j", c=4)))
    cfm_done = ctx.dve.cnt

    wob = ar.alloc(4 * 1024, BF16)
    wog = ctx.dgroup()
    load_weight(ctx, blk, W["L2f_wo"][l], wob, wait=lambda g: ctx.pe.wait(g), group=wog)
    for t in range(NT):
        ob = [ctx.bank(), ctx.bank()]

        @blk.tensor
        def _(t_, t=t, ob=ob):
            wog.wait(t_)
            if t == 0:
                ctx.dve.wait(t_, cfm_done)
            ctx.dve.wait(t_)
            for nb in range(2):
                for kc in range(4):
                    mm = t_.matmul(ob[nb][:, 0:512],
                                   cfm[:, kc * T + t * 128: kc * T + (t + 1) * 128],
                                   wob[:, kc * 1024 + nb * 512: kc * 1024 + (nb + 1) * 512],
                                   start=(kc == 0), stop=(kc == 3))
                ctx.pe.inc(mm)
        mmc = ctx.pe.cnt

        @blk.vector
        def _(v, t=t, ob=ob, mmc=mmc):
            ctx.pe.wait(v, mmc)
            v.wait_ge(ctx.dve.sem, ctx.dve.cnt)
            for nb in range(2):
                ctx.dve.inc(v.tensor_tensor(out=ctx.X[:, t * D + nb * 512: t * D + (nb + 1) * 512],
                                            in0=ctx.X[:, t * D + nb * 512: t * D + (nb + 1) * 512],
                                            in1=ob[nb][:, 0:512], op=OP.add))
                v.wait_ge(ctx.dve.sem, ctx.dve.cnt)
    attnres_done = ctx.dve.cnt

    ar.pop()


def emit_L2f_ffn(ctx, blk, W, dram, l, final):
    T, NT = ctx.T, ctx.NT
    ar = ctx.arena
    ar.push()
    ar.push()
    hsh2 = ar.alloc(NCH * T, BF16)
    hpc = ar.alloc(8, F32)

    hpg = ctx.hgroup()

    @blk.sync
    def _(s):
        ctx.dve.wait(s)
        hpg.issue(s.dma_start(out=hpc[:, 0:8], in_=W["L2f_x2prevc"][:, :]))

    @blk.vector
    def _(v):
        hpg.wait(v)
        ctx.dve.inc(v.engine_nop())
    hd = ctx.dve.cnt

    tr_marks = {}
    for t in range(NT):
        @blk.vector
        def _(v, t=t):
            v.wait_ge(ctx.dve.sem, ctx.dve.cnt)
            if t >= 2:
                ctx.pe.wait(v, tr_marks[t - 2])
            ctx.dve.inc(v.tensor_copy(out=ctx.CB[:, (t % 2) * D:(t % 2 + 1) * D],
                                      in_=ctx.X[:, t * D:(t + 1) * D]))
        cc_ = ctx.dve.cnt

        @blk.tensor
        def _(t_, t=t, cc_=cc_):
            ctx.dve.wait(t_, cc_)
            ctx.dve.wait(t_)
            for c in range(NCH):
                mm = t_.transpose(ctx.pt[:, c * 128:(c + 1) * 128],
                                  ctx.CB[:, (t % 2) * D + c * 128:(t % 2) * D + (c + 1) * 128],
                                  ctx.ident[:])
                ctx.pe.inc(mm)
        tr_marks[t] = ctx.pe.cnt

        @blk.vector
        def _(v, t=t):
            ctx.pe.wait(v, tr_marks[t])
            if t == 0:
                ctx.dve.wait(v, hd)
            src3 = ctx.pt[:, 0:1024].rearrange("p (c j) -> p c j", c=NCH)
            dst3 = hsh2.rearrange("p (c t) -> p c t", t=T)
            ctx.dve.inc(v.tensor_copy(out=dst3[:, 0:4, t * 128:(t + 1) * 128],
                                      in_=src3[:, 0:4, :]))
            v.wait_ge(ctx.dve.sem, ctx.dve.cnt)
            ncols = 128 if t < NT - 1 else 127
            ctx.dve.inc(v.tensor_copy(out=dst3[:, 4:8, t * 128 + 1: t * 128 + 1 + ncols],
                                      in_=src3[:, 4:8, 0:ncols]))
            if t == 0:
                v.wait_ge(ctx.dve.sem, ctx.dve.cnt)
                ctx.dve.inc(v.tensor_copy(out=dst3[:, 4:8, 0:1],
                                          in_=hpc[:, 4:8].unsqueeze(-1)))
    hsh2_done = ctx.dve.cnt

    gcol = ctx.cols("l2f_lng", 8)
    bcol = ctx.cols("l2f_lnb", 8)
    lng = ctx.hgroup()

    @blk.scalar
    def _(s, l=l):
        lng.issue(s.dma_start(out=gcol, in_=W["L2f_lng_c"][l]))
        lng.issue(s.dma_start(out=bcol, in_=W["L2f_lnb_c"][l]))

    rowreg = ar.alloc(T, F32)
    srow = rowreg
    mu2r = ar.alloc(T, F32)
    s2row = ar.alloc(T, F32)
    mbc = ar.alloc(512, F32)
    rbc = ar.alloc(512, F32)
    sqb = ar.alloc(T, BF16)
    fpl = ar.alloc(512, F32)
    onecol = ar.alloc(1, BF16)
    TGF = min(512, T)

    @blk.vector
    def _(v):
        v.wait_ge(ctx.dve.sem, ctx.dve.cnt)
        ctx.dve.inc(v.memset(onecol[:, 0:1], 1.0))
    oc_done = ctx.dve.cnt

    for tg in range(T // TGF):
        b1 = ctx.ps[tg % 3]
        b2 = ctx.ps[3 + tg % 3]
        sq_acts = {}
        for c in range(NCH):
            @blk.scalar
            def _(s, c=c, tg=tg):
                if c == 0 and tg == 0:
                    ctx.dve.wait(s)
                s.wait_ge(ctx.act.sem, ctx.act.cnt)
                ctx.pe.wait(s)
                ctx.act.inc(s.activation(out=sqb[:, 0:TGF],
                                         in_=hsh2[:, c * T + tg * TGF: c * T + (tg + 1) * TGF],
                                         func=AF.Square))
            sq_acts[c] = ctx.act.cnt

            @blk.tensor
            def _(t_, c=c, tg=tg, b1=b1, b2=b2):
                ctx.act.wait(t_, sq_acts[c])
                if c == 0:
                    ctx.dve.wait(t_)
                mm1 = t_.matmul(b1[0:1, 0:TGF], onecol[:, 0:1],
                                hsh2[:, c * T + tg * TGF: c * T + (tg + 1) * TGF],
                                start=(c == 0), stop=(c == NCH - 1), skip_group_check=True)
                mm2 = t_.matmul(b2[0:1, 0:TGF], onecol[:, 0:1], sqb[:, 0:TGF],
                                start=(c == 0), stop=(c == NCH - 1), skip_group_check=True)
                ctx.pe.inc(mm2)
        smm = ctx.pe.cnt

        @blk.vector
        def _(v, tg=tg, b1=b1, b2=b2, smm=smm):
            ctx.pe.wait(v, smm)
            v.wait_ge(ctx.dve.sem, ctx.dve.cnt)
            ctx.dve.inc(v.tensor_copy(out=srow[0:1, tg * TGF:(tg + 1) * TGF], in_=b1[0:1, 0:TGF]))
            v.wait_ge(ctx.dve.sem, ctx.dve.cnt)
            ctx.dve.inc(v.tensor_copy(out=s2row[0:1, tg * TGF:(tg + 1) * TGF], in_=b2[0:1, 0:TGF]))

    @blk.vector
    def _(v):
        v.wait_ge(ctx.dve.sem, ctx.dve.cnt)
        ctx.dve.inc(v.tensor_scalar(out=srow[0:1, 0:T], in0=srow[0:1, 0:T],
                                    scalar1=1.0 / D, scalar2=None, op0=OP.mult))
        v.wait_ge(ctx.dve.sem, ctx.dve.cnt)
        ctx.dve.inc(v.tensor_tensor(out=mu2r[0:1, 0:T], in0=srow[0:1, 0:T],
                                    in1=srow[0:1, 0:T], op=OP.mult))
        v.wait_ge(ctx.dve.sem, ctx.dve.cnt)
        ctx.dve.inc(v.scalar_tensor_tensor(out=s2row[0:1, 0:T], in0=s2row[0:1, 0:T],
                                           scalar=1.0 / D, in1=mu2r[0:1, 0:T],
                                           op0=OP.mult, op1=OP.subtract))
    st_dve = ctx.dve.cnt

    @blk.scalar
    def _(s, st_dve=st_dve):
        ctx.dve.wait(s, st_dve)
        s.wait_ge(ctx.act.sem, ctx.act.cnt)
        ctx.act.inc(s.activation(out=s2row[0:1, 0:T], in_=s2row[0:1, 0:T], func=AF.Ln,
                                 bias=ctx.constcol(LN_EPS)[0:1, 0:1]))
        s.wait_ge(ctx.act.sem, ctx.act.cnt)
        ctx.act.inc(s.activation(out=s2row[0:1, 0:T], in_=s2row[0:1, 0:T], func=AF.Exp,
                                 scale=-0.5))
    rs_done = ctx.act.cnt

    @blk.vector
    def _(v):
        lng.wait(v)

    for tg in range(T // TGF):
        bbm = ctx.bank()
        bbr = ctx.bank()

        @blk.tensor
        def _(t_, tg=tg, bbm=bbm, bbr=bbr):
            ctx.act.wait(t_, rs_done)
            ctx.dve.wait(t_)
            ctx.pe.inc(t_.matmul(bbm[:, 0:TGF], ctx.ones_rowf[:, 0:128],
                                 srow[0:1, tg * TGF:(tg + 1) * TGF], start=True, stop=True))
            ctx.pe.inc(t_.matmul(bbr[:, 0:TGF], ctx.ones_rowf[:, 0:128],
                                 s2row[0:1, tg * TGF:(tg + 1) * TGF], start=True, stop=True))
        bmm = ctx.pe.cnt

        @blk.vector
        def _(v, tg=tg, bbm=bbm, bbr=bbr, bmm=bmm):
            ctx.pe.wait(v, bmm)
            v.wait_ge(ctx.dve.sem, ctx.dve.cnt)
            ctx.dve.inc(v.tensor_copy(out=mbc[:, 0:TGF], in_=bbm[:, 0:TGF]))
            v.wait_ge(ctx.dve.sem, ctx.dve.cnt)
            ctx.dve.inc(v.tensor_copy(out=rbc[:, 0:TGF], in_=bbr[:, 0:TGF]))
            for c in range(NCH):
                v.wait_ge(ctx.dve.sem, ctx.dve.cnt)
                ctx.dve.inc(v.tensor_tensor(out=fpl[:, 0:TGF],
                                            in0=hsh2[:, c * T + tg * TGF: c * T + (tg + 1) * TGF],
                                            in1=mbc[:, 0:TGF], op=OP.subtract))
                v.wait_ge(ctx.dve.sem, ctx.dve.cnt)
                ctx.dve.inc(v.tensor_tensor(out=fpl[:, 0:TGF], in0=fpl[:, 0:TGF],
                                            in1=rbc[:, 0:TGF], op=OP.mult))
                v.wait_ge(ctx.dve.sem, ctx.dve.cnt)
                ctx.dve.inc(v.tensor_scalar(out=hsh2[:, c * T + tg * TGF: c * T + (tg + 1) * TGF],
                                            in0=fpl[:, 0:TGF],
                                            scalar1=gcol[:, c:c + 1], scalar2=bcol[:, c:c + 1],
                                            op0=OP.mult, op1=OP.add))
    hn_done = ctx.dve.cnt

    PAIRS = [(128, 128)] * 21 + [(42, 42)]
    offs1 = np.cumsum([0] + [a + g for a, g in PAIRS]).tolist()
    offs2 = np.cumsum([0] + [a for a, g in PAIRS]).tolist()
    h2buf = ar.alloc(22 * TGF, BF16)
    w1p = [ar.alloc(8 * 256, BF16) for _ in range(2)]
    w2p = [ar.alloc(512, BF16) for _ in range(2)]
    gsc = ar.alloc(TGF, F32)
    asc = ar.alloc(TGF, F32)
    b1c2 = ctx.cols("l2f_b1", 44)
    b2row = ar.alloc(1024, BF16)
    bg2 = ctx.hgroup()
    bg3 = ctx.dgroup()

    @blk.scalar
    def _(s, l=l):
        bg2.issue(s.dma_start(out=b1c2, in_=W["L2f_b1c"][l]))

    @blk.gpsimd
    def _(g, l=l):
        ctx.pe.wait(g)
        bg3.issue(g.dma_start(out=b2row[0:1, 0:1024], in_=W["L2f_b2r"][l:l + 1, :]))

    ctx.w1p_grps = []
    ctx.w2p_grps = []
    lns3 = ctx.cols("l2f_lns", 8)
    for tg in range(T // TGF):
        for pc, (sa, sg) in enumerate(PAIRS):
            wg1 = ctx.dgroup()

            @blk.gpsimd
            def _(g, l=l, pc=pc, sa=sa, sg=sg):
                ctx.pe.wait(g)
                if len(ctx.w1p_grps) >= 2:
                    ctx.w1p_grps[-2].wait(g)
                wg1.issue(g.dma_start(
                    out=w1p[pc % 2][0:128, 0:8 * (sa + sg)].rearrange("p (f c) -> p f c", f=NCH),
                    in_=W["L2f_w1"][l][:, offs1[pc]:offs1[pc + 1]]
                    .rearrange("(f p) c -> p f c", p=128)))
            ctx.w1p_grps.append(wg1)

            ba = ctx.ps[(2 * pc) % 5]
            bgg = ctx.ps[(2 * pc + 1) % 5]

            @blk.tensor
            def _(t_, tg=tg, pc=pc, sa=sa, sg=sg, ba=ba, bgg=bgg):
                wg1.wait(t_)
                if tg == 0 and pc == 0:
                    ctx.dve.wait(t_, hn_done)
                ctx.dve.wait(t_)
                ctx.act.wait(t_)
                for kc in range(NCH):
                    mma = t_.matmul(ba[0:sa, 0:TGF],
                                    w1p[pc % 2][:, kc * (sa + sg): kc * (sa + sg) + sa],
                                    hsh2[:, kc * T + tg * TGF: kc * T + (tg + 1) * TGF],
                                    start=(kc == 0), stop=(kc == NCH - 1))
                    mmg = t_.matmul(bgg[0:sg, 0:TGF],
                                    w1p[pc % 2][:, kc * (sa + sg) + sa: kc * (sa + sg) + sa + sg],
                                    hsh2[:, kc * T + tg * TGF: kc * T + (tg + 1) * TGF],
                                    start=(kc == 0), stop=(kc == NCH - 1))
                ctx.pe.inc(mmg)
            w1mm = ctx.pe.cnt

            @blk.scalar
            def _(s, pc=pc, sg=sg, bgg=bgg, w1mm=w1mm):
                ctx.pe.wait(s, w1mm)
                if pc == 0:
                    bg2.wait(s)
                ctx.dve.wait(s)
                s.wait_ge(ctx.act.sem, ctx.act.cnt)
                ctx.act.inc(s.activation(out=gsc[0:sg, 0:TGF], in_=bgg[0:sg, 0:TGF],
                                         func=AF.Gelu, bias=b1c2[0:sg, 2 * pc + 1:2 * pc + 2]))
            geld = ctx.act.cnt

            @blk.vector
            def _(v, pc=pc, sa=sa, ba=ba, geld=geld):
                ctx.act.wait(v, geld)
                v.wait_ge(ctx.dve.sem, ctx.dve.cnt)
                ctx.dve.inc(v.tensor_scalar(out=asc[0:sa, 0:TGF], in0=ba[0:sa, 0:TGF],
                                            scalar1=b1c2[0:sa, 2 * pc:2 * pc + 1],
                                            scalar2=None, op0=OP.add))
                v.wait_ge(ctx.dve.sem, ctx.dve.cnt)
                ctx.dve.inc(v.tensor_tensor(out=h2buf[0:sa, pc * TGF:(pc + 1) * TGF],
                                            in0=asc[0:sa, 0:TGF], in1=gsc[0:sa, 0:TGF],
                                            op=OP.mult))
        h2done = ctx.dve.cnt

        nt4f = TGF // 128
        for nb in range(2):
            banks = [ctx.ps[5], ctx.ps[6]] + ([ctx.ps[0], ctx.ps[1]] if nt4f > 2 else [])
            for pc, (sa, sg) in enumerate(PAIRS):
                wg2 = ctx.dgroup()

                @blk.gpsimd
                def _(g, l=l, nb=nb, pc=pc, sa=sa):
                    ctx.pe.wait(g)
                    if len(ctx.w2p_grps) >= 2:
                        ctx.w2p_grps[-2].wait(g)
                    wg2.issue(g.dma_start(
                        out=w2p[pc % 2][0:sa, 0:512],
                        in_=W["L2f_w2"][l][offs2[pc]:offs2[pc + 1], nb * 512:(nb + 1) * 512]))
                ctx.w2p_grps.append(wg2)

                @blk.tensor
                def _(t_, tg=tg, nb=nb, pc=pc, sa=sa, banks=banks, h2done=h2done):
                    wg2.wait(t_)
                    bg3.wait(t_)
                    ctx.dve.wait(t_, h2done)
                    if pc == 0:
                        ctx.dve.wait(t_)
                        ctx.act.wait(t_)
                        for ti in range(nt4f):
                            bias_mm(ctx, t_, banks[ti][:, 0:512],
                                    b2row[0:1, nb * 512:(nb + 1) * 512], start=True)
                    for ti in range(nt4f):
                        mm = t_.matmul(banks[ti][:, 0:512],
                                       h2buf[0:sa, pc * TGF + ti * 128: pc * TGF + (ti + 1) * 128],
                                       w2p[pc % 2][0:sa, 0:512],
                                       start=False, stop=(pc == 21))
                    ctx.pe.inc(mm)
            w2mm = ctx.pe.cnt

            @blk.vector
            def _(v, tg=tg, nb=nb, banks=banks, w2mm=w2mm):
                ctx.pe.wait(v, w2mm)
                for ti in range(nt4f):
                    t = tg * nt4f + ti
                    last = (l == 1)
                    v.wait_ge(ctx.dve.sem, ctx.dve.cnt)
                    ctx.dve.inc(v.tensor_tensor(
                        out=ctx.X[:, t * D + nb * 512: t * D + (nb + 1) * 512],
                        in0=ctx.X[:, t * D + nb * 512: t * D + (nb + 1) * 512],
                        in1=banks[ti][:, 0:512], op=OP.add))
    ar.pop()

    if final:
        sums = ctx.cols("lnsum", NT)
        xint = [ar.alloc(1024, F32) for _ in range(2)]
        ctx.xin_grps = []
        for t in range(NT):
            xg = ctx.hgroup()

            @blk.sync
            def _(s, t=t):
                ctx.dve.wait(s)
                if len(ctx.xin_grps) >= 2:
                    ctx.xin_grps[-2].wait(s)
                xg.issue(s.dma_start(out=xint[t % 2][:, 0:1024],
                                     in_=W["l2f_xin"][t * 128:(t + 1) * 128, :]))
            ctx.xin_grps.append(xg)

            @blk.vector
            def _(v, t=t, xg=xg):
                xg.wait(v)
                v.wait_ge(ctx.dve.sem, ctx.dve.cnt)
                ctx.dve.inc(v.scalar_tensor_tensor(
                    out=ctx.X[:, t * D:(t + 1) * D], in0=xint[t % 2][:, 0:1024],
                    scalar=1.0, in1=ctx.X[:, t * D:(t + 1) * D],
                    op0=OP.mult, op1=OP.add, accum_out=sums[:, t:t + 1]))
        post_ln(ctx, blk)
    ar.pop()





from concourse.bass_utils import run_bass_kernel_spmd
import ml_dtypes

GS = 4
T_FULL = globals().get("T_OVERRIDE", 2048)
BASS_DEBUG = globals().get("BASS_DEBUG", False)
CMASK = 32768.0
BIGM = 1e30
BF16NP = ml_dtypes.bfloat16
_PAIRS = [(128, 128)] * 21 + [(42, 42)]


def _fmcols(v):
    return np.ascontiguousarray(v.reshape(8, 128).T).astype(np.float32)


def _interleave_w1_l2f(w):
    a, g = w[:, :2730], w[:, 2730:]
    out, o = [], 0
    for sa, sg in _PAIRS:
        out.append(a[:, o:o + sa]); out.append(g[:, o:o + sg]); o += sa
    return np.ascontiguousarray(np.concatenate(out, 1))


def _b1cols_l2f(bv):
    a, g = bv[:2730], bv[2730:]
    out = np.zeros((128, 44), np.float32)
    o = 0
    for i, (sa, sg) in enumerate(_PAIRS):
        out[0:sa, 2 * i] = a[o:o + sa]; out[0:sg, 2 * i + 1] = g[o:o + sg]; o += sa
    return out


def _interleave_gu(wg, wu):
    out = []
    for j in range(12):
        out.append(wg[:, j * 128:(j + 1) * 128]); out.append(wu[:, j * 128:(j + 1) * 128])
    return np.concatenate(out, 1)


def prep_shared(I):
    S = {}
    f32 = lambda a: np.ascontiguousarray(a, dtype=np.float32)
    S["L0_W1T"] = f32(I["L0_attn_W1"].T)
    S["L0_b1"] = f32(I["L0_attn_b1"][None])
    S["L0_W23T"] = f32(np.concatenate([I["L0_attn_W2"].T, I["L0_attn_W3"].T], 1))
    S["L0_b23"] = f32(np.concatenate([I["L0_attn_b2"], I["L0_attn_b3"]])[None])
    S["L0_selT"] = f32(I["L0_ffn_expert_sel"].T)
    S["L0_keys"] = f32(I["L0_ffn_keys"])
    S["L0_vals"] = f32(I["L0_ffn_values"])
    S["L0_rs"] = f32(I["L0_ffn_route_scale"].reshape(1, 1))
    S["L1a_ln1g"] = f32(I["L1_attn_ln1_g"][None]); S["L1a_ln1b"] = f32(I["L1_attn_ln1_b"][None])
    S["L1a_ln2g"] = f32(I["L1_attn_ln2_g"][None]); S["L1a_ln2b"] = f32(I["L1_attn_ln2_b"][None])
    S["L1a_mixk_c"] = _fmcols(I["L1_attn_mix_k"]); S["L1a_mixv_c"] = _fmcols(I["L1_attn_mix_v"])
    S["L1a_mixr_c"] = _fmcols(I["L1_attn_mix_r"])
    S["L1a_cmixk_c"] = _fmcols(I["L1_attn_cmix_k"]); S["L1a_cmixr_c"] = _fmcols(I["L1_attn_cmix_r"])
    S["L1a_u_c"] = _fmcols(I["L1_attn_time_first"])
    S["L1a_WkT"] = f32(I["L1_attn_Wk_t"].T); S["L1a_WvT"] = f32(I["L1_attn_Wv_t"].T)
    S["L1a_WrT"] = f32(I["L1_attn_Wr_t"].T)
    S["L1a_WkcT"] = f32(I["L1_attn_Wk_c"].T); S["L1a_WrcT"] = f32(I["L1_attn_Wr_c"].T)
    S["L1a_WvcT"] = f32(I["L1_attn_Wv_c"].T)
    S["L1f_Wg"] = f32(I["L1_ffn_Wg"]); S["L1f_bg"] = f32(I["L1_ffn_bg"][None])
    S["L1f_W1"] = f32(I["L1_ffn_W1"])
    S["L1f_b1c"] = np.stack([I["L1_ffn_b1"][e].reshape(32, 128).T for e in range(8)]).astype(np.float32)
    S["L1f_W2"] = f32(I["L1_ffn_W2"]); S["L1f_b2"] = f32(I["L1_ffn_b2"])
    S["L2a_WqT"] = f32(I["L2_attn_Wq"].T); S["L2a_bqc"] = _fmcols(I["L2_attn_bq"])
    S["L2a_WvT"] = f32(I["L2_attn_Wv"].T); S["L2a_bv"] = f32(I["L2_attn_bv"][None])
    S["L2a_A"] = f32(I["L2_attn_anomaly_matrix"])
    S["L2f_wqkv"] = f32(I["L2_ffn_wqkv"])
    qs, ks = I["L2_ffn_q_scale"], I["L2_ffn_k_scale"]
    S["L2f_qs"] = f32(qs); S["L2f_qsr"] = f32(np.concatenate([qs[:, 32:], qs[:, :32]], 1))
    S["L2f_ks"] = f32(ks); S["L2f_ksr"] = f32(np.concatenate([ks[:, 32:], ks[:, :32]], 1))
    P_, J_ = np.meshgrid(np.arange(128), np.arange(128), indexing='ij')
    S["L2f_mleft"] = np.where(J_ < P_, -BIGM, 0.0).astype(np.float32)
    S["L2f_mdiag"] = np.where(J_ > P_, -BIGM, 0.0).astype(np.float32)
    S["L2f_wo"] = f32(I["L2_ffn_wo"])
    S["L2f_lng_c"] = np.stack([_fmcols(I["L2_ffn_ln_g"][l]) for l in range(2)])
    S["L2f_lnb_c"] = np.stack([_fmcols(I["L2_ffn_ln_b"][l]) for l in range(2)])
    S["L2f_w1"] = np.stack([_interleave_w1_l2f(I["L2_ffn_w1"][l]) for l in range(2)])
    S["L2f_b1c"] = np.stack([_b1cols_l2f(I["L2_ffn_b1"][l]) for l in range(2)])
    S["L2f_w2"] = f32(I["L2_ffn_w2"]); S["L2f_b2r"] = f32(I["L2_ffn_b2"])
    S["L3a_Wqkv"] = f32(I["L3_attn_W_qkv"])
    bq = I["L3_attn_b_qkv"]
    S["L3a_bqc"] = _fmcols(bq[0:1024]); S["L3a_bkc"] = _fmcols(bq[1024:2048])
    S["L3a_bv"] = f32(bq[2048:3072][None])
    S["L3a_WpT"] = f32(I["L3_attn_W_proj"]); S["L3a_bp"] = f32(I["L3_attn_b_proj"][None])
    S["L3a_tri"] = np.where(P_ > J_, -CMASK, 0.0).astype(np.float32)
    S["L3f_gT"] = f32(I["L3_ffn_gate_w"].T)
    wup, wdn = I["L3_ffn_w_up"], I["L3_ffn_w_down"]
    Wgu = np.zeros((9, 1024, 3072), np.float32)
    Wd = np.zeros((9, 1536, 1024), np.float32)
    for e in range(8):
        Wgu[e] = _interleave_gu(wup[e, :1536, :].T, wup[e, 1536:, :].T)
        Wd[e] = wdn[e].T
    Wgu[8] = _interleave_gu(I["L3_ffn_sg_gate"].T, I["L3_ffn_sg_up"].T)
    Wd[8] = I["L3_ffn_sg_down"].T
    S["L3f_Wgu"] = Wgu; S["L3f_Wd"] = Wd
    return S


def prep_core(cid):
    C = {}
    q = cid % GS
    g0 = q * T_FULL
    pm = np.zeros((128, 4), np.float32)
    pm[:, :q] = 1
    C["L1a_pm"] = pm
    C["L1a_hvalid"] = np.full((128, 1), 1.0 if q > 0 else 0.0, np.float32)
    gi = (g0 + np.arange(T_FULL)) // 128
    C["L3a_qstair"] = (gi[None, :] <= np.arange(64)[:, None]).astype(np.float32)
    C["L3a_kstair"] = (-CMASK * (gi[None, :] == np.arange(64)[:, None])).astype(np.float32)
    SQ = GS * T_FULL
    half = np.arange(0, 64, 2, dtype=np.float32)
    inv_freq = 1.0 / 10000.0 ** (half / 64)
    tpos = (g0 + np.arange(T_FULL)).astype(np.float32)
    freqs = tpos[:, None] * inv_freq[None, :]
    freqs2 = np.concatenate([freqs, freqs], -1)
    base = (half + 0.4 * 64) / (1.4 * 64)
    power = (tpos - SQ // 2) / 256.0
    scl = base[None, :] ** power[:, None]
    scl2 = np.concatenate([scl, scl], -1)
    C["L2f_tbl"] = np.stack([np.cos(freqs2), np.sin(freqs2), scl2, 1.0 / scl2],
                            axis=1).astype(np.float32)
    bm = np.zeros((4, 128, 512), np.float32)
    if q == 0:
        for t in range(4):
            bm[t, :, 0:(4 - t) * 128] = -BIGM
    C["L2f_bmask"] = bm
    return C


T_ = T_FULL
STAGES = {
    "l0a": dict(
        emit=lambda ctx, blk, d: emit_L0a(ctx, blk, d),
        ins=[("L0_W1T", [1024, 1024], F32), ("L0_b1", [1, 1024], F32),
             ("L0_W23T", [1024, 32], F32), ("L0_b23", [1, 32], F32)],
        internals=[], outs=[], load_x=True, store_x=True),
    "l0f": dict(
        emit=lambda ctx, blk, d: emit_L0f(ctx, blk, d),
        ins=[("L0_selT", [1024, 16], F32), ("L0_keys", [16, 1024, 256], F32),
             ("L0_vals", [16, 256, 1024], F32), ("L0_rs", [1, 1], F32)],
        internals=[], outs=[], load_x=True, store_x=True),
    "l1aA": dict(
        emit=lambda ctx, blk, d: emit_L1a_stage(ctx, blk, d, d, 'A', GS),
        ins=[("L1a_ln1g", [1, 1024], F32), ("L1a_ln1b", [1, 1024], F32),
             ("L1a_mixk_c", [128, 8], F32), ("L1a_mixv_c", [128, 8], F32),
             ("L1a_mixr_c", [128, 8], F32), ("L1a_cmixk_c", [128, 8], F32),
             ("L1a_cmixr_c", [128, 8], F32), ("L1a_u_c", [128, 8], F32),
             ("L1a_WkT", [1024, 1024], F32), ("L1a_WvT", [1024, 1024], F32),
             ("L1a_WrT", [1024, 1024], F32),
             ("L1a_xprev", [1, 1024], F32), ("L1a_hvalid", [128, 1], F32)],
        internals=[("l1a_hrs", [1, 1024], BF16)],
        outs=[("l1a_k", [1024, T_], BF16), ("l1a_v", [1024, T_], BF16),
              ("l1a_r", [T_, 1024], BF16), ("l1a_s", [128, 8], F32)],
        load_x=True, store_x=False),
    "l1aD1": dict(
        emit=lambda ctx, blk, d: emit_L1a_stage(ctx, blk, d, d, 'D1', GS),
        ins=[("L1a_pm", [128, 4], F32), ("l1a_sa", [GS * 128, 8], F32),
             ("l1a_k", [1024, T_], BF16), ("l1a_v", [1024, T_], BF16)],
        internals=[],
        outs=[("l1a_dp", [1024, T_], F32), ("l1a_wl", [1024, T_], F32),
              ("l1a_rr", [1024, T_], F32), ("l1a_wr", [128, 16], F32)],
        load_x=False, store_x=False),
    "l1aD2": dict(
        emit=lambda ctx, blk, d: emit_L1a_stage(ctx, blk, d, d, 'D2', GS),
        ins=[("L1a_pm", [128, 4], F32), ("L1a_u_c", [128, 8], F32),
             ("l1a_wra", [GS * 128, 16], F32),
             ("l1a_k", [1024, T_], BF16), ("l1a_v", [1024, T_], BF16),
             ("l1a_r", [T_, 1024], BF16),
             ("l1a_dp", [1024, T_], F32), ("l1a_wl", [1024, T_], F32),
             ("l1a_rr", [1024, T_], F32)],
        internals=[], outs=[], load_x=True, store_x=True),
    "l1aE": dict(
        emit=lambda ctx, blk, d: emit_L1a_stage(ctx, blk, d, d, 'E', GS),
        ins=[("L1a_ln2g", [1, 1024], F32), ("L1a_ln2b", [1, 1024], F32),
             ("L1a_mixk_c", [128, 8], F32), ("L1a_mixv_c", [128, 8], F32),
             ("L1a_mixr_c", [128, 8], F32), ("L1a_cmixk_c", [128, 8], F32),
             ("L1a_cmixr_c", [128, 8], F32),
             ("L1a_WkcT", [1024, 4096], F32), ("L1a_WrcT", [1024, 1024], F32),
             ("L1a_WvcT", [4096, 1024], F32),
             ("L1a_x2prev", [1, 1024], F32), ("L1a_hvalid", [128, 1], F32),
             ("l1a_x1", [T_, 1024], F32)],
        internals=[("l1a_hrs", [1, 1024], BF16), ("l1a_rc", [T_, 1024], BF16)],
        outs=[], load_x=True, store_x=True),
    "l1f": dict(
        emit=lambda ctx, blk, d: emit_L1f(ctx, blk, d),
        ins=[("L1f_Wg", [1024, 8], F32), ("L1f_bg", [1, 8], F32),
             ("L1f_W1", [8, 1024, 4096], F32), ("L1f_b1c", [8, 128, 32], F32),
             ("L1f_W2", [8, 4096, 1024], F32), ("L1f_b2", [8, 1024], F32)],
        internals=[], outs=[], load_x=True, store_x=True),
    "l2a_pre": dict(
        emit=lambda ctx, blk, d: emit_L2a_pre(ctx, blk, d, d),
        ins=[("L2a_WqT", [1024, 1024], F32), ("L2a_bqc", [128, 8], F32),
             ("L2a_WvT", [1024, 1024], F32), ("L2a_bv", [1, 1024], F32),
             ("L2a_A", [1024, 1024], F32)],
        internals=[],
        outs=[("l2a_tq", [1024, T_], BF16), ("l2a_v", [T_, 1024], BF16),
              ("l2a_qf", [1024, T_], BF16), ("l2a_nrm", [128, T_ // 128], F32)],
        load_x=True, store_x=False),
    "l2a_flash": dict(
        emit=lambda ctx, blk, d: emit_L2a_flash(ctx, blk, d, d, GS),
        ins=[("l2a_qf", [1024, T_], BF16), ("l2a_nrm", [128, T_ // 128], F32),
             ("l2a_tqa", [GS * 1024, T_], BF16), ("l2a_va", [GS * T_, 1024], BF16)],
        internals=[], outs=[], load_x=True, store_x=True),
    "l3a_pre": dict(
        emit=lambda ctx, blk, d: emit_L3a_pre(ctx, blk, d, d),
        ins=[("L3a_Wqkv", [1024, 3072], F32), ("L3a_bqc", [128, 8], F32),
             ("L3a_bkc", [128, 8], F32), ("L3a_bv", [1, 1024], F32),
             ("L3a_kstair", [64, T_], F32)],
        internals=[],
        outs=[("l3a_k", [2048, T_], BF16), ("l3a_v", [T_, 1040], BF16),
              ("l3a_qf", [1024, T_], BF16)],
        load_x=True, store_x=False),
    "l3a_flash": dict(
        emit=lambda ctx, blk, d: emit_L3a_flash(ctx, blk, d, d, GS),
        ins=[("L3a_qstair", [64, T_], F32), ("L3a_tri", [128, 128], F32),
             ("L3a_WpT", [1024, 1024], F32), ("L3a_bp", [1, 1024], F32),
             ("l3a_qf", [1024, T_], BF16), ("l3a_k", [2048, T_], BF16),
             ("l3a_v", [T_, 1040], BF16),
             ("l3a_ka", [GS * 2048, T_], BF16), ("l3a_va", [GS * T_, 1040], BF16)],
        internals=[], outs=[], load_x=True, store_x=True),
    "l3f": dict(
        emit=lambda ctx, blk, d: emit_L3f(ctx, blk, d),
        ins=[("L3f_gT", [1024, 8], F32), ("L3f_Wgu", [9, 1024, 3072], F32),
             ("L3f_Wd", [9, 1536, 1024], F32)],
        internals=[], outs=[], load_x=True, store_x=True),
}
for _l in range(2):
    STAGES[f"l2f_qkv{_l}"] = dict(
        emit=(lambda l: lambda ctx, blk, d: emit_L2f_qkv(ctx, blk, d, d, l))(_l),
        ins=[("L2f_wqkv", [2, 1024, 1536], F32), ("L2f_qs", [2, 64], F32),
             ("L2f_qsr", [2, 64], F32), ("L2f_ks", [2, 64], F32),
             ("L2f_ksr", [2, 64], F32), ("L2f_tbl", [T_, 4, 64], F32),
             ("L2f_xprevc", [128, 8], F32)],
        internals=[],
        outs=[("l2f_qh", [512, T_], BF16), ("l2f_khx", [512, T_], BF16),
              ("l2f_vt", [T_, 512], BF16)],
        load_x=True, store_x=False)
    STAGES[f"l2f_attn{_l}"] = dict(
        emit=(lambda l: lambda ctx, blk, d: emit_L2f_attn(ctx, blk, d, d, l))(_l),
        ins=[("L2f_mleft", [128, 128], F32), ("L2f_mdiag", [128, 128], F32),
             ("L2f_bmask", [4, 128, 512], F32), ("L2f_wo", [2, 512, 1024], F32),
             ("l2f_qh", [512, T_], BF16), ("l2f_khx", [512, T_], BF16),
             ("l2f_vt", [T_, 512], BF16),
             ("L2f_khp", [512, 512], BF16), ("L2f_vhp", [512, 512], BF16)],
        internals=[], outs=[], load_x=True, store_x=True)
    STAGES[f"l2f_ffn{_l}"] = dict(
        emit=(lambda l: lambda ctx, blk, d: emit_L2f_ffn(ctx, blk, d, d, l, final=(l == 1)))(_l),
        ins=[("L2f_lng_c", [2, 128, 8], F32), ("L2f_lnb_c", [2, 128, 8], F32),
             ("L2f_w1", [2, 1024, 5460], F32), ("L2f_b1c", [2, 128, 44], F32),
             ("L2f_w2", [2, 2730, 1024], F32), ("L2f_b2r", [2, 1024], F32),
             ("L2f_x2prevc", [128, 8], F32)]
            + ([("l2f_xin", [T_, 1024], F32)] if _l == 1 else []),
        internals=[], outs=[], load_x=True, store_x=True)

STAGE_ORDER = ["l0a", "l0f", "l1aA", "l1aD1", "l1aD2", "l1aE", "l1f",
               "l2a_pre", "l2a_flash",
               "l2f_qkv0", "l2f_attn0", "l2f_ffn0",
               "l2f_qkv1", "l2f_attn1", "l2f_ffn1",
               "l3a_pre", "l3a_flash", "l3f"]

_NC_CACHE = {}


def build_stage(name):
    if name in _NC_CACHE:
        return _NC_CACHE[name]
    cfg = STAGES[name]
    nc = bass.Bass("TRN2", target_bir_lowering=False, debug=BASS_DEBUG)
    dram = {}
    if cfg["load_x"]:
        dram["xin"] = nc.dram_tensor("xin", [T_FULL, 1024], F32, kind="ExternalInput").ap()
    for n, shp, dt in cfg["ins"]:
        dram[n] = nc.dram_tensor(n, shp, dt, kind="ExternalInput").ap()
    for n, shp, dt in cfg["internals"]:
        dram[n] = nc.dram_tensor(n, shp, dt, kind="Internal").ap()
    for n, shp, dt in cfg["outs"]:
        dram[n] = nc.dram_tensor(n, shp, dt, kind="ExternalOutput").ap()
    if cfg["store_x"]:
        xout = nc.dram_tensor("xout", [T_FULL, 1024], F32, kind="ExternalOutput").ap()
    with contextlib.ExitStack() as es:
        ctx = Ctx(nc, es, T_FULL)
        blk = es.enter_context(nc.Block())
        ctx.init_consts(blk)
        if cfg["load_x"]:
            xg = ctx.hgroup()

            @blk.sync
            def _(s):
                xg.issue(s.dma_start(
                    out=ctx.X[:].rearrange("p (t c) -> p t c", c=1024),
                    in_=dram["xin"].rearrange("(t p) c -> p t c", p=128)))
        ctx.fence_all(blk)
        cfg["emit"](ctx, blk, dram)

        if cfg["store_x"]:
            og = ctx.hgroup()

            @blk.sync
            def _(s):
                ctx.dve.wait(s)
                ctx.act.wait(s)
                ctx.pe.wait(s)
                og.issue(s.dma_start(
                    out=xout.rearrange("(t p) c -> p t c", p=128),
                    in_=ctx.X[:].rearrange("p (t c) -> p t c", c=1024)))
                og.wait(s)
        else:
            @blk.sync
            def _(s):
                ctx.dve.wait(s)
                ctx.act.wait(s)
                ctx.pe.wait(s)
    nc.finalize()
    _NC_CACHE[name] = nc
    return nc


_PREP_CORE_CACHE = None


def _zrow():
    return np.zeros((1, 1024), np.float32)


def _prevrow(arrs, c):
    return _zrow() if c % GS == 0 else np.ascontiguousarray(arrs[c - 1][T_FULL - 1:T_FULL, :])


def _prevcols(arrs, c):
    if c % GS == 0:
        return np.zeros((128, 8), np.float32)
    return np.ascontiguousarray(arrs[c - 1][T_FULL - 1].reshape(8, 128).T.astype(np.float32))


def _gather(rows_by_core, key, c):
    g0 = (c // GS) * GS
    return np.ascontiguousarray(np.concatenate([rows_by_core[r][key] for r in range(g0, g0 + GS)], 0))


def kernel(**inputs):
    global _PREP_CORE_CACHE
    S = prep_shared(inputs)
    if _PREP_CORE_CACHE is None:
        _PREP_CORE_CACHE = [prep_core(c) for c in range(8)]
    C = _PREP_CORE_CACHE
    x = np.ascontiguousarray(inputs["L0_attn_x"], dtype=np.float32)
    xs = [np.ascontiguousarray(x[c // GS, (c % GS) * T_FULL:(c % GS + 1) * T_FULL])
          for c in range(8)]

    def run(name, ex=None, xcur=None):
        nc = build_stage(name)
        cfg = STAGES[name]
        in_maps = []
        for c in range(8):
            m = {}
            if cfg["load_x"]:
                m["xin"] = xcur[c]
            for k, _shp, _dt in cfg["ins"]:
                if ex is not None and k in ex[c]:
                    m[k] = ex[c][k]
                elif k in C[c]:
                    m[k] = C[c][k]
                else:
                    m[k] = S[k]
            in_maps.append(m)
        res = run_bass_kernel_spmd(nc, in_maps, core_ids=list(range(8)))
        return [res.results[c] for c in range(8)]

    r = run("l0a", xcur=xs)
    xs = [r[c]["xout"] for c in range(8)]
    r = run("l0f", xcur=xs)
    x1s = [r[c]["xout"] for c in range(8)]

    ex = [{"L1a_xprev": _prevrow(x1s, c)} for c in range(8)]
    rA = run("l1aA", ex=ex, xcur=x1s)
    ex = [{"l1a_sa": _gather(rA, "l1a_s", c),
           "l1a_k": rA[c]["l1a_k"], "l1a_v": rA[c]["l1a_v"]} for c in range(8)]
    rD1 = run("l1aD1", ex=ex)
    ex = [{"l1a_wra": _gather(rD1, "l1a_wr", c),
           "l1a_k": rA[c]["l1a_k"], "l1a_v": rA[c]["l1a_v"], "l1a_r": rA[c]["l1a_r"],
           "l1a_dp": rD1[c]["l1a_dp"], "l1a_wl": rD1[c]["l1a_wl"],
           "l1a_rr": rD1[c]["l1a_rr"]} for c in range(8)]
    rD2 = run("l1aD2", ex=ex, xcur=x1s)
    x2s = [rD2[c]["xout"] for c in range(8)]
    ex = [{"L1a_x2prev": _prevrow(x2s, c), "l1a_x1": x1s[c]} for c in range(8)]
    r = run("l1aE", ex=ex, xcur=x2s)
    xs = [r[c]["xout"] for c in range(8)]

    r = run("l1f", xcur=xs)
    xs = [r[c]["xout"] for c in range(8)]

    rP = run("l2a_pre", xcur=xs)
    ex = [{"l2a_qf": rP[c]["l2a_qf"], "l2a_nrm": rP[c]["l2a_nrm"],
           "l2a_tqa": _gather(rP, "l2a_tq", c),
           "l2a_va": _gather(rP, "l2a_v", c)} for c in range(8)]
    r = run("l2a_flash", ex=ex, xcur=xs)
    xs = [r[c]["xout"] for c in range(8)]

    xin_l2f = xs
    for l in range(2):
        ex = [{"L2f_xprevc": _prevcols(xs, c)} for c in range(8)]
        rQ = run(f"l2f_qkv{l}", ex=ex, xcur=xs)
        ex = []
        for c in range(8):
            if c % GS == 0:
                khp = np.zeros((512, 512), BF16NP)
                vhp = np.zeros((512, 512), BF16NP)
            else:
                khp = np.ascontiguousarray(rQ[c - 1]["l2f_khx"][:, T_FULL - 512:])
                vhp = np.ascontiguousarray(rQ[c - 1]["l2f_vt"][T_FULL - 512:, :])
            ex.append({"l2f_qh": rQ[c]["l2f_qh"], "l2f_khx": rQ[c]["l2f_khx"],
                       "l2f_vt": rQ[c]["l2f_vt"], "L2f_khp": khp, "L2f_vhp": vhp})
        r = run(f"l2f_attn{l}", ex=ex, xcur=xs)
        xs = [r[c]["xout"] for c in range(8)]
        ex = [{"L2f_x2prevc": _prevcols(xs, c)} for c in range(8)]
        if l == 1:
            for c in range(8):
                ex[c]["l2f_xin"] = xin_l2f[c]
        r = run(f"l2f_ffn{l}", ex=ex, xcur=xs)
        xs = [r[c]["xout"] for c in range(8)]

    rP = run("l3a_pre", xcur=xs)
    ex = [{"l3a_qf": rP[c]["l3a_qf"], "l3a_k": rP[c]["l3a_k"], "l3a_v": rP[c]["l3a_v"],
           "l3a_ka": _gather(rP, "l3a_k", c),
           "l3a_va": _gather(rP, "l3a_v", c)} for c in range(8)]
    r = run("l3a_flash", ex=ex, xcur=xs)
    xs = [r[c]["xout"] for c in range(8)]

    r = run("l3f", xcur=xs)
    xs = [r[c]["xout"] for c in range(8)]

    out = np.zeros((2, GS * T_FULL, 1024), np.float32)
    for c in range(8):
        out[c // GS, (c % GS) * T_FULL:(c % GS + 1) * T_FULL] = np.asarray(xs[c])
    return out


# version 04
# speedup vs baseline: 1.2060x; 1.2060x over previous
import numpy as np
import contextlib
import concourse.bass as bass
import concourse.mybir as mybir
from concourse.alu_op_type import AluOpType

F32 = mybir.dt.float32
BF16 = mybir.dt.bfloat16
AF = mybir.ActivationFunctionType
AX = mybir.AxisListType
OP = AluOpType

D = 1024
NCH = 8
LN_EPS = 1e-5


def chunks(total, step):
    return [(s, min(step, total - s)) for s in range(0, total, step)]


class Eng:
    def __init__(self, sem, step=1):
        self.sem = sem
        self.step = step
        self.cnt = 0

    def inc(self, inst):
        inst.then_inc(self.sem, self.step)
        self.cnt += 1
        return self.cnt

    def wait(self, h, cnt=None):
        c = self.cnt if cnt is None else cnt
        if c > 0:
            h.wait_ge(self.sem, c * self.step)


class DmaGroup:
    def __init__(self, eng):
        self.eng = eng
        self.target = eng.cnt

    def issue(self, inst):
        self.eng.inc(inst)
        self.target = self.eng.cnt

    def wait(self, h):
        if self.target > 0:
            h.wait_ge(self.eng.sem, self.target * 16)


class Arena:
    def __init__(self, tensor, nbytes_per_part):
        self.t = tensor
        self.nbytes = nbytes_per_part
        self.off = 0
        self.marks = []

    def push(self):
        self.marks.append(self.off)

    def pop(self):
        self.off = self.marks.pop()

    def alloc(self, cols, dtype):
        esz = 4 if dtype == F32 else 2
        self.off = (self.off + 63) // 64 * 64
        start = self.off
        self.off += cols * esz
        assert self.off <= self.nbytes, f"arena overflow: {self.off} > {self.nbytes}"
        base = self.t[:].bitcast(dtype) if dtype != BF16 else self.t[:]
        return base[:, start // esz: start // esz + cols]


class Ctx:
    def __init__(self, nc, es, T_LOC):
        self.nc = nc
        self.es = es
        self.T = T_LOC
        self.NT = T_LOC // 128

        sem = lambda n: es.enter_context(nc.semaphore(n))
        self.pe = Eng(sem("s_pe"))
        self.dve = Eng(sem("s_dve"))
        self.act = Eng(sem("s_act"))
        self.pool = Eng(sem("s_pool"))
        self.q_cc = Eng(sem("q_cc"), step=16)
        self._dsems = [Eng(sem(f"dq{i}"), step=16) for i in range(24)]
        self._dsi = 0
        self._hsems = [Eng(sem(f"hq{i}"), step=16) for i in range(16)]
        self._hsi = 0

        sb = lambda n, shape, dt: es.enter_context(nc.sbuf_tensor(n, shape, dt))
        self.X = sb("X", [128, self.NT * D], F32)
        ar_cols = (206 * 1024 - self.NT * D * 4 - 8 * 1024) // 2
        self.AR = sb("AR", [128, ar_cols], BF16)
        self.arena = Arena(self.AR, ar_cols * 2)
        self.COL = sb("COLS", [128, 768], F32)
        self.colmap = {}
        self.coloff = 0
        self.constvals = [1.0, LN_EPS, 1e-9, 1e-12, 1e-8, 0.5, 1e-24]
        self.constmap = {}
        self.ones_row = sb("ones_row", [1, 128], BF16)
        self.ones_rowf = sb("ones_rowf", [1, 128], F32)
        self.ident = sb("ident", [128, 128], BF16)
        self.identf = sb("identf", [128, 128], F32)
        self.CB = sb("CB", [128, 2 * D], BF16)
        self.ps = [es.enter_context(nc.psum_tensor(f"ps{i}", [128, 512], F32)) for i in range(7)]
        self.pt = es.enter_context(nc.psum_tensor("pt", [128, 1024], BF16))
        self._bank = 0

    def dgroup(self):
        e = self._dsems[self._dsi % len(self._dsems)]
        self._dsi += 1
        return DmaGroup(e)

    def hgroup(self):
        e = self._hsems[self._hsi % len(self._hsems)]
        self._hsi += 1
        return DmaGroup(e)

    def cols(self, name, n=None):
        if name not in self.colmap:
            assert n is not None
            self.colmap[name] = (self.coloff, n)
            self.coloff += n
            assert self.coloff <= 768
        o, n2 = self.colmap[name]
        return self.COL[:, o:o + n2]

    def bank(self, i=None):
        if i is None:
            b = self.ps[self._bank]
            self._bank = (self._bank + 1) % 7
            return b
        return self.ps[i]

    def constcol(self, val):
        assert val in self.constmap, f"const {val} not pre-registered"
        return self.cols(self.constmap[val], 1)

    def fence_all(self, blk):
        engs = [self.pe, self.dve, self.act, self.pool]
        qs = [self.q_cc] + self._dsems + self._hsems
        for name in ["tensor", "vector", "scalar", "gpsimd", "sync"]:
            def f(h):
                for e in engs + qs:
                    e.wait(h)
            getattr(blk, name)(f)

    def init_consts(self, blk):
        pool, dve = self.pool, self.dve
        for v_ in self.constvals:
            self.constmap[v_] = f"c_{v_}"
            self.cols(self.constmap[v_], 1)

        @blk.vector
        def _(v):
            v.memset(self.ones_row[:], 1.0)
            v.memset(self.ones_rowf[:], 1.0)
            v.memset(self.COL[:], 0.0)
            dve.inc(v.engine_nop())
            v.wait_ge(dve.sem, dve.cnt)
            for val in self.constvals:
                v.memset(self.cols(self.constmap[val]), val)
            dve.inc(v.engine_nop())

        @blk.gpsimd
        def _(g):
            g.memset(self.ident[:], 0.0)
            self.pool.inc(g.engine_nop())
            g.wait_ge(self.pool.sem, self.pool.cnt)
            g.affine_select(out=self.ident[:], in_=self.ident[:],
                            compare_op=OP.not_equal, fill=1.0,
                            base=0, pattern=[[-1, 128]], channel_multiplier=1)
            g.memset(self.identf[:], 0.0)
            self.pool.inc(g.engine_nop())
            g.wait_ge(self.pool.sem, self.pool.cnt)
            g.affine_select(out=self.identf[:], in_=self.identf[:],
                            compare_op=OP.not_equal, fill=1.0,
                            base=0, pattern=[[-1, 128]], channel_multiplier=1)
            self.pool.inc(g.engine_nop())



def load_weight(ctx, blk, dram_ap, dst, wait=None, group=None):
    K, N = dram_ap.shape
    kch = chunks(K, 128)
    grp = group or ctx.dgroup()

    @blk.gpsimd
    def _(g):
        if wait is not None:
            wait(g)
        for i, (k0, ksz) in enumerate(kch):
            grp.issue(g.dma_start(out=dst[0:ksz, i * N:(i + 1) * N],
                                  in_=dram_ap[k0:k0 + ksz, :]))
    return grp


def tm_to_fm(ctx, blk, src_f32, dst_fm, tiles=None, Tdst=None, scale_cols=None):
    tiles = list(range(ctx.NT)) if tiles is None else tiles
    Tdst = Tdst or ctx.T
    cast_cnt = {}
    tr_cnt = {}
    evac_cnt = {}
    dve_entry = ctx.dve.cnt
    for i, t in enumerate(tiles):
        @blk.vector
        def _(v, i=i, t=t):
            if i == 0:
                v.wait_ge(ctx.dve.sem, dve_entry)
            if i >= 2:
                ctx.pe.wait(v, tr_cnt[i - 2])
            src = src_f32[:, t * D:(t + 1) * D]
            if scale_cols is None:
                ctx.dve.inc(v.tensor_copy(out=ctx.CB[:, (i % 2) * D:(i % 2 + 1) * D], in_=src))
            else:
                ctx.dve.inc(v.tensor_scalar(out=ctx.CB[:, (i % 2) * D:(i % 2 + 1) * D],
                                            in0=src, scalar1=scale_cols(t), scalar2=None,
                                            op0=OP.mult))
            cast_cnt[i] = ctx.dve.cnt

        @blk.tensor
        def _(t_, i=i):
            ctx.dve.wait(t_, cast_cnt[i])
            if i >= 1:
                ctx.dve.wait(t_, evac_cnt[i - 1])
            for c in range(NCH):
                mm = t_.transpose(ctx.pt[:, c * 128:(c + 1) * 128],
                                  ctx.CB[:, (i % 2) * D + c * 128:(i % 2) * D + (c + 1) * 128],
                                  ctx.ident[:])
                ctx.pe.inc(mm)
            tr_cnt[i] = ctx.pe.cnt

        @blk.vector
        def _(v, i=i):
            ctx.pe.wait(v, tr_cnt[i])
            src3 = ctx.pt[:, 0:1024].rearrange("p (c j) -> p c j", c=NCH)
            dst3 = dst_fm.rearrange("p (c t) -> p c t", t=Tdst)[:, :, i * 128:(i + 1) * 128]
            ctx.dve.inc(v.tensor_copy(out=dst3, in_=src3))
            evac_cnt[i] = ctx.dve.cnt


def bias_mm(ctx, t_, psum_ap, brow_ap, start=False, stop=False):
    n = psum_ap.shape[-1]
    return t_.matmul(psum_ap, ctx.ones_row[:, 0:psum_ap.shape[0]], brow_ap[0:1, 0:n],
                     start=start, stop=stop)


def resid_update(ctx, blk, get_y, accum_name="lnsum", scale_y=None):
    sums = ctx.cols(accum_name, ctx.NT)

    @blk.vector
    def _(v):
        for t in range(ctx.NT):
            y = get_y(t)
            ctx.dve.inc(v.scalar_tensor_tensor(
                out=ctx.X[:, t * D:(t + 1) * D], in0=y,
                scalar=1.0 if scale_y is None else scale_y,
                in1=ctx.X[:, t * D:(t + 1) * D],
                op0=OP.mult, op1=OP.add,
                accum_out=sums[:, t:t + 1]))


def post_ln(ctx, blk, scr=None):
    sums = ctx.cols("lnsum")
    sumsq = ctx.cols("lnsq", ctx.NT)
    mu = ctx.cols("lnmu", ctx.NT)
    rstd = ctx.cols("lnrstd", ctx.NT)
    ctx.arena.push()
    scr = ctx.arena.alloc(D, BF16) if scr is None else scr
    dve0 = ctx.dve.cnt

    @blk.scalar
    def _(s):
        ctx.dve.wait(s)
        for t in range(ctx.NT):
            if t > 0:
                s.wait_ge(ctx.act.sem, ctx.act.cnt)
            ctx.act.inc(s.activation(out=scr[:, 0:D], in_=ctx.X[:, t * D:(t + 1) * D],
                                     func=AF.Square, accum_out=sumsq[:, t:t + 1]))

    @blk.vector
    def _(v):
        ctx.act.wait(v)
        ctx.dve.inc(v.tensor_scalar(out=mu, in0=sums, scalar1=1.0 / D, scalar2=None, op0=OP.mult))
        v.wait_ge(ctx.dve.sem, ctx.dve.cnt)
        ctx.dve.inc(v.tensor_tensor(out=rstd, in0=mu, in1=mu, op=OP.mult))
        v.wait_ge(ctx.dve.sem, ctx.dve.cnt)
        ctx.dve.inc(v.scalar_tensor_tensor(out=sumsq, in0=sumsq, scalar=1.0 / D,
                                           in1=rstd, op0=OP.mult, op1=OP.subtract))

    @blk.scalar
    def _(s):
        ctx.dve.wait(s)
        ctx.act.inc(s.activation(out=sumsq, in_=sumsq, func=AF.Ln, bias=ctx.constcol(LN_EPS)))
        s.wait_ge(ctx.act.sem, ctx.act.cnt)
        ctx.act.inc(s.activation(out=rstd, in_=sumsq, func=AF.Exp, scale=-0.5))

    @blk.vector
    def _(v):
        ctx.act.wait(v)
        for t in range(ctx.NT):
            ctx.dve.inc(v.tensor_scalar(out=ctx.X[:, t * D:(t + 1) * D],
                                        in0=ctx.X[:, t * D:(t + 1) * D],
                                        scalar1=mu[:, t:t + 1], scalar2=rstd[:, t:t + 1],
                                        op0=OP.subtract, op1=OP.mult))
    ctx.arena.pop()



def emit_L0a(ctx, blk, W):
    T, NT = ctx.T, ctx.NT
    ar = ctx.arena
    ar.push()
    xfm = ar.alloc(NCH * T, BF16)
    w1s = ar.alloc(8 * 1024, BF16)
    w23s = ar.alloc(8 * 32, BF16)
    rows = ar.alloc(2048, BF16)
    dscr = [ar.alloc(1024, F32) for _ in range(2)]
    escr = [ar.alloc(1024, F32) for _ in range(2)]
    uscr = [ar.alloc(1024, F32) for _ in range(2)]
    bc = ctx.cols("l0bc", 96)
    cb = ctx.cols("l0cb", 2)
    sums = ctx.cols("lnsum", NT)

    wg = ctx.dgroup()
    load_weight(ctx, blk, W["L0_W1T"], w1s, group=wg)
    load_weight(ctx, blk, W["L0_W23T"], w23s, group=wg)

    @blk.gpsimd
    def _(g):
        wg.issue(g.dma_start(out=rows[0:1, 0:1024], in_=W["L0_b1"][0:1, :]))
        wg.issue(g.dma_start(out=rows[0:1, 1024:1024 + 32], in_=W["L0_b23"][0:1, :]))

    tm_to_fm(ctx, blk, ctx.X[:], xfm)
    fm_dve = ctx.dve.cnt

    consumed = {}
    for t in range(NT):
        par = t % 2
        pb = [ctx.bank(), ctx.bank(), ctx.bank()]

        @blk.tensor
        def _(t_, t=t, pb=pb):
            if t == 0:
                ctx.dve.wait(t_, fm_dve)
                wg.wait(t_)
            if t >= 2:
                ctx.act.wait(t_, consumed[t - 2][0])
                ctx.dve.wait(t_, consumed[t - 2][1])
            for nb in range(2):
                for kc in range(NCH):
                    t_.matmul(pb[nb][:, 0:512],
                              xfm[:, kc * T + t * 128: kc * T + (t + 1) * 128],
                              w1s[:, kc * 1024 + nb * 512: kc * 1024 + (nb + 1) * 512],
                              start=(kc == 0), stop=False)
                ctx.pe.inc(bias_mm(ctx, t_, pb[nb][:, 0:512],
                                   rows[0:1, nb * 512:(nb + 1) * 512], stop=True))
            for kc in range(NCH):
                t_.matmul(pb[2][:, 0:32],
                          xfm[:, kc * T + t * 128: kc * T + (t + 1) * 128],
                          w23s[:, kc * 32:(kc + 1) * 32],
                          start=(kc == 0), stop=False)
            ctx.pe.inc(bias_mm(ctx, t_, pb[2][:, 0:32], rows[0:1, 1024:1024 + 32], stop=True))
        mm_done = ctx.pe.cnt

        @blk.scalar
        def _(s, t=t, pb=pb, par=par):
            ctx.pe.wait(s, mm_done)
            if t >= 2:
                ctx.act.wait(s, consumed[t - 2][0])
                ctx.dve.wait(s, consumed[t - 2][1])
            for nb in range(2):
                ctx.act.inc(s.activation(out=escr[par][:, nb * 512:(nb + 1) * 512],
                                         in_=pb[nb][:, 0:512], func=AF.Exp))
            s.wait_ge(ctx.act.sem, ctx.act.cnt)
            ctx.act.inc(s.activation(out=dscr[par][:, 0:1024], in_=escr[par][:, 0:1024],
                                     func=AF.Ln, bias=ctx.constcol(1.0)))
        act_done = ctx.act.cnt

        @blk.vector
        def _(v, t=t, pb=pb, par=par):
            ctx.pe.wait(v, mm_done)
            if t >= 2:
                ctx.dve.wait(v, consumed[t - 2][1])
            ctx.dve.inc(v.tensor_copy(out=bc[:, par * 48:par * 48 + 32], in_=pb[2][:, 0:32]))
            v.wait_ge(ctx.dve.sem, ctx.dve.cnt)
            ctx.dve.inc(v.tensor_tensor(out=bc[:, par * 48 + 32:par * 48 + 48],
                                        in0=bc[:, par * 48:par * 48 + 16],
                                        in1=bc[:, par * 48 + 16:par * 48 + 32],
                                        op=OP.mult))
            v.wait_ge(ctx.dve.sem, ctx.dve.cnt)
            ctx.dve.inc(v.tensor_scalar(out=bc[:, par * 48 + 32:par * 48 + 48],
                                        in0=bc[:, par * 48 + 32:par * 48 + 48],
                                        scalar1=1.0, scalar2=None, op0=OP.mult,
                                        op1=OP.add, accum_out=cb[:, par:par + 1]))
            ctx.act.wait(v, act_done)
            v.wait_ge(ctx.dve.sem, ctx.dve.cnt)
            ctx.dve.inc(v.tensor_scalar(out=uscr[par][:, 0:1024], in0=dscr[par][:, 0:1024],
                                        scalar1=cb[:, par:par + 1], scalar2=1.0,
                                        op0=OP.mult, op1=OP.add))
            v.wait_ge(ctx.dve.sem, ctx.dve.cnt)
            ctx.dve.inc(v.scalar_tensor_tensor(out=ctx.X[:, t * D:(t + 1) * D],
                                               in0=uscr[par][:, 0:1024], scalar=1.0,
                                               in1=ctx.X[:, t * D:(t + 1) * D],
                                               op0=OP.mult, op1=OP.mult,
                                               accum_out=sums[:, t:t + 1]))
        consumed[t] = (act_done, ctx.dve.cnt)

    post_ln(ctx, blk)
    ar.pop()



def emit_L0f(ctx, blk, W):
    T, NT = ctx.T, ctx.NT
    NE, H = 16, 256
    ar = ctx.arena
    ar.push()
    xfm = ar.alloc(NCH * T, BF16)
    selS = ar.alloc(8 * NE, BF16)
    keysS = [ar.alloc(8 * H, BF16) for _ in range(2)]
    valsS = [ar.alloc(2 * 1024, BF16) for _ in range(2)]
    sfm = ar.alloc(2 * 512, BF16)
    wts = ar.alloc(NT * NE, F32)
    probs = ctx.cols("l0f_probs", 16)
    top8 = ctx.cols("l0f_top8", 8)
    wcol = ctx.cols("l0f_w", 4)
    sums = ctx.cols("lnsum", NT)

    selg = ctx.dgroup()
    load_weight(ctx, blk, W["L0_selT"], selS, group=selg)

    rsg = ctx.hgroup()

    @blk.sync
    def _(s):
        rsg.issue(s.dma_start(out=wcol[:, 2:3],
                              in_=W["L0_rs"][0:1, 0:1].partition_broadcast(128).squeeze(1)))

    tm_to_fm(ctx, blk, ctx.X[:], xfm)
    fm_dve = ctx.dve.cnt

    for t in range(NT):
        b = ctx.bank()

        @blk.tensor
        def _(t_, t=t, b=b):
            if t == 0:
                ctx.dve.wait(t_, fm_dve)
                selg.wait(t_)
            if t >= 4:
                ctx.act.wait(t_)
            for kc in range(NCH):
                mm = t_.matmul(b[:, 0:NE], xfm[:, kc * T + t * 128: kc * T + (t + 1) * 128],
                               selS[:, kc * NE:(kc + 1) * NE], start=(kc == 0), stop=(kc == NCH - 1))
            ctx.pe.inc(mm)
        mmc = ctx.pe.cnt

        @blk.scalar
        def _(s, t=t, b=b):
            ctx.pe.wait(s, mmc)
            if t > 0:
                ctx.dve.wait(s)
            ctx.act.inc(s.activation(out=probs, in_=b[:, 0:NE], func=AF.Sigmoid))

        @blk.vector
        def _(v, t=t):
            ctx.act.wait(v)
            if t == 0:
                rsg.wait(v)
            ctx.dve.inc(v.max(out=top8, in_=probs))
            v.wait_ge(ctx.dve.sem, ctx.dve.cnt)
            ctx.dve.inc(v.tensor_scalar(out=wts[:, t * NE:(t + 1) * NE], in0=probs,
                                        scalar1=top8[:, 3:4], scalar2=None, op0=OP.is_ge))
            v.wait_ge(ctx.dve.sem, ctx.dve.cnt)
            ctx.dve.inc(v.tensor_tensor(out=wts[:, t * NE:(t + 1) * NE],
                                        in0=wts[:, t * NE:(t + 1) * NE], in1=probs, op=OP.mult,))
            v.wait_ge(ctx.dve.sem, ctx.dve.cnt)
            ctx.dve.inc(v.tensor_scalar(out=probs, in0=wts[:, t * NE:(t + 1) * NE],
                                        scalar1=1.0, scalar2=None, op0=OP.mult,
                                        op1=OP.add, accum_out=wcol[:, 0:1]))
            v.wait_ge(ctx.dve.sem, ctx.dve.cnt)
            ctx.dve.inc(v.tensor_scalar(out=wcol[:, 0:1], in0=wcol[:, 0:1],
                                        scalar1=ctx.constcol(1e-9), scalar2=None, op0=OP.max))
            v.wait_ge(ctx.dve.sem, ctx.dve.cnt)
            ctx.dve.inc(v.reciprocal(out=wcol[:, 1:2], in_=wcol[:, 0:1]))
            v.wait_ge(ctx.dve.sem, ctx.dve.cnt)
            ctx.dve.inc(v.tensor_scalar(out=wcol[:, 1:2], in0=wcol[:, 1:2],
                                        scalar1=wcol[:, 2:3], scalar2=None, op0=OP.mult))
            v.wait_ge(ctx.dve.sem, ctx.dve.cnt)
            ctx.dve.inc(v.tensor_scalar(out=wts[:, t * NE:(t + 1) * NE],
                                        in0=wts[:, t * NE:(t + 1) * NE],
                                        scalar1=wcol[:, 1:2], scalar2=None, op0=OP.mult))
    routing_done = ctx.dve.cnt

    lns = ctx.cols("l0f_lns", 2)
    expert_q = {}

    def load_expert(e, par):
        grp = ctx.dgroup()

        @blk.gpsimd
        def _(g, e=e, par=par):
            if e >= 2:
                ctx.pe.wait(g)
            for kc in range(NCH):
                grp.issue(g.dma_start(out=keysS[par][0:128, kc * H:(kc + 1) * H],
                                      in_=W["L0_keys"][e, kc * 128:(kc + 1) * 128, :]))
            for hc in range(2):
                grp.issue(g.dma_start(out=valsS[par][0:128, hc * 1024:(hc + 1) * 1024],
                                      in_=W["L0_vals"][e, hc * 128:(hc + 1) * 128, :]))
        expert_q[e] = grp

    load_expert(0, 0)
    for e in range(NE):
        par = e % 2
        if e + 1 < NE:
            load_expert(e + 1, (e + 1) % 2)
        for tg in range(NT // 4):
            sb_banks = [ctx.bank(), ctx.bank()]
            vb = [ctx.bank(), ctx.bank()]

            @blk.tensor
            def _(t_, e=e, par=par, tg=tg, sb_banks=sb_banks):
                expert_q[e].wait(t_)
                if e == 0 and tg == 0:
                    ctx.dve.wait(t_, routing_done)
                ctx.act.wait(t_)
                ctx.dve.wait(t_)
                for hc in range(2):
                    for kc in range(NCH):
                        mm = t_.matmul(sb_banks[hc][:, 0:512],
                                       keysS[par][:, kc * H + hc * 128: kc * H + (hc + 1) * 128],
                                       xfm[:, kc * T + tg * 512: kc * T + (tg + 1) * 512],
                                       start=(kc == 0), stop=(kc == NCH - 1))
                    ctx.pe.inc(mm)
            sc_mm = ctx.pe.cnt

            @blk.scalar
            def _(s, sb_banks=sb_banks, sc_mm=sc_mm):
                ctx.pe.wait(s, sc_mm)
                for hc in range(2):
                    if hc:
                        s.wait_ge(ctx.act.sem, ctx.act.cnt)
                    ctx.act.inc(s.activation(out=sfm[:, hc * 512:(hc + 1) * 512],
                                             in_=sb_banks[hc][:, 0:512], func=AF.Relu))
            relu_done = ctx.act.cnt

            for ti in range(4):
                mm_done = {}

                @blk.tensor
                def _(t_, e=e, par=par, tg=tg, ti=ti, relu_done=relu_done, mm_done=mm_done):
                    if ti == 0:
                        ctx.act.wait(t_, relu_done)
                    ctx.dve.wait(t_)
                    for nb in range(2):
                        for hc in range(2):
                            mm = t_.matmul(vb[nb][:, 0:512],
                                           sfm[:, hc * 512 + ti * 128: hc * 512 + (ti + 1) * 128],
                                           valsS[par][:, hc * 1024 + nb * 512: hc * 1024 + nb * 512 + 512],
                                           start=(hc == 0), stop=(hc == 1))
                        ctx.pe.inc(mm)
                        mm_done[nb] = ctx.pe.cnt

                @blk.vector
                def _(v, e=e, tg=tg, ti=ti, mm_done=mm_done):
                    t = tg * 4 + ti
                    last = (e == NE - 1)
                    for nb in range(2):
                        ctx.pe.wait(v, mm_done[nb])
                        ctx.dve.inc(v.scalar_tensor_tensor(
                            out=ctx.X[:, t * D + nb * 512: t * D + (nb + 1) * 512],
                            in0=vb[nb][:, 0:512],
                            scalar=wts[:, t * NE + e: t * NE + e + 1],
                            in1=ctx.X[:, t * D + nb * 512: t * D + (nb + 1) * 512],
                            op0=OP.mult, op1=OP.add,
                            accum_out=lns[:, nb:nb + 1] if last else None))
                    if last:
                        v.wait_ge(ctx.dve.sem, ctx.dve.cnt)
                        ctx.dve.inc(v.tensor_scalar(
                            out=sums[:, t:t + 1], in0=lns[:, 0:1],
                            scalar1=lns[:, 1:2], scalar2=None, op0=OP.add))

    post_ln(ctx, blk)
    ar.pop()



def emit_L1f(ctx, blk, W):
    T, NT = ctx.T, ctx.NT
    NE, DF = 8, 4096
    NG = T // 1024
    ar = ctx.arena
    ar.push()
    xfm = ar.alloc(NCH * 1024, BF16)
    hbuf = ar.alloc(32 * 1024, BF16)
    w2buf = ar.alloc(32 * 512, BF16)
    w1buf = [ar.alloc(8 * 512, BF16) for _ in range(2)]
    wgS = ar.alloc(8 * NE, BF16)
    rows = ar.alloc(1056, BF16)
    b1c = ar.alloc(64, F32)
    wts = ar.alloc(NT * NE, F32)
    sums = ctx.cols("lnsum", NT)
    pcol = ctx.cols("l1f_p", 12)
    lns16 = ctx.cols("l1f_lns", 16)
    probs = ctx.cols("l1f_probs", 8)
    top8 = ctx.cols("l1f_top8", 8)

    ctx.w1q_hist = []
    ctx.w1q_grps = []
    wg0 = ctx.dgroup()
    load_weight(ctx, blk, W["L1f_Wg"], wgS, group=wg0)

    @blk.gpsimd
    def _(g):
        wg0.issue(g.dma_start(out=rows[0:1, 1024:1032], in_=W["L1f_bg"][0:1, :]))

    tm_to_fm(ctx, blk, ctx.X[:], xfm, tiles=list(range(8)), Tdst=1024)
    fm_dve = ctx.dve.cnt

    def routing(t, xfm_ap, toff):
        b = ctx.bank()

        @blk.tensor
        def _(t_, t=t, b=b):
            if t == 0:
                ctx.dve.wait(t_, fm_dve)
                wg0.wait(t_)
            ctx.act.wait(t_)
            ctx.dve.wait(t_)
            for kc in range(NCH):
                t_.matmul(b[:, 0:NE], xfm_ap[:, kc * 1024 + toff * 128:kc * 1024 + (toff + 1) * 128],
                          wgS[:, kc * NE:(kc + 1) * NE], start=(kc == 0), stop=False)
            ctx.pe.inc(bias_mm(ctx, t_, b[:, 0:NE], rows[0:1, 1024:1032], stop=True))
        mmc = ctx.pe.cnt

        @blk.vector
        def _(v, b=b):
            ctx.pe.wait(v, mmc)
            ctx.dve.inc(v.reduce_max(out=pcol[:, 0:1], in_=b[:, 0:NE], axis=AX.X, negate=True))

        @blk.scalar
        def _(s, b=b):
            ctx.dve.wait(s)
            ctx.act.inc(s.activation(out=probs, in_=b[:, 0:NE], func=AF.Exp,
                                     bias=pcol[:, 0:1], accum_out=pcol[:, 1:2]))

        @blk.vector
        def _(v, t=t):
            ctx.act.wait(v)
            ctx.dve.inc(v.reciprocal(out=pcol[:, 2:3], in_=pcol[:, 1:2]))
            v.wait_ge(ctx.dve.sem, ctx.dve.cnt)
            ctx.dve.inc(v.tensor_scalar(out=probs, in0=probs, scalar1=pcol[:, 2:3],
                                        scalar2=None, op0=OP.mult))
            v.wait_ge(ctx.dve.sem, ctx.dve.cnt)
            ctx.dve.inc(v.max(out=top8, in_=probs))
            v.wait_ge(ctx.dve.sem, ctx.dve.cnt)
            ctx.dve.inc(v.tensor_scalar(out=wts[:, t * NE:(t + 1) * NE], in0=probs,
                                        scalar1=top8[:, 1:2], scalar2=None, op0=OP.is_ge))
            v.wait_ge(ctx.dve.sem, ctx.dve.cnt)
            ctx.dve.inc(v.tensor_tensor(out=wts[:, t * NE:(t + 1) * NE],
                                        in0=wts[:, t * NE:(t + 1) * NE], in1=probs, op=OP.mult))
            v.wait_ge(ctx.dve.sem, ctx.dve.cnt)
            ctx.dve.inc(v.tensor_scalar(out=probs, in0=wts[:, t * NE:(t + 1) * NE],
                                        scalar1=1.0, scalar2=None, op0=OP.mult,
                                        op1=OP.add, accum_out=pcol[:, 4:5]))
            v.wait_ge(ctx.dve.sem, ctx.dve.cnt)
            ctx.dve.inc(v.tensor_scalar(out=pcol[:, 5:6], in0=pcol[:, 4:5],
                                        scalar1=0.0, scalar2=None, op0=OP.is_equal))
            v.wait_ge(ctx.dve.sem, ctx.dve.cnt)
            ctx.dve.inc(v.tensor_tensor(out=pcol[:, 4:5], in0=pcol[:, 4:5],
                                        in1=pcol[:, 5:6], op=OP.add))
            v.wait_ge(ctx.dve.sem, ctx.dve.cnt)
            ctx.dve.inc(v.reciprocal(out=pcol[:, 2:3], in_=pcol[:, 4:5]))
            v.wait_ge(ctx.dve.sem, ctx.dve.cnt)
            ctx.dve.inc(v.tensor_scalar(out=wts[:, t * NE:(t + 1) * NE],
                                        in0=wts[:, t * NE:(t + 1) * NE],
                                        scalar1=pcol[:, 2:3], scalar2=None, op0=OP.mult))

    for g in range(NG):
        if g > 0:
            tm_to_fm(ctx, blk, ctx.X[:, g * 8 * D:(g + 1) * 8 * D], xfm, tiles=list(range(8)), Tdst=1024)
        for ti in range(8):
            routing(g * 8 + ti, xfm, ti)
        routing_done = ctx.dve.cnt

        for e in range(NE):
            bg1 = ctx.hgroup()

            bg2 = ctx.dgroup()

            @blk.scalar
            def _(s, e=e):
                ctx.act.wait(s)
                bg1.issue(s.dma_start(out=b1c[:, 0:32], in_=W["L1f_b1c"][e]))

            @blk.gpsimd
            def _(g, e=e):
                ctx.pe.wait(g)
                bg2.issue(g.dma_start(out=rows[0:1, 0:1024], in_=W["L1f_b2"][e:e + 1, :]))

            for q in range(8):
                wq = ctx.dgroup()

                @blk.gpsimd
                def _(gg, e=e, q=q):
                    qh = ctx.w1q_hist
                    if len(qh) >= 2:
                        ctx.pe.wait(gg, qh[-2])
                    qg_ = ctx.w1q_grps
                    if len(qg_) >= 2:
                        qg_[-2].wait(gg)
                    for fc in range(NCH):
                        wq.issue(gg.dma_start(
                            out=w1buf[q % 2][0:128, fc * 512:(fc + 1) * 512],
                            in_=W["L1f_W1"][e, fc * 128:(fc + 1) * 128, q * 512:(q + 1) * 512]))

                for hc in range(4):
                    for ts_ in range(2):
                        b = ctx.bank()

                        @blk.tensor
                        def _(t_, e=e, g=g, q=q, hc=hc, ts_=ts_, b=b):
                            wq.wait(t_)
                            if e == 0 and q == 0 and hc == 0 and ts_ == 0:
                                ctx.dve.wait(t_, routing_done)
                            ctx.act.wait(t_)
                            ctx.dve.wait(t_)
                            for fc in range(NCH):
                                mm = t_.matmul(b[:, 0:512],
                                               w1buf[q % 2][:, fc * 512 + hc * 128: fc * 512 + (hc + 1) * 128],
                                               xfm[:, fc * 1024 + ts_ * 512: fc * 1024 + (ts_ + 1) * 512],
                                               start=(fc == 0), stop=(fc == NCH - 1))
                            ctx.pe.inc(mm)
                        mmc = ctx.pe.cnt

                        @blk.scalar
                        def _(s, q=q, hc=hc, ts_=ts_, b=b, mmc=mmc):
                            ctx.pe.wait(s, mmc)
                            if q == 0 and hc == 0 and ts_ == 0:
                                bg1.wait(s)
                            hglob = q * 4 + hc
                            ctx.act.inc(s.activation(
                                out=hbuf[:, hglob * 1024 + ts_ * 512: hglob * 1024 + (ts_ + 1) * 512],
                                in_=b[:, 0:512], func=AF.Gelu, bias=b1c[:, hglob:hglob + 1]))
                ctx.w1q_hist.append(ctx.pe.cnt)
                ctx.w1q_grps.append(wq)
            h_done = ctx.act.cnt

            for vh in range(2):
                w2g = ctx.dgroup()

                @blk.gpsimd
                def _(gg, e=e, vh=vh):
                    ctx.pe.wait(gg)
                    w2r = W["L1f_W2"][e].rearrange("(hc p) v -> p hc v", p=128)
                    wv = w2r[:, :, vh * 512:(vh + 1) * 512]
                    w2g.issue(gg.dma_start(
                        out=w2buf[0:128, :].rearrange("p (hc v) -> p hc v", hc=32), in_=wv))

                for tset in ([0, 1, 2, 3, 4, 5], [6, 7]):
                    banks = {ti: ctx.bank() for ti in tset}
                    evac_dve = {}

                    @blk.tensor
                    def _(t_, e=e, g=g, vh=vh, tset=tset, banks=banks):
                        w2g.wait(t_)
                        bg2.wait(t_)
                        ctx.act.wait(t_, h_done)
                        ctx.dve.wait(t_)
                        for ti in tset:
                            bias_mm(ctx, t_, banks[ti][:, 0:512],
                                    rows[0:1, vh * 512:(vh + 1) * 512],
                                    start=True)
                        for hc in range(32):
                            for ti in tset:
                                mm = t_.matmul(banks[ti][:, 0:512],
                                               hbuf[:, hc * 1024 + ti * 128: hc * 1024 + (ti + 1) * 128],
                                               w2buf[:, hc * 512:(hc + 1) * 512],
                                               start=False, stop=(hc == 31))
                                if hc == 31:
                                    ctx.pe.inc(mm)
                                    evac_dve[ti] = ctx.pe.cnt

                    @blk.vector
                    def _(v, e=e, g=g, vh=vh, tset=tset, banks=banks, evac_dve=evac_dve):
                        for ti in tset:
                            ctx.pe.wait(v, evac_dve[ti])
                            t = g * 8 + ti
                            last = (e == NE - 1)
                            ctx.dve.inc(v.scalar_tensor_tensor(
                                out=ctx.X[:, t * D + vh * 512: t * D + (vh + 1) * 512],
                                in0=banks[ti][:, 0:512],
                                scalar=wts[:, t * NE + e: t * NE + e + 1],
                                in1=ctx.X[:, t * D + vh * 512: t * D + (vh + 1) * 512],
                                op0=OP.mult, op1=OP.add,
                                accum_out=lns16[:, vh * 8 + ti:vh * 8 + ti + 1] if last else None))
                            if last and vh == 1:
                                v.wait_ge(ctx.dve.sem, ctx.dve.cnt)
                                ctx.dve.inc(v.tensor_scalar(
                                    out=sums[:, t:t + 1], in0=lns16[:, ti:ti + 1],
                                    scalar1=lns16[:, 8 + ti:9 + ti], scalar2=None, op0=OP.add))

    post_ln(ctx, blk)
    ar.pop()



def emit_L3f(ctx, blk, W):
    T, NT = ctx.T, ctx.NT
    NE = 8
    NG = T // 1024
    ar = ctx.arena
    ar.push()
    xfm = ar.alloc(NCH * 1024, BF16)
    hbuf = ar.alloc(12 * 1024, BF16)
    wdbuf = ar.alloc(12 * 512, BF16)
    wgubuf = [ar.alloc(8 * 512, BF16) for _ in range(2)]
    gscr = [ar.alloc(1024, F32) for _ in range(2)]
    gateS = ar.alloc(8 * NE, BF16)
    wts = ar.alloc(NT * NE, F32)
    lcol = ctx.cols("l3f_l", 8)
    top8 = ctx.cols("l3f_top8", 8)
    pcol = ctx.cols("l3f_p", 8)
    lns16 = ctx.cols("l3f_lns", 16)
    sums = ctx.cols("lnsum", NT)

    wg0 = ctx.dgroup()
    load_weight(ctx, blk, W["L3f_gT"], gateS, group=wg0)
    tm_to_fm(ctx, blk, ctx.X[:], xfm, tiles=list(range(8)), Tdst=1024)
    fm_dve = ctx.dve.cnt

    def routing(t, xfm_ap, toff):
        b = ctx.bank()

        @blk.tensor
        def _(t_, t=t, b=b, toff=toff):
            if t == 0:
                ctx.dve.wait(t_, fm_dve)
                wg0.wait(t_)
            ctx.act.wait(t_)
            ctx.dve.wait(t_)
            for kc in range(NCH):
                mm = t_.matmul(b[:, 0:NE],
                               xfm_ap[:, kc * 1024 + toff * 128:kc * 1024 + (toff + 1) * 128],
                               gateS[:, kc * NE:(kc + 1) * NE], start=(kc == 0), stop=(kc == NCH - 1))
            ctx.pe.inc(mm)
        mmc = ctx.pe.cnt

        @blk.vector
        def _(v, b=b):
            ctx.pe.wait(v, mmc)
            ctx.dve.inc(v.tensor_copy(out=lcol, in_=b[:, 0:NE]))
            v.wait_ge(ctx.dve.sem, ctx.dve.cnt)
            ctx.dve.inc(v.reduce_max(out=pcol[:, 0:1], in_=lcol, axis=AX.X, negate=True))
            v.wait_ge(ctx.dve.sem, ctx.dve.cnt)
            ctx.dve.inc(v.max(out=top8, in_=lcol))

        @blk.scalar
        def _(s):
            ctx.dve.wait(s)
            ctx.act.inc(s.activation(out=pcol[:, 1:2], in_=top8[:, 1:2], func=AF.Exp,
                                     bias=pcol[:, 0:1]))
            s.wait_ge(ctx.act.sem, ctx.act.cnt)
            ctx.act.inc(s.activation(out=probs_ap, in_=lcol, func=AF.Exp, bias=pcol[:, 0:1]))

        @blk.vector
        def _(v, t=t):
            ctx.act.wait(v)
            ctx.dve.inc(v.tensor_scalar(out=wts[:, t * NE:(t + 1) * NE], in0=lcol,
                                        scalar1=top8[:, 1:2], scalar2=None, op0=OP.is_ge))
            v.wait_ge(ctx.dve.sem, ctx.dve.cnt)
            ctx.dve.inc(v.tensor_tensor(out=wts[:, t * NE:(t + 1) * NE],
                                        in0=wts[:, t * NE:(t + 1) * NE], in1=probs_ap, op=OP.mult))
            v.wait_ge(ctx.dve.sem, ctx.dve.cnt)
            ctx.dve.inc(v.tensor_scalar(out=probs_ap, in0=wts[:, t * NE:(t + 1) * NE],
                                        scalar1=1.0, scalar2=None, op0=OP.mult,
                                        op1=OP.add, accum_out=pcol[:, 2:3]))
            v.wait_ge(ctx.dve.sem, ctx.dve.cnt)
            ctx.dve.inc(v.reciprocal(out=pcol[:, 3:4], in_=pcol[:, 2:3]))
            v.wait_ge(ctx.dve.sem, ctx.dve.cnt)
            ctx.dve.inc(v.tensor_scalar(out=wts[:, t * NE:(t + 1) * NE],
                                        in0=wts[:, t * NE:(t + 1) * NE],
                                        scalar1=pcol[:, 3:4], scalar2=None, op0=OP.mult))

    probs_ap = ctx.cols("l3f_probs", 8)
    ctx.w1q_hist2 = []
    ctx.w1q_grps2 = []
    for g in range(NG):
        if g > 0:
            tm_to_fm(ctx, blk, ctx.X[:, g * 8 * D:(g + 1) * 8 * D], xfm, tiles=list(range(8)), Tdst=1024)
        for ti in range(8):
            routing(g * 8 + ti, xfm, ti)
        routing_done = ctx.dve.cnt

        for e in range(NE + 1):
            for q in range(6):
                wq = ctx.dgroup()

                @blk.gpsimd
                def _(gg, e=e, q=q):
                    qh = ctx.w1q_hist2
                    if len(qh) >= 2:
                        ctx.pe.wait(gg, qh[-2])
                    qg_ = ctx.w1q_grps2
                    if len(qg_) >= 2:
                        qg_[-2].wait(gg)
                    for fc in range(NCH):
                        wq.issue(gg.dma_start(
                            out=wgubuf[q % 2][0:128, fc * 512:(fc + 1) * 512],
                            in_=W["L3f_Wgu"][e, fc * 128:(fc + 1) * 128, q * 512:(q + 1) * 512]))

                for pj in range(2):
                    j = q * 2 + pj
                    for ts_ in range(2):
                        bg_ = ctx.bank()
                        bu_ = ctx.bank()

                        @blk.tensor
                        def _(t_, e=e, g=g, q=q, pj=pj, ts_=ts_, bg_=bg_, bu_=bu_):
                            wq.wait(t_)
                            if e == 0 and q == 0 and pj == 0 and ts_ == 0:
                                ctx.dve.wait(t_, routing_done)
                            ctx.act.wait(t_)
                            ctx.dve.wait(t_)
                            for which, bb in ((0, bg_), (1, bu_)):
                                off = (pj * 2 + which) * 128
                                for fc in range(NCH):
                                    mm = t_.matmul(bb[:, 0:512],
                                                   wgubuf[q % 2][:, fc * 512 + off: fc * 512 + off + 128],
                                                   xfm[:, fc * 1024 + ts_ * 512: fc * 1024 + (ts_ + 1) * 512],
                                                   start=(fc == 0), stop=(fc == NCH - 1))
                                ctx.pe.inc(mm)
                        mmc = ctx.pe.cnt

                        @blk.scalar
                        def _(s, ts_=ts_, bg_=bg_, mmc=mmc, par=ts_):
                            ctx.pe.wait(s, mmc)
                            ctx.dve.wait(s)
                            ctx.act.inc(s.activation(out=gscr[par][:, 0:512], in_=bg_[:, 0:512],
                                                     func=AF.Silu))
                        silu_done = ctx.act.cnt

                        @blk.vector
                        def _(v, j=j, ts_=ts_, bu_=bu_, silu_done=silu_done, par=ts_):
                            ctx.act.wait(v, silu_done)
                            ctx.dve.inc(v.tensor_tensor(
                                out=hbuf[:, j * 1024 + ts_ * 512: j * 1024 + (ts_ + 1) * 512],
                                in0=gscr[par][:, 0:512], in1=bu_[:, 0:512], op=OP.mult))
                ctx.w1q_hist2.append(ctx.pe.cnt)
                ctx.w1q_grps2.append(wq)
            h_done = ctx.dve.cnt

            for vh in range(2):
                w2g = ctx.dgroup()

                @blk.gpsimd
                def _(gg, e=e, vh=vh):
                    ctx.pe.wait(gg)
                    for hc in range(12):
                        w2g.issue(gg.dma_start(
                            out=wdbuf[0:128, hc * 512:(hc + 1) * 512],
                            in_=W["L3f_Wd"][e, hc * 128:(hc + 1) * 128, vh * 512:(vh + 1) * 512]))

                for tset in ([0, 1, 2, 3, 4, 5], [6, 7]):
                    banks = {ti: ctx.bank() for ti in tset}
                    evac_pe = {}

                    @blk.tensor
                    def _(t_, e=e, vh=vh, tset=tset, banks=banks, evac_pe=evac_pe):
                        w2g.wait(t_)
                        ctx.dve.wait(t_)
                        for hc in range(12):
                            for ti in tset:
                                mm = t_.matmul(banks[ti][:, 0:512],
                                               hbuf[:, hc * 1024 + ti * 128: hc * 1024 + (ti + 1) * 128],
                                               wdbuf[:, hc * 512:(hc + 1) * 512],
                                               start=(hc == 0), stop=(hc == 11))
                                if hc == 11:
                                    ctx.pe.inc(mm)
                                    evac_pe[ti] = ctx.pe.cnt

                    @blk.vector
                    def _(v, e=e, g=g, vh=vh, tset=tset, banks=banks, evac_pe=evac_pe):
                        for ti in tset:
                            ctx.pe.wait(v, evac_pe[ti])
                            t = g * 8 + ti
                            last = (e == NE)
                            ctx.dve.inc(v.scalar_tensor_tensor(
                                out=ctx.X[:, t * D + vh * 512: t * D + (vh + 1) * 512],
                                in0=banks[ti][:, 0:512],
                                scalar=1.0 if e == NE else wts[:, t * NE + e: t * NE + e + 1],
                                in1=ctx.X[:, t * D + vh * 512: t * D + (vh + 1) * 512],
                                op0=OP.mult, op1=OP.add,
                                accum_out=lns16[:, vh * 8 + ti:vh * 8 + ti + 1] if last else None))
                            if last and vh == 1:
                                v.wait_ge(ctx.dve.sem, ctx.dve.cnt)
                                ctx.dve.inc(v.tensor_scalar(
                                    out=sums[:, t:t + 1], in0=lns16[:, ti:ti + 1],
                                    scalar1=lns16[:, 8 + ti:9 + ti], scalar2=None, op0=OP.add))

    post_ln(ctx, blk)
    ar.pop()



def emit_L2a_pre(ctx, blk, W, dram, gsize=4):
    T, NT = ctx.T, ctx.NT
    TK = gsize * T
    KBS = min(512, T)
    SCS = KBS // 128
    NKB = TK // KBS
    ar = ctx.arena
    ar.push()
    norm = ctx.cols("l2a_norm", NT)
    rscale = ctx.cols("l2a_rsc", NT)
    sums = ctx.cols("lnsum", NT)
    qfm = ar.alloc(NCH * T, BF16)
    scr = ar.alloc(2048, BF16)
    sqscr = ar.alloc(1024, F32)

    @blk.scalar
    def _(s):
        ctx.dve.wait(s)
        for t in range(NT):
            if t > 0:
                s.wait_ge(ctx.act.sem, ctx.act.cnt)
            ctx.act.inc(s.activation(out=sqscr[:, 0:1024], in_=ctx.X[:, t * D:(t + 1) * D],
                                     func=AF.Square, accum_out=norm[:, t:t + 1]))
        s.wait_ge(ctx.act.sem, ctx.act.cnt)
        ctx.act.inc(s.activation(out=norm, in_=norm, func=AF.Sqrt))

    @blk.vector
    def _(v):
        ctx.act.wait(v)
        ctx.dve.inc(v.tensor_scalar(out=rscale, in0=norm, scalar1=ctx.constcol(1e-9),
                                    scalar2=None, op0=OP.add))
        v.wait_ge(ctx.dve.sem, ctx.dve.cnt)
        ctx.dve.inc(v.reciprocal(out=rscale, in_=rscale))

    ar.push()
    sfm = ar.alloc(NCH * T, BF16)
    wbuf = ar.alloc(8 * 1024, BF16)
    bqc = ctx.cols("l2a_bqc", 8)

    tm_to_fm(ctx, blk, ctx.X[:], sfm, scale_cols=lambda t: rscale[:, t:t + 1])
    sfm_done = ctx.dve.cnt

    wgq = ctx.dgroup()
    load_weight(ctx, blk, W["L2a_WqT"], wbuf, group=wgq)
    bq_g = ctx.hgroup()

    @blk.sync
    def _(s):
        bq_g.issue(s.dma_start(out=bqc, in_=W["L2a_bqc"]))

    TG = min(512, T)
    for oc in range(NCH):
        for tg in range(T // TG):
            b = ctx.bank()

            @blk.tensor
            def _(t_, oc=oc, tg=tg, b=b):
                if oc == 0 and tg == 0:
                    wgq.wait(t_)
                    ctx.dve.wait(t_, sfm_done)
                ctx.dve.wait(t_)
                for kc in range(NCH):
                    mm = t_.matmul(b[:, 0:TG],
                                   wbuf[:, kc * 1024 + oc * 128: kc * 1024 + (oc + 1) * 128],
                                   sfm[:, kc * T + tg * TG: kc * T + (tg + 1) * TG],
                                   start=(kc == 0), stop=(kc == NCH - 1))
                ctx.pe.inc(mm)
            mmc = ctx.pe.cnt

            @blk.vector
            def _(v, oc=oc, tg=tg, b=b, mmc=mmc):
                ctx.pe.wait(v, mmc)
                if oc == 0 and tg == 0:
                    bq_g.wait(v)
                ctx.dve.inc(v.tensor_scalar(out=qfm[:, oc * T + tg * TG: oc * T + (tg + 1) * TG],
                                            in0=b[:, 0:TG], scalar1=bqc[:, oc:oc + 1],
                                            scalar2=None, op0=OP.add))
    q_done = ctx.dve.cnt

    wgv = ctx.dgroup()
    load_weight(ctx, blk, W["L2a_WvT"], wbuf,
                wait=lambda h: (ctx.pe.wait(h), wgq.wait(h)), group=wgv)
    bv_g = ctx.dgroup()

    @blk.gpsimd
    def _(g):
        ctx.act.wait(g)
        bv_g.issue(g.dma_start(out=scr[0:1, 0:1024], in_=W["L2a_bv"][0:1, :]))

    vst = ctx.hgroup()
    for t in range(NT):
        vb = [ctx.bank(), ctx.bank()]

        @blk.tensor
        def _(t_, t=t, vb=vb):
            if t == 0:
                wgv.wait(t_)
                bv_g.wait(t_)
            ctx.dve.wait(t_)
            ctx.act.wait(t_)
            for nb in range(2):
                bias_mm(ctx, t_, vb[nb][:, 0:512], scr[0:1, nb * 512:(nb + 1) * 512], start=True)
                for kc in range(NCH):
                    mm = t_.matmul(vb[nb][:, 0:512],
                                   sfm[:, kc * T + t * 128: kc * T + (t + 1) * 128],
                                   wbuf[:, kc * 1024 + nb * 512: kc * 1024 + (nb + 1) * 512],
                                   start=False, stop=(kc == NCH - 1))
                ctx.pe.inc(mm)
        mmc = ctx.pe.cnt

        @blk.scalar
        def _(s, t=t, vb=vb, mmc=mmc):
            ctx.pe.wait(s, mmc)
            vst.wait(s)
            for nb in range(2):
                if nb:
                    s.wait_ge(ctx.act.sem, ctx.act.cnt)
                ctx.act.inc(s.activation(out=scr[:, 1024 + nb * 512:1024 + (nb + 1) * 512],
                                         in_=vb[nb][:, 0:512], func=AF.Copy, bias=0.0))
        ev = ctx.act.cnt

        @blk.sync
        def _(s, t=t, ev=ev):
            ctx.act.wait(s, ev)
            vst.issue(s.dma_start(out=dram["l2a_v"][t * 128:(t + 1) * 128, :],
                                  in_=scr[:, 1024:2048]))

    wga = ctx.dgroup()
    load_weight(ctx, blk, W["L2a_A"], wbuf,
                wait=lambda h: (ctx.pe.wait(h), wgv.wait(h)), group=wga)
    tqst = ctx.hgroup()
    tqfm = sfm
    for oc in range(NCH):
        for tg in range(T // TG):
            b = ctx.bank()

            @blk.tensor
            def _(t_, oc=oc, tg=tg, b=b):
                if oc == 0 and tg == 0:
                    wga.wait(t_)
                    ctx.dve.wait(t_, q_done)
                ctx.dve.wait(t_)
                ctx.act.wait(t_)
                for kc in range(NCH):
                    mm = t_.matmul(b[:, 0:TG],
                                   wbuf[:, kc * 1024 + oc * 128: kc * 1024 + (oc + 1) * 128],
                                   qfm[:, kc * T + tg * TG: kc * T + (tg + 1) * TG],
                                   start=(kc == 0), stop=(kc == NCH - 1))
                ctx.pe.inc(mm)
            mmc = ctx.pe.cnt

            @blk.scalar
            def _(s, oc=oc, tg=tg, b=b, mmc=mmc):
                ctx.pe.wait(s, mmc)
                s.wait_ge(ctx.act.sem, ctx.act.cnt)
                ctx.act.inc(s.activation(out=tqfm[:, oc * T + tg * TG: oc * T + (tg + 1) * TG],
                                         in_=b[:, 0:TG], func=AF.Copy, bias=0.0))
    tq_act = ctx.act.cnt

    @blk.sync
    def _(s):
        ctx.act.wait(s, tq_act)
        tqst.issue(s.dma_start(
            out=dram["l2a_tq"][:, :].rearrange("(c p) t -> p c t", p=128),
            in_=tqfm[:].rearrange("p (c t) -> p c t", c=NCH)))

    qfst = ctx.hgroup()

    @blk.sync
    def _(s):
        ctx.dve.wait(s, q_done)
        ctx.act.wait(s)
        qfst.issue(s.dma_start(
            out=dram["l2a_qf"][:, :].rearrange("(c p) t -> p c t", p=128),
            in_=qfm[:].rearrange("p (c t) -> p c t", c=NCH)))
        qfst.issue(s.dma_start(out=dram["l2a_nrm"][:, :], in_=norm))

    @blk.sync
    def _(s):
        vst.wait(s)
        tqst.wait(s)
        qfst.wait(s)
    ar.pop()
    ar.pop()


def emit_L2a_flash(ctx, blk, W, dram, gsize=4):
    T, NT = ctx.T, ctx.NT
    TK = gsize * T
    KBS = min(512, T)
    SCS = KBS // 128
    NKB = TK // KBS
    ar = ctx.arena
    ar.push()
    norm = ctx.cols("l2a_norm", NT)
    sums = ctx.cols("lnsum", NT)
    qfm = ar.alloc(NCH * T, BF16)
    qfg = ctx.hgroup()

    @blk.sync
    def _(s):
        ctx.dve.wait(s)
        qfg.issue(s.dma_start(out=qfm[:].rearrange("p (c t) -> p c t", c=NCH),
                              in_=W["l2a_qf"][:, :].rearrange("(c p) t -> p c t", p=128)))
        qfg.issue(s.dma_start(out=norm, in_=W["l2a_nrm"][:, :]))

    @blk.vector
    def _(v):
        qfg.wait(v)
        ctx.dve.inc(v.engine_nop())

    acc = [ar.alloc(1024, F32) for _ in range(2)]
    tqkb = [ar.alloc(NCH * KBS, BF16) for _ in range(2)]
    vkb = [ar.alloc(SCS * 1024, BF16) for _ in range(2)]
    ptb = ar.alloc(SCS * 128, BF16)
    pb_ = ar.alloc(KBS, BF16)
    fcol = ctx.cols("l2a_f", 12)

    def load_kb(kb, wait_pe):
        kbg = ctx.dgroup()

        @blk.gpsimd
        def _(g, kb=kb, wait_pe=wait_pe):
            if wait_pe is not None:
                ctx.pe.wait(g, wait_pe)
            r = (kb * KBS) // T
            c0 = (kb * KBS) % T
            kbg.issue(g.dma_start(
                out=tqkb[kb % 2][0:128, :].rearrange("p (c t) -> p c t", c=NCH),
                in_=W["l2a_tqa"][:, :].rearrange("(rc p) t -> p rc t", p=128)[:, r * NCH:(r + 1) * NCH, c0:c0 + KBS]))
            kbg.issue(g.dma_start(
                out=vkb[kb % 2][0:128, :].rearrange("p (kc v) -> p kc v", kc=SCS),
                in_=W["l2a_va"][kb * KBS:(kb + 1) * KBS, :].rearrange("(kc p) v -> p kc v", p=128)))
        return kbg

    for qp in range(NT // 2):
        pair = [qp * 2, qp * 2 + 1]
        if qp == 0:
            kb_grp = {0: load_kb(0, None)}
            kb_pe = {}

        @blk.vector
        def _(v, qp=qp):
            ctx.dve.wait(v)
            for par in range(2):
                v.memset(fcol[:, par * 2:par * 2 + 1], 3.0e38)
                v.memset(fcol[:, 4 + par:5 + par], 0.0)
                ctx.dve.inc(v.memset(acc[par][:, 0:1024], 0.0))

        for kb in range(NKB):
            if kb + 1 < NKB:
                kb_grp[kb + 1] = load_kb(kb + 1, kb_pe.get(kb - 1, ctx.pe.cnt))
            for par in range(2):
                t = pair[par]
                sb = ctx.bank()
                pv = [ctx.bank(), ctx.bank()]

                @blk.tensor
                def _(t_, t=t, kb=kb, sb=sb):
                    kb_grp[kb].wait(t_)
                    ctx.dve.wait(t_)
                    ctx.act.wait(t_)
                    for kc in range(NCH):
                        mm = t_.matmul(sb[:, 0:KBS],
                                       qfm[:, kc * T + t * 128: kc * T + (t + 1) * 128],
                                       tqkb[kb % 2][:, kc * KBS:(kc + 1) * KBS],
                                       start=(kc == 0), stop=(kc == NCH - 1))
                    ctx.pe.inc(mm)
                mmc = ctx.pe.cnt

                @blk.vector
                def _(v, kb=kb, sb=sb, par=par, mmc=mmc):
                    ctx.pe.wait(v, mmc)
                    ctx.act.wait(v)
                    ctx.dve.inc(v.reduce_max(out=fcol[:, 8:9], in_=sb[:, 0:KBS], axis=AX.X, negate=True))
                    v.wait_ge(ctx.dve.sem, ctx.dve.cnt)
                    ctx.dve.inc(v.tensor_scalar(out=fcol[:, 9:10], in0=fcol[:, 8:9],
                                                scalar1=fcol[:, par * 2:par * 2 + 1], scalar2=None,
                                                op0=OP.min))
                    v.wait_ge(ctx.dve.sem, ctx.dve.cnt)
                    ctx.dve.inc(v.tensor_scalar(out=fcol[:, 10:11], in0=fcol[:, 9:10],
                                                scalar1=fcol[:, par * 2:par * 2 + 1], scalar2=None,
                                                op0=OP.subtract))
                    v.wait_ge(ctx.dve.sem, ctx.dve.cnt)
                    ctx.dve.inc(v.tensor_copy(out=fcol[:, par * 2:par * 2 + 1], in_=fcol[:, 9:10]))
                    v.wait_ge(ctx.dve.sem, ctx.dve.cnt)
                    ctx.dve.inc(v.tensor_scalar(out=fcol[:, 11:12], in0=fcol[:, 9:10],
                                                scalar1=1.0 / 32.0, scalar2=None, op0=OP.mult))
                sm_dve = ctx.dve.cnt

                @blk.scalar
                def _(s, sb=sb, sm_dve=sm_dve):
                    ctx.dve.wait(s, sm_dve)
                    ctx.act.inc(s.activation(out=fcol[:, 10:11], in_=fcol[:, 10:11],
                                             func=AF.Exp, scale=1.0 / 32.0))
                    s.wait_ge(ctx.act.sem, ctx.act.cnt)
                    ctx.act.inc(s.activation(out=pb_[:, 0:KBS], in_=sb[:, 0:KBS],
                                             func=AF.Exp, scale=1.0 / 32.0,
                                             bias=fcol[:, 11:12], accum_out=fcol[:, 8:9]))
                exp_done = ctx.act.cnt

                @blk.vector
                def _(v, par=par, exp_done=exp_done):
                    ctx.act.wait(v, exp_done)
                    ctx.dve.inc(v.scalar_tensor_tensor(out=fcol[:, 4 + par:5 + par],
                                                       in0=fcol[:, 4 + par:5 + par],
                                                       scalar=fcol[:, 10:11],
                                                       in1=fcol[:, 8:9], op0=OP.mult, op1=OP.add))

                @blk.tensor
                def _(t_, exp_done=exp_done):
                    ctx.act.wait(t_, exp_done)
                    ctx.dve.wait(t_)
                    for sc in range(SCS):
                        tr = t_.transpose(ctx.pt[:, sc * 128:(sc + 1) * 128],
                                          pb_[:, sc * 128:(sc + 1) * 128], ctx.ident[:])
                    ctx.pe.inc(tr)
                trc = ctx.pe.cnt

                @blk.vector
                def _(v, trc=trc):
                    ctx.pe.wait(v, trc)
                    ctx.dve.inc(v.tensor_copy(out=ptb[:, 0:SCS * 128], in_=ctx.pt[:, 0:SCS * 128]))
                ptc = ctx.dve.cnt

                @blk.tensor
                def _(t_, kb=kb, pv=pv, ptc=ptc):
                    ctx.dve.wait(t_, ptc)
                    for nb in range(2):
                        for sc in range(SCS):
                            mm = t_.matmul(pv[nb][:, 0:512],
                                           ptb[:, sc * 128:(sc + 1) * 128],
                                           vkb[kb % 2][:, sc * 1024 + nb * 512: sc * 1024 + (nb + 1) * 512],
                                           start=(sc == 0), stop=(sc == SCS - 1))
                        ctx.pe.inc(mm)
                pvc = ctx.pe.cnt

                @blk.vector
                def _(v, par=par, pv=pv, pvc=pvc):
                    ctx.pe.wait(v, pvc)
                    for nb in range(2):
                        ctx.dve.inc(v.scalar_tensor_tensor(
                            out=acc[par][:, nb * 512:(nb + 1) * 512],
                            in0=acc[par][:, nb * 512:(nb + 1) * 512],
                            scalar=fcol[:, 10:11], in1=pv[nb][:, 0:512],
                            op0=OP.mult, op1=OP.add))
            kb_pe[kb] = ctx.pe.cnt

        for par in range(2):
            t = pair[par]

            @blk.vector
            def _(v, t=t, par=par):
                v.wait_ge(ctx.dve.sem, ctx.dve.cnt)
                ctx.dve.inc(v.reciprocal(out=fcol[:, 8:9], in_=fcol[:, 4 + par:5 + par]))
                v.wait_ge(ctx.dve.sem, ctx.dve.cnt)
                ctx.dve.inc(v.tensor_scalar(out=fcol[:, 8:9], in0=fcol[:, 8:9],
                                            scalar1=norm[:, t:t + 1], scalar2=None, op0=OP.mult))
                v.wait_ge(ctx.dve.sem, ctx.dve.cnt)
                ctx.dve.inc(v.scalar_tensor_tensor(
                    out=ctx.X[:, t * D:(t + 1) * D], in0=acc[par][:, 0:1024],
                    scalar=fcol[:, 8:9], in1=ctx.X[:, t * D:(t + 1) * D],
                    op0=OP.mult, op1=OP.add, accum_out=sums[:, t:t + 1]))
        if qp + 1 < NT // 2:
            kb_grp = {0: load_kb(0, kb_pe[NKB - 2] if NKB >= 2 else None)}
            kb_pe = {}

    post_ln(ctx, blk)
    ar.pop()



def emit_L3a_pre(ctx, blk, W, dram, gsize=4):
    T, NT = ctx.T, ctx.NT
    NH = 16
    TK = gsize * T
    TG = min(512, T)
    QGS = TG
    NQG = T // QGS
    NTI = QGS // 128
    KQS = min(2048, T)
    NKQ = TK // KQS
    NKCQ = KQS // 128
    ar = ctx.arena
    ar.push()
    sums = ctx.cols("lnsum", NT)
    qfm = ar.alloc(NCH * T, BF16)

    ar.push()
    xfm = ar.alloc(NCH * T, BF16)
    kst = ar.alloc(T, BF16)
    wqk = [ar.alloc(8 * 128, BF16) for _ in range(2)]
    stair_sb = ar.alloc(T, BF16)
    bqc = ctx.cols("l3a_bqc", 8)
    bkc = ctx.cols("l3a_bkc", 8)

    tm_to_fm(ctx, blk, ctx.X[:], xfm)
    fm_dve = ctx.dve.cnt

    bg = ctx.hgroup()
    sg = ctx.dgroup()

    @blk.sync
    def _(s):
        bg.issue(s.dma_start(out=bqc, in_=W["L3a_bqc"]))
        bg.issue(s.dma_start(out=bkc, in_=W["L3a_bkc"]))

    @blk.gpsimd
    def _(g):
        sg.issue(g.dma_start(out=stair_sb[0:64, :], in_=W["L3a_kstair"]))

    kstore = ctx.hgroup()

    @blk.gpsimd
    def _(g):
        sg.wait(g)
        for h in range(NH):
            sg.issue(g.dma_start(out=dram["l3a_k"][128 * h + 64:128 * (h + 1), :],
                                 in_=stair_sb[0:64, 0:T]))

    wqk_grps = []
    for oc in range(16):
        wg_ = ctx.dgroup()

        @blk.gpsimd
        def _(g, oc=oc):
            ctx.pe.wait(g)
            if len(wqk_grps) >= 2:
                wqk_grps[-2].wait(g)
            wg_.issue(g.dma_start(out=wqk[oc % 2][0:128, 0:8 * 128].rearrange("p (f c) -> p f c", f=NCH),
                                  in_=W["L3a_Wqkv"][:, oc * 128:(oc + 1) * 128]
                                  .rearrange("(f p) c -> p f c", p=128)))
        wqk_grps.append(wg_)

        for tg in range(T // TG):
            b = ctx.bank()

            @blk.tensor
            def _(t_, oc=oc, tg=tg, b=b):
                wg_.wait(t_)
                if oc == 0 and tg == 0:
                    ctx.dve.wait(t_, fm_dve)
                ctx.dve.wait(t_)
                ctx.act.wait(t_)
                for kc in range(NCH):
                    mm = t_.matmul(b[:, 0:TG],
                                   wqk[oc % 2][:, kc * 128:(kc + 1) * 128],
                                   xfm[:, kc * T + tg * TG: kc * T + (tg + 1) * TG],
                                   start=(kc == 0), stop=(kc == NCH - 1))
                ctx.pe.inc(mm)
            mmc = ctx.pe.cnt

            @blk.vector
            def _(v, oc=oc, tg=tg, b=b, mmc=mmc):
                ctx.pe.wait(v, mmc)
                if oc == 0 and tg == 0:
                    bg.wait(v)
                if oc < 8:
                    ctx.dve.inc(v.tensor_scalar(out=qfm[:, oc * T + tg * TG: oc * T + (tg + 1) * TG],
                                                in0=b[:, 0:TG], scalar1=bqc[:, oc:oc + 1],
                                                scalar2=0.125, op0=OP.add, op1=OP.mult))
                else:
                    if tg == 0:
                        kstore.wait(v)
                    ctx.dve.inc(v.tensor_scalar(out=kst[:, tg * TG:(tg + 1) * TG],
                                                in0=b[:, 0:TG], scalar1=bkc[:, oc - 8:oc - 7],
                                                scalar2=None, op0=OP.add))
        if oc >= 8:
            kd = ctx.dve.cnt

            @blk.sync
            def _(s, oc=oc, kd=kd):
                ctx.dve.wait(s, kd)
                c = oc - 8
                kstore.issue(s.dma_start(out=dram["l3a_k"][128 * (2 * c):128 * (2 * c) + 64, :],
                                         in_=kst[0:64, 0:T]))
                kstore.issue(s.dma_start(out=dram["l3a_k"][128 * (2 * c + 1):128 * (2 * c + 1) + 64, :],
                                         in_=kst[64:128, 0:T]))

    wvbuf = ar.alloc(8 * 1024, BF16)
    vrow = ar.alloc(1024, BF16)
    vtile = ar.alloc(1056, BF16)
    wgv = ctx.dgroup()
    load_weight(ctx, blk, W["L3a_Wqkv"][:, 2048:3072], wvbuf,
                wait=lambda h: ctx.pe.wait(h), group=wgv)
    bvg = ctx.dgroup()

    @blk.gpsimd
    def _(g):
        bvg.issue(g.dma_start(out=vrow[0:1, 0:1024], in_=W["L3a_bv"][0:1, :]))

    vstore = ctx.hgroup()
    for t in range(NT):
        vb = [ctx.bank(), ctx.bank()]

        @blk.tensor
        def _(t_, t=t, vb=vb):
            if t == 0:
                wgv.wait(t_)
                bvg.wait(t_)
            ctx.dve.wait(t_)
            ctx.act.wait(t_)
            for nb in range(2):
                bias_mm(ctx, t_, vb[nb][:, 0:512], vrow[0:1, nb * 512:(nb + 1) * 512], start=True)
                for kc in range(NCH):
                    mm = t_.matmul(vb[nb][:, 0:512],
                                   xfm[:, kc * T + t * 128: kc * T + (t + 1) * 128],
                                   wvbuf[:, kc * 1024 + nb * 512: kc * 1024 + (nb + 1) * 512],
                                   start=False, stop=(kc == NCH - 1))
                ctx.pe.inc(mm)
        mmc = ctx.pe.cnt

        @blk.vector
        def _(v, t=t, vb=vb, mmc=mmc):
            ctx.pe.wait(v, mmc)
            vstore.wait(v)
            ctx.dve.inc(v.memset(vtile[:, 0:1056], 1.0))
            v.wait_ge(ctx.dve.sem, ctx.dve.cnt)
            dst = vtile[:, 0:1040].rearrange("p (h c) -> p h c", c=65)
            for nb in range(2):
                ctx.dve.inc(v.tensor_copy(
                    out=dst[:, nb * 8:(nb + 1) * 8, 0:64],
                    in_=vb[nb][:, 0:512].rearrange("p (h c) -> p h c", c=64)))
        vd = ctx.dve.cnt

        @blk.sync
        def _(s, t=t, vd=vd):
            ctx.dve.wait(s, vd)
            vstore.issue(s.dma_start(out=dram["l3a_v"][t * 128:(t + 1) * 128, :],
                                     in_=vtile[:, 0:1040]))

    qfst = ctx.hgroup()
    qfd = ctx.dve.cnt

    @blk.sync
    def _(s, qfd=qfd):
        ctx.dve.wait(s, qfd)
        qfst.issue(s.dma_start(
            out=dram["l3a_qf"][:, :].rearrange("(c p) t -> p c t", p=128),
            in_=qfm[:].rearrange("p (c t) -> p c t", c=NCH)))

    @blk.sync
    def _(s):
        kstore.wait(s)
        vstore.wait(s)
        qfst.wait(s)
        sg.wait(s)
    ar.pop()
    ar.pop()


def emit_L3a_flash2(ctx, blk, W, dram, gsize=4):
    T, NT = ctx.T, ctx.NT
    NH = 16
    TG = min(512, T)
    QGS = TG
    NQG = T // QGS
    NTI = QGS // 128
    KQS = min(2048, T)
    NKCQ = KQS // 128
    ar = ctx.arena
    ar.push()
    qfm = ar.alloc(NCH * T, BF16)
    ctxfm = ar.alloc(NCH * T, BF16)
    qsb = ar.alloc(T, BF16)
    qfg = ctx.hgroup()

    @blk.sync
    def _(s):
        ctx.dve.wait(s)
        qfg.issue(s.dma_start(out=qfm[:].rearrange("p (c t) -> p c t", c=NCH),
                              in_=W["l3a_qf"][:, :].rearrange("(c p) t -> p c t", p=128)))

    @blk.vector
    def _(v):
        qfg.wait(v)
        ctx.dve.inc(v.engine_nop())

    ar.push()
    kdiag = ar.alloc(16 * QGS, BF16)
    vdiag = ar.alloc(NTI * 1040, BF16)
    QB = ar.alloc(4 * QGS, BF16)
    kq_t = [ar.alloc(KQS, BF16) for _ in range(4)]
    vkb = [ar.alloc(1040, BF16) for _ in range(2)]
    pb = [ar.alloc(QGS, BF16) for _ in range(2)]
    tri = ar.alloc(128, F32)
    rlb = ar.alloc(QGS, BF16)
    rlf = ar.alloc(QGS, F32)
    rl32 = ar.alloc(QGS, F32)
    qsg = ctx.dgroup()

    @blk.gpsimd
    def _(g):
        ctx.dve.wait(g)
        ctx.pe.wait(g)
        qsg.issue(g.dma_start(out=tri[0:128, 0:128], in_=W["L3a_tri"]))
        qsg.issue(g.dma_start(out=qsb[0:64, 0:T], in_=W["L3a_qstair"]))

    first_flash = [True]
    vkb_grps = []
    pv_marks = []
    bk = [0]
    for qg in range(NQG):
        NKQ_j = min(((qg + 1) * gsize * QGS + KQS - 1) // KQS, gsize * T // KQS)
        dgq = ctx.hgroup()

        @blk.sync
        def _(s, qg=qg):
            ctx.pe.wait(s)
            ctx.dve.wait(s)
            dgq.issue(s.dma_start(
                out=kdiag[0:128, :].rearrange("p (h t) -> p h t", h=16),
                in_=W["l3a_kd"][:, qg * QGS:(qg + 1) * QGS].rearrange("(h p) t -> p h t", p=128)))
            dgq.issue(s.dma_start(
                out=vdiag[0:128, :].rearrange("p (c v) -> p c v", c=NTI),
                in_=W["l3a_vd"][qg * QGS:(qg + 1) * QGS, :].rearrange("(c p) v -> p c v", p=128)))

        for hg in range(4):
            heads = [hg * 4 + i for i in range(4)]
            cbanks = {h: ctx.ps[i] for i, h in enumerate(heads)}
            rot = [4, 5, 6]
            ri = [0]

            def lbank():
                b = ctx.ps[rot[ri[0] % 3]]
                ri[0] += 1
                return b

            @blk.vector
            def _(v, qg=qg, heads=heads):
                ctx.pe.wait(v)
                if first_flash[0]:
                    qsg.wait(v)
                    first_flash[0] = False
                for i, h in enumerate(heads):
                    ctx.dve.inc(v.tensor_copy(
                        out=QB[0:64, i * QGS:(i + 1) * QGS],
                        in_=qfm[64 * (h % 2):64 * (h % 2) + 64,
                                (h // 2) * T + qg * QGS:(h // 2) * T + (qg + 1) * QGS]))
                    ctx.dve.inc(v.tensor_copy(
                        out=QB[64:128, i * QGS:(i + 1) * QGS],
                        in_=qsb[0:64, qg * QGS:(qg + 1) * QGS]))
            qb_done = ctx.dve.cnt

            for kq in range(NKQ_j):
                kqg = ctx.dgroup()

                @blk.gpsimd
                def _(g, kq=kq, heads=heads):
                    ctx.pe.wait(g)
                    for i, h in enumerate(heads):
                        kqg.issue(g.dma_start(
                            out=kq_t[i][0:128, 0:KQS],
                            in_=W["l3a_ka"][:, :].rearrange("(r hh p) t -> p r hh t", p=128, hh=16)
                            [:, kq, h, 0:KQS]))

                for kc in range(NKCQ):
                    vg = ctx.dgroup()
                    n = bk[0]

                    @blk.gpsimd
                    def _(g, kq=kq, kc=kc, n=n):
                        if n >= 2:
                            ctx.pe.wait(g, pv_marks[n - 2])
                        if len(vkb_grps) >= 2:
                            vkb_grps[-2].wait(g)
                        vg.issue(g.dma_start(
                            out=vkb[n % 2][0:128, 0:1040],
                            in_=W["l3a_va"][(kq * NKCQ + kc) * 128:(kq * NKCQ + kc + 1) * 128, :]))
                    vkb_grps.append(vg)

                    for i, h in enumerate(heads):
                        sb = lbank()

                        @blk.tensor
                        def _(t_, i=i, kq=kq, kc=kc, sb=sb):
                            if kq == 0 and kc == 0 and i == 0:
                                ctx.dve.wait(t_, qb_done)
                            kqg.wait(t_)
                            vg.wait(t_)
                            ctx.act.wait(t_)
                            ctx.dve.wait(t_)
                            mm = t_.matmul(sb[:, 0:QGS],
                                           kq_t[i][:, kc * 128:(kc + 1) * 128],
                                           QB[:, i * QGS:(i + 1) * QGS],
                                           start=True, stop=True)
                            ctx.pe.inc(mm)
                        smm = ctx.pe.cnt

                        @blk.scalar
                        def _(s, i=i, sb=sb, smm=smm, n=n):
                            ctx.pe.wait(s, smm)
                            s.wait_ge(ctx.act.sem, ctx.act.cnt)
                            ctx.act.inc(s.activation(out=pb[(n * 4 + i) % 2][:, 0:QGS],
                                                     in_=sb[:, 0:QGS], func=AF.Exp))
                        expd = ctx.act.cnt

                        @blk.tensor
                        def _(t_, h=h, i=i, kq=kq, kc=kc, expd=expd, n=n):
                            ctx.act.wait(t_, expd)
                            mm = t_.matmul(cbanks[h][0:65, 0:QGS],
                                           vkb[n % 2][:, 65 * h:65 * h + 65],
                                           pb[(n * 4 + i) % 2][:, 0:QGS],
                                           start=(kq == 0 and kc == 0), stop=False,
                                           skip_group_check=True)
                            ctx.pe.inc(mm)
                    pv_marks.append(ctx.pe.cnt)
                    bk[0] += 1

            for i, h in enumerate(heads):
                for ti in range(NTI):
                    sb = lbank()

                    @blk.tensor
                    def _(t_, i=i, h=h, ti=ti, sb=sb):
                        dgq.wait(t_)
                        ctx.act.wait(t_)
                        ctx.dve.wait(t_)
                        mm = t_.matmul(sb[:, 0:128],
                                       kdiag[0:64, h * QGS + ti * 128: h * QGS + (ti + 1) * 128],
                                       QB[0:64, i * QGS + ti * 128: i * QGS + (ti + 1) * 128],
                                       start=True, stop=True)
                        ctx.pe.inc(mm)
                    smm = ctx.pe.cnt

                    @blk.vector
                    def _(v, sb=sb, smm=smm):
                        ctx.pe.wait(v, smm)
                        ctx.dve.inc(v.tensor_tensor(out=sb[:, 0:128], in0=tri[:, 0:128],
                                                    in1=sb[:, 0:128], op=OP.add))
                    trid = ctx.dve.cnt

                    @blk.scalar
                    def _(s, ti=ti, sb=sb, trid=trid):
                        ctx.dve.wait(s, trid)
                        s.wait_ge(ctx.act.sem, ctx.act.cnt)
                        ctx.act.inc(s.activation(out=pb[ti % 2][:, 0:128], in_=sb[:, 0:128],
                                                 func=AF.Exp))
                    expd = ctx.act.cnt

                    @blk.tensor
                    def _(t_, h=h, ti=ti, expd=expd):
                        ctx.act.wait(t_, expd)
                        mm = t_.matmul(cbanks[h][0:65, ti * 128:(ti + 1) * 128],
                                       vdiag[:, ti * 1040 + 65 * h: ti * 1040 + 65 * h + 65],
                                       pb[ti % 2][:, 0:128],
                                       start=False, stop=(ti == NTI - 1), skip_group_check=True)
                        ctx.pe.inc(mm)
            pv_done = ctx.pe.cnt

            for i, h in enumerate(heads):
                @blk.vector
                def _(v, h=h, pv_done=pv_done):
                    ctx.pe.wait(v, pv_done)
                    ctx.dve.wait(v)
                    ctx.dve.inc(v.reciprocal(out=rl32[0:1, 0:QGS], in_=cbanks[h][64:65, 0:QGS]))
                    v.wait_ge(ctx.dve.sem, ctx.dve.cnt)
                    ctx.dve.inc(v.tensor_copy(out=rlb[0:1, 0:QGS], in_=rl32[0:1, 0:QGS]))
                rc = ctx.dve.cnt

                b2 = lbank()

                @blk.tensor
                def _(t_, b2=b2, rc=rc):
                    ctx.dve.wait(t_, rc)
                    ctx.pe.inc(t_.matmul(b2[0:64, 0:QGS], ctx.ones_row[:, 0:64],
                                         rlb[0:1, 0:QGS], start=True, stop=True))
                bcc = ctx.pe.cnt

                @blk.vector
                def _(v, qg=qg, h=h, b2=b2, bcc=bcc):
                    ctx.pe.wait(v, bcc)
                    ctx.dve.inc(v.tensor_copy(out=rlf[0:64, 0:QGS], in_=b2[0:64, 0:QGS]))
                    v.wait_ge(ctx.dve.sem, ctx.dve.cnt)
                    ctx.dve.inc(v.tensor_tensor(
                        out=ctxfm[64 * (h % 2):64 * (h % 2) + 64,
                                  (h // 2) * T + qg * QGS:(h // 2) * T + (qg + 1) * QGS],
                        in0=cbanks[h][0:64, 0:QGS],
                        in1=rlf[0:64, 0:QGS],
                        op=OP.mult))

    ar.pop()
    cst = ctx.hgroup()

    @blk.sync
    def _(s):
        ctx.dve.wait(s)
        cst.issue(s.dma_start(
            out=dram["l3a_ctx"][:, :].rearrange("(c p) t -> p c t", p=128),
            in_=ctxfm[:].rearrange("p (c t) -> p c t", c=NCH)))

    @blk.sync
    def _(s):
        cst.wait(s)
    ar.pop()


def emit_L3a_apply(ctx, blk, W, dram):
    T, NT = ctx.T, ctx.NT
    ar = ctx.arena
    ar.push()
    sums = ctx.cols("lnsum", NT)
    lns = ctx.cols("l3a_lns", 2)
    ctxfm = ar.alloc(NCH * T, BF16)
    wpb = ar.alloc(8 * 1024, BF16)
    prow = ar.alloc(1024, BF16)
    cg = ctx.hgroup()

    @blk.sync
    def _(s):
        ctx.dve.wait(s)
        cg.issue(s.dma_start(out=ctxfm[:].rearrange("p (c t) -> p c t", c=NCH),
                             in_=W["l3a_ctxo"][:, :].rearrange("(c p) t -> p c t", p=128)))

    @blk.vector
    def _(v):
        cg.wait(v)
        ctx.dve.inc(v.engine_nop())

    wgp = ctx.dgroup()
    load_weight(ctx, blk, W["L3a_WpT"], wpb, wait=lambda g: ctx.pe.wait(g), group=wgp)

    @blk.gpsimd
    def _(g):
        wgp.issue(g.dma_start(out=prow[0:1, 0:1024], in_=W["L3a_bp"][0:1, :]))

    for t in range(NT):
        ob = [ctx.bank(), ctx.bank()]

        @blk.tensor
        def _(t_, t=t, ob=ob):
            if t == 0:
                wgp.wait(t_)
            ctx.dve.wait(t_)
            ctx.act.wait(t_)
            for nb in range(2):
                bias_mm(ctx, t_, ob[nb][:, 0:512], prow[0:1, nb * 512:(nb + 1) * 512], start=True)
                for kc in range(NCH):
                    mm = t_.matmul(ob[nb][:, 0:512],
                                   ctxfm[:, kc * T + t * 128: kc * T + (t + 1) * 128],
                                   wpb[:, kc * 1024 + nb * 512: kc * 1024 + (nb + 1) * 512],
                                   start=False, stop=(kc == NCH - 1))
                ctx.pe.inc(mm)
        mmc = ctx.pe.cnt

        @blk.vector
        def _(v, t=t, ob=ob, mmc=mmc):
            ctx.pe.wait(v, mmc)
            for nb in range(2):
                ctx.dve.inc(v.scalar_tensor_tensor(
                    out=ctx.X[:, t * D + nb * 512: t * D + (nb + 1) * 512],
                    in0=ob[nb][:, 0:512], scalar=1.0,
                    in1=ctx.X[:, t * D + nb * 512: t * D + (nb + 1) * 512],
                    op0=OP.mult, op1=OP.add,
                    accum_out=lns[:, nb:nb + 1]))
            v.wait_ge(ctx.dve.sem, ctx.dve.cnt)
            ctx.dve.inc(v.tensor_scalar(out=sums[:, t:t + 1], in0=lns[:, 0:1],
                                        scalar1=lns[:, 1:2], scalar2=None, op0=OP.add))

    post_ln(ctx, blk)
    ar.pop()



def halo_from_row(ctx, blk, W, dram, rowkey, validkey, scratchkey, grow, brow, hpc, gb_wait):
    hrow = ctx.cols("hfr_s", 4)
    ar = ctx.arena
    rbuf = ar.alloc(1024, F32)
    rb16 = ar.alloc(1024, BF16)
    vcol = ctx.cols("hfr_v", 1)
    g1 = ctx.hgroup()

    @blk.sync
    def _(s):
        ctx.dve.wait(s)
        ctx.act.wait(s)
        g1.issue(s.dma_start(out=rbuf[0:1, 0:1024], in_=W[rowkey][0:1, :]))
        g1.issue(s.dma_start(out=vcol, in_=W[validkey]))

    @blk.scalar
    def _(s):
        g1.wait(s)
        s.wait_ge(ctx.act.sem, ctx.act.cnt)
        ctx.act.inc(s.activation(out=rb16[0:1, 0:1024], in_=rbuf[0:1, 0:1024],
                                 func=AF.Copy, bias=0.0, accum_out=hrow[0:1, 0:1]))
        s.wait_ge(ctx.act.sem, ctx.act.cnt)
        ctx.act.inc(s.activation(out=rb16[0:1, 0:1024], in_=rbuf[0:1, 0:1024],
                                 func=AF.Square, accum_out=hrow[0:1, 1:2]))

    @blk.vector
    def _(v):
        ctx.act.wait(v)
        v.wait_ge(ctx.dve.sem, ctx.dve.cnt)
        ctx.dve.inc(v.tensor_scalar(out=hrow[0:1, 0:1], in0=hrow[0:1, 0:1],
                                    scalar1=1.0 / D, scalar2=None, op0=OP.mult))
        v.wait_ge(ctx.dve.sem, ctx.dve.cnt)
        ctx.dve.inc(v.tensor_tensor(out=hrow[0:1, 2:3], in0=hrow[0:1, 0:1],
                                    in1=hrow[0:1, 0:1], op=OP.mult))
        v.wait_ge(ctx.dve.sem, ctx.dve.cnt)
        ctx.dve.inc(v.scalar_tensor_tensor(out=hrow[0:1, 1:2], in0=hrow[0:1, 1:2],
                                           scalar=1.0 / D, in1=hrow[0:1, 2:3],
                                           op0=OP.mult, op1=OP.subtract))

    @blk.scalar
    def _(s):
        ctx.dve.wait(s)
        s.wait_ge(ctx.act.sem, ctx.act.cnt)
        ctx.act.inc(s.activation(out=hrow[0:1, 1:2], in_=hrow[0:1, 1:2], func=AF.Ln,
                                 bias=ctx.constcol(LN_EPS)[0:1, 0:1]))
        s.wait_ge(ctx.act.sem, ctx.act.cnt)
        ctx.act.inc(s.activation(out=hrow[0:1, 3:4], in_=hrow[0:1, 1:2], func=AF.Exp,
                                 scale=-0.5))

    @blk.vector
    def _(v):
        ctx.act.wait(v)
        if gb_wait is not None:
            gb_wait(v)
        v.wait_ge(ctx.dve.sem, ctx.dve.cnt)
        ctx.dve.inc(v.tensor_scalar(out=rbuf[0:1, 0:1024], in0=rbuf[0:1, 0:1024],
                                    scalar1=hrow[0:1, 0:1], scalar2=hrow[0:1, 3:4],
                                    op0=OP.subtract, op1=OP.mult))
        v.wait_ge(ctx.dve.sem, ctx.dve.cnt)
        if grow is not None:
            ctx.dve.inc(v.tensor_tensor(out=rbuf[0:1, 0:1024], in0=rbuf[0:1, 0:1024],
                                        in1=grow[0:1, 0:1024], op=OP.mult))
            v.wait_ge(ctx.dve.sem, ctx.dve.cnt)
            ctx.dve.inc(v.tensor_tensor(out=rbuf[0:1, 0:1024], in0=rbuf[0:1, 0:1024],
                                        in1=brow[0:1, 0:1024], op=OP.add))
            v.wait_ge(ctx.dve.sem, ctx.dve.cnt)
        ctx.dve.inc(v.tensor_copy(out=rb16[0:1, 0:1024], in_=rbuf[0:1, 0:1024]))
    rowd = ctx.dve.cnt

    stg = ctx.hgroup()

    @blk.sync
    def _(s):
        ctx.dve.wait(s, rowd)
        stg.issue(s.dma_start(out=dram[scratchkey][0:1, :], in_=rb16[0:1, 0:1024]))

    ldg = ctx.hgroup()

    @blk.sync
    def _(s):
        stg.wait(s)
        ctx.dve.wait(s)
        ldg.issue(s.dma_start(out=hpc[:, 0:8],
                              in_=dram[scratchkey][0:1, :].rearrange("o (c p) -> p (o c)", p=128),
                              allow_slow_non_contiguous=True))

    @blk.vector
    def _(v):
        ldg.wait(v)
        v.wait_ge(ctx.dve.sem, ctx.dve.cnt)
        ctx.dve.inc(v.tensor_scalar(out=hpc[:, 0:8], in0=hpc[:, 0:8],
                                    scalar1=vcol[:, 0:1], scalar2=None, op0=OP.mult))
    return ctx.dve.cnt


def ln_affine(ctx, blk, gb_psum_rows, wait_grp=None):
    for row, dst in gb_psum_rows:
        b = [ctx.bank(), ctx.bank()]

        @blk.tensor
        def _(t_, row=row, b=b):
            if wait_grp is not None:
                wait_grp.wait(t_)
            ctx.dve.wait(t_)
            ctx.act.wait(t_)
            for nb in range(2):
                ctx.pe.inc(t_.matmul(b[nb][:, 0:512], ctx.ones_row[:, 0:128],
                                     row[0:1, nb * 512:(nb + 1) * 512], start=True, stop=True))
        mmc = ctx.pe.cnt

        @blk.vector
        def _(v, dst=dst, b=b, mmc=mmc):
            ctx.pe.wait(v, mmc)
            for nb in range(2):
                ctx.dve.inc(v.tensor_copy(out=dst[:, nb * 512:(nb + 1) * 512], in_=b[nb][:, 0:512]))


def emit_L1a_stage(ctx, blk, W, dram, stage, gsize=4):
    T, NT = ctx.T, ctx.NT
    ar = ctx.arena
    ar.push()
    sums = ctx.cols("lnsum", NT)
    mu = ctx.cols("l1a_mu", NT)
    rstd = ctx.cols("l1a_rstd", NT)
    ssq = ctx.cols("l1a_ssq", NT)
    ssum = ctx.cols("l1a_ssum", NT)
    carr = ctx.cols("l1a_carr", 48)
    gbuf = ar.alloc(1024, BF16)
    bbuf = ar.alloc(1024, BF16)
    gb_bc = [ar.alloc(1024, BF16), ar.alloc(1024, BF16)]
    TG = min(512, T)
    nt4 = TG // 128

    def build_sh(mix_idx, ready_markers, hsrc, hpcols, hdone):
        for c in range(NCH):
            @blk.vector
            def _(v, c=c):
                if ready_markers:
                    for m in ready_markers:
                        ctx.pe.wait(v, m)
                    ready_markers.clear()
                v.wait_ge(ctx.dve.sem, ctx.dve.cnt)
                if c == 0:
                    ctx.dve.wait(v, hdone)
                    ctx.act.wait(v)
                ctx.dve.inc(v.tensor_tensor(out=sh[:, c * T + 1: c * T + T],
                                            in0=hsrc[:, c * T + 1: c * T + T],
                                            in1=hsrc[:, c * T: c * T + T - 1], op=OP.subtract))
                v.wait_ge(ctx.dve.sem, ctx.dve.cnt)
                ctx.dve.inc(v.scalar_tensor_tensor(out=sh[:, c * T + 1: c * T + T],
                                                   in0=sh[:, c * T + 1: c * T + T],
                                                   scalar=mixc[:, mix_idx * 8 + c: mix_idx * 8 + c + 1],
                                                   in1=hsrc[:, c * T: c * T + T - 1],
                                                   op0=OP.mult, op1=OP.add))
                v.wait_ge(ctx.dve.sem, ctx.dve.cnt)
                ctx.dve.inc(v.tensor_tensor(out=sh[:, c * T: c * T + 1],
                                            in0=hsrc[:, c * T: c * T + 1],
                                            in1=hpcols[:, c:c + 1], op=OP.subtract))
                v.wait_ge(ctx.dve.sem, ctx.dve.cnt)
                ctx.dve.inc(v.scalar_tensor_tensor(out=sh[:, c * T: c * T + 1],
                                                   in0=sh[:, c * T: c * T + 1],
                                                   scalar=mixc[:, mix_idx * 8 + c: mix_idx * 8 + c + 1],
                                                   in1=hpcols[:, c:c + 1],
                                                   op0=OP.mult, op1=OP.add))
        return ctx.dve.cnt

    if stage == 'A':
        ar.push()
        hfm = ar.alloc(NCH * T, BF16)
        htile = [ar.alloc(1024, F32) for _ in range(2)]
        mixc = ctx.cols("l1a_mix", 8 * 5)

        rowsg = ctx.dgroup()

        @blk.gpsimd
        def _(g):
            ctx.pe.wait(g)
            ctx.dve.wait(g)
            rowsg.issue(g.dma_start(out=gbuf[0:1, 0:1024], in_=W["L1a_ln1g"][0:1, :]))
            rowsg.issue(g.dma_start(out=bbuf[0:1, 0:1024], in_=W["L1a_ln1b"][0:1, :]))

        mixg = ctx.hgroup()

        @blk.sync
        def _(s):
            for i, nm in enumerate(["L1a_mixk_c", "L1a_mixv_c", "L1a_mixr_c",
                                    "L1a_cmixk_c", "L1a_cmixr_c"]):
                mixg.issue(s.dma_start(out=mixc[:, i * 8:(i + 1) * 8], in_=W[nm]))
            mixg.issue(s.dma_start(out=carr[:, 8:16], in_=W["L1a_u_c"]))

        @blk.scalar
        def _(s):
            mixg.wait(s)
            ctx.dve.wait(s)
            ctx.act.inc(s.activation(out=carr[:, 8:16], in_=carr[:, 8:16], func=AF.Exp))
        eu_done = ctx.act.cnt

        @blk.gpsimd
        def _(g):
            rowsg.wait(g)
            ctx.pool.inc(g.engine_nop())
        ln_affine(ctx, blk, [(gbuf[0:1, 0:1024], gb_bc[0]), (bbuf[0:1, 0:1024], gb_bc[1])],
                  wait_grp=rowsg)
        gb_done = ctx.dve.cnt

        def ln_pass(src_tile_fn, gbc, bbc, out_fn, sums_to=None):
            for t in range(NT):
                @blk.scalar
                def _(s, t=t):
                    ctx.dve.wait(s)
                    if t > 0:
                        s.wait_ge(ctx.act.sem, ctx.act.cnt)
                    src = src_tile_fn(t)
                    ctx.act.inc(s.activation(out=htile[1][:, 0:1024], in_=src, func=AF.Copy,
                                             bias=0.0, accum_out=ssum[:, t:t + 1]))
                    s.wait_ge(ctx.act.sem, ctx.act.cnt)
                    ctx.act.inc(s.activation(out=htile[1][:, 0:1024], in_=src, func=AF.Square,
                                             accum_out=ssq[:, t:t + 1]))

            @blk.vector
            def _(v):
                ctx.act.wait(v)
                ctx.dve.inc(v.tensor_scalar(out=mu, in0=ssum, scalar1=1.0 / D, scalar2=None, op0=OP.mult))
                v.wait_ge(ctx.dve.sem, ctx.dve.cnt)
                ctx.dve.inc(v.tensor_tensor(out=rstd, in0=mu, in1=mu, op=OP.mult))
                v.wait_ge(ctx.dve.sem, ctx.dve.cnt)
                ctx.dve.inc(v.scalar_tensor_tensor(out=ssq, in0=ssq, scalar=1.0 / D,
                                                   in1=rstd, op0=OP.mult, op1=OP.subtract))

            @blk.scalar
            def _(s):
                ctx.dve.wait(s)
                ctx.act.inc(s.activation(out=ssq, in_=ssq, func=AF.Ln, bias=ctx.constcol(LN_EPS)))
                s.wait_ge(ctx.act.sem, ctx.act.cnt)
                ctx.act.inc(s.activation(out=rstd, in_=ssq, func=AF.Exp, scale=-0.5))

            for t in range(NT):
                @blk.vector
                def _(v, t=t):
                    ctx.act.wait(v)
                    if t == 0:
                        ctx.dve.wait(v, gb_done)
                    par = t % 2
                    v.wait_ge(ctx.dve.sem, ctx.dve.cnt)
                    src = src_tile_fn(t)
                    ctx.dve.inc(v.tensor_scalar(out=htile[par][:, 0:1024], in0=src,
                                                scalar1=mu[:, t:t + 1], scalar2=rstd[:, t:t + 1],
                                                op0=OP.subtract, op1=OP.mult))
                    v.wait_ge(ctx.dve.sem, ctx.dve.cnt)
                    ctx.dve.inc(v.tensor_tensor(out=htile[par][:, 0:1024], in0=htile[par][:, 0:1024],
                                                in1=gbc, op=OP.mult))
                    v.wait_ge(ctx.dve.sem, ctx.dve.cnt)
                    ctx.dve.inc(v.tensor_tensor(out=htile[par][:, 0:1024], in0=htile[par][:, 0:1024],
                                                in1=bbc, op=OP.add))
                    out_fn(v, t, htile[par][:, 0:1024])

        tr_state = {"pe": None}

        def to_fm(v, t, src, dst):
            pass

        hfm_done_markers = {}

        def out_h(v, t, src):
            par = t % 2
            v.wait_ge(ctx.dve.sem, ctx.dve.cnt)
            if t >= 2:
                ctx.pe.wait(v, hfm_done_markers[t - 2][0])
            ctx.dve.inc(v.tensor_copy(out=ctx.CB[:, par * D:(par + 1) * D], in_=src))
            cast_cnt = ctx.dve.cnt

            @blk.tensor
            def _(t_, t=t, par=par, cast_cnt=cast_cnt):
                ctx.dve.wait(t_, cast_cnt)
                ctx.dve.wait(t_)
                for c in range(NCH):
                    mm = t_.transpose(ctx.pt[:, c * 128:(c + 1) * 128],
                                      ctx.CB[:, par * D + c * 128:(par + 1 - 1) * D + par * 0 + c * 128 + 128],
                                      ctx.ident[:])
                    ctx.pe.inc(mm)
            hfm_done_markers[t] = (ctx.pe.cnt, None)

            @blk.vector
            def _(v2, t=t):
                ctx.pe.wait(v2, hfm_done_markers[t][0])
                src3 = ctx.pt[:, 0:1024].rearrange("p (c j) -> p c j", c=NCH)
                dst3 = hfm.rearrange("p (c t) -> p c t", t=T)[:, :, t * 128:(t + 1) * 128]
                ctx.dve.inc(v2.tensor_copy(out=dst3, in_=src3))

        ln_pass(lambda t: ctx.X[:, t * D:(t + 1) * D], gb_bc[0], gb_bc[1], out_h)
        h_done = ctx.dve.cnt

        hprevc = ar.alloc(8, BF16)
        halo_done = halo_from_row(ctx, blk, W, dram, "L1a_xprev", "L1a_hvalid", "l1a_hrs",
                                  gbuf[0:1, 0:1024], bbuf[0:1, 0:1024], hprevc,
                                  gb_wait=lambda v: ctx.dve.wait(v, gb_done))
        sh = ar.alloc(NCH * T, BF16)
        wpc = [ar.alloc(8 * 128, BF16) for _ in range(2)]
        kvevac = ar.alloc(T, BF16)
        srtile = ar.alloc(4 * 1024, BF16)

        def proj_fm_spill(wname, dname, sh_done, extra_waits=()):
            st = ctx.hgroup()
            pgr = []
            for oc in range(NCH):
                wg_ = ctx.dgroup()

                @blk.gpsimd
                def _(g, oc=oc):
                    ctx.pe.wait(g)
                    if len(ctx.wpc_grps) >= 2:
                        ctx.wpc_grps[-2].wait(g)
                    wg_.issue(g.dma_start(out=wpc[oc % 2][0:128, 0:8 * 128].rearrange("p (f c) -> p f c", f=NCH),
                                          in_=W[wname][:, oc * 128:(oc + 1) * 128]
                                          .rearrange("(f p) c -> p f c", p=128)))
                ctx.wpc_grps.append(wg_)

                for tg in range(T // TG):
                    b = ctx.bank()

                    @blk.tensor
                    def _(t_, oc=oc, tg=tg, b=b):
                        wg_.wait(t_)
                        if oc == 0 and tg == 0:
                            ctx.dve.wait(t_, sh_done)
                        ctx.dve.wait(t_)
                        ctx.act.wait(t_)
                        for kc in range(NCH):
                            mm = t_.matmul(b[:, 0:TG],
                                           wpc[oc % 2][:, kc * 128:(kc + 1) * 128],
                                           sh[:, kc * T + tg * TG: kc * T + (tg + 1) * TG],
                                           start=(kc == 0), stop=(kc == NCH - 1))
                        ctx.pe.inc(mm)
                    mmc = ctx.pe.cnt

                    @blk.scalar
                    def _(s, oc=oc, tg=tg, b=b, mmc=mmc):
                        ctx.pe.wait(s, mmc)
                        if tg == 0:
                            st.wait(s)
                            for g2 in extra_waits:
                                g2.wait(s)
                        s.wait_ge(ctx.act.sem, ctx.act.cnt)
                        ctx.act.inc(s.activation(out=kvevac[:, tg * TG:(tg + 1) * TG],
                                                 in_=b[:, 0:TG], func=AF.Copy, bias=0.0))
                evd = ctx.act.cnt

                @blk.sync
                def _(s, oc=oc, evd=evd):
                    ctx.act.wait(s, evd)
                    st.issue(s.dma_start(out=dram[dname][oc * 128:(oc + 1) * 128, :],
                                         in_=kvevac[:, 0:T]))
            return st

        ctx.wpc_grps = []
        shk_done = build_sh(0, [], hfm, hprevc, halo_done)
        kst_grp = proj_fm_spill("L1a_WkT", "l1a_k", shk_done)
        markers = [ctx.pe.cnt]
        shv_done = build_sh(1, markers, hfm, hprevc, halo_done)
        vst_grp = proj_fm_spill("L1a_WvT", "l1a_v", shv_done, extra_waits=(kst_grp,))
        markers = [ctx.pe.cnt]
        shr_done = build_sh(2, markers, hfm, hprevc, halo_done)

        rst_grp = ctx.hgroup()
        wrhalf = ar.alloc(8 * 512, BF16)
        srtile = ar.alloc(2 * 512, BF16)
        nt4 = TG // 128
        ctx.wrh_grps = []
        for nb in range(2):
            wrg = ctx.dgroup()

            @blk.gpsimd
            def _(g, nb=nb):
                ctx.pe.wait(g)
                if len(ctx.wrh_grps) >= 1:
                    ctx.wrh_grps[-1].wait(g)
                for kc in range(NCH):
                    wrg.issue(g.dma_start(out=wrhalf[0:128, kc * 512:(kc + 1) * 512],
                                          in_=W["L1a_WrT"][kc * 128:(kc + 1) * 128,
                                                           nb * 512:(nb + 1) * 512]))
            ctx.wrh_grps.append(wrg)
            for t in range(NT):
                b_ = ctx.bank()

                @blk.tensor
                def _(t_, nb=nb, t=t, b_=b_):
                    wrg.wait(t_)
                    if nb == 0 and t == 0:
                        ctx.dve.wait(t_, shr_done)
                    ctx.act.wait(t_)
                    ctx.dve.wait(t_)
                    for kc in range(NCH):
                        mm = t_.matmul(b_[:, 0:512],
                                       sh[:, kc * T + t * 128: kc * T + (t + 1) * 128],
                                       wrhalf[:, kc * 512:(kc + 1) * 512],
                                       start=(kc == 0), stop=(kc == NCH - 1))
                    ctx.pe.inc(mm)
                mmc = ctx.pe.cnt

                @blk.scalar
                def _(s, nb=nb, t=t, b_=b_, mmc=mmc):
                    ctx.pe.wait(s, mmc)
                    rst_grp.wait(s)
                    s.wait_ge(ctx.act.sem, ctx.act.cnt)
                    ctx.act.inc(s.activation(out=srtile[:, (t % 2) * 512:(t % 2 + 1) * 512],
                                             in_=b_[:, 0:512], func=AF.Sigmoid))
                evd = ctx.act.cnt

                @blk.sync
                def _(s, nb=nb, t=t, evd=evd):
                    ctx.act.wait(s, evd)
                    rst_grp.issue(s.dma_start(
                        out=dram["l1a_r"][t * 128:(t + 1) * 128, nb * 512:(nb + 1) * 512],
                        in_=srtile[:, (t % 2) * 512:(t % 2 + 1) * 512]))
        r_spilled = ctx.pe.cnt

        markers = [ctx.pe.cnt]
        ar.pop()
        delta = ar.alloc(NCH * T, BF16)
        ar.push()
        ctx.kv_grps = []
        kj = [ar.alloc(T, BF16) for _ in range(2)]
        vj = [ar.alloc(T, BF16) for _ in range(2)]
        pe_ = [ar.alloc(T, F32) for _ in range(2)]
        psc = ar.alloc(T, F32)
        send = ar.alloc(16, F32)

        def load_kv(j, which, grp=None):
            g2 = grp or ctx.dgroup()

            @blk.gpsimd
            def _(g, j=j, which=which):
                ctx.act.wait(g)
                ctx.dve.wait(g)
                if len(ctx.kv_grps) >= 2:
                    ctx.kv_grps[-2].wait(g)
                if which in ("k", "kv"):
                    g2.issue(g.dma_start(out=kj[j % 2][0:128, 0:T],
                                         in_=dram["l1a_k"][j * 128:(j + 1) * 128, :]))
                if which in ("v", "kv"):
                    g2.issue(g.dma_start(out=vj[j % 2][0:128, 0:T],
                                         in_=dram["l1a_v"][j * 128:(j + 1) * 128, :]))
            ctx.kv_grps.append(g2)
            return g2

        for j in range(NCH):
            grp = load_kv(j, "k")

            @blk.scalar
            def _(s, j=j):
                grp.wait(s)
                kst_grp.wait(s)
                s.wait_ge(ctx.act.sem, ctx.act.cnt)
                ctx.dve.wait(s)
                ctx.act.inc(s.activation(out=pe_[j % 2][:, 0:T], in_=kj[j % 2][:, 0:T], func=AF.Exp))
            ed = ctx.act.cnt

            @blk.vector
            def _(v, j=j, ed=ed):
                ctx.act.wait(v, ed)
                ctx.dve.inc(v.tensor_tensor_scan(out=psc[:, 0:T], data0=pe_[j % 2][:, 0:T],
                                                 data1=pe_[j % 2][:, 0:T], initial=0.0,
                                                 op0=OP.add, op1=OP.bypass))
                v.wait_ge(ctx.dve.sem, ctx.dve.cnt)
                ctx.dve.inc(v.tensor_copy(out=send[:, j:j + 1], in_=psc[:, T - 1:T]))
        send_done = ctx.dve.cnt


        sndst = ctx.hgroup()

        @blk.sync
        def _(s):
            ctx.dve.wait(s, send_done)
            sndst.issue(s.dma_start(out=dram["l1a_s"][:, :], in_=send[:, 0:8]))

        @blk.sync
        def _(s):
            sndst.wait(s)
            kst_grp.wait(s)
            vst_grp.wait(s)
            rst_grp.wait(s)

        ar.pop()
        return

    if stage in ('D1', 'D2'):
        ctx.kv_grps = []
        kj = [ar.alloc(T, BF16) for _ in range(2)]
        vj = [ar.alloc(T, BF16) for _ in range(2)]
        pe_ = [ar.alloc(T, F32) for _ in range(2)]
        psc = ar.alloc(T, F32)
        send = ar.alloc(16, F32)
        pd = ar.alloc(T, F32)
        pm_ = ar.alloc(T, F32)
        pr2 = ar.alloc(T, F32)
        pbw = ar.alloc(T, F32)
        pRR = ar.alloc(T, F32)
        wrcols = ar.alloc(32, F32)

        def load_kv(j, which, grp=None):
            g2 = grp or ctx.dgroup()

            @blk.gpsimd
            def _(g, j=j, which=which):
                ctx.act.wait(g)
                ctx.dve.wait(g)
                if len(ctx.kv_grps) >= 2:
                    ctx.kv_grps[-2].wait(g)
                if which in ("k", "kv"):
                    g2.issue(g.dma_start(out=kj[j % 2][0:128, 0:T],
                                         in_=W["l1a_k"][j * 128:(j + 1) * 128, :]))
                if which in ("v", "kv"):
                    g2.issue(g.dma_start(out=vj[j % 2][0:128, 0:T],
                                         in_=W["l1a_v"][j * 128:(j + 1) * 128, :]))
            ctx.kv_grps.append(g2)
            return g2

    if stage == 'D1':
        sga = ar.alloc(4 * 8, F32)
        pmb = ctx.cols("l1a_pm", 4)
        sgg = ctx.hgroup()
        pmg = ctx.hgroup()

        @blk.sync
        def _(s):
            ctx.dve.wait(s)
            sgg.issue(s.dma_start(out=sga[0:128, 0:32].rearrange("p (r c) -> p r c", r=4),
                                  in_=W["l1a_sa"][:, :].rearrange("(r p) c -> p r c", p=128)))
            pmg.issue(s.dma_start(out=pmb, in_=W["L1a_pm"]))

        @blk.vector
        def _(v):
            sgg.wait(v)
            pmg.wait(v)
            v.wait_ge(ctx.dve.sem, ctx.dve.cnt)
            v.memset(carr[:, 0:8], 0.0)
            ctx.dve.inc(v.engine_nop())
            for r in range(gsize):
                v.wait_ge(ctx.dve.sem, ctx.dve.cnt)
                ctx.dve.inc(v.scalar_tensor_tensor(out=carr[:, 0:8], in0=sga[:, r * 8:(r + 1) * 8],
                                                   scalar=pmb[:, r:r + 1], in1=carr[:, 0:8],
                                                   op0=OP.mult, op1=OP.add))
        din_done = ctx.dve.cnt

        spl = ctx.hgroup()

        for j in range(NCH):
            grp = load_kv(j, "kv")

            @blk.scalar
            def _(s, j=j):
                grp.wait(s)
                pass
                s.wait_ge(ctx.act.sem, ctx.act.cnt)
                ctx.dve.wait(s)
                ctx.act.inc(s.activation(out=pe_[j % 2][:, 0:T], in_=kj[j % 2][:, 0:T], func=AF.Exp))
            ed = ctx.act.cnt

            @blk.vector
            def _(v, j=j, ed=ed):
                ctx.act.wait(v, ed)
                if j == 0:
                    ctx.dve.wait(v, din_done)
                else:
                    spl.wait(v)
                v.wait_ge(ctx.dve.sem, ctx.dve.cnt)
                e = pe_[j % 2][:, 0:T]
                ctx.dve.inc(v.tensor_tensor_scan(out=psc[:, 0:T], data0=e, data1=e,
                                                 initial=carr[:, j:j + 1], op0=OP.add, op1=OP.bypass))
                v.wait_ge(ctx.dve.sem, ctx.dve.cnt)
                ctx.dve.inc(v.tensor_tensor(out=pd[:, 0:T], in0=psc[:, 0:T], in1=e, op=OP.subtract))
                v.wait_ge(ctx.dve.sem, ctx.dve.cnt)
                ctx.dve.inc(v.tensor_tensor(out=pm_[:, 0:T], in0=pd[:, 0:T], in1=e, op=OP.max))
                v.wait_ge(ctx.dve.sem, ctx.dve.cnt)
                ctx.dve.inc(v.reciprocal(out=pr2[:, 0:T], in_=pm_[:, 0:T]))
                v.wait_ge(ctx.dve.sem, ctx.dve.cnt)
                ctx.dve.inc(v.tensor_tensor(out=pm_[:, 0:T], in0=psc[:, 0:T], in1=pr2[:, 0:T], op=OP.mult))
                v.wait_ge(ctx.dve.sem, ctx.dve.cnt)
                ctx.dve.inc(v.tensor_tensor(out=psc[:, 0:T], in0=e, in1=vj[j % 2][:, 0:T], op=OP.mult))
                v.wait_ge(ctx.dve.sem, ctx.dve.cnt)
                ctx.dve.inc(v.tensor_tensor(out=psc[:, 0:T], in0=psc[:, 0:T], in1=pm_[:, 0:T], op=OP.mult))
                v.wait_ge(ctx.dve.sem, ctx.dve.cnt)
                ctx.dve.inc(v.tensor_tensor_scan(out=pbw[:, 0:T], data0=pm_[:, 0:T], data1=psc[:, 0:T],
                                                 initial=0.0, op0=OP.mult, op1=OP.add))
                v.wait_ge(ctx.dve.sem, ctx.dve.cnt)
                ctx.dve.inc(v.tensor_tensor_scan(out=pRR[:, 0:T], data0=pm_[:, 0:T], data1=pm_[:, 0:T],
                                                 initial=1.0, op0=OP.mult, op1=OP.bypass))
                v.wait_ge(ctx.dve.sem, ctx.dve.cnt)
                ctx.dve.inc(v.tensor_copy(out=wrcols[:, j:j + 1], in_=pbw[:, T - 1:T]))
                v.wait_ge(ctx.dve.sem, ctx.dve.cnt)
                ctx.dve.inc(v.tensor_copy(out=wrcols[:, 8 + j:9 + j], in_=pRR[:, T - 1:T]))
            pld = ctx.dve.cnt

            @blk.sync
            def _(s, j=j, pld=pld):
                ctx.dve.wait(s, pld)
                spl.issue(s.dma_start(out=dram["l1a_dp"][j * 128:(j + 1) * 128, :], in_=pd[:, 0:T]))
                spl.issue(s.dma_start(out=dram["l1a_wl"][j * 128:(j + 1) * 128, :], in_=pbw[:, 0:T]))
                spl.issue(s.dma_start(out=dram["l1a_rr"][j * 128:(j + 1) * 128, :], in_=pRR[:, 0:T]))

        wrst = ctx.hgroup()

        @blk.sync
        def _(s):
            ctx.dve.wait(s)
            wrst.issue(s.dma_start(out=dram["l1a_wr"][:, :], in_=wrcols[:, 0:16]))

        @blk.sync
        def _(s):
            wrst.wait(s)
            spl.wait(s)

        ar.pop()
        return

    if stage == 'D2':
        wra = ar.alloc(4 * 16, F32)
        wrg2 = ctx.hgroup()
        pmb = ctx.cols("l1a_pm", 4)
        pmg = ctx.hgroup()

        @blk.sync
        def _(s):
            ctx.dve.wait(s)
            wrg2.issue(s.dma_start(out=wra[0:128, 0:64].rearrange("p (r c) -> p r c", r=4),
                                   in_=W["l1a_wra"][:, :].rearrange("(r p) c -> p r c", p=128)))
            pmg.issue(s.dma_start(out=pmb, in_=W["L1a_pm"]))
            pmg.issue(s.dma_start(out=carr[:, 8:16], in_=W["L1a_u_c"]))

        @blk.scalar
        def _(s):
            pmg.wait(s)
            s.wait_ge(ctx.act.sem, ctx.act.cnt)
            ctx.act.inc(s.activation(out=carr[:, 8:16], in_=carr[:, 8:16], func=AF.Exp))
        eu_done = ctx.act.cnt

        @blk.vector
        def _(v):
            wrg2.wait(v)
            pmg.wait(v)
            v.wait_ge(ctx.dve.sem, ctx.dve.cnt)
            v.memset(carr[:, 16:24], 0.0)
            ctx.dve.inc(v.tensor_scalar(out=carr[:, 24:28], in0=pmb, scalar1=-1.0, scalar2=1.0,
                                        op0=OP.mult, op1=OP.add))
            for r in range(gsize - 1):
                v.wait_ge(ctx.dve.sem, ctx.dve.cnt)
                ctx.dve.inc(v.tensor_scalar(out=carr[:, 32:40], in0=wra[:, r * 16 + 8:r * 16 + 16],
                                            scalar1=pmb[:, r:r + 1], scalar2=carr[:, 24 + r:25 + r],
                                            op0=OP.mult, op1=OP.add))
                v.wait_ge(ctx.dve.sem, ctx.dve.cnt)
                ctx.dve.inc(v.tensor_tensor(out=carr[:, 16:24], in0=carr[:, 16:24],
                                            in1=carr[:, 32:40], op=OP.mult))
                v.wait_ge(ctx.dve.sem, ctx.dve.cnt)
                ctx.dve.inc(v.tensor_scalar(out=carr[:, 40:48], in0=wra[:, r * 16:r * 16 + 8],
                                            scalar1=pmb[:, r:r + 1], scalar2=None, op0=OP.mult))
                v.wait_ge(ctx.dve.sem, ctx.dve.cnt)
                ctx.dve.inc(v.tensor_tensor(out=carr[:, 16:24], in0=carr[:, 16:24],
                                            in1=carr[:, 40:48], op=OP.add))
        win_done = ctx.dve.cnt

        delta = ar.alloc(NCH * T, BF16)

        @blk.vector
        def _(v):
            ctx.act.wait(v, eu_done)
            v.wait_ge(ctx.dve.sem, ctx.dve.cnt)
            ctx.dve.inc(v.memset(delta[:, 0:NCH * T], 0.0))
        pwl, prr_, pdp_ = pbw, pRR, pr2
        sigp = [ar.alloc(128, BF16) for _ in range(2)]

        for j in range(NCH):
            grp = load_kv(j, "kv")
            grp2 = ctx.hgroup()

            @blk.sync
            def _(s, j=j):
                ctx.dve.wait(s)
                pass
                grp2.issue(s.dma_start(out=pwl[:, 0:T], in_=W["l1a_wl"][j * 128:(j + 1) * 128, :]))
                grp2.issue(s.dma_start(out=prr_[:, 0:T], in_=W["l1a_rr"][j * 128:(j + 1) * 128, :]))
                grp2.issue(s.dma_start(out=pdp_[:, 0:T], in_=W["l1a_dp"][j * 128:(j + 1) * 128, :]))

            @blk.scalar
            def _(s, j=j):
                grp.wait(s)
                s.wait_ge(ctx.act.sem, ctx.act.cnt)
                ctx.dve.wait(s)
                ctx.act.inc(s.activation(out=pe_[j % 2][:, 0:T], in_=kj[j % 2][:, 0:T], func=AF.Exp))
            ed = ctx.act.cnt

            @blk.vector
            def _(v, j=j, ed=ed):
                ctx.act.wait(v, ed)
                grp2.wait(v)
                ctx.pe.wait(v)
                if j == 0:
                    ctx.dve.wait(v, win_done)
                v.wait_ge(ctx.dve.sem, ctx.dve.cnt)
                e = pe_[j % 2][:, 0:T]
                ctx.dve.inc(v.tensor_scalar(out=psc[:, 0:T], in0=e, scalar1=carr[:, 8 + j:9 + j],
                                            scalar2=None, op0=OP.mult))
                v.wait_ge(ctx.dve.sem, ctx.dve.cnt)
                ctx.dve.inc(v.tensor_tensor(out=pd[:, 0:T], in0=pdp_[:, 0:T], in1=psc[:, 0:T], op=OP.max))
                v.wait_ge(ctx.dve.sem, ctx.dve.cnt)
                ctx.dve.inc(v.scalar_tensor_tensor(out=pd[:, 0:T], in0=pd[:, 0:T], scalar=1e-8,
                                                   in1=pdp_[:, 0:T], op0=OP.mult, op1=OP.add))
                v.wait_ge(ctx.dve.sem, ctx.dve.cnt)
                ctx.dve.inc(v.tensor_tensor(out=pd[:, 0:T], in0=pd[:, 0:T], in1=psc[:, 0:T], op=OP.add))
                v.wait_ge(ctx.dve.sem, ctx.dve.cnt)
                ctx.dve.inc(v.reciprocal(out=pd[:, 0:T], in_=pd[:, 0:T]))
                v.wait_ge(ctx.dve.sem, ctx.dve.cnt)
                ctx.dve.inc(v.scalar_tensor_tensor(out=pm_[:, 0:T], in0=prr_[:, 0:T],
                                                   scalar=carr[:, 16 + j:17 + j],
                                                   in1=pwl[:, 0:T], op0=OP.mult, op1=OP.add))
                v.wait_ge(ctx.dve.sem, ctx.dve.cnt)
                ctx.dve.inc(v.tensor_tensor(out=psc[:, 0:T], in0=psc[:, 0:T], in1=vj[j % 2][:, 0:T], op=OP.mult))
                v.wait_ge(ctx.dve.sem, ctx.dve.cnt)
                ctx.dve.inc(v.tensor_tensor(out=psc[:, 1:T], in0=psc[:, 1:T], in1=pm_[:, 0:T - 1], op=OP.add))
                v.wait_ge(ctx.dve.sem, ctx.dve.cnt)
                ctx.dve.inc(v.tensor_scalar(out=psc[:, 0:1], in0=psc[:, 0:1],
                                            scalar1=carr[:, 16 + j:17 + j], scalar2=None, op0=OP.add))
                v.wait_ge(ctx.dve.sem, ctx.dve.cnt)
                ctx.dve.inc(v.tensor_tensor(out=psc[:, 0:T], in0=psc[:, 0:T], in1=pd[:, 0:T], op=OP.mult))
            wkv_done = ctx.dve.cnt

            for t in range(NT):
                sgp = ctx.hgroup()

                @blk.scalar
                def _(s, j=j, t=t):
                    ctx.dve.wait(s)
                    sgp.issue(s.dma_start(out=sigp[t % 2][0:128, 0:128],
                                          in_=W["l1a_r"][t * 128:(t + 1) * 128, j * 128:(j + 1) * 128]))
                b = ctx.bank()

                @blk.tensor
                def _(t_, j=j, t=t, b=b, wkv_done=wkv_done):
                    ctx.dve.wait(t_, wkv_done)
                    ctx.dve.wait(t_)
                    ctx.pe.inc(t_.transpose(b[:, 0:128], psc[:, t * 128:(t + 1) * 128], ctx.identf[:]))
                trd = ctx.pe.cnt

                @blk.vector
                def _(v, j=j, t=t, b=b, trd=trd):
                    ctx.pe.wait(v, trd)
                    sgp.wait(v)
                    ctx.dve.inc(v.tensor_tensor(out=delta[:, t * D + j * 128:(t) * D + (j + 1) * 128],
                                                in0=sigp[t % 2][:, 0:128], in1=b[:, 0:128], op=OP.mult))


        for t in range(NT):
            @blk.vector
            def _(v, t=t):
                v.wait_ge(ctx.dve.sem, ctx.dve.cnt)
                ctx.pe.wait(v)
                ctx.dve.inc(v.tensor_tensor(out=ctx.X[:, t * D:(t + 1) * D],
                                            in0=ctx.X[:, t * D:(t + 1) * D],
                                            in1=delta[:, t * D:(t + 1) * D], op=OP.add))

        ar.pop()
        return

    if stage == 'E':
        mixc = ctx.cols("l1a_mix", 8 * 5)
        mixg = ctx.hgroup()

        @blk.sync
        def _(s):
            ctx.dve.wait(s)
            for i, nm in enumerate(["L1a_mixk_c", "L1a_mixv_c", "L1a_mixr_c",
                                    "L1a_cmixk_c", "L1a_cmixr_c"]):
                mixg.issue(s.dma_start(out=mixc[:, i * 8:(i + 1) * 8], in_=W[nm]))

        @blk.vector
        def _(v):
            mixg.wait(v)
            ctx.dve.inc(v.engine_nop())

        delta = ar.alloc(NCH * T, BF16)

        @blk.vector
        def _(v):
            v.wait_ge(ctx.dve.sem, ctx.dve.cnt)
            ctx.dve.inc(v.memset(delta[:, 0:NCH * T], 0.0))
        delta1_done = ctx.dve.cnt
        rowsg = ctx.dgroup()

        @blk.gpsimd
        def _(g):
            ctx.pe.wait(g)
            ctx.dve.wait(g)
            rowsg.issue(g.dma_start(out=gbuf[0:1, 0:1024], in_=W["L1a_ln2g"][0:1, :]))
            rowsg.issue(g.dma_start(out=bbuf[0:1, 0:1024], in_=W["L1a_ln2b"][0:1, :]))
        ln_affine(ctx, blk, [(gbuf[0:1, 0:1024], gb_bc[0]), (bbuf[0:1, 0:1024], gb_bc[1])],
                  wait_grp=rowsg)
        gb_done = ctx.dve.cnt

        sh = ar.alloc(NCH * T, BF16)
        ar.push()
        h2fm = ar.alloc(NCH * T, BF16)
        htile = [ar.alloc(1024, F32), ar.alloc(1024, F32)]
        x2t = [ar.alloc(1024, F32)]
        hprevc2 = ar.alloc(8, BF16)

        def src_x2(t):
            return x2t[t % 2][:, 0:1024]

        for t in range(NT):
            @blk.vector
            def _(v, t=t):
                if t == 0:
                    ctx.dve.wait(v, delta1_done)
                    ctx.act.wait(v)
                    ctx.pe.wait(v)
                ctx.act.wait(v)
                v.wait_ge(ctx.dve.sem, ctx.dve.cnt)
                ctx.dve.inc(v.tensor_tensor(out=x2t[0][:, 0:1024], in0=ctx.X[:, t * D:(t + 1) * D],
                                            in1=delta[:, t * D:(t + 1) * D], op=OP.add))

            @blk.scalar
            def _(s, t=t):
                ctx.dve.wait(s)
                s.wait_ge(ctx.act.sem, ctx.act.cnt)
                ctx.act.inc(s.activation(out=htile[1][:, 0:1024], in_=x2t[0][:, 0:1024],
                                         func=AF.Copy, bias=0.0, accum_out=ssum[:, t:t + 1]))
                s.wait_ge(ctx.act.sem, ctx.act.cnt)
                ctx.act.inc(s.activation(out=htile[1][:, 0:1024], in_=x2t[0][:, 0:1024],
                                         func=AF.Square, accum_out=ssq[:, t:t + 1]))

        @blk.vector
        def _(v):
            ctx.act.wait(v)
            v.wait_ge(ctx.dve.sem, ctx.dve.cnt)
            ctx.dve.inc(v.tensor_scalar(out=mu, in0=ssum, scalar1=1.0 / D, scalar2=None, op0=OP.mult))
            v.wait_ge(ctx.dve.sem, ctx.dve.cnt)
            ctx.dve.inc(v.tensor_tensor(out=rstd, in0=mu, in1=mu, op=OP.mult))
            v.wait_ge(ctx.dve.sem, ctx.dve.cnt)
            ctx.dve.inc(v.scalar_tensor_tensor(out=ssq, in0=ssq, scalar=1.0 / D,
                                               in1=rstd, op0=OP.mult, op1=OP.subtract))

        @blk.scalar
        def _(s):
            ctx.dve.wait(s)
            s.wait_ge(ctx.act.sem, ctx.act.cnt)
            ctx.act.inc(s.activation(out=ssq, in_=ssq, func=AF.Ln, bias=ctx.constcol(LN_EPS)))
            s.wait_ge(ctx.act.sem, ctx.act.cnt)
            ctx.act.inc(s.activation(out=rstd, in_=ssq, func=AF.Exp, scale=-0.5))
        stats2 = ctx.act.cnt

        h2_markers = {}
        for t in range(NT):
            @blk.vector
            def _(v, t=t):
                ctx.act.wait(v, stats2)
                if t == 0:
                    ctx.dve.wait(v, gb_done)
                par = t % 2
                v.wait_ge(ctx.dve.sem, ctx.dve.cnt)
                ctx.dve.inc(v.tensor_tensor(out=x2t[0][:, 0:1024], in0=ctx.X[:, t * D:(t + 1) * D],
                                            in1=delta[:, t * D:(t + 1) * D], op=OP.add))
                v.wait_ge(ctx.dve.sem, ctx.dve.cnt)
                ctx.dve.inc(v.tensor_scalar(out=htile[par][:, 0:1024], in0=x2t[0][:, 0:1024],
                                            scalar1=mu[:, t:t + 1], scalar2=rstd[:, t:t + 1],
                                            op0=OP.subtract, op1=OP.mult))
                v.wait_ge(ctx.dve.sem, ctx.dve.cnt)
                ctx.dve.inc(v.tensor_tensor(out=htile[par][:, 0:1024], in0=htile[par][:, 0:1024],
                                            in1=gb_bc[0], op=OP.mult))
                v.wait_ge(ctx.dve.sem, ctx.dve.cnt)
                ctx.dve.inc(v.tensor_tensor(out=htile[par][:, 0:1024], in0=htile[par][:, 0:1024],
                                            in1=gb_bc[1], op=OP.add))
                if t >= 2:
                    ctx.pe.wait(v, h2_markers[t - 2])
                v.wait_ge(ctx.dve.sem, ctx.dve.cnt)
                ctx.dve.inc(v.tensor_copy(out=ctx.CB[:, par * D:(par + 1) * D], in_=htile[par][:, 0:1024]))
            cast_c = ctx.dve.cnt

            @blk.tensor
            def _(t_, t=t, cast_c=cast_c):
                ctx.dve.wait(t_, cast_c)
                ctx.dve.wait(t_)
                par = t % 2
                for c in range(NCH):
                    mm = t_.transpose(ctx.pt[:, c * 128:(c + 1) * 128],
                                      ctx.CB[:, par * D + c * 128: par * D + c * 128 + 128],
                                      ctx.ident[:])
                    ctx.pe.inc(mm)
            h2_markers[t] = ctx.pe.cnt

            @blk.vector
            def _(v, t=t):
                ctx.pe.wait(v, h2_markers[t])
                src3 = ctx.pt[:, 0:1024].rearrange("p (c j) -> p c j", c=NCH)
                dst3 = h2fm.rearrange("p (c t) -> p c t", t=T)[:, :, t * 128:(t + 1) * 128]
                ctx.dve.inc(v.tensor_copy(out=dst3, in_=src3))
        h2_done = ctx.dve.cnt

        halo2_done = halo_from_row(ctx, blk, W, dram, "L1a_x2prev", "L1a_hvalid", "l1a_hrs",
                                   gbuf[0:1, 0:1024], bbuf[0:1, 0:1024], hprevc2,
                                   gb_wait=lambda v: ctx.dve.wait(v, gb_done))
        shrc_done = build_sh(4, [], h2fm, hprevc2, halo2_done)
        rcst_grp = ctx.hgroup()
        wrchalf = ar.alloc(8 * 512, BF16)
        srtile2 = ar.alloc(2 * 512, BF16)
        ctx.wrch_grps = []
        for nb in range(2):
            wrcg = ctx.dgroup()

            @blk.gpsimd
            def _(g, nb=nb):
                ctx.pe.wait(g)
                if len(ctx.wrch_grps) >= 1:
                    ctx.wrch_grps[-1].wait(g)
                for kc in range(NCH):
                    wrcg.issue(g.dma_start(out=wrchalf[0:128, kc * 512:(kc + 1) * 512],
                                           in_=W["L1a_WrcT"][kc * 128:(kc + 1) * 128,
                                                             nb * 512:(nb + 1) * 512]))
            ctx.wrch_grps.append(wrcg)
            for t in range(NT):
                b_ = ctx.bank()

                @blk.tensor
                def _(t_, nb=nb, t=t, b_=b_):
                    wrcg.wait(t_)
                    if nb == 0 and t == 0:
                        ctx.dve.wait(t_, shrc_done)
                    ctx.act.wait(t_)
                    ctx.dve.wait(t_)
                    for kc in range(NCH):
                        mm = t_.matmul(b_[:, 0:512],
                                       sh[:, kc * T + t * 128: kc * T + (t + 1) * 128],
                                       wrchalf[:, kc * 512:(kc + 1) * 512],
                                       start=(kc == 0), stop=(kc == NCH - 1))
                    ctx.pe.inc(mm)
                mmc = ctx.pe.cnt

                @blk.scalar
                def _(s, nb=nb, t=t, b_=b_, mmc=mmc):
                    ctx.pe.wait(s, mmc)
                    rcst_grp.wait(s)
                    s.wait_ge(ctx.act.sem, ctx.act.cnt)
                    ctx.act.inc(s.activation(out=srtile2[:, (t % 2) * 512:(t % 2 + 1) * 512],
                                             in_=b_[:, 0:512], func=AF.Sigmoid))
                evd = ctx.act.cnt

                @blk.sync
                def _(s, nb=nb, t=t, evd=evd):
                    ctx.act.wait(s, evd)
                    rcst_grp.issue(s.dma_start(
                        out=dram["l1a_rc"][t * 128:(t + 1) * 128, nb * 512:(nb + 1) * 512],
                        in_=srtile2[:, (t % 2) * 512:(t % 2 + 1) * 512]))

        markers = [ctx.pe.cnt]
        shkc_done = build_sh(3, markers, h2fm, hprevc2, halo2_done)
        ar.pop()

        ctx.wkc_grps = []
        ctx.wvc_grps = []
        kcsq = ar.alloc(32 * TG, BF16)
        wkcb = [ar.alloc(8 * 128, BF16) for _ in range(2)]
        wvcb = [ar.alloc(512, BF16) for _ in range(2)]
        rcld = [ar.alloc(512, BF16) for _ in range(2)]
        relsc = ar.alloc(TG, F32)
        lns2 = ctx.cols("l1a_lns2", 2)

        for tg in range(T // TG):
            for oc in range(32):
                wkg = ctx.dgroup()

                @blk.gpsimd
                def _(g, tg=tg, oc=oc):
                    ctx.pe.wait(g)
                    ctx.dve.wait(g)
                    if len(ctx.wkc_grps) >= 2:
                        ctx.wkc_grps[-2].wait(g)
                    wkg.issue(g.dma_start(out=wkcb[oc % 2][0:128, 0:8 * 128].rearrange("p (f c) -> p f c", f=NCH),
                                          in_=W["L1a_WkcT"][:, oc * 128:(oc + 1) * 128]
                                          .rearrange("(f p) c -> p f c", p=128)))
                ctx.wkc_grps.append(wkg)
                b = ctx.bank()

                @blk.tensor
                def _(t_, tg=tg, oc=oc, b=b):
                    wkg.wait(t_)
                    if tg == 0 and oc == 0:
                        ctx.dve.wait(t_, shkc_done)
                    ctx.act.wait(t_)
                    ctx.dve.wait(t_)
                    for kc in range(NCH):
                        mm = t_.matmul(b[:, 0:TG],
                                       wkcb[oc % 2][:, kc * 128:(kc + 1) * 128],
                                       sh[:, kc * T + tg * TG: kc * T + (tg + 1) * TG],
                                       start=(kc == 0), stop=(kc == NCH - 1))
                    ctx.pe.inc(mm)
                mmc = ctx.pe.cnt

                @blk.scalar
                def _(s, oc=oc, b=b, mmc=mmc):
                    ctx.pe.wait(s, mmc)
                    ctx.dve.wait(s)
                    s.wait_ge(ctx.act.sem, ctx.act.cnt)
                    ctx.act.inc(s.activation(out=relsc[:, 0:TG], in_=b[:, 0:TG], func=AF.Relu))
                    s.wait_ge(ctx.act.sem, ctx.act.cnt)
                    ctx.act.inc(s.activation(out=kcsq[:, oc * TG:(oc + 1) * TG], in_=relsc[:, 0:TG],
                                             func=AF.Square))
            kc_done = ctx.act.cnt

            for nb in range(2):
                banks = [ctx.bank() for _ in range(nt4)]
                for kc in range(32):
                    wvg = ctx.dgroup()

                    @blk.gpsimd
                    def _(g, nb=nb, kc=kc):
                        ctx.pe.wait(g)
                        if len(ctx.wvc_grps) >= 2:
                            ctx.wvc_grps[-2].wait(g)
                        wvg.issue(g.dma_start(out=wvcb[kc % 2][0:128, 0:512],
                                              in_=W["L1a_WvcT"][kc * 128:(kc + 1) * 128,
                                                                nb * 512:(nb + 1) * 512]))
                    ctx.wvc_grps.append(wvg)

                    @blk.tensor
                    def _(t_, tg=tg, nb=nb, kc=kc, banks=banks):
                        wvg.wait(t_)
                        ctx.act.wait(t_, kc_done)
                        if kc == 0:
                            ctx.dve.wait(t_)
                        for ti in range(nt4):
                            mm = t_.matmul(banks[ti][:, 0:512],
                                           kcsq[:, kc * TG + ti * 128: kc * TG + (ti + 1) * 128],
                                           wvcb[kc % 2][:, 0:512],
                                           start=(kc == 0), stop=(kc == 31))
                        ctx.pe.inc(mm)
                vv_done = ctx.pe.cnt

                for ti in range(nt4):
                    rcg = ctx.hgroup()

                    @blk.scalar
                    def _(s, tg=tg, nb=nb, ti=ti):
                        ctx.dve.wait(s)
                        rcst_grp.wait(s)
                        t = tg * nt4 + ti
                        rcg.issue(s.dma_start(out=rcld[ti % 2][0:128, 0:512],
                                              in_=dram["l1a_rc"][t * 128:(t + 1) * 128,
                                                                 nb * 512:(nb + 1) * 512]))

                    @blk.vector
                    def _(v, tg=tg, nb=nb, ti=ti, banks=banks, vv_done=vv_done):
                        ctx.pe.wait(v, vv_done)
                        rcg.wait(v)
                        t = tg * nt4 + ti
                        v.wait_ge(ctx.dve.sem, ctx.dve.cnt)
                        ctx.dve.inc(v.tensor_tensor(out=rcld[ti % 2][:, 0:512], in0=rcld[ti % 2][:, 0:512],
                                                    in1=banks[ti][:, 0:512], op=OP.mult))
                        v.wait_ge(ctx.dve.sem, ctx.dve.cnt)
                        ctx.dve.inc(v.tensor_tensor(out=delta[:, t * D + nb * 512: t * D + (nb + 1) * 512],
                                                    in0=delta[:, t * D + nb * 512: t * D + (nb + 1) * 512],
                                                    in1=rcld[ti % 2][:, 0:512], op=OP.add))

        x1t = ar.alloc(1024, F32)
        for t in range(NT):
            gg = ctx.hgroup()

            @blk.sync
            def _(s, t=t, gg=gg):
                ctx.dve.wait(s)
                gg.issue(s.dma_start(out=x1t[:, 0:1024],
                                     in_=W["l1a_x1"][t * 128:(t + 1) * 128, :]))

            @blk.vector
            def _(v, t=t, gg=gg):
                gg.wait(v)
                v.wait_ge(ctx.dve.sem, ctx.dve.cnt)
                ctx.dve.inc(v.tensor_tensor(out=ctx.X[:, t * D:(t + 1) * D],
                                            in0=ctx.X[:, t * D:(t + 1) * D],
                                            in1=delta[:, t * D:(t + 1) * D], op=OP.add))
                v.wait_ge(ctx.dve.sem, ctx.dve.cnt)
                ctx.dve.inc(v.scalar_tensor_tensor(out=ctx.X[:, t * D:(t + 1) * D],
                                                   in0=x1t[:, 0:1024], scalar=1.0,
                                                   in1=ctx.X[:, t * D:(t + 1) * D],
                                                   op0=OP.mult, op1=OP.add,
                                                   accum_out=sums[:, t:t + 1]))
        post_ln(ctx, blk)
        ar.pop()

        return


def emit_L2f_qkv(ctx, blk, W, dram, l):
    T, NT = ctx.T, ctx.NT
    NH = 8
    ar = ctx.arena
    ar.push()
    rsn = ctx.cols("l2f_rsn", 8)
    qh = ar.alloc(4 * T, BF16)
    kh = ar.alloc(4 * T, BF16)
    vtm = ar.alloc(NT * 512, BF16)
    hsh = ar.alloc(NCH * T, BF16)
    wpart = ar.alloc(8 * 512, BF16)
    tbl = [ar.alloc(4 * 64, F32) for _ in range(2)]
    qksb = ar.alloc(4 * 64, F32)
    t1 = ar.alloc(64, F32)
    t2 = ar.alloc(64, F32)
    ts_ = ar.alloc(64, F32)
    qe = ar.alloc(512, F32)
    qsq = ar.alloc(512, F32)
    bqc_ = ar.alloc(512, BF16)
    qrow = ar.alloc(256, BF16)
    hpc = ar.alloc(8, F32)

    hpg = ctx.hgroup()

    @blk.sync
    def _(s):
        ctx.dve.wait(s)
        hpg.issue(s.dma_start(out=hpc[:, 0:8], in_=W["L2f_xprevc"][:, :]))

    @blk.vector
    def _(v):
        hpg.wait(v)
        ctx.dve.inc(v.engine_nop())
    hd = ctx.dve.cnt

    tr_marks = {}
    for t in range(NT):
        @blk.vector
        def _(v, t=t):
            v.wait_ge(ctx.dve.sem, ctx.dve.cnt)
            if t >= 2:
                ctx.pe.wait(v, tr_marks[t - 2])
            ctx.dve.inc(v.tensor_copy(out=ctx.CB[:, (t % 2) * D:(t % 2 + 1) * D],
                                      in_=ctx.X[:, t * D:(t + 1) * D]))
        cc_ = ctx.dve.cnt

        @blk.tensor
        def _(t_, t=t, cc_=cc_):
            ctx.dve.wait(t_, cc_)
            ctx.dve.wait(t_)
            for c in range(NCH):
                mm = t_.transpose(ctx.pt[:, c * 128:(c + 1) * 128],
                                  ctx.CB[:, (t % 2) * D + c * 128:(t % 2) * D + (c + 1) * 128],
                                  ctx.ident[:])
                ctx.pe.inc(mm)
        tr_marks[t] = ctx.pe.cnt

        @blk.vector
        def _(v, t=t):
            ctx.pe.wait(v, tr_marks[t])
            if t == 0:
                ctx.dve.wait(v, hd)
            src3 = ctx.pt[:, 0:1024].rearrange("p (c j) -> p c j", c=NCH)
            dst3 = hsh.rearrange("p (c t) -> p c t", t=T)
            ctx.dve.inc(v.tensor_copy(out=dst3[:, 0:4, t * 128:(t + 1) * 128],
                                      in_=src3[:, 0:4, :]))
            v.wait_ge(ctx.dve.sem, ctx.dve.cnt)
            ncols = 128 if t < NT - 1 else 127
            ctx.dve.inc(v.tensor_copy(out=dst3[:, 4:8, t * 128 + 1: t * 128 + 1 + ncols],
                                      in_=src3[:, 4:8, 0:ncols]))
            if t == 0:
                v.wait_ge(ctx.dve.sem, ctx.dve.cnt)
                ctx.dve.inc(v.tensor_copy(out=dst3[:, 4:8, 0:1],
                                          in_=hpc[:, 4:8].unsqueeze(-1)))
    hsh_done = ctx.dve.cnt


    qg_ = ctx.dgroup()

    @blk.gpsimd
    def _(g, l=l):
        ctx.pe.wait(g)
        ctx.dve.wait(g)
        for i, nm in enumerate(["L2f_qs", "L2f_qsr", "L2f_ks", "L2f_ksr"]):
            qg_.issue(g.dma_start(out=qrow[0:1, i * 64:(i + 1) * 64], in_=W[nm][l:l + 1, :]))
    bb_ = [ctx.bank(), ctx.bank(), ctx.bank(), ctx.bank()]

    @blk.tensor
    def _(t_):
        qg_.wait(t_)
        ctx.dve.wait(t_)
        ctx.act.wait(t_)
        for i in range(4):
            ctx.pe.inc(t_.matmul(bb_[i][:, 0:64], ctx.ones_row[:, 0:128],
                                 qrow[0:1, i * 64:(i + 1) * 64], start=True, stop=True))
    qsb_mm = ctx.pe.cnt

    @blk.vector
    def _(v):
        ctx.pe.wait(v, qsb_mm)
        v.wait_ge(ctx.dve.sem, ctx.dve.cnt)
        for i in range(4):
            sc = 8.0 if i < 2 else 1.0
            ctx.dve.inc(v.tensor_scalar(out=qksb[:, i * 64:(i + 1) * 64], in0=bb_[i][:, 0:64],
                                        scalar1=sc, scalar2=None, op0=OP.mult))
    qsb_done = ctx.dve.cnt

    ctx.wpart_grps = []
    for which in range(2):
        wg_ = ctx.dgroup()

        @blk.gpsimd
        def _(g, l=l, which=which):
            ctx.pe.wait(g)
            if len(ctx.wpart_grps) >= 1:
                ctx.wpart_grps[-1].wait(g)
            wg_.issue(g.dma_start(
                out=wpart[0:128, 0:8 * 512].rearrange("p (f c) -> p f c", f=NCH),
                in_=W["L2f_wqkv"][l][:, which * 512:(which + 1) * 512]
                .rearrange("(f p) c -> p f c", p=128)))

        ctx.wpart_grps.append(wg_)
        for t in range(NT):
            tg_ = ctx.hgroup()

            @blk.scalar
            def _(s, l=l, t=t):
                ctx.dve.wait(s)
                tg_.issue(s.dma_start(out=tbl[t % 2][0:128, 0:256].rearrange("p (i c) -> p i c", i=4),
                                      in_=W["L2f_tbl"][t * 128:(t + 1) * 128, :, :]))
            b = ctx.bank()

            @blk.tensor
            def _(t_, t=t, b=b, which=which):
                wg_.wait(t_)
                if t == 0 and which == 0:
                    ctx.dve.wait(t_, hsh_done)
                ctx.dve.wait(t_)
                ctx.act.wait(t_)
                for kc in range(NCH):
                    mm = t_.matmul(b[:, 0:512],
                                   hsh[:, kc * T + t * 128: kc * T + (t + 1) * 128],
                                   wpart[:, kc * 512:(kc + 1) * 512],
                                   start=(kc == 0), stop=(kc == NCH - 1))
                ctx.pe.inc(mm)
            mmc = ctx.pe.cnt

            @blk.scalar
            def _(s, t=t, b=b, mmc=mmc):
                ctx.pe.wait(s, mmc)
                ctx.dve.wait(s)
                s.wait_ge(ctx.act.sem, ctx.act.cnt)
                ctx.act.inc(s.activation(out=qsq[:, 0:512], in_=b[:, 0:512], func=AF.Square))
            sqd = ctx.act.cnt

            @blk.vector
            def _(v, t=t, b=b, sqd=sqd, which=which):
                ctx.act.wait(v, sqd)
                tg_.wait(v)
                if t == 0:
                    ctx.dve.wait(v, qsb_done)
                v.wait_ge(ctx.dve.sem, ctx.dve.cnt)
                ctx.dve.inc(v.tensor_reduce(out=rsn, in_=qsq[:, 0:512].rearrange("p (h d) -> p h d", h=8),
                                            axis=AX.X, op=OP.add))
                v.wait_ge(ctx.dve.sem, ctx.dve.cnt)
                ctx.dve.inc(v.tensor_scalar(out=rsn, in0=rsn, scalar1=ctx.constcol(1e-24),
                                            scalar2=None, op0=OP.max))
            sqd2 = ctx.dve.cnt

            @blk.scalar
            def _(s, sqd2=sqd2):
                ctx.dve.wait(s, sqd2)
                s.wait_ge(ctx.act.sem, ctx.act.cnt)
                ctx.act.inc(s.activation(out=rsn, in_=rsn, func=AF.Ln))
                s.wait_ge(ctx.act.sem, ctx.act.cnt)
                ctx.act.inc(s.activation(out=rsn, in_=rsn, func=AF.Exp, scale=-0.5))
            rsd = ctx.act.cnt

            @blk.vector
            def _(v, t=t, b=b, rsd=rsd, which=which):
                ctx.act.wait(v, rsd)
                v.wait_ge(ctx.dve.sem, ctx.dve.cnt)
                q3 = b[:, 0:512].rearrange("p (h d) -> p h d", h=8)
                rb = rsn.unsqueeze(-1).broadcast_to([128, 8, 64])
                ctx.dve.inc(v.tensor_tensor(out=q3, in0=q3, in1=rb, op=OP.mult))
                v.wait_ge(ctx.dve.sem, ctx.dve.cnt)
                o = 0 if which == 0 else 2
                sci = 2 if which == 0 else 3
                ctx.dve.inc(v.tensor_tensor(out=ts_[:, 0:64], in0=tbl[t % 2][:, 0:64],
                                            in1=tbl[t % 2][:, sci * 64:(sci + 1) * 64], op=OP.mult))
                v.wait_ge(ctx.dve.sem, ctx.dve.cnt)
                ctx.dve.inc(v.tensor_tensor(out=t1[:, 0:64], in0=ts_[:, 0:64],
                                            in1=qksb[:, o * 64:(o + 1) * 64], op=OP.mult))
                v.wait_ge(ctx.dve.sem, ctx.dve.cnt)
                ctx.dve.inc(v.tensor_tensor(out=ts_[:, 0:64], in0=tbl[t % 2][:, 64:128],
                                            in1=tbl[t % 2][:, sci * 64:(sci + 1) * 64], op=OP.mult))
                v.wait_ge(ctx.dve.sem, ctx.dve.cnt)
                ctx.dve.inc(v.tensor_tensor(out=t2[:, 0:64], in0=ts_[:, 0:64],
                                            in1=qksb[:, (o + 1) * 64:(o + 2) * 64], op=OP.mult))
                v.wait_ge(ctx.dve.sem, ctx.dve.cnt)
                qn3 = q3
                qe3 = qe[:, 0:512].rearrange("p (h d) -> p h d", h=8)
                t1b = t1[:, 0:64].unsqueeze(1).broadcast_to([128, 8, 64])
                ctx.dve.inc(v.tensor_tensor(out=qe3, in0=qn3, in1=t1b, op=OP.mult))
                v.wait_ge(ctx.dve.sem, ctx.dve.cnt)
                t2a = t2[:, 0:32].unsqueeze(1).broadcast_to([128, 8, 32])
                ctx.dve.inc(v.tensor_tensor(out=qsq[:, 0:512].rearrange("p (h d) -> p h d", h=8)[:, :, 0:32],
                                            in0=qn3[:, :, 32:64], in1=t2a, op=OP.mult))
                v.wait_ge(ctx.dve.sem, ctx.dve.cnt)
                ctx.dve.inc(v.tensor_tensor(out=qe3[:, :, 0:32], in0=qe3[:, :, 0:32],
                                            in1=qsq[:, 0:512].rearrange("p (h d) -> p h d", h=8)[:, :, 0:32],
                                            op=OP.subtract))
                v.wait_ge(ctx.dve.sem, ctx.dve.cnt)
                t2b = t2[:, 32:64].unsqueeze(1).broadcast_to([128, 8, 32])
                ctx.dve.inc(v.tensor_tensor(out=qsq[:, 0:512].rearrange("p (h d) -> p h d", h=8)[:, :, 32:64],
                                            in0=qn3[:, :, 0:32], in1=t2b, op=OP.mult))
                v.wait_ge(ctx.dve.sem, ctx.dve.cnt)
                ctx.dve.inc(v.tensor_tensor(out=qe3[:, :, 32:64], in0=qe3[:, :, 32:64],
                                            in1=qsq[:, 0:512].rearrange("p (h d) -> p h d", h=8)[:, :, 32:64],
                                            op=OP.add))
                v.wait_ge(ctx.dve.sem, ctx.dve.cnt)
                ctx.dve.inc(v.tensor_copy(out=bqc_[:, 0:512], in_=qe[:, 0:512]))
            casted = ctx.dve.cnt

            @blk.tensor
            def _(t_, t=t, casted=casted):
                ctx.dve.wait(t_, casted)
                ctx.dve.wait(t_)
                for c in range(4):
                    mm = t_.transpose(ctx.pt[:, c * 128:(c + 1) * 128],
                                      bqc_[:, c * 128:(c + 1) * 128], ctx.ident[:])
                    ctx.pe.inc(mm)
            trd = ctx.pe.cnt

            @blk.vector
            def _(v, t=t, trd=trd, which=which):
                ctx.pe.wait(v, trd)
                dst = (qh if which == 0 else kh).rearrange("p (c t) -> p c t", t=T)
                src3 = ctx.pt[:, 0:512].rearrange("p (c j) -> p c j", c=4)
                ctx.dve.inc(v.tensor_copy(out=dst[:, :, t * 128:(t + 1) * 128], in_=src3))
    qk_done = ctx.dve.cnt

    wg_ = ctx.dgroup()

    @blk.gpsimd
    def _(g, l=l):
        ctx.pe.wait(g)
        if len(ctx.wpart_grps) >= 1:
            ctx.wpart_grps[-1].wait(g)
        wg_.issue(g.dma_start(
            out=wpart[0:128, 0:8 * 512].rearrange("p (f c) -> p f c", f=NCH),
            in_=W["L2f_wqkv"][l][:, 1024:1536].rearrange("(f p) c -> p f c", p=128)))
    ctx.wpart_grps.append(wg_)
    for t in range(NT):
        b = ctx.bank()

        @blk.tensor
        def _(t_, t=t, b=b):
            wg_.wait(t_)
            ctx.dve.wait(t_)
            ctx.act.wait(t_)
            for kc in range(NCH):
                mm = t_.matmul(b[:, 0:512],
                               hsh[:, kc * T + t * 128: kc * T + (t + 1) * 128],
                               wpart[:, kc * 512:(kc + 1) * 512],
                               start=(kc == 0), stop=(kc == NCH - 1))
            ctx.pe.inc(mm)
        mmc = ctx.pe.cnt

        @blk.scalar
        def _(s, t=t, b=b, mmc=mmc):
            ctx.pe.wait(s, mmc)
            s.wait_ge(ctx.act.sem, ctx.act.cnt)
            ctx.act.inc(s.activation(out=vtm[:, t * 512:(t + 1) * 512], in_=b[:, 0:512],
                                     func=AF.Copy, bias=0.0))
    v_done = ctx.act.cnt


    outg = ctx.hgroup()

    @blk.sync
    def _(s):
        ctx.dve.wait(s, qk_done)
        ctx.act.wait(s, v_done)
        outg.issue(s.dma_start(out=dram["l2f_qh"][:, :].rearrange("(c p) t -> p c t", p=128),
                               in_=qh.rearrange("p (c t) -> p c t", t=T)))
        outg.issue(s.dma_start(out=dram["l2f_khx"][:, :].rearrange("(c p) t -> p c t", p=128),
                               in_=kh.rearrange("p (c t) -> p c t", t=T)))
        outg.issue(s.dma_start(out=dram["l2f_vt"][:, :].rearrange("(w p) c -> p w c", p=128),
                               in_=vtm[:, 0:NT * 512].rearrange("p (w c) -> p w c", c=512)))

    @blk.sync
    def _(s):
        outg.wait(s)
    ar.pop()


def emit_L2f_attn(ctx, blk, W, dram, l):
    T, NT = ctx.T, ctx.NT
    NH = 8
    ar = ctx.arena
    ar.push()
    nmc = ctx.cols("l2f_nm", 12)
    mleft = ar.alloc(128, F32)
    mdiag = ar.alloc(128, F32)
    qh = ar.alloc(4 * T, BF16)
    kh = ar.alloc(4 * T, BF16)
    vtm = ar.alloc(NT * 512, BF16)
    ctm = ar.alloc(NT * 512, BF16)
    ar.push()
    khp = ar.alloc(4 * 512, BF16)
    vhp = ar.alloc(4 * 512, BF16)
    cg = ctx.hgroup()

    @blk.sync
    def _(s):
        ctx.dve.wait(s)
        cg.issue(s.dma_start(out=mleft[0:128, 0:128], in_=W["L2f_mleft"][:, :]))
        cg.issue(s.dma_start(out=mdiag[0:128, 0:128], in_=W["L2f_mdiag"][:, :]))
        cg.issue(s.dma_start(out=qh.rearrange("p (c t) -> p c t", t=T),
                             in_=W["l2f_qh"][:, :].rearrange("(c p) t -> p c t", p=128)))
        cg.issue(s.dma_start(out=kh.rearrange("p (c t) -> p c t", t=T),
                             in_=W["l2f_khx"][:, :].rearrange("(c p) t -> p c t", p=128)))
        cg.issue(s.dma_start(out=vtm[:, 0:NT * 512].rearrange("p (w c) -> p w c", c=512),
                             in_=W["l2f_vt"][:, :].rearrange("(w p) c -> p w c", p=128)))
        cg.issue(s.dma_start(out=khp[0:128, 0:4 * 512].rearrange("p (c t) -> p c t", c=4),
                             in_=W["L2f_khp"][:, :].rearrange("(c p) t -> p c t", p=128)))
        cg.issue(s.dma_start(out=vhp[0:128, 0:4 * 512].rearrange("p (w c) -> p w c", w=4),
                             in_=W["L2f_vhp"][:, :].rearrange("(w p) c -> p w c", p=128)))

    @blk.vector
    def _(v):
        cg.wait(v)
        ctx.dve.inc(v.engine_nop())
    kvh_done = ctx.dve.cnt
    v_done = ctx.act.cnt

    pA = ar.alloc(512, BF16)
    pB = ar.alloc(128, BF16)
    ptb2 = ar.alloc(640, BF16)
    bmq = ar.alloc(512, F32)
    rot6 = [0]
    def lbank():
        b = ctx.ps[rot6[0] % 6]
        rot6[0] += 1
        return b
    for t in range(NT):
        bmg = ctx.hgroup()
        if t < 4:
            @blk.scalar
            def _(s, l=l, t=t):
                ctx.dve.wait(s)
                bmg.issue(s.dma_start(out=bmq[0:128, 0:512],
                                      in_=W["L2f_bmask"][t]))
        cb = ctx.ps[6]

        for h in range(NH):
            sA = lbank()
            sB = lbank()

            @blk.tensor
            def _(t_, t=t, h=h, sA=sA, sB=sB):
                if t == 0 and h == 0:
                    ctx.dve.wait(t_, kvh_done)
                    ctx.act.wait(t_, v_done)
                ctx.dve.wait(t_)
                ctx.act.wait(t_)
                lq = qh[64 * (h % 2):64 * (h % 2) + 64,
                        (h // 2) * T + t * 128:(h // 2) * T + (t + 1) * 128]
                for ki in range(5):
                    kb = t - 4 + ki
                    if kb < 0:
                        ksrc = khp[64 * (h % 2):64 * (h % 2) + 64,
                                   (h // 2) * 512 + (kb + 4) * 128:(h // 2) * 512 + (kb + 5) * 128]
                    else:
                        ksrc = kh[64 * (h % 2):64 * (h % 2) + 64,
                                  (h // 2) * T + kb * 128:(h // 2) * T + (kb + 1) * 128]

                    dstp = sA[:, ki * 128:(ki + 1) * 128] if ki < 4 else sB[:, 0:128]
                    mm = t_.matmul(dstp, lq, ksrc, start=True, stop=True)
                ctx.pe.inc(mm)
            smm = ctx.pe.cnt

            @blk.vector
            def _(v, t=t, h=h, sA=sA, sB=sB, smm=smm):
                ctx.pe.wait(v, smm)
                ctx.act.wait(v)
                if t < 4:
                    bmg.wait(v)
                v.wait_ge(ctx.dve.sem, ctx.dve.cnt)
                ctx.dve.inc(v.tensor_tensor(out=sA[:, 0:128], in0=sA[:, 0:128],
                                            in1=mleft[:, 0:128], op=OP.add))
                v.wait_ge(ctx.dve.sem, ctx.dve.cnt)
                ctx.dve.inc(v.tensor_tensor(out=sB[:, 0:128], in0=sB[:, 0:128],
                                            in1=mdiag[:, 0:128], op=OP.add))
                v.wait_ge(ctx.dve.sem, ctx.dve.cnt)
                if t < 4:
                    ctx.dve.inc(v.tensor_tensor(out=sA[:, 0:512], in0=sA[:, 0:512],
                                                in1=bmq[:, 0:512], op=OP.add))
                    v.wait_ge(ctx.dve.sem, ctx.dve.cnt)
                ctx.dve.inc(v.reduce_max(out=nmc[:, 0:1], in_=sA[:, 0:512], axis=AX.X, negate=True))
                v.wait_ge(ctx.dve.sem, ctx.dve.cnt)
                ctx.dve.inc(v.reduce_max(out=nmc[:, 1:2], in_=sB[:, 0:128], axis=AX.X, negate=True))
                v.wait_ge(ctx.dve.sem, ctx.dve.cnt)
                ctx.dve.inc(v.tensor_scalar(out=nmc[:, 0:1], in0=nmc[:, 0:1],
                                            scalar1=nmc[:, 1:2], scalar2=None, op0=OP.min))
            nmd = ctx.dve.cnt

            @blk.scalar
            def _(s, sA=sA, sB=sB, nmd=nmd):
                ctx.dve.wait(s, nmd)
                s.wait_ge(ctx.act.sem, ctx.act.cnt)
                ctx.act.inc(s.activation(out=pA[:, 0:512], in_=sA[:, 0:512], func=AF.Exp,
                                         bias=nmc[:, 0:1], accum_out=nmc[:, 2:3]))
                s.wait_ge(ctx.act.sem, ctx.act.cnt)
                ctx.act.inc(s.activation(out=pB[:, 0:128], in_=sB[:, 0:128], func=AF.Exp,
                                         bias=nmc[:, 0:1], accum_out=nmc[:, 3:4]))
            expd = ctx.act.cnt

            @blk.vector
            def _(v, expd=expd):
                ctx.act.wait(v, expd)
                v.wait_ge(ctx.dve.sem, ctx.dve.cnt)
                ctx.dve.inc(v.tensor_scalar(out=nmc[:, 2:3], in0=nmc[:, 2:3],
                                            scalar1=nmc[:, 3:4], scalar2=None, op0=OP.add))
                v.wait_ge(ctx.dve.sem, ctx.dve.cnt)
                ctx.dve.inc(v.reciprocal(out=nmc[:, 4:5], in_=nmc[:, 2:3]))
                v.wait_ge(ctx.dve.sem, ctx.dve.cnt)
                ctx.dve.inc(v.tensor_scalar(out=pA[:, 0:512], in0=pA[:, 0:512],
                                            scalar1=nmc[:, 4:5], scalar2=None, op0=OP.mult))
                v.wait_ge(ctx.dve.sem, ctx.dve.cnt)
                ctx.dve.inc(v.tensor_scalar(out=pB[:, 0:128], in0=pB[:, 0:128],
                                            scalar1=nmc[:, 4:5], scalar2=None, op0=OP.mult))
            pdone = ctx.dve.cnt

            @blk.tensor
            def _(t_, pdone=pdone):
                ctx.dve.wait(t_, pdone)
                ctx.dve.wait(t_)
                for ki in range(4):
                    mm = t_.transpose(ctx.pt[:, ki * 128:(ki + 1) * 128],
                                      pA[:, ki * 128:(ki + 1) * 128], ctx.ident[:])
                    ctx.pe.inc(mm)
            trA = ctx.pe.cnt

            @blk.vector
            def _(v, trA=trA):
                ctx.pe.wait(v, trA)
                ctx.dve.inc(v.tensor_copy(out=ptb2[:, 0:512], in_=ctx.pt[:, 0:512]))
            evA = ctx.dve.cnt

            @blk.tensor
            def _(t_, evA=evA):
                ctx.dve.wait(t_, evA)
                ctx.pe.inc(t_.transpose(ctx.pt[:, 0:128], pB[:, 0:128], ctx.ident[:]))
            trB = ctx.pe.cnt

            @blk.vector
            def _(v, trB=trB):
                ctx.pe.wait(v, trB)
                ctx.dve.inc(v.tensor_copy(out=ptb2[:, 512:640], in_=ctx.pt[:, 0:128]))
            evB = ctx.dve.cnt

            @blk.tensor
            def _(t_, t=t, h=h, cb=cb, evB=evB):
                ctx.dve.wait(t_, evB)
                if h == 0:
                    ctx.act.wait(t_)
                for ki in range(5):
                    kb = t - 4 + ki
                    if kb < 0:
                        vsrc = vhp[:, (kb + 4) * 512 + 64 * h:(kb + 4) * 512 + 64 * h + 64]
                    else:
                        vsrc = vtm[:, kb * 512 + 64 * h: kb * 512 + 64 * h + 64]
                    mm = t_.matmul(cb[:, 64 * h:64 * h + 64],
                                   ptb2[:, ki * 128:(ki + 1) * 128], vsrc,
                                   start=(h == 0 and ki == 0), stop=(h == NH - 1 and ki == 4),
                                   skip_group_check=True)
                ctx.pe.inc(mm)
        pvd = ctx.pe.cnt

        @blk.scalar
        def _(s, t=t, cb=cb, pvd=pvd):
            ctx.pe.wait(s, pvd)
            s.wait_ge(ctx.act.sem, ctx.act.cnt)
            ctx.act.inc(s.activation(out=ctm[:, t * 512:(t + 1) * 512], in_=cb[:, 0:512],
                                     func=AF.Copy, bias=0.0))
    attn_done = ctx.act.cnt

    ar.pop()
    cfm = ar.alloc(4 * T, BF16)
    for t in range(NT):
        @blk.tensor
        def _(t_, t=t):
            ctx.act.wait(t_, attn_done)
            ctx.dve.wait(t_)
            for c in range(4):
                mm = t_.transpose(ctx.pt[:, c * 128:(c + 1) * 128],
                                  ctm[:, t * 512 + c * 128: t * 512 + (c + 1) * 128],
                                  ctx.ident[:])
                ctx.pe.inc(mm)
        trd = ctx.pe.cnt

        @blk.vector
        def _(v, t=t, trd=trd):
            ctx.pe.wait(v, trd)
            dst = cfm.rearrange("p (c t) -> p c t", t=T)
            ctx.dve.inc(v.tensor_copy(out=dst[:, :, t * 128:(t + 1) * 128],
                                      in_=ctx.pt[:, 0:512].rearrange("p (c j) -> p c j", c=4)))
    cfm_done = ctx.dve.cnt

    wob = ar.alloc(4 * 1024, BF16)
    wog = ctx.dgroup()
    load_weight(ctx, blk, W["L2f_wo"][l], wob, wait=lambda g: ctx.pe.wait(g), group=wog)
    for t in range(NT):
        ob = [ctx.bank(), ctx.bank()]

        @blk.tensor
        def _(t_, t=t, ob=ob):
            wog.wait(t_)
            if t == 0:
                ctx.dve.wait(t_, cfm_done)
            ctx.dve.wait(t_)
            for nb in range(2):
                for kc in range(4):
                    mm = t_.matmul(ob[nb][:, 0:512],
                                   cfm[:, kc * T + t * 128: kc * T + (t + 1) * 128],
                                   wob[:, kc * 1024 + nb * 512: kc * 1024 + (nb + 1) * 512],
                                   start=(kc == 0), stop=(kc == 3))
                ctx.pe.inc(mm)
        mmc = ctx.pe.cnt

        @blk.vector
        def _(v, t=t, ob=ob, mmc=mmc):
            ctx.pe.wait(v, mmc)
            v.wait_ge(ctx.dve.sem, ctx.dve.cnt)
            for nb in range(2):
                ctx.dve.inc(v.tensor_tensor(out=ctx.X[:, t * D + nb * 512: t * D + (nb + 1) * 512],
                                            in0=ctx.X[:, t * D + nb * 512: t * D + (nb + 1) * 512],
                                            in1=ob[nb][:, 0:512], op=OP.add))
                v.wait_ge(ctx.dve.sem, ctx.dve.cnt)
    attnres_done = ctx.dve.cnt

    ar.pop()


def emit_L2f_ffn(ctx, blk, W, dram, l, final):
    T, NT = ctx.T, ctx.NT
    ar = ctx.arena
    ar.push()
    ar.push()
    hsh2 = ar.alloc(NCH * T, BF16)
    hpc = ar.alloc(8, F32)

    hpg = ctx.hgroup()

    @blk.sync
    def _(s):
        ctx.dve.wait(s)
        hpg.issue(s.dma_start(out=hpc[:, 0:8], in_=W["L2f_x2prevc"][:, :]))

    @blk.vector
    def _(v):
        hpg.wait(v)
        ctx.dve.inc(v.engine_nop())
    hd = ctx.dve.cnt

    tr_marks = {}
    for t in range(NT):
        @blk.vector
        def _(v, t=t):
            v.wait_ge(ctx.dve.sem, ctx.dve.cnt)
            if t >= 2:
                ctx.pe.wait(v, tr_marks[t - 2])
            ctx.dve.inc(v.tensor_copy(out=ctx.CB[:, (t % 2) * D:(t % 2 + 1) * D],
                                      in_=ctx.X[:, t * D:(t + 1) * D]))
        cc_ = ctx.dve.cnt

        @blk.tensor
        def _(t_, t=t, cc_=cc_):
            ctx.dve.wait(t_, cc_)
            ctx.dve.wait(t_)
            for c in range(NCH):
                mm = t_.transpose(ctx.pt[:, c * 128:(c + 1) * 128],
                                  ctx.CB[:, (t % 2) * D + c * 128:(t % 2) * D + (c + 1) * 128],
                                  ctx.ident[:])
                ctx.pe.inc(mm)
        tr_marks[t] = ctx.pe.cnt

        @blk.vector
        def _(v, t=t):
            ctx.pe.wait(v, tr_marks[t])
            if t == 0:
                ctx.dve.wait(v, hd)
            src3 = ctx.pt[:, 0:1024].rearrange("p (c j) -> p c j", c=NCH)
            dst3 = hsh2.rearrange("p (c t) -> p c t", t=T)
            ctx.dve.inc(v.tensor_copy(out=dst3[:, 0:4, t * 128:(t + 1) * 128],
                                      in_=src3[:, 0:4, :]))
            v.wait_ge(ctx.dve.sem, ctx.dve.cnt)
            ncols = 128 if t < NT - 1 else 127
            ctx.dve.inc(v.tensor_copy(out=dst3[:, 4:8, t * 128 + 1: t * 128 + 1 + ncols],
                                      in_=src3[:, 4:8, 0:ncols]))
            if t == 0:
                v.wait_ge(ctx.dve.sem, ctx.dve.cnt)
                ctx.dve.inc(v.tensor_copy(out=dst3[:, 4:8, 0:1],
                                          in_=hpc[:, 4:8].unsqueeze(-1)))
    hsh2_done = ctx.dve.cnt

    gcol = ctx.cols("l2f_lng", 8)
    bcol = ctx.cols("l2f_lnb", 8)
    lng = ctx.hgroup()

    @blk.scalar
    def _(s, l=l):
        lng.issue(s.dma_start(out=gcol, in_=W["L2f_lng_c"][l]))
        lng.issue(s.dma_start(out=bcol, in_=W["L2f_lnb_c"][l]))

    rowreg = ar.alloc(T, F32)
    srow = rowreg
    mu2r = ar.alloc(T, F32)
    s2row = ar.alloc(T, F32)
    mbc = ar.alloc(512, F32)
    rbc = ar.alloc(512, F32)
    sqb = ar.alloc(T, BF16)
    fpl = ar.alloc(512, F32)
    onecol = ar.alloc(1, BF16)
    TGF = min(512, T)

    @blk.vector
    def _(v):
        v.wait_ge(ctx.dve.sem, ctx.dve.cnt)
        ctx.dve.inc(v.memset(onecol[:, 0:1], 1.0))
    oc_done = ctx.dve.cnt

    for tg in range(T // TGF):
        b1 = ctx.ps[tg % 3]
        b2 = ctx.ps[3 + tg % 3]
        sq_acts = {}
        for c in range(NCH):
            @blk.scalar
            def _(s, c=c, tg=tg):
                if c == 0 and tg == 0:
                    ctx.dve.wait(s)
                s.wait_ge(ctx.act.sem, ctx.act.cnt)
                ctx.pe.wait(s)
                ctx.act.inc(s.activation(out=sqb[:, 0:TGF],
                                         in_=hsh2[:, c * T + tg * TGF: c * T + (tg + 1) * TGF],
                                         func=AF.Square))
            sq_acts[c] = ctx.act.cnt

            @blk.tensor
            def _(t_, c=c, tg=tg, b1=b1, b2=b2):
                ctx.act.wait(t_, sq_acts[c])
                if c == 0:
                    ctx.dve.wait(t_)
                mm1 = t_.matmul(b1[0:1, 0:TGF], onecol[:, 0:1],
                                hsh2[:, c * T + tg * TGF: c * T + (tg + 1) * TGF],
                                start=(c == 0), stop=(c == NCH - 1), skip_group_check=True)
                mm2 = t_.matmul(b2[0:1, 0:TGF], onecol[:, 0:1], sqb[:, 0:TGF],
                                start=(c == 0), stop=(c == NCH - 1), skip_group_check=True)
                ctx.pe.inc(mm2)
        smm = ctx.pe.cnt

        @blk.vector
        def _(v, tg=tg, b1=b1, b2=b2, smm=smm):
            ctx.pe.wait(v, smm)
            v.wait_ge(ctx.dve.sem, ctx.dve.cnt)
            ctx.dve.inc(v.tensor_copy(out=srow[0:1, tg * TGF:(tg + 1) * TGF], in_=b1[0:1, 0:TGF]))
            v.wait_ge(ctx.dve.sem, ctx.dve.cnt)
            ctx.dve.inc(v.tensor_copy(out=s2row[0:1, tg * TGF:(tg + 1) * TGF], in_=b2[0:1, 0:TGF]))

    @blk.vector
    def _(v):
        v.wait_ge(ctx.dve.sem, ctx.dve.cnt)
        ctx.dve.inc(v.tensor_scalar(out=srow[0:1, 0:T], in0=srow[0:1, 0:T],
                                    scalar1=1.0 / D, scalar2=None, op0=OP.mult))
        v.wait_ge(ctx.dve.sem, ctx.dve.cnt)
        ctx.dve.inc(v.tensor_tensor(out=mu2r[0:1, 0:T], in0=srow[0:1, 0:T],
                                    in1=srow[0:1, 0:T], op=OP.mult))
        v.wait_ge(ctx.dve.sem, ctx.dve.cnt)
        ctx.dve.inc(v.scalar_tensor_tensor(out=s2row[0:1, 0:T], in0=s2row[0:1, 0:T],
                                           scalar=1.0 / D, in1=mu2r[0:1, 0:T],
                                           op0=OP.mult, op1=OP.subtract))
    st_dve = ctx.dve.cnt

    @blk.scalar
    def _(s, st_dve=st_dve):
        ctx.dve.wait(s, st_dve)
        s.wait_ge(ctx.act.sem, ctx.act.cnt)
        ctx.act.inc(s.activation(out=s2row[0:1, 0:T], in_=s2row[0:1, 0:T], func=AF.Ln,
                                 bias=ctx.constcol(LN_EPS)[0:1, 0:1]))
        s.wait_ge(ctx.act.sem, ctx.act.cnt)
        ctx.act.inc(s.activation(out=s2row[0:1, 0:T], in_=s2row[0:1, 0:T], func=AF.Exp,
                                 scale=-0.5))
    rs_done = ctx.act.cnt

    @blk.vector
    def _(v):
        lng.wait(v)

    for tg in range(T // TGF):
        bbm = ctx.bank()
        bbr = ctx.bank()

        @blk.tensor
        def _(t_, tg=tg, bbm=bbm, bbr=bbr):
            ctx.act.wait(t_, rs_done)
            ctx.dve.wait(t_)
            ctx.pe.inc(t_.matmul(bbm[:, 0:TGF], ctx.ones_rowf[:, 0:128],
                                 srow[0:1, tg * TGF:(tg + 1) * TGF], start=True, stop=True))
            ctx.pe.inc(t_.matmul(bbr[:, 0:TGF], ctx.ones_rowf[:, 0:128],
                                 s2row[0:1, tg * TGF:(tg + 1) * TGF], start=True, stop=True))
        bmm = ctx.pe.cnt

        @blk.vector
        def _(v, tg=tg, bbm=bbm, bbr=bbr, bmm=bmm):
            ctx.pe.wait(v, bmm)
            v.wait_ge(ctx.dve.sem, ctx.dve.cnt)
            ctx.dve.inc(v.tensor_copy(out=mbc[:, 0:TGF], in_=bbm[:, 0:TGF]))
            v.wait_ge(ctx.dve.sem, ctx.dve.cnt)
            ctx.dve.inc(v.tensor_copy(out=rbc[:, 0:TGF], in_=bbr[:, 0:TGF]))
            for c in range(NCH):
                v.wait_ge(ctx.dve.sem, ctx.dve.cnt)
                ctx.dve.inc(v.tensor_tensor(out=fpl[:, 0:TGF],
                                            in0=hsh2[:, c * T + tg * TGF: c * T + (tg + 1) * TGF],
                                            in1=mbc[:, 0:TGF], op=OP.subtract))
                v.wait_ge(ctx.dve.sem, ctx.dve.cnt)
                ctx.dve.inc(v.tensor_tensor(out=fpl[:, 0:TGF], in0=fpl[:, 0:TGF],
                                            in1=rbc[:, 0:TGF], op=OP.mult))
                v.wait_ge(ctx.dve.sem, ctx.dve.cnt)
                ctx.dve.inc(v.tensor_scalar(out=hsh2[:, c * T + tg * TGF: c * T + (tg + 1) * TGF],
                                            in0=fpl[:, 0:TGF],
                                            scalar1=gcol[:, c:c + 1], scalar2=bcol[:, c:c + 1],
                                            op0=OP.mult, op1=OP.add))
    hn_done = ctx.dve.cnt

    PAIRS = [(128, 128)] * 21 + [(42, 42)]
    offs1 = np.cumsum([0] + [a + g for a, g in PAIRS]).tolist()
    offs2 = np.cumsum([0] + [a for a, g in PAIRS]).tolist()
    h2buf = ar.alloc(22 * TGF, BF16)
    w1p = [ar.alloc(8 * 256, BF16) for _ in range(2)]
    w2p = [ar.alloc(512, BF16) for _ in range(2)]
    gsc = ar.alloc(TGF, F32)
    asc = ar.alloc(TGF, F32)
    b1c2 = ctx.cols("l2f_b1", 44)
    b2row = ar.alloc(1024, BF16)
    bg2 = ctx.hgroup()
    bg3 = ctx.dgroup()

    @blk.scalar
    def _(s, l=l):
        bg2.issue(s.dma_start(out=b1c2, in_=W["L2f_b1c"][l]))

    @blk.gpsimd
    def _(g, l=l):
        ctx.pe.wait(g)
        bg3.issue(g.dma_start(out=b2row[0:1, 0:1024], in_=W["L2f_b2r"][l:l + 1, :]))

    ctx.w1p_grps = []
    ctx.w2p_grps = []
    lns3 = ctx.cols("l2f_lns", 8)
    for tg in range(T // TGF):
        for pc, (sa, sg) in enumerate(PAIRS):
            wg1 = ctx.dgroup()

            @blk.gpsimd
            def _(g, l=l, pc=pc, sa=sa, sg=sg):
                ctx.pe.wait(g)
                if len(ctx.w1p_grps) >= 2:
                    ctx.w1p_grps[-2].wait(g)
                wg1.issue(g.dma_start(
                    out=w1p[pc % 2][0:128, 0:8 * (sa + sg)].rearrange("p (f c) -> p f c", f=NCH),
                    in_=W["L2f_w1"][l][:, offs1[pc]:offs1[pc + 1]]
                    .rearrange("(f p) c -> p f c", p=128)))
            ctx.w1p_grps.append(wg1)

            ba = ctx.ps[(2 * pc) % 5]
            bgg = ctx.ps[(2 * pc + 1) % 5]

            @blk.tensor
            def _(t_, tg=tg, pc=pc, sa=sa, sg=sg, ba=ba, bgg=bgg):
                wg1.wait(t_)
                if tg == 0 and pc == 0:
                    ctx.dve.wait(t_, hn_done)
                ctx.dve.wait(t_)
                ctx.act.wait(t_)
                for kc in range(NCH):
                    mma = t_.matmul(ba[0:sa, 0:TGF],
                                    w1p[pc % 2][:, kc * (sa + sg): kc * (sa + sg) + sa],
                                    hsh2[:, kc * T + tg * TGF: kc * T + (tg + 1) * TGF],
                                    start=(kc == 0), stop=(kc == NCH - 1))
                    mmg = t_.matmul(bgg[0:sg, 0:TGF],
                                    w1p[pc % 2][:, kc * (sa + sg) + sa: kc * (sa + sg) + sa + sg],
                                    hsh2[:, kc * T + tg * TGF: kc * T + (tg + 1) * TGF],
                                    start=(kc == 0), stop=(kc == NCH - 1))
                ctx.pe.inc(mmg)
            w1mm = ctx.pe.cnt

            @blk.scalar
            def _(s, pc=pc, sg=sg, bgg=bgg, w1mm=w1mm):
                ctx.pe.wait(s, w1mm)
                if pc == 0:
                    bg2.wait(s)
                ctx.dve.wait(s)
                s.wait_ge(ctx.act.sem, ctx.act.cnt)
                ctx.act.inc(s.activation(out=gsc[0:sg, 0:TGF], in_=bgg[0:sg, 0:TGF],
                                         func=AF.Gelu, bias=b1c2[0:sg, 2 * pc + 1:2 * pc + 2]))
            geld = ctx.act.cnt

            @blk.vector
            def _(v, pc=pc, sa=sa, ba=ba, geld=geld):
                ctx.act.wait(v, geld)
                v.wait_ge(ctx.dve.sem, ctx.dve.cnt)
                ctx.dve.inc(v.tensor_scalar(out=asc[0:sa, 0:TGF], in0=ba[0:sa, 0:TGF],
                                            scalar1=b1c2[0:sa, 2 * pc:2 * pc + 1],
                                            scalar2=None, op0=OP.add))
                v.wait_ge(ctx.dve.sem, ctx.dve.cnt)
                ctx.dve.inc(v.tensor_tensor(out=h2buf[0:sa, pc * TGF:(pc + 1) * TGF],
                                            in0=asc[0:sa, 0:TGF], in1=gsc[0:sa, 0:TGF],
                                            op=OP.mult))
        h2done = ctx.dve.cnt

        nt4f = TGF // 128
        for nb in range(2):
            banks = [ctx.ps[5], ctx.ps[6]] + ([ctx.ps[0], ctx.ps[1]] if nt4f > 2 else [])
            for pc, (sa, sg) in enumerate(PAIRS):
                wg2 = ctx.dgroup()

                @blk.gpsimd
                def _(g, l=l, nb=nb, pc=pc, sa=sa):
                    ctx.pe.wait(g)
                    if len(ctx.w2p_grps) >= 2:
                        ctx.w2p_grps[-2].wait(g)
                    wg2.issue(g.dma_start(
                        out=w2p[pc % 2][0:sa, 0:512],
                        in_=W["L2f_w2"][l][offs2[pc]:offs2[pc + 1], nb * 512:(nb + 1) * 512]))
                ctx.w2p_grps.append(wg2)

                @blk.tensor
                def _(t_, tg=tg, nb=nb, pc=pc, sa=sa, banks=banks, h2done=h2done):
                    wg2.wait(t_)
                    bg3.wait(t_)
                    ctx.dve.wait(t_, h2done)
                    if pc == 0:
                        ctx.dve.wait(t_)
                        ctx.act.wait(t_)
                        for ti in range(nt4f):
                            bias_mm(ctx, t_, banks[ti][:, 0:512],
                                    b2row[0:1, nb * 512:(nb + 1) * 512], start=True)
                    for ti in range(nt4f):
                        mm = t_.matmul(banks[ti][:, 0:512],
                                       h2buf[0:sa, pc * TGF + ti * 128: pc * TGF + (ti + 1) * 128],
                                       w2p[pc % 2][0:sa, 0:512],
                                       start=False, stop=(pc == 21))
                    ctx.pe.inc(mm)
            w2mm = ctx.pe.cnt

            @blk.vector
            def _(v, tg=tg, nb=nb, banks=banks, w2mm=w2mm):
                ctx.pe.wait(v, w2mm)
                for ti in range(nt4f):
                    t = tg * nt4f + ti
                    last = (l == 1)
                    v.wait_ge(ctx.dve.sem, ctx.dve.cnt)
                    ctx.dve.inc(v.tensor_tensor(
                        out=ctx.X[:, t * D + nb * 512: t * D + (nb + 1) * 512],
                        in0=ctx.X[:, t * D + nb * 512: t * D + (nb + 1) * 512],
                        in1=banks[ti][:, 0:512], op=OP.add))
    ar.pop()

    if final:
        sums = ctx.cols("lnsum", NT)
        xint = [ar.alloc(1024, F32) for _ in range(2)]
        ctx.xin_grps = []
        for t in range(NT):
            xg = ctx.hgroup()

            @blk.sync
            def _(s, t=t):
                ctx.dve.wait(s)
                if len(ctx.xin_grps) >= 2:
                    ctx.xin_grps[-2].wait(s)
                xg.issue(s.dma_start(out=xint[t % 2][:, 0:1024],
                                     in_=W["l2f_xin"][t * 128:(t + 1) * 128, :]))
            ctx.xin_grps.append(xg)

            @blk.vector
            def _(v, t=t, xg=xg):
                xg.wait(v)
                v.wait_ge(ctx.dve.sem, ctx.dve.cnt)
                ctx.dve.inc(v.scalar_tensor_tensor(
                    out=ctx.X[:, t * D:(t + 1) * D], in0=xint[t % 2][:, 0:1024],
                    scalar=1.0, in1=ctx.X[:, t * D:(t + 1) * D],
                    op0=OP.mult, op1=OP.add, accum_out=sums[:, t:t + 1]))
        post_ln(ctx, blk)
    ar.pop()





from concourse.bass_utils import run_bass_kernel_spmd
import ml_dtypes

GS = 4
T_FULL = globals().get("T_OVERRIDE", 2048)
BASS_DEBUG = globals().get("BASS_DEBUG", False)
CMASK = 32768.0
BIGM = 1e30
BF16NP = ml_dtypes.bfloat16
_PAIRS = [(128, 128)] * 21 + [(42, 42)]


def _fmcols(v):
    return np.ascontiguousarray(v.reshape(8, 128).T).astype(np.float32)


def _interleave_w1_l2f(w):
    a, g = w[:, :2730], w[:, 2730:]
    out, o = [], 0
    for sa, sg in _PAIRS:
        out.append(a[:, o:o + sa]); out.append(g[:, o:o + sg]); o += sa
    return np.ascontiguousarray(np.concatenate(out, 1))


def _b1cols_l2f(bv):
    a, g = bv[:2730], bv[2730:]
    out = np.zeros((128, 44), np.float32)
    o = 0
    for i, (sa, sg) in enumerate(_PAIRS):
        out[0:sa, 2 * i] = a[o:o + sa]; out[0:sg, 2 * i + 1] = g[o:o + sg]; o += sa
    return out


def _interleave_gu(wg, wu):
    out = []
    for j in range(12):
        out.append(wg[:, j * 128:(j + 1) * 128]); out.append(wu[:, j * 128:(j + 1) * 128])
    return np.concatenate(out, 1)


def prep_shared(I):
    S = {}
    f32 = lambda a: np.ascontiguousarray(a, dtype=np.float32)
    S["L0_W1T"] = f32(I["L0_attn_W1"].T)
    S["L0_b1"] = f32(I["L0_attn_b1"][None])
    S["L0_W23T"] = f32(np.concatenate([I["L0_attn_W2"].T, I["L0_attn_W3"].T], 1))
    S["L0_b23"] = f32(np.concatenate([I["L0_attn_b2"], I["L0_attn_b3"]])[None])
    S["L0_selT"] = f32(I["L0_ffn_expert_sel"].T)
    S["L0_keys"] = f32(I["L0_ffn_keys"])
    S["L0_vals"] = f32(I["L0_ffn_values"])
    S["L0_rs"] = f32(I["L0_ffn_route_scale"].reshape(1, 1))
    S["L1a_ln1g"] = f32(I["L1_attn_ln1_g"][None]); S["L1a_ln1b"] = f32(I["L1_attn_ln1_b"][None])
    S["L1a_ln2g"] = f32(I["L1_attn_ln2_g"][None]); S["L1a_ln2b"] = f32(I["L1_attn_ln2_b"][None])
    S["L1a_mixk_c"] = _fmcols(I["L1_attn_mix_k"]); S["L1a_mixv_c"] = _fmcols(I["L1_attn_mix_v"])
    S["L1a_mixr_c"] = _fmcols(I["L1_attn_mix_r"])
    S["L1a_cmixk_c"] = _fmcols(I["L1_attn_cmix_k"]); S["L1a_cmixr_c"] = _fmcols(I["L1_attn_cmix_r"])
    S["L1a_u_c"] = _fmcols(I["L1_attn_time_first"])
    S["L1a_WkT"] = f32(I["L1_attn_Wk_t"].T); S["L1a_WvT"] = f32(I["L1_attn_Wv_t"].T)
    S["L1a_WrT"] = f32(I["L1_attn_Wr_t"].T)
    S["L1a_WkcT"] = f32(I["L1_attn_Wk_c"].T); S["L1a_WrcT"] = f32(I["L1_attn_Wr_c"].T)
    S["L1a_WvcT"] = f32(I["L1_attn_Wv_c"].T)
    S["L1f_Wg"] = f32(I["L1_ffn_Wg"]); S["L1f_bg"] = f32(I["L1_ffn_bg"][None])
    S["L1f_W1"] = f32(I["L1_ffn_W1"])
    S["L1f_b1c"] = np.stack([I["L1_ffn_b1"][e].reshape(32, 128).T for e in range(8)]).astype(np.float32)
    S["L1f_W2"] = f32(I["L1_ffn_W2"]); S["L1f_b2"] = f32(I["L1_ffn_b2"])
    S["L2a_WqT"] = f32(I["L2_attn_Wq"].T); S["L2a_bqc"] = _fmcols(I["L2_attn_bq"])
    S["L2a_WvT"] = f32(I["L2_attn_Wv"].T); S["L2a_bv"] = f32(I["L2_attn_bv"][None])
    S["L2a_A"] = f32(I["L2_attn_anomaly_matrix"])
    S["L2f_wqkv"] = f32(I["L2_ffn_wqkv"])
    qs, ks = I["L2_ffn_q_scale"], I["L2_ffn_k_scale"]
    S["L2f_qs"] = f32(qs); S["L2f_qsr"] = f32(np.concatenate([qs[:, 32:], qs[:, :32]], 1))
    S["L2f_ks"] = f32(ks); S["L2f_ksr"] = f32(np.concatenate([ks[:, 32:], ks[:, :32]], 1))
    P_, J_ = np.meshgrid(np.arange(128), np.arange(128), indexing='ij')
    S["L2f_mleft"] = np.where(J_ < P_, -BIGM, 0.0).astype(np.float32)
    S["L2f_mdiag"] = np.where(J_ > P_, -BIGM, 0.0).astype(np.float32)
    S["L2f_wo"] = f32(I["L2_ffn_wo"])
    S["L2f_lng_c"] = np.stack([_fmcols(I["L2_ffn_ln_g"][l]) for l in range(2)])
    S["L2f_lnb_c"] = np.stack([_fmcols(I["L2_ffn_ln_b"][l]) for l in range(2)])
    S["L2f_w1"] = np.stack([_interleave_w1_l2f(I["L2_ffn_w1"][l]) for l in range(2)])
    S["L2f_b1c"] = np.stack([_b1cols_l2f(I["L2_ffn_b1"][l]) for l in range(2)])
    S["L2f_w2"] = f32(I["L2_ffn_w2"]); S["L2f_b2r"] = f32(I["L2_ffn_b2"])
    S["L3a_Wqkv"] = f32(I["L3_attn_W_qkv"])
    bq = I["L3_attn_b_qkv"]
    S["L3a_bqc"] = _fmcols(bq[0:1024]); S["L3a_bkc"] = _fmcols(bq[1024:2048])
    S["L3a_bv"] = f32(bq[2048:3072][None])
    S["L3a_WpT"] = f32(I["L3_attn_W_proj"]); S["L3a_bp"] = f32(I["L3_attn_b_proj"][None])
    S["L3a_tri"] = np.where(P_ > J_, -CMASK, 0.0).astype(np.float32)
    S["L3f_gT"] = f32(I["L3_ffn_gate_w"].T)
    wup, wdn = I["L3_ffn_w_up"], I["L3_ffn_w_down"]
    Wgu = np.zeros((9, 1024, 3072), np.float32)
    Wd = np.zeros((9, 1536, 1024), np.float32)
    for e in range(8):
        Wgu[e] = _interleave_gu(wup[e, :1536, :].T, wup[e, 1536:, :].T)
        Wd[e] = wdn[e].T
    Wgu[8] = _interleave_gu(I["L3_ffn_sg_gate"].T, I["L3_ffn_sg_up"].T)
    Wd[8] = I["L3_ffn_sg_down"].T
    S["L3f_Wgu"] = Wgu; S["L3f_Wd"] = Wd
    return S


def prep_core(cid):
    C = {}
    q = cid % GS
    g0 = q * T_FULL
    pm = np.zeros((128, 4), np.float32)
    pm[:, :q] = 1
    C["L1a_pm"] = pm
    C["L1a_hvalid"] = np.full((128, 1), 1.0 if q > 0 else 0.0, np.float32)
    gi = (g0 + np.arange(T_FULL)) // 128
    C["L3a_kstair"] = (-CMASK * (gi[None, :] == np.arange(64)[:, None])).astype(np.float32)
    QGS_ = min(512, T_FULL)
    cols = []
    for j in range(T_FULL // QGS_):
        g_ = GS * j + q
        gij = (g_ * QGS_ + np.arange(QGS_)) // 128
        cols.append(gij[None, :] <= np.arange(64)[:, None])
    C["L3a_qstair"] = np.concatenate(cols, 1).astype(np.float32)
    SQ = GS * T_FULL
    half = np.arange(0, 64, 2, dtype=np.float32)
    inv_freq = 1.0 / 10000.0 ** (half / 64)
    tpos = (g0 + np.arange(T_FULL)).astype(np.float32)
    freqs = tpos[:, None] * inv_freq[None, :]
    freqs2 = np.concatenate([freqs, freqs], -1)
    base = (half + 0.4 * 64) / (1.4 * 64)
    power = (tpos - SQ // 2) / 256.0
    scl = base[None, :] ** power[:, None]
    scl2 = np.concatenate([scl, scl], -1)
    C["L2f_tbl"] = np.stack([np.cos(freqs2), np.sin(freqs2), scl2, 1.0 / scl2],
                            axis=1).astype(np.float32)
    bm = np.zeros((4, 128, 512), np.float32)
    if q == 0:
        for t in range(4):
            bm[t, :, 0:(4 - t) * 128] = -BIGM
    C["L2f_bmask"] = bm
    return C


T_ = T_FULL
STAGES = {
    "l0a": dict(
        emit=lambda ctx, blk, d: emit_L0a(ctx, blk, d),
        ins=[("L0_W1T", [1024, 1024], F32), ("L0_b1", [1, 1024], F32),
             ("L0_W23T", [1024, 32], F32), ("L0_b23", [1, 32], F32)],
        internals=[], outs=[], load_x=True, store_x=True),
    "l0f": dict(
        emit=lambda ctx, blk, d: emit_L0f(ctx, blk, d),
        ins=[("L0_selT", [1024, 16], F32), ("L0_keys", [16, 1024, 256], F32),
             ("L0_vals", [16, 256, 1024], F32), ("L0_rs", [1, 1], F32)],
        internals=[], outs=[], load_x=True, store_x=True),
    "l1aA": dict(
        emit=lambda ctx, blk, d: emit_L1a_stage(ctx, blk, d, d, 'A', GS),
        ins=[("L1a_ln1g", [1, 1024], F32), ("L1a_ln1b", [1, 1024], F32),
             ("L1a_mixk_c", [128, 8], F32), ("L1a_mixv_c", [128, 8], F32),
             ("L1a_mixr_c", [128, 8], F32), ("L1a_cmixk_c", [128, 8], F32),
             ("L1a_cmixr_c", [128, 8], F32), ("L1a_u_c", [128, 8], F32),
             ("L1a_WkT", [1024, 1024], F32), ("L1a_WvT", [1024, 1024], F32),
             ("L1a_WrT", [1024, 1024], F32),
             ("L1a_xprev", [1, 1024], F32), ("L1a_hvalid", [128, 1], F32)],
        internals=[("l1a_hrs", [1, 1024], BF16)],
        outs=[("l1a_k", [1024, T_], BF16), ("l1a_v", [1024, T_], BF16),
              ("l1a_r", [T_, 1024], BF16), ("l1a_s", [128, 8], F32)],
        load_x=True, store_x=False),
    "l1aD1": dict(
        emit=lambda ctx, blk, d: emit_L1a_stage(ctx, blk, d, d, 'D1', GS),
        ins=[("L1a_pm", [128, 4], F32), ("l1a_sa", [GS * 128, 8], F32),
             ("l1a_k", [1024, T_], BF16), ("l1a_v", [1024, T_], BF16)],
        internals=[],
        outs=[("l1a_dp", [1024, T_], F32), ("l1a_wl", [1024, T_], F32),
              ("l1a_rr", [1024, T_], F32), ("l1a_wr", [128, 16], F32)],
        load_x=False, store_x=False),
    "l1aD2": dict(
        emit=lambda ctx, blk, d: emit_L1a_stage(ctx, blk, d, d, 'D2', GS),
        ins=[("L1a_pm", [128, 4], F32), ("L1a_u_c", [128, 8], F32),
             ("l1a_wra", [GS * 128, 16], F32),
             ("l1a_k", [1024, T_], BF16), ("l1a_v", [1024, T_], BF16),
             ("l1a_r", [T_, 1024], BF16),
             ("l1a_dp", [1024, T_], F32), ("l1a_wl", [1024, T_], F32),
             ("l1a_rr", [1024, T_], F32)],
        internals=[], outs=[], load_x=True, store_x=True),
    "l1aE": dict(
        emit=lambda ctx, blk, d: emit_L1a_stage(ctx, blk, d, d, 'E', GS),
        ins=[("L1a_ln2g", [1, 1024], F32), ("L1a_ln2b", [1, 1024], F32),
             ("L1a_mixk_c", [128, 8], F32), ("L1a_mixv_c", [128, 8], F32),
             ("L1a_mixr_c", [128, 8], F32), ("L1a_cmixk_c", [128, 8], F32),
             ("L1a_cmixr_c", [128, 8], F32),
             ("L1a_WkcT", [1024, 4096], F32), ("L1a_WrcT", [1024, 1024], F32),
             ("L1a_WvcT", [4096, 1024], F32),
             ("L1a_x2prev", [1, 1024], F32), ("L1a_hvalid", [128, 1], F32),
             ("l1a_x1", [T_, 1024], F32)],
        internals=[("l1a_hrs", [1, 1024], BF16), ("l1a_rc", [T_, 1024], BF16)],
        outs=[], load_x=True, store_x=True),
    "l1f": dict(
        emit=lambda ctx, blk, d: emit_L1f(ctx, blk, d),
        ins=[("L1f_Wg", [1024, 8], F32), ("L1f_bg", [1, 8], F32),
             ("L1f_W1", [8, 1024, 4096], F32), ("L1f_b1c", [8, 128, 32], F32),
             ("L1f_W2", [8, 4096, 1024], F32), ("L1f_b2", [8, 1024], F32)],
        internals=[], outs=[], load_x=True, store_x=True),
    "l2a_pre": dict(
        emit=lambda ctx, blk, d: emit_L2a_pre(ctx, blk, d, d),
        ins=[("L2a_WqT", [1024, 1024], F32), ("L2a_bqc", [128, 8], F32),
             ("L2a_WvT", [1024, 1024], F32), ("L2a_bv", [1, 1024], F32),
             ("L2a_A", [1024, 1024], F32)],
        internals=[],
        outs=[("l2a_tq", [1024, T_], BF16), ("l2a_v", [T_, 1024], BF16),
              ("l2a_qf", [1024, T_], BF16), ("l2a_nrm", [128, T_ // 128], F32)],
        load_x=True, store_x=False),
    "l2a_flash": dict(
        emit=lambda ctx, blk, d: emit_L2a_flash(ctx, blk, d, d, GS),
        ins=[("l2a_qf", [1024, T_], BF16), ("l2a_nrm", [128, T_ // 128], F32),
             ("l2a_tqa", [GS * 1024, T_], BF16), ("l2a_va", [GS * T_, 1024], BF16)],
        internals=[], outs=[], load_x=True, store_x=True),
    "l3a_pre": dict(
        emit=lambda ctx, blk, d: emit_L3a_pre(ctx, blk, d, d),
        ins=[("L3a_Wqkv", [1024, 3072], F32), ("L3a_bqc", [128, 8], F32),
             ("L3a_bkc", [128, 8], F32), ("L3a_bv", [1, 1024], F32),
             ("L3a_kstair", [64, T_], F32)],
        internals=[],
        outs=[("l3a_k", [2048, T_], BF16), ("l3a_v", [T_, 1040], BF16),
              ("l3a_qf", [1024, T_], BF16)],
        load_x=True, store_x=False),
    "l3a_flash": dict(
        emit=lambda ctx, blk, d: emit_L3a_flash2(ctx, blk, d, d, GS),
        ins=[("L3a_qstair", [64, T_], F32), ("L3a_tri", [128, 128], F32),
             ("l3a_qf", [1024, T_], BF16), ("l3a_kd", [2048, T_], BF16),
             ("l3a_vd", [T_, 1040], BF16),
             ("l3a_ka", [GS * 2048, T_], BF16), ("l3a_va", [GS * T_, 1040], BF16)],
        internals=[], outs=[("l3a_ctx", [1024, T_], BF16)],
        load_x=False, store_x=False),
    "l3f": dict(
        emit=lambda ctx, blk, d: (emit_L3a_apply(ctx, blk, d, d), emit_L3f(ctx, blk, d)),
        ins=[("L3a_WpT", [1024, 1024], F32), ("L3a_bp", [1, 1024], F32),
             ("l3a_ctxo", [1024, T_], BF16),
             ("L3f_gT", [1024, 8], F32), ("L3f_Wgu", [9, 1024, 3072], F32),
             ("L3f_Wd", [9, 1536, 1024], F32)],
        internals=[], outs=[], load_x=True, store_x=True),
}
for _l in range(2):
    STAGES[f"l2f_qkv{_l}"] = dict(
        emit=(lambda l: lambda ctx, blk, d: emit_L2f_qkv(ctx, blk, d, d, l))(_l),
        ins=[("L2f_wqkv", [2, 1024, 1536], F32), ("L2f_qs", [2, 64], F32),
             ("L2f_qsr", [2, 64], F32), ("L2f_ks", [2, 64], F32),
             ("L2f_ksr", [2, 64], F32), ("L2f_tbl", [T_, 4, 64], F32),
             ("L2f_xprevc", [128, 8], F32)],
        internals=[],
        outs=[("l2f_qh", [512, T_], BF16), ("l2f_khx", [512, T_], BF16),
              ("l2f_vt", [T_, 512], BF16)],
        load_x=True, store_x=False)
    STAGES[f"l2f_attn{_l}"] = dict(
        emit=(lambda l: lambda ctx, blk, d: emit_L2f_attn(ctx, blk, d, d, l))(_l),
        ins=[("L2f_mleft", [128, 128], F32), ("L2f_mdiag", [128, 128], F32),
             ("L2f_bmask", [4, 128, 512], F32), ("L2f_wo", [2, 512, 1024], F32),
             ("l2f_qh", [512, T_], BF16), ("l2f_khx", [512, T_], BF16),
             ("l2f_vt", [T_, 512], BF16),
             ("L2f_khp", [512, 512], BF16), ("L2f_vhp", [512, 512], BF16)],
        internals=[], outs=[], load_x=True, store_x=True)
    STAGES[f"l2f_ffn{_l}"] = dict(
        emit=(lambda l: lambda ctx, blk, d: emit_L2f_ffn(ctx, blk, d, d, l, final=(l == 1)))(_l),
        ins=[("L2f_lng_c", [2, 128, 8], F32), ("L2f_lnb_c", [2, 128, 8], F32),
             ("L2f_w1", [2, 1024, 5460], F32), ("L2f_b1c", [2, 128, 44], F32),
             ("L2f_w2", [2, 2730, 1024], F32), ("L2f_b2r", [2, 1024], F32),
             ("L2f_x2prevc", [128, 8], F32)]
            + ([("l2f_xin", [T_, 1024], F32)] if _l == 1 else []),
        internals=[], outs=[], load_x=True, store_x=True)

STAGE_ORDER = ["l0a", "l0f", "l1aA", "l1aD1", "l1aD2", "l1aE", "l1f",
               "l2a_pre", "l2a_flash",
               "l2f_qkv0", "l2f_attn0", "l2f_ffn0",
               "l2f_qkv1", "l2f_attn1", "l2f_ffn1",
               "l3a_pre", "l3a_flash", "l3f"]

_NC_CACHE = {}


def build_stage(name):
    if name in _NC_CACHE:
        return _NC_CACHE[name]
    cfg = STAGES[name]
    nc = bass.Bass("TRN2", target_bir_lowering=False, debug=BASS_DEBUG)
    dram = {}
    if cfg["load_x"]:
        dram["xin"] = nc.dram_tensor("xin", [T_FULL, 1024], F32, kind="ExternalInput").ap()
    for n, shp, dt in cfg["ins"]:
        dram[n] = nc.dram_tensor(n, shp, dt, kind="ExternalInput").ap()
    for n, shp, dt in cfg["internals"]:
        dram[n] = nc.dram_tensor(n, shp, dt, kind="Internal").ap()
    for n, shp, dt in cfg["outs"]:
        dram[n] = nc.dram_tensor(n, shp, dt, kind="ExternalOutput").ap()
    if cfg["store_x"]:
        xout = nc.dram_tensor("xout", [T_FULL, 1024], F32, kind="ExternalOutput").ap()
    with contextlib.ExitStack() as es:
        ctx = Ctx(nc, es, T_FULL)
        blk = es.enter_context(nc.Block())
        ctx.init_consts(blk)
        if cfg["load_x"]:
            xg = ctx.hgroup()

            @blk.sync
            def _(s):
                xg.issue(s.dma_start(
                    out=ctx.X[:].rearrange("p (t c) -> p t c", c=1024),
                    in_=dram["xin"].rearrange("(t p) c -> p t c", p=128)))
        ctx.fence_all(blk)
        cfg["emit"](ctx, blk, dram)

        if cfg["store_x"]:
            og = ctx.hgroup()

            @blk.sync
            def _(s):
                ctx.dve.wait(s)
                ctx.act.wait(s)
                ctx.pe.wait(s)
                og.issue(s.dma_start(
                    out=xout.rearrange("(t p) c -> p t c", p=128),
                    in_=ctx.X[:].rearrange("p (t c) -> p t c", c=1024)))
                og.wait(s)
        else:
            @blk.sync
            def _(s):
                ctx.dve.wait(s)
                ctx.act.wait(s)
                ctx.pe.wait(s)
    nc.finalize()
    _NC_CACHE[name] = nc
    return nc


_PREP_CORE_CACHE = None


def _zrow():
    return np.zeros((1, 1024), np.float32)


def _prevrow(arrs, c):
    return _zrow() if c % GS == 0 else np.ascontiguousarray(arrs[c - 1][T_FULL - 1:T_FULL, :])


def _prevcols(arrs, c):
    if c % GS == 0:
        return np.zeros((128, 8), np.float32)
    return np.ascontiguousarray(arrs[c - 1][T_FULL - 1].reshape(8, 128).T.astype(np.float32))


def _gather(rows_by_core, key, c):
    g0 = (c // GS) * GS
    return np.ascontiguousarray(np.concatenate([rows_by_core[r][key] for r in range(g0, g0 + GS)], 0))


def kernel(**inputs):
    global _PREP_CORE_CACHE
    S = prep_shared(inputs)
    if _PREP_CORE_CACHE is None:
        _PREP_CORE_CACHE = [prep_core(c) for c in range(8)]
    C = _PREP_CORE_CACHE
    x = np.ascontiguousarray(inputs["L0_attn_x"], dtype=np.float32)
    xs = [np.ascontiguousarray(x[c // GS, (c % GS) * T_FULL:(c % GS + 1) * T_FULL])
          for c in range(8)]

    def run(name, ex=None, xcur=None):
        nc = build_stage(name)
        cfg = STAGES[name]
        in_maps = []
        for c in range(8):
            m = {}
            if cfg["load_x"]:
                m["xin"] = xcur[c]
            for k, _shp, _dt in cfg["ins"]:
                if ex is not None and k in ex[c]:
                    m[k] = ex[c][k]
                elif k in C[c]:
                    m[k] = C[c][k]
                else:
                    m[k] = S[k]
            in_maps.append(m)
        res = run_bass_kernel_spmd(nc, in_maps, core_ids=list(range(8)))
        return [res.results[c] for c in range(8)]

    r = run("l0a", xcur=xs)
    xs = [r[c]["xout"] for c in range(8)]
    r = run("l0f", xcur=xs)
    x1s = [r[c]["xout"] for c in range(8)]

    ex = [{"L1a_xprev": _prevrow(x1s, c)} for c in range(8)]
    rA = run("l1aA", ex=ex, xcur=x1s)
    ex = [{"l1a_sa": _gather(rA, "l1a_s", c),
           "l1a_k": rA[c]["l1a_k"], "l1a_v": rA[c]["l1a_v"]} for c in range(8)]
    rD1 = run("l1aD1", ex=ex)
    ex = [{"l1a_wra": _gather(rD1, "l1a_wr", c),
           "l1a_k": rA[c]["l1a_k"], "l1a_v": rA[c]["l1a_v"], "l1a_r": rA[c]["l1a_r"],
           "l1a_dp": rD1[c]["l1a_dp"], "l1a_wl": rD1[c]["l1a_wl"],
           "l1a_rr": rD1[c]["l1a_rr"]} for c in range(8)]
    rD2 = run("l1aD2", ex=ex, xcur=x1s)
    x2s = [rD2[c]["xout"] for c in range(8)]
    ex = [{"L1a_x2prev": _prevrow(x2s, c), "l1a_x1": x1s[c]} for c in range(8)]
    r = run("l1aE", ex=ex, xcur=x2s)
    xs = [r[c]["xout"] for c in range(8)]

    r = run("l1f", xcur=xs)
    xs = [r[c]["xout"] for c in range(8)]

    rP = run("l2a_pre", xcur=xs)
    ex = [{"l2a_qf": rP[c]["l2a_qf"], "l2a_nrm": rP[c]["l2a_nrm"],
           "l2a_tqa": _gather(rP, "l2a_tq", c),
           "l2a_va": _gather(rP, "l2a_v", c)} for c in range(8)]
    r = run("l2a_flash", ex=ex, xcur=xs)
    xs = [r[c]["xout"] for c in range(8)]

    xin_l2f = xs
    for l in range(2):
        ex = [{"L2f_xprevc": _prevcols(xs, c)} for c in range(8)]
        rQ = run(f"l2f_qkv{l}", ex=ex, xcur=xs)
        ex = []
        for c in range(8):
            if c % GS == 0:
                khp = np.zeros((512, 512), BF16NP)
                vhp = np.zeros((512, 512), BF16NP)
            else:
                khp = np.ascontiguousarray(rQ[c - 1]["l2f_khx"][:, T_FULL - 512:])
                vhp = np.ascontiguousarray(rQ[c - 1]["l2f_vt"][T_FULL - 512:, :])
            ex.append({"l2f_qh": rQ[c]["l2f_qh"], "l2f_khx": rQ[c]["l2f_khx"],
                       "l2f_vt": rQ[c]["l2f_vt"], "L2f_khp": khp, "L2f_vhp": vhp})
        r = run(f"l2f_attn{l}", ex=ex, xcur=xs)
        xs = [r[c]["xout"] for c in range(8)]
        ex = [{"L2f_x2prevc": _prevcols(xs, c)} for c in range(8)]
        if l == 1:
            for c in range(8):
                ex[c]["l2f_xin"] = xin_l2f[c]
        r = run(f"l2f_ffn{l}", ex=ex, xcur=xs)
        xs = [r[c]["xout"] for c in range(8)]

    rP = run("l3a_pre", xcur=xs)
    QGS = min(512, T_FULL)
    NQG = T_FULL // QGS
    ex = []
    for c in range(8):
        b, q = c // GS, c % GS
        qfx, kd, vd = [], [], []
        for j in range(NQG):
            g = GS * j + q
            sc = GS * b + (g * QGS) // T_FULL
            off = (g * QGS) % T_FULL
            qfx.append(rP[sc]["l3a_qf"][:, off:off + QGS])
            kd.append(rP[sc]["l3a_k"][:, off:off + QGS])
            vd.append(rP[sc]["l3a_v"][off:off + QGS, :])
        ex.append({"l3a_qf": np.ascontiguousarray(np.concatenate(qfx, 1)),
                   "l3a_kd": np.ascontiguousarray(np.concatenate(kd, 1)),
                   "l3a_vd": np.ascontiguousarray(np.concatenate(vd, 0)),
                   "l3a_ka": _gather(rP, "l3a_k", c),
                   "l3a_va": _gather(rP, "l3a_v", c)})
    rF = run("l3a_flash", ex=ex)
    ex = []
    for c in range(8):
        b, q = c // GS, c % GS
        blocks = []
        for i in range(NQG):
            g_own = q * NQG + i
            cp = GS * b + g_own % GS
            jp = g_own // GS
            blocks.append(rF[cp]["l3a_ctx"][:, jp * QGS:(jp + 1) * QGS])
        ex.append({"l3a_ctxo": np.ascontiguousarray(np.concatenate(blocks, 1))})
    r = run("l3f", ex=ex, xcur=xs)
    xs = [r[c]["xout"] for c in range(8)]

    out = np.zeros((2, GS * T_FULL, 1024), np.float32)
    for c in range(8):
        out[c // GS, (c % GS) * T_FULL:(c % GS + 1) * T_FULL] = np.asarray(xs[c])
    return out
